# version 13
# baseline (speedup 1.0000x reference)
import math
import numpy as np
import ml_dtypes

import concourse.bass as bass
import concourse.mybir as mybir
from concourse.bass_utils import run_bass_kernel_spmd

F32 = mybir.dt.float32
BF16 = mybir.dt.bfloat16
I32 = mybir.dt.int32
I16 = mybir.dt.int16
AF = mybir.ActivationFunctionType
ALU = mybir.AluOpType
AX = mybir.AxisListType

S = 4096
DM = 1024
NSLAB = 16
SLAB = 256

BFD = ml_dtypes.bfloat16
LN2 = math.log(2.0)

SR_STAGE = 0
SR_KMAX = 512
SR_KNEG = 514
SR_DELTA = 516
SR_TANH = 518
SR_DQ2P = 520
SR_DQ2N = 522
SR_ONES = 528

KE_CQ = 0
KE_KV = 4096
KE_UVO = 8192
KE_WQC = 10240
KE_BQC = 10752
KE_BFIN = 11776
KE_UQT = 12800
KE_UKT = 13056
KE_UVT = 13312
KE_WQR = 13568


class Sem:
    __slots__ = ("h", "n")

    def __init__(self, h):
        self.h = h
        self.n = 0


def _ap_span(ap):
    if not isinstance(ap, bass.AP):
        return None
    es = mybir.dt.size(ap.dtype)
    pat = list(ap.ap)
    reach = 0
    fel = 1
    for k, (stp, cnt) in enumerate(pat):
        if k == 0:
            continue
        reach += abs(int(stp)) * (int(cnt) - 1)
        fel *= int(cnt)
    lo = int(ap.offset) * es
    return (ap.tensor.name, lo, lo + (reach + 1) * es, fel)


def _overlap(a, b):
    return a[0] == b[0] and a[1] < b[2] and b[1] < a[2]


class Rec:
    ENGS = ("t", "v", "s", "g", "y")
    OWN = {"t": "st", "v": "sv", "s": "ss", "g": "sg"}

    def __init__(self, nc):
        self.nc = nc
        self.q = {e: [] for e in self.ENGS}
        self.sems = {}
        self._hist = {e: [] for e in self.ENGS}

    def sem(self, name):
        s = Sem(self.nc.alloc_semaphore(name=name))
        self.sems[name] = s
        return s

    def op(self, e, meth, *a, w=(), i=(), **kw):
        waits = [(s.h, v) for (s, v) in w if s is not None and v > 0]
        if e in self.OWN and meth not in (None,):
            aps = [x for x in a if isinstance(x, bass.AP)]
            aps += [x for x in kw.values() if isinstance(x, bass.AP)]
            wr = _ap_span(aps[0]) if aps else None
            rds = [s2 for s2 in (_ap_span(x) for x in aps[1:]) if s2 is not None]
            own = self.sems[self.OWN[e]]
            need = 0
            for pw, cnt in self._hist[e]:
                for rd in rds:
                    if _overlap(rd, pw) and (rd[3] < 512 or pw[3] < 512):
                        need = max(need, cnt)
            if need:
                waits.append((own.h, need))
            self._hist[e] = self._hist[e][-1:] + ([(wr, own.n + 1)] if wr else [])
        self.q[e].append((meth, a, kw, waits,
                          [(s.h, d) for (s, d) in i] if i else None))
        for s, d in i:
            s.n += d

    def replay(self, e, eng):
        for meth, a, kw, waits, incs in self.q[e]:
            for h, v in waits:
                eng.wait_ge(h, v)
            if meth is None:
                continue
            try:
                ins = getattr(eng, meth)(*a, **kw)
            except Exception:
                import sys as _sys
                print(f"REPLAY-FAIL eng={e} meth={meth} "
                      f"args={[getattr(x, 'shape', x) for x in a]}", file=_sys.stderr)
                raise
            if incs:
                for h, d in incs:
                    ins.then_inc(h, d)


def host_constants():
    c = {}
    c["c_ident"] = np.eye(128, dtype=BFD)
    c["c_iota16"] = np.tile(np.arange(16, dtype=np.float32)[None, :], (128, 1)).astype(BFD)
    j = np.arange(1024)
    e16 = np.zeros((128, 1024), dtype=np.float32)
    e16[:16] = (j[None, :] // 64 == np.arange(16)[:, None])
    c["c_e16"] = e16.astype(BFD)
    return c


def host_layout(inp):
    d = {}
    f32 = np.float32
    asc = np.ascontiguousarray
    d["w_q"] = asc(inp["L0_attn_Wq"], dtype=f32)
    d["w_k"] = asc(inp["L0_attn_Wk"], dtype=f32)
    d["w_v"] = asc(inp["L0_attn_Wv"], dtype=f32)
    for nm, key in (("b_q", "L0_attn_bq"), ("b_k", "L0_attn_bk"), ("b_v", "L0_attn_bv")):
        d[nm] = asc(np.asarray(inp[key]).reshape(1, 1024), dtype=f32)
    for f in ("q", "k", "v"):
        w1 = np.asarray(inp[f"L0_ffn_{f}_w1"])
        d[f"w1_{f}"] = asc(w1.transpose(1, 0, 2).reshape(1024, 1024), dtype=f32)
        d[f"w2_{f}"] = asc(np.asarray(inp[f"L0_ffn_{f}_w2"]).reshape(1024, 1024), dtype=f32)
        d[f"b1_{f}"] = asc(np.asarray(inp[f"L0_ffn_{f}_b1"]).reshape(1, 1024), dtype=f32)
        d[f"b2_{f}"] = asc(inp[f"L0_ffn_{f}_b2"], dtype=f32)
    nwt = np.concatenate([np.asarray(inp[f"L0_ffn_{f}_node_w"]).T for f in ("q", "k", "v")],
                         axis=1)
    d["node_wt"] = asc(nwt, dtype=f32)
    nb_ = np.concatenate([np.asarray(inp[f"L0_ffn_{f}_node_b"]) for f in ("q", "k", "v")])
    d["node_b"] = asc(nb_.reshape(1, 45), dtype=f32)
    d["w_dq"] = asc(inp["L1_attn_W_DQ"], dtype=f32)
    d["b_dq"] = asc(np.asarray(inp["L1_attn_b_DQ"]).reshape(1, 32), dtype=f32)
    d["w_kv"] = asc(np.concatenate([inp["L1_attn_W_DKV"], inp["L1_attn_W_KR"]], axis=1),
                    dtype=f32)
    d["b_kv"] = asc(np.concatenate([np.asarray(inp["L1_attn_b_DKV"]),
                                    np.asarray(inp["L1_attn_b_KR"])]).reshape(1, 64),
                    dtype=f32)
    d["w_uqT"] = asc(np.asarray(inp["L1_attn_W_UQ"]).T.reshape(8, 128, 32), dtype=f32)
    d["w_ukT"] = asc(np.asarray(inp["L1_attn_W_UK"]).T.reshape(8, 128, 32), dtype=f32)
    d["w_uvT"] = asc(np.asarray(inp["L1_attn_W_UV"]).T.reshape(8, 128, 32), dtype=f32)
    d["w_qr"] = asc(inp["L1_attn_W_QR"], dtype=f32)
    d["b_qr"] = asc(np.asarray(inp["L1_attn_b_QR"]).reshape(1, 256), dtype=f32)
    d["b_uq"] = asc(np.asarray(inp["L1_attn_b_UQ"]).reshape(8, 128), dtype=f32)
    d["b_uv"] = asc(np.asarray(inp["L1_attn_b_UV"]).reshape(8, 128), dtype=f32)
    d["w_o"] = asc(inp["L1_attn_W_O"], dtype=f32)
    d["b_o"] = asc(np.asarray(inp["L1_attn_b_O"]).reshape(1, 1024), dtype=f32)
    d["w_f"] = asc(inp["L1_ffn_weight"], dtype=f32)
    d["b_f"] = asc(np.asarray(inp["L1_ffn_bias"]).reshape(1, 1024), dtype=f32)
    d["delta_w"] = asc(inp["L1_ffn_delta_w"], dtype=f32)
    d["delta_act"] = asc(np.asarray(inp["L1_ffn_delta_act"]).reshape(1, 1), dtype=f32)
    return d


class MK:
    def __init__(self, nc, dbg=None):
        self.nc = nc
        self.dbg = dbg or {}
        self.r = Rec(nc)
        r = self.r
        self.st = r.sem("st")
        self.sv = r.sem("sv")
        self.ss = r.sem("ss")
        self.sg = r.sem("sg")
        self.dy = r.sem("dy")
        self.dg = r.sem("dg")
        self.dq2 = r.sem("dq2")
        self._declare_io()
        self._alloc()

    def _declare_io(self):
        nc = self.nc
        T = nc.dram_tensor
        self.h = {}
        H = self.h
        H["x"] = T("x", [S, DM], F32, kind="ExternalInput")
        H["out"] = T("out", [S, DM], F32, kind="ExternalOutput")
        for nm, sh, dt in (("c_ident", [128, 128], BF16),
                           ("c_iota16", [128, 16], BF16), ("c_e16", [128, 1024], BF16)):
            H[nm] = T(nm, sh, dt, kind="ExternalInput")
        for nm, sh in (
            ("w_q", [DM, DM]), ("w_k", [DM, DM]), ("w_v", [DM, DM]),
            ("b_q", [1, DM]), ("b_k", [1, DM]), ("b_v", [1, DM]),
            ("w1_q", [DM, DM]), ("w1_k", [DM, DM]), ("w1_v", [DM, DM]),
            ("w2_q", [DM, DM]), ("w2_k", [DM, DM]), ("w2_v", [DM, DM]),
            ("b1_q", [1, DM]), ("b1_k", [1, DM]), ("b1_v", [1, DM]),
            ("b2_q", [16, DM]), ("b2_k", [16, DM]), ("b2_v", [16, DM]),
            ("node_wt", [DM, 45]), ("node_b", [1, 45]),
            ("w_dq", [DM, 32]), ("b_dq", [1, 32]),
            ("w_kv", [DM, 64]), ("b_kv", [1, 64]),
            ("w_uqT", [8, 128, 32]), ("w_ukT", [8, 128, 32]), ("w_uvT", [8, 128, 32]),
            ("w_qr", [32, 256]), ("b_qr", [1, 256]),
            ("b_uq", [8, 128]), ("b_uv", [8, 128]),
            ("w_o", [DM, DM]), ("b_o", [1, DM]),
            ("w_f", [DM, DM]), ("b_f", [1, DM]),
            ("delta_w", [DM, 32]), ("delta_act", [1, 1]),
        ):
            H[nm] = T(nm, sh, F32, kind="ExternalInput")
        dump = set(self.dbg.get("dump", ()))
        pre = set(self.dbg.get("preload", ()))

        def scratch(nm, sh, dt):
            kind = "Internal"
            if nm in dump:
                kind = "ExternalOutput"
            elif nm in pre:
                kind = "ExternalInput"
            H[nm] = T(nm, sh, dt, kind=kind)

        scratch("x1", [S, DM], F32)
        scratch("x2", [S, DM], F32)
        scratch("x3", [S, DM], F32)
        scratch("xlnT", [16, 128, 2048], BF16)
        scratch("qTs", [16, 128, 2048], BF16)
        for nm in self.dbg.get("dump_sbuf", ()):
            H["dbg_" + nm] = T("dbg_" + nm, [128, self._dbg_cols(nm)], BF16,
                               kind="ExternalOutput")

    def _dbg_cols(self, nm):
        return {"KT": 8 * S, "VT": 8 * S, "WBa": 8192, "WBb": 8192,
                "TSL": 4096, "LEB": 4096, "SM1": 2048}.get(nm, 4096)

    def _alloc(self):
        nc = self.nc
        A = nc.alloc_sbuf_tensor
        self.KT = A("KT", [128, 8 * S], BF16)
        self.VT = A("VT", [128, 8 * S], BF16)
        self.WBa = A("WBa", [128, 8192], BF16)
        self.WBb = A("WBb", [128, 8192], BF16)
        self.XS = A("XS", [128, 2048], F32)
        self.XO = A("XO", [128, 2048], F32)
        self.TSL = A("TSL", [128, 4096], BF16)
        self.LEB = A("LEB", [128, 4096], BF16)
        self.SM1 = A("SM1", [128, 2048], BF16)
        self.IDN = A("IDN", [128, 128], BF16)
        self.IOTA = A("IOTA", [128, 16], BF16)
        self.E16 = A("E16", [128, 1024], BF16)
        self.ONESC = A("ONESC", [128, 1], BF16)
        self.ONESR = A("ONESR", [128, 512], BF16)
        self.ROWS = A("ROWS", [128, 1024], BF16)
        self.B2C = A("B2C", [128, 1024], BF16)
        self.NWT = A("NWT", [128, 360], BF16)
        self.NBR = A("NBR", [128, 45], BF16)
        self.ROUTE = A("ROUTE", [128, 32], F32)
        self.RSC = A("RSC", [128, 64], F32)
        self.OHT = A("OHT", [128, 256], BF16)
        self.STAT = A("STAT", [128, 16], F32)
        self.SROW = A("SROW", [1, 576], F32)
        self.SCL = A("SCL", [128, 8], F32)
        self.FQ4 = A("FQ4", [128, 128], F32)
        Ap = nc.alloc_psum_tensor
        self.PSA = Ap("PSA", [128, 2048], F32)
        self.PSB = Ap("PSB", [128, 2048], F32)

    def dma(self, q, out, in_, w=(), **kw):
        sem = self.dy if q == "y" else (self.dq2 if q == "s" else self.dg)
        self.r.op(q, "dma_start", w=w, i=((sem, 16),), out=out, in_=in_, **kw)

    def mm(self, out, lhsT, rhs, start=True, stop=True, w=()):
        self.r.op("t", "matmul", out, lhsT, rhs, start=start, stop=stop,
                  w=w, i=((self.st, 1),))

    def mmT(self, out_psum, in_sb, w=()):
        self.mm(out_psum, in_sb, self.IDN[:, :], w=w)

    def act(self, out, in_, func, w=(), **kw):
        self.r.op("s", "activation", out, in_, func, w=w, i=((self.ss, 1),), **kw)

    def ts(self, e, out, in0, s1, s2, op0, op1=None, w=()):
        sem = {"v": self.sv, "g": self.sg}[e]
        if op1 is None:
            self.r.op(e, "tensor_scalar", out, in0, s1, None, op0, w=w, i=((sem, 1),))
        else:
            self.r.op(e, "tensor_scalar", out, in0, s1, s2, op0, op1, w=w, i=((sem, 1),))

    def tt(self, e, out, in0, in1, op, w=()):
        sem = {"v": self.sv, "g": self.sg}[e]
        self.r.op(e, "tensor_tensor", out, in0, in1, op, w=w, i=((sem, 1),))

    def stt(self, e, out, in0, scalar, in1, op0, op1, w=()):
        sem = {"v": self.sv, "g": self.sg}[e]
        self.r.op(e, "scalar_tensor_tensor", out, in0, scalar, in1, op0, op1,
                  w=w, i=((sem, 1),))

    def vcopy(self, out, in_, w=()):
        self.r.op("v", "tensor_copy", out, in_, w=w, i=((self.sv, 1),))

    def vred(self, out, in_, op, w=(), **kw):
        self.r.op("v", "tensor_reduce", out, in_, AX.X, op, w=w, i=((self.sv, 1),), **kw)

    def vrecip(self, out, in_, w=()):
        self.r.op("v", "reciprocal", out, in_, w=w, i=((self.sv, 1),))

    def memset(self, e, ap, val, w=()):
        sem = {"v": self.sv, "g": self.sg}[e]
        self.r.op(e, "memset", ap, val, w=w, i=((sem, 1),))

    def g_t(self):
        return (self.st, self.st.n)

    def g_v(self):
        return (self.sv, self.sv.n)

    def g_s(self):
        return (self.ss, self.ss.n)

    def g_g(self):
        return (self.sg, self.sg.n)

    def g_y(self):
        return (self.dy, self.dy.n)

    def g_gd(self):
        return (self.dg, self.dg.n)

    def fence(self):
        counts = {nm: s.n for nm, s in self.r.sems.items()}
        sems = self.r.sems
        for e in Rec.ENGS:
            own = {"t": "st", "v": "sv", "s": "ss", "g": "sg"}.get(e)
            w = [(sems[nm], v) for nm, v in counts.items() if nm != own and v > 0]
            self.r.op(e, None, w=w)

    def ln_tile(self, xoff, col, out_ap, scratch, w_x=(), w_apply=None):
        ST = self.STAT
        c = 8 * col
        xs = self.XS[:, xoff:xoff + 1024]
        self.vred(ST[:, c + 0:c + 1], xs, ALU.add, w=w_x)
        self.stt("v", scratch, xs, 1.0, xs, ALU.mult, ALU.mult)
        self.vred(ST[:, c + 1:c + 2], scratch, ALU.add)
        self.ts("v", ST[:, c + 2:c + 3], ST[:, c + 0:c + 1], 1.0 / DM, None, ALU.mult)
        self.ts("v", ST[:, c + 3:c + 4], ST[:, c + 1:c + 2], 1.0 / DM, None, ALU.mult)
        self.tt("v", ST[:, c + 4:c + 5], ST[:, c + 2:c + 3], ST[:, c + 2:c + 3], ALU.mult)
        self.tt("v", ST[:, c + 3:c + 4], ST[:, c + 3:c + 4], ST[:, c + 4:c + 5],
                ALU.subtract)
        self.ts("v", ST[:, c + 3:c + 4], ST[:, c + 3:c + 4], 1e-5, None, ALU.add)
        v_ = ST[:, c + 3:c + 4]
        y = ST[:, c + 5:c + 6]
        t = ST[:, c + 6:c + 7]
        self.ts("v", t.bitcast(I32), v_.bitcast(I32), 1, None, ALU.logical_shift_right)
        self.ts("v", y.bitcast(I32), t.bitcast(I32), -1, 0x5F3759DF, ALU.mult, ALU.add)
        for _ in range(2):
            self.tt("v", t, y, y, ALU.mult)
            self.tt("v", t, t, v_, ALU.mult)
            self.ts("v", t, t, -0.5, 1.5, ALU.mult, ALU.add)
            self.tt("v", y, y, t, ALU.mult)
        mb = ST[:, c + 7:c + 8]
        self.stt("v", mb, ST[:, c + 2:c + 3], -1.0, y, ALU.mult, ALU.mult)
        wa = (self.g_v(),) + (tuple(w_apply) if w_apply is not None else (self.g_t(),))
        self.act(out_ap, xs, AF.Identity, bias=mb, scale=y, w=wa)

    def transpose_slab(self, src, base, w_first=()):
        for u in range(2):
            for c in range(8):
                w = ()
                if u == 0 and c == 0:
                    w = tuple(w_first) + (self.g_s(),)
                elif u == 1 and c == 0:
                    w = (self.g_s(),)
                self.mmT(self.PSA[:, 1024 + c * 128:1024 + (c + 1) * 128],
                         src[:, base + u * 1024 + c * 128:base + u * 1024 + (c + 1) * 128],
                         w=w)
            out = bass.AP(self.TSL, u * 128, [[4096, 128], [256, 8], [1, 128]])
            self.act(out, self.PSA[:, 1024:2048], AF.Copy, w=(self.g_t(),))

    def p0(self):
        H = self.h
        self.dma("y", self.IDN[:, :], H["c_ident"][:, :])
        self.dma("y", self.IOTA[:, :], H["c_iota16"][:, :])
        self.dma("y", self.E16[:, :], H["c_e16"][:, :])
        self.memset("v", self.ONESC[:, :], 1.0)
        self.memset("v", self.ONESR[:, :], 0.0)
        self.memset("v", self.ONESR[0:1, :], 1.0)
        self.memset("v", self.ROWS[:, :], 0.0)
        self.memset("v", self.B2C[:, :], 0.0)
        self.memset("v", self.NBR[:, :], 0.0)
        self.memset("v", self.OHT[:, :], 0.0)
        self.memset("v", self.SROW[0:1, SR_STAGE:SR_STAGE + 128], 1.0)
        self.memset("v", self.SROW[0:1, SR_ONES:SR_ONES + 32], 1.0)
        self.dma("g", self.SROW[0:1, SR_DELTA:SR_DELTA + 1], H["delta_act"][:, :])
        gd = self.g_gd()
        self.act(self.SROW[0:1, SR_TANH:SR_TANH + 1],
                 self.SROW[0:1, SR_DELTA:SR_DELTA + 1], AF.Tanh, scale=4.0, w=(gd,))
        self.act(self.SROW[0:1, SR_DQ2P:SR_DQ2P + 1],
                 self.SROW[0:1, SR_TANH:SR_TANH + 1], AF.Exp, scale=0.5 * LN2)
        self.act(self.SROW[0:1, SR_DQ2N:SR_DQ2N + 1],
                 self.SROW[0:1, SR_TANH:SR_TANH + 1], AF.Exp, scale=-0.5 * LN2)
        sg_ = self.g_s()
        self.mm(self.PSB[0:128, 1024:1025], self.SROW[0:1, SR_STAGE:SR_STAGE + 128],
                self.SROW[0:1, SR_DQ2P:SR_DQ2P + 1], w=(sg_, self.g_v()))
        self.mm(self.PSB[0:128, 1025:1026], self.SROW[0:1, SR_STAGE:SR_STAGE + 128],
                self.SROW[0:1, SR_DQ2N:SR_DQ2N + 1])
        self.vcopy(self.SCL[:, 0:2], self.PSB[:, 1024:1026], w=(self.g_t(),))

    def pA1(self):
        H = self.h
        self.dma("g", bass.AP(self.WBa, 0, [[8192, 128], [1024, 8], [1, 1024]]),
                 bass.AP(H["w_q"], 0, [[1024, 128], [131072, 8], [1, 1024]]))
        self.dma("g", bass.AP(self.WBb, 0, [[8192, 128], [1024, 8], [1, 1024]]),
                 bass.AP(H["w_k"], 0, [[1024, 128], [131072, 8], [1, 1024]]))
        self.dma("g", self.ROWS[0:1, :], H["b_q"][:, :])
        self.dma("g", self.B2C[0:1, :], H["b_k"][:, :])
        wg = self.g_gd()
        lnb = self.LEB
        sc_f32 = self.LEB[:, 2048:4096].bitcast(F32)
        xs_free = None
        lnb_free = ((self.st, 0),)
        for s_ in range(NSLAB):
            p2 = s_ % 2
            w = [xs_free] if xs_free else []
            self.dma("y", bass.AP(self.XS, 0, [[2048, 128], [1024, 2], [1, 1024]]),
                     bass.AP(H["x"], s_ * SLAB * DM, [[1024, 128], [131072, 2], [1, 1024]]),
                     w=w)
            xdma = self.g_y()
            for u in range(2):
                self.ln_tile(u * 1024, u, lnb[:, u * 1024:(u + 1) * 1024], sc_f32,
                             w_x=(xdma,) if u == 0 else (), w_apply=lnb_free)
            xs_free = self.g_s()
            self.transpose_slab(lnb, 0)
            lnb_free = (self.g_t(),)
            tev = self.g_s()
            qstage = self.XO[:, p2 * 1024:(p2 + 1) * 1024].bitcast(BF16)
            for oc in range(8):
                for kc in range(8):
                    self.mm(self.PSB[:, oc * 256:(oc + 1) * 256],
                            self.WBa[:, kc * 1024 + oc * 128:kc * 1024 + oc * 128 + 128],
                            self.TSL[:, kc * 256:(kc + 1) * 256],
                            start=(kc == 0), stop=False,
                            w=(tev, wg) if (oc == 0 and kc == 0) else ())
                self.mm(self.PSB[:, oc * 256:(oc + 1) * 256],
                        self.ROWS[:, oc * 128:(oc + 1) * 128], self.ONESR[:, 0:256],
                        start=False, stop=True)
            self.act(qstage, self.PSB[:, :], AF.Copy, w=(self.g_t(), self.g_y()))
            qev = self.g_s()
            for oc in range(8):
                for kc in range(8):
                    self.mm(self.PSB[:, oc * 256:(oc + 1) * 256],
                            self.WBb[:, kc * 1024 + oc * 128:kc * 1024 + oc * 128 + 128],
                            self.TSL[:, kc * 256:(kc + 1) * 256],
                            start=(kc == 0), stop=False,
                            w=(qev,) if (oc == 0 and kc == 0) else ())
                self.mm(self.PSB[:, oc * 256:(oc + 1) * 256],
                        self.B2C[:, oc * 128:(oc + 1) * 128], self.ONESR[:, 0:256],
                        start=False, stop=True)
            self.act(bass.AP(self.KT, s_ * 256, [[32768, 128], [4096, 8], [1, 256]]),
                     self.PSB[:, :], AF.Copy, w=(self.g_t(),))
            self.dma("y", bass.AP(H["xlnT"], s_ * 262144, [[2048, 128], [1, 2048]]),
                     bass.AP(self.TSL, 0, [[4096, 128], [1, 2048]]),
                     w=(tev,))
            self.dma("y", bass.AP(H["qTs"], s_ * 262144, [[2048, 128], [1, 2048]]),
                     qstage,
                     w=(qev,))

    def pA2(self):
        H = self.h
        self.dma("g", bass.AP(self.WBa, 0, [[8192, 128], [1024, 8], [1, 1024]]),
                 bass.AP(H["w_v"], 0, [[1024, 128], [131072, 8], [1, 1024]]))
        self.dma("g", self.ROWS[0:1, :], H["b_v"][:, :])
        wg = self.g_gd()
        for half in range(2):
            self.mm(self.PSA[:, half * 512:(half + 1) * 512], self.ONESR[:, 0:128],
                    self.ROWS[:, half * 512:(half + 1) * 512],
                    w=(wg,) if half == 0 else ())
        self.vcopy(self.SM1[:, 0:1024], self.PSA[:, 0:1024], w=(self.g_t(),))
        bvt = self.g_v()
        tsl_free = None
        for s_ in range(NSLAB):
            w = [tsl_free] if tsl_free else []
            self.dma("y", bass.AP(self.TSL, 0, [[4096, 128], [1, 2048]]),
                     bass.AP(H["xlnT"], s_ * 262144, [[2048, 128], [1, 2048]]),
                     w=w)
            xdma = self.g_y()
            for u in range(2):
                po = u * 1024
                for kc in range(8):
                    for half in range(2):
                        self.mm(self.PSA[:, po + half * 512:po + (half + 1) * 512],
                                self.TSL[:, kc * 256 + u * 128:kc * 256 + u * 128 + 128],
                                self.WBa[:, kc * 1024 + half * 512:
                                         kc * 1024 + (half + 1) * 512],
                                start=(kc == 0), stop=(kc == 7),
                                w=(xdma, self.g_v()) if kc == 0 and half == 0 else ())
                j = 2 * s_ + u
                self.tt("v", self.VT[:, j * 1024:(j + 1) * 1024],
                        self.PSA[:, po:po + 1024], self.SM1[:, 0:1024], ALU.add,
                        w=(self.g_t(), bvt))
            tsl_free = self.g_t()

    def pB(self):
        H = self.h
        exp_free = [None, None]
        qts_free = {}
        xs_free = None
        xo_gate = None
        leb_h0 = [None, None]
        leb_all = [None, None]
        vg = {}
        SM2 = self.SM1[:, 1536:2048]

        def tmax(a, b):
            if a is None:
                return b
            if b is None:
                return a
            return (a[0], max(a[1], b[1]))

        for nb in range(16):
            p = nb % 2
            self.dma("s", bass.AP(self.TSL, p * 2048, [[4096, 128], [1, 2048]]),
                     bass.AP(H["qTs"], nb * 262144, [[2048, 128], [1, 2048]]),
                     w=[qts_free[p]] if p in qts_free else [])
            qdma = (self.dq2, self.dq2.n)
            self.dma("y", bass.AP(self.XS, 0, [[2048, 128], [1024, 2], [1, 1024]]),
                     bass.AP(H["x"], nb * 256 * DM, [[1024, 128], [131072, 2], [1, 1024]]),
                     w=[xs_free] if xs_free else [])
            xsdma = self.g_y()
            for m in range(34):
                if m < 32:
                    mp = m % 2
                    sc_half = [None, None]
                    for halfg in range(2):
                        wlist = []
                        if exp_free[halfg] is not None:
                            wlist.append(exp_free[halfg])
                        if m == 0 and halfg == 0:
                            wlist.append(qdma)
                        for gg in range(4):
                            g = halfg * 4 + gg
                            self.mm(self.PSA[:, halfg * 1024 + gg * 256:
                                             halfg * 1024 + (gg + 1) * 256],
                                    self.KT[:, g * S + m * 128:g * S + (m + 1) * 128],
                                    self.TSL[:, p * 2048 + g * 256:
                                             p * 2048 + (g + 1) * 256],
                                    w=wlist if gg == 0 else ())
                        sc_half[halfg] = self.g_t()
                if m >= 2:
                    mc = m - 2
                    mpc = mc % 2
                    vn, gn = vg[mc]
                    for g in range(8):
                        for ns in range(2):
                            wlist = []
                            if g == 0 and ns == 0:
                                wlist = [vn]
                                if mc == 0 and xo_gate is not None:
                                    wlist.append(self.g_v())
                            elif g == 6 and ns == 0:
                                wlist = [gn]
                            self.mm(self.PSB[:, ns * 1024 + g * 128:
                                             ns * 1024 + (g + 1) * 128],
                                    self.LEB[:, mpc * 2048 + g * 256 + ns * 128:
                                             mpc * 2048 + g * 256 + ns * 128 + 128],
                                    self.VT[:, mc * 1024 + g * 128:
                                            mc * 1024 + (g + 1) * 128],
                                    start=(mc == 0 and g % 4 == 0), stop=(mc == 31),
                                    w=wlist)
                        if g == 3:
                            leb_h0[mpc] = self.g_t()
                    leb_all[mpc] = self.g_t()
                if m < 32:
                    e_half = [None, None]
                    for halfg in range(2):
                        if halfg == 0:
                            wexp = [tmax(sc_half[1], leb_h0[mp])]
                        else:
                            wexp = [tmax(sc_half[1], leb_all[mp])]
                        self.act(self.LEB[:, mp * 2048 + halfg * 1024:
                                          mp * 2048 + (halfg + 1) * 1024],
                                 self.PSA[:, halfg * 1024:(halfg + 1) * 1024], AF.Exp,
                                 scale=1.0 / 32.0, w=wexp)
                        exp_free[halfg] = self.g_s()
                        e_half[halfg] = self.g_s()
                    eb = mp * 2048
                    so = mp * 256
                    self.tt("v", bass.AP(self.SM1, 0, [[2048, 128], [256, 2], [1, 256]]),
                            bass.AP(self.LEB, eb, [[4096, 128], [512, 2], [1, 256]]),
                            bass.AP(self.LEB, eb + 256, [[4096, 128], [512, 2], [1, 256]]),
                            ALU.add, w=(e_half[0],))
                    self.tt("v", bass.AP(self.SM1, 512, [[2048, 128], [256, 2], [1, 256]]),
                            bass.AP(self.LEB, eb + 1024, [[4096, 128], [512, 2], [1, 256]]),
                            bass.AP(self.LEB, eb + 1280, [[4096, 128], [512, 2], [1, 256]]),
                            ALU.add, w=(e_half[1],))
                    self.tt("v", bass.AP(self.SM1, 1024, [[2048, 128], [256, 2], [1, 256]]),
                            bass.AP(self.SM1, 0, [[2048, 128], [512, 2], [1, 256]]),
                            bass.AP(self.SM1, 256, [[2048, 128], [512, 2], [1, 256]]),
                            ALU.add)
                    self.tt("v", SM2[:, so:so + 256], self.SM1[:, 1024:1280],
                            self.SM1[:, 1280:1536], ALU.add)
                    D = SM2[:, so:so + 256]
                    seed = self.SM1[:, 0:256].bitcast(I16)
                    t = self.SM1[:, 256:512]
                    self.ts("v", seed, D.bitcast(I16), -1, 0x7EF4, ALU.mult, ALU.add)
                    r0 = seed.bitcast(BF16)
                    self.tt("v", t, D, r0, ALU.mult)
                    self.ts("v", t, t, -1.0, 2.0, ALU.mult, ALU.add)
                    self.tt("v", SM2[:, so:so + 256], r0, t, ALU.mult)
                    rn = self.g_v()
                    rd6 = bass.AP(self.SM1, 1536 + so, [[2048, 128], [0, 6], [1, 256]])
                    self.tt("v", bass.AP(self.LEB, eb, [[4096, 128], [256, 6], [1, 256]]),
                            bass.AP(self.LEB, eb, [[4096, 128], [256, 6], [1, 256]]),
                            rd6, ALU.mult)
                    vn = self.g_v()
                    rd2 = bass.AP(self.SM1, 1536 + so, [[2048, 128], [0, 2], [1, 256]])
                    self.tt("g", bass.AP(self.LEB, eb + 1536, [[4096, 128], [256, 2], [1, 256]]),
                            bass.AP(self.LEB, eb + 1536, [[4096, 128], [256, 2], [1, 256]]),
                            rd2, ALU.mult, w=(rn,))
                    gn = self.g_g()
                    vg[m] = (vn, gn)
            ctx_done = self.g_t()
            qts_free[p] = ctx_done
            for u in range(2):
                wl = [ctx_done, xsdma] if u == 0 else []
                if u == 0 and xo_gate is not None:
                    wl.append(xo_gate)
                self.stt("v", self.XO[:, u * 1024:(u + 1) * 1024],
                         self.PSB[:, u * 1024:(u + 1) * 1024], 1.0,
                         self.XS[:, u * 1024:(u + 1) * 1024],
                         ALU.mult, ALU.add, w=wl)
            xs_free = self.g_v()
            self.dma("y", bass.AP(H["x1"], nb * 256 * DM, [[1024, 128], [131072, 2], [1, 1024]]),
                     bass.AP(self.XO, 0, [[2048, 128], [1024, 2], [1, 1024]]),
                     w=(self.g_v(),))
            xo_gate = self.g_y()

    def fff(self, f, s_, p2, dest, tbase=0, w_tsl=()):
        H = self.h
        fo = {"q": 0, "k": 1, "v": 2}[f]
        for u in range(2):
            po = 1024 + u * 256
            for kc in range(8):
                self.mm(self.PSA[:, po:po + 15],
                        self.TSL[:, tbase + kc * 256 + u * 128:
                                 tbase + kc * 256 + u * 128 + 128],
                        self.NWT[:, kc * 45 + fo * 15:kc * 45 + fo * 15 + 15],
                        start=(kc == 0), stop=False,
                        w=(self.g_s(), self.g_v()) + tuple(w_tsl)
                        if (u == 0 and kc == 0) else ())
            self.mm(self.PSA[:, po:po + 15], self.ONESR[:, 0:128],
                    self.NBR[:, fo * 15:fo * 15 + 15], start=False, stop=True)
        tr = self.g_t()
        for u in range(2):
            self.vcopy(self.ROUTE[:, u * 15:u * 15 + 15],
                       self.PSA[:, 1024 + u * 256:1024 + u * 256 + 15],
                       w=(tr,) if u == 0 else ())
        R = self.ROUTE
        C = self.RSC

        def Sv(j):
            return bass.AP(R, j, [[32, 128], [15, 2], [1, 1]])

        def c2(a):
            return C[:, a:a + 2]

        b0, b1, b2, b3 = c2(32), c2(34), c2(36), c2(38)
        c0, c1, cc2 = c2(40), c2(42), c2(44)
        sel, d, t = c2(46), c2(48), c2(50)
        n = [c2(52 + 2 * k) for k in range(4)]
        leaf = c2(60)
        mq = [c2(0 + 2 * k) for k in range(8)]
        Z = self.RSC[:, 16:18]
        O1 = self.RSC[:, 18:20]
        self.tt("v", b0, Sv(0), Z, ALU.is_ge, w=(self.g_v(), self.g_g()))
        self.tt("v", d, Sv(2), Sv(1), ALU.subtract)
        self.tt("v", sel, d, b0, ALU.mult)
        self.tt("v", sel, sel, Sv(1), ALU.add)
        self.tt("v", b1, sel, Z, ALU.is_ge)
        self.tt("v", c0, O1, b0, ALU.subtract)
        self.tt("v", c1, O1, b1, ALU.subtract)
        self.tt("v", n[0], c0, c1, ALU.mult)
        self.tt("v", n[1], c0, b1, ALU.mult)
        self.tt("v", n[2], b0, c1, ALU.mult)
        self.tt("v", n[3], b0, b1, ALU.mult)
        self.tt("v", sel, Sv(3), n[0], ALU.mult)
        for k, jj in enumerate((4, 5, 6)):
            self.tt("v", t, Sv(jj), n[k + 1], ALU.mult)
            self.tt("v", sel, sel, t, ALU.add)
        self.tt("v", b2, sel, Z, ALU.is_ge)
        self.tt("v", cc2, O1, b2, ALU.subtract)
        for k in range(4):
            self.tt("v", mq[2 * k], n[k], cc2, ALU.mult)
            self.tt("v", mq[2 * k + 1], n[k], b2, ALU.mult)
        self.tt("v", sel, Sv(7), mq[0], ALU.mult)
        for k in range(1, 8):
            self.tt("v", t, Sv(7 + k), mq[k], ALU.mult)
            self.tt("v", sel, sel, t, ALU.add)
        self.tt("v", b3, sel, Z, ALU.is_ge)
        self.tt("v", leaf, b0, b0, ALU.add)
        self.tt("v", leaf, leaf, b1, ALU.add)
        self.tt("v", leaf, leaf, leaf, ALU.add)
        self.tt("v", leaf, leaf, b2, ALU.add)
        self.tt("v", leaf, leaf, leaf, ALU.add)
        self.tt("v", leaf, leaf, b3, ALU.add)
        for u in range(2):
            self.ts("v", self.SM1[:, 2016 + u * 16:2016 + (u + 1) * 16], self.IOTA[:, :],
                    leaf[:, u:u + 1], None, ALU.is_equal,
                    w=(self.g_t(),) if u == 0 else ())
        ohv = self.g_v()
        w1_half = None
        for oc in range(8):
            for kc in range(8):
                self.mm(self.PSB[:, oc * 256:(oc + 1) * 256],
                        self.WBa[:, kc * 1024 + oc * 128:kc * 1024 + oc * 128 + 128],
                        self.TSL[:, tbase + kc * 256:tbase + (kc + 1) * 256],
                        start=(kc == 0), stop=False,
                        w=(self.g_s(),) if (oc == 0 and kc == 0) else ())
            self.mm(self.PSB[:, oc * 256:(oc + 1) * 256],
                    self.ROWS[:, oc * 128:(oc + 1) * 128],
                    self.ONESR[:, 0:256], start=False, stop=True)
            if oc == 3:
                w1_half = self.g_t()
        w1_done = self.g_t()
        for u in range(2):
            self.mm(self.PSA[0:16, 1536 + u * 128:1536 + (u + 1) * 128],
                    self.SM1[:, 2016 + u * 16:2016 + (u + 1) * 16], self.IDN[:, :],
                    w=(ohv,) if u == 0 else ())
        ohmm = self.g_t()
        self.vcopy(self.OHT[0:16, 0:256], self.PSA[0:16, 1536:1792], w=(ohmm,))
        oht = self.g_v()
        for c in range(8):
            dst = c * 256 if c < 4 else 1024 + (c - 4) * 256
            self.mm(self.PSA[:, dst:dst + 256],
                    self.E16[:, c * 128:(c + 1) * 128], self.OHT[:, 0:256],
                    w=(oht, self.g_v()) if c == 0 else ())
        maskmm = self.g_t()
        relu_half = [None, None]
        for half in range(2):
            wre = [w1_half if half == 0 else ohmm]
            if half == 0:
                wre.append(self.g_v())
            self.act(self.SM1[:, half * 1024:(half + 1) * 1024],
                     self.PSB[:, half * 1024:(half + 1) * 1024], AF.Relu, w=wre)
            relu_half[half] = self.g_s()
        hm = [None, None]
        for half in range(2):
            self.tt("v", self.SM1[:, half * 1024:(half + 1) * 1024],
                    self.SM1[:, half * 1024:(half + 1) * 1024],
                    self.PSA[:, half * 1024:(half + 1) * 1024], ALU.mult,
                    w=(maskmm, relu_half[half]))
            hm[half] = self.g_v()
        if dest in ("qT", "kT"):
            for kc in range(8):
                for oc in range(8):
                    wlist = ()
                    if oc == 0 and kc == 0:
                        wlist = (hm[0],)
                    elif oc == 0 and kc == 4:
                        wlist = (hm[1],)
                    self.mm(self.PSB[:, oc * 256:(oc + 1) * 256],
                            self.WBb[:, kc * 1024 + oc * 128:kc * 1024 + oc * 128 + 128],
                            self.SM1[:, kc * 256:(kc + 1) * 256],
                            start=(kc == 0), stop=False, w=wlist)
            for oc in range(8):
                self.mm(self.PSB[:, oc * 256:(oc + 1) * 256],
                        self.B2C[:, oc * 128:(oc + 1) * 128], self.OHT[:, 0:256],
                        start=False, stop=True)
            if dest == "qT":
                qstage = self.XO[:, p2 * 1024:(p2 + 1) * 1024].bitcast(BF16)
                self.act(qstage, self.PSB[:, :], AF.Copy, w=(self.g_t(), self.g_y()))
                qev = self.g_s()
                self.dma("y", bass.AP(H["qTs"], s_ * 262144, [[2048, 128], [1, 2048]]),
                         qstage,
                         w=(qev,))
            else:
                self.act(bass.AP(self.KT, s_ * 256, [[32768, 128], [4096, 8], [1, 256]]),
                         self.PSB[:, :], AF.Copy, w=(self.g_t(),))
        else:
            for u in range(2):
                po = u * 1024
                for kc in range(8):
                    for half in range(2):
                        wlist = ()
                        if u == 0 and half == 0 and kc == 0:
                            wlist = (hm[0],)
                        elif u == 0 and half == 0 and kc == 4:
                            wlist = (hm[1],)
                        self.mm(self.PSB[:, po + half * 512:po + (half + 1) * 512],
                                self.SM1[:, kc * 256 + u * 128:kc * 256 + u * 128 + 128],
                                self.WBb[:, kc * 1024 + half * 512:
                                         kc * 1024 + (half + 1) * 512],
                                start=(kc == 0), stop=False, w=wlist)
                for half in range(2):
                    self.mm(self.PSB[:, po + half * 512:po + (half + 1) * 512],
                            self.OHT[:, u * 128:(u + 1) * 128],
                            self.B2C[:, half * 512:(half + 1) * 512],
                            start=False, stop=True)
                j = 2 * s_ + u
                self.act(self.VT[:, j * 1024:(j + 1) * 1024],
                         self.PSB[:, po:po + 1024], AF.Copy, w=(self.g_t(),))
        return w1_done

    def _load_fff_weights(self, f):
        H = self.h
        self.memset("v", self.RSC[:, 16:18], 0.0, w=(self.g_t(),))
        self.memset("v", self.RSC[:, 18:20], 1.0)
        self.dma("g", bass.AP(self.WBa, 0, [[8192, 128], [1024, 8], [1, 1024]]),
                 bass.AP(H[f"w1_{f}"], 0, [[1024, 128], [131072, 8], [1, 1024]]))
        self.dma("g", bass.AP(self.WBb, 0, [[8192, 128], [1024, 8], [1, 1024]]),
                 bass.AP(H[f"w2_{f}"], 0, [[1024, 128], [131072, 8], [1, 1024]]))
        self.dma("g", self.ROWS[0:1, :], H[f"b1_{f}"][:, :])
        self.dma("g", self.B2C[0:16, :], H[f"b2_{f}"][:, :])
        return self.g_gd()

    def pC1(self):
        H = self.h
        self.memset("v", self.OHT[:, :], 0.0)
        wg = self._load_fff_weights("q")
        self.dma("g", bass.AP(self.NWT, 0, [[360, 128], [45, 8], [1, 45]]),
                 bass.AP(H["node_wt"], 0, [[45, 128], [45 * 128, 8], [1, 45]]))
        self.dma("g", self.NBR[0:1, 0:45], H["node_b"][:, :])
        lnb = self.LEB
        sc_f32 = self.LEB[:, 2048:4096].bitcast(F32)
        xs_free = [None, None]
        lnb_free = ((self.st, 0),)
        for s_ in range(NSLAB):
            p2 = s_ % 2
            xdma = [None, None]
            for u in range(2):
                w = [xs_free[u]] if xs_free[u] else []
                if s_ == 0 and u == 0:
                    w.append(wg)
                self.dma("s", bass.AP(self.XS, u * 1024, [[2048, 128], [1, 1024]]),
                         bass.AP(H["x1"], s_ * SLAB * DM + u * 131072,
                                 [[1024, 128], [1, 1024]]),
                         w=w)
                xdma[u] = (self.dq2, self.dq2.n)
            for u in range(2):
                self.ln_tile(u * 1024, u, lnb[:, u * 1024:(u + 1) * 1024], sc_f32,
                             w_x=(xdma[u],), w_apply=lnb_free)
                xs_free[u] = self.g_s()
            self.transpose_slab(lnb, 0, w_first=(self.g_v(),))
            lnb_free = (self.g_t(),)
            tev = self.g_s()
            self.dma("y", bass.AP(H["xlnT"], s_ * 262144, [[2048, 128], [1, 2048]]),
                     bass.AP(self.TSL, 0, [[4096, 128], [1, 2048]]),
                     w=(tev,))
            self.fff("q", s_, p2, "qT")

    def pC1b(self):
        H = self.h
        wg = self._load_fff_weights("k")
        tsl_free = {}
        for s_ in range(NSLAB):
            pr = s_ % 2
            w = [tsl_free[pr]] if pr in tsl_free else []
            if s_ == 0:
                w.append(wg)
            self.dma("y", bass.AP(self.TSL, pr * 2048, [[4096, 128], [1, 2048]]),
                     bass.AP(H["xlnT"], s_ * 262144, [[2048, 128], [1, 2048]]), w=w)
            tdma = self.g_y()
            tsl_free[pr] = self.fff("k", s_, pr, "kT", tbase=pr * 2048,
                                    w_tsl=(tdma,))

    def pC2(self):
        H = self.h
        wg = self._load_fff_weights("v")
        tsl_free = {}
        for s_ in range(NSLAB):
            pr = s_ % 2
            w = [tsl_free[pr]] if pr in tsl_free else []
            if s_ == 0:
                w.append(wg)
            self.dma("y", bass.AP(self.TSL, pr * 2048, [[4096, 128], [1, 2048]]),
                     bass.AP(H["xlnT"], s_ * 262144, [[2048, 128], [1, 2048]]), w=w)
            tdma = self.g_y()
            tsl_free[pr] = self.fff("v", s_, pr, "v", tbase=pr * 2048,
                                    w_tsl=(tdma,))

    def pD(self):
        H = self.h
        QSQ = self.SM1
        for piece in range(8):
            for c in range(8):
                self.tt("v", QSQ[:, 0:512],
                        self.KT[:, c * S + piece * 512:c * S + (piece + 1) * 512],
                        self.KT[:, c * S + piece * 512:c * S + (piece + 1) * 512],
                        ALU.mult, w=(self.g_t(),))
                self.mm(self.PSA[0:1, 0:512], self.ONESC[:, 0:1], QSQ[:, 0:512],
                        start=(c == 0), stop=(c == 7), w=(self.g_v(),))
            self.act(self.SROW[0:1, SR_STAGE:SR_STAGE + 512], self.PSA[0:1, 0:512],
                     AF.Sqrt, w=(self.g_t(),))
            if piece == 0:
                self.vred(self.SROW[0:1, SR_KMAX:SR_KMAX + 1],
                          self.SROW[0:1, SR_STAGE:SR_STAGE + 512], ALU.max,
                          w=(self.g_s(),))
            else:
                self.vred(self.SROW[0:1, SR_KMAX + 1:SR_KMAX + 2],
                          self.SROW[0:1, SR_STAGE:SR_STAGE + 512], ALU.max,
                          w=(self.g_s(),))
                self.tt("v", self.SROW[0:1, SR_KMAX:SR_KMAX + 1],
                        self.SROW[0:1, SR_KMAX:SR_KMAX + 1],
                        self.SROW[0:1, SR_KMAX + 1:SR_KMAX + 2], ALU.max)
        self.ts("v", self.SROW[0:1, SR_KNEG:SR_KNEG + 1],
                self.SROW[0:1, SR_KMAX:SR_KMAX + 1], -1.0, None, ALU.mult)
        kstar = self.g_v()
        exp_free = [None, None]
        leb_free = [None, None]
        qts_free = {}
        xs_free = None
        xo_gate = None
        for nb in range(16):
            p = nb % 2
            self.dma("y", bass.AP(self.TSL, p * 2048, [[4096, 128], [1, 2048]]),
                     bass.AP(H["qTs"], nb * 262144, [[2048, 128], [1, 2048]]),
                     w=[qts_free[p]] if p in qts_free else [])
            self.dma("y", bass.AP(self.XS, 0, [[2048, 128], [1024, 2], [1, 1024]]),
                     bass.AP(H["x1"], nb * 256 * DM, [[1024, 128], [131072, 2], [1, 1024]]),
                     w=[xs_free] if xs_free else [])
            qdma = self.g_y()
            self.tt("v", QSQ[:, 0:2048], self.TSL[:, p * 2048:p * 2048 + 2048],
                    self.TSL[:, p * 2048:p * 2048 + 2048], ALU.mult, w=(qdma, kstar))
            qsqv = self.g_v()
            for c in range(8):
                self.mm(self.PSA[0:1, 1280:1536], self.ONESC[:, 0:1],
                        QSQ[:, c * 256:(c + 1) * 256], start=(c == 0), stop=(c == 7),
                        w=(qsqv, self.g_s()) if c == 0 else ())
            self.act(self.SROW[0:1, SR_STAGE:SR_STAGE + 256], self.PSA[0:1, 1280:1536],
                     AF.Sqrt, w=(self.g_t(),))
            self.ts("v", self.ROWS[0:1, 0:256], self.SROW[0:1, SR_STAGE:SR_STAGE + 256],
                    self.SROW[0:1, SR_KNEG:SR_KNEG + 1], None, ALU.mult, w=(self.g_s(),))
            negb = self.g_v()
            expd_p = [None, None]
            for pair in range(17):
                if pair < 16:
                    for mi in range(2):
                        m = pair * 2 + mi
                        pcol = (pair % 2) * 512 + mi * 256
                        wlist = []
                        if mi == 0 and exp_free[pair % 2] is not None:
                            wlist.append(exp_free[pair % 2])
                        if m == 0:
                            wlist.append(negb)
                        for kc in range(8):
                            self.mm(self.PSA[:, pcol:pcol + 256],
                                    self.KT[:, kc * S + m * 128:kc * S + (m + 1) * 128],
                                    self.TSL[:, p * 2048 + kc * 256:
                                             p * 2048 + (kc + 1) * 256],
                                    start=(kc == 0), stop=False,
                                    w=wlist if kc == 0 else ())
                        self.mm(self.PSA[:, pcol:pcol + 256], self.ONESR[:, 0:128],
                                self.ROWS[:, 0:256], start=False, stop=True)
                    sc_done = self.g_t()
                    wexp = [sc_done]
                    if leb_free[pair % 2] is not None:
                        wexp.append(leb_free[pair % 2])
                    self.act(self.LEB[:, (pair % 2) * 512:(pair % 2) * 512 + 512],
                             self.PSA[:, (pair % 2) * 512:(pair % 2) * 512 + 512],
                             AF.Exp, scale=1.0 / 32.0, w=wexp)
                    expd_p[pair % 2] = self.g_s()
                    exp_free[pair % 2] = expd_p[pair % 2]
                if pair >= 1:
                    pp = pair - 1
                    expd = expd_p[pp % 2]
                    for mi in range(2):
                        mm_ = pp * 2 + mi
                        lb = (pp % 2) * 512 + mi * 256
                        for ns in range(2):
                            wl = []
                            if mi == 0 and ns == 0:
                                wl.append(expd)
                                if mm_ == 0 and xo_gate is not None:
                                    wl.append(self.g_v())
                            for half in range(2):
                                self.mm(self.PSB[:, ns * 1024 + half * 512:
                                                 ns * 1024 + (half + 1) * 512],
                                        self.LEB[:, lb + ns * 128:lb + ns * 128 + 128],
                                        self.VT[:, mm_ * 1024 + half * 512:
                                                mm_ * 1024 + (half + 1) * 512],
                                        start=(mm_ == 0), stop=(mm_ == 31),
                                        w=wl if half == 0 else ())
                        self.mm(self.PSA[0:1, 1024:1280], self.ONESC[:, 0:1],
                                self.LEB[:, lb:lb + 256],
                                start=(mm_ == 0), stop=(mm_ == 31))
                    leb_free[pp % 2] = self.g_t()
            ctx_done = self.g_t()
            qts_free[p] = ctx_done
            self.act(self.ROWS[0:1, 256:512], self.PSA[0:1, 1024:1280], AF.Copy,
                     w=(ctx_done,))
            drow = self.g_s()
            for u in range(2):
                self.mm(self.PSA[0:128, 1536 + u:1537 + u],
                        self.ROWS[:, 256 + u * 128:256 + (u + 1) * 128],
                        self.ONESR[:, 0:1], w=(drow,) if u == 0 else ())
            self.vrecip(self.SCL[:, 2:4], self.PSA[:, 1536:1538], w=(self.g_t(),))
            for u in range(2):
                wl = []
                if u == 0 and xo_gate is not None:
                    wl.append(xo_gate)
                self.stt("v", self.XO[:, u * 1024:(u + 1) * 1024],
                         self.PSB[:, u * 1024:(u + 1) * 1024], self.SCL[:, 2 + u:3 + u],
                         self.XS[:, u * 1024:(u + 1) * 1024],
                         ALU.mult, ALU.add, w=wl)
            xs_free = self.g_v()
            self.dma("y", bass.AP(H["x2"], nb * 256 * DM, [[1024, 128], [131072, 2], [1, 1024]]),
                     bass.AP(self.XO, 0, [[2048, 128], [1024, 2], [1, 1024]]),
                     w=(self.g_v(),))
            xo_gate = self.g_y()

    def pE(self):
        H = self.h
        KT = self.KT
        self.memset("v", KT[:, KE_CQ:KE_CQ + 4096], 0.0, w=(self.g_t(),))
        self.memset("v", KT[:, KE_KV:KE_KV + S], 0.0)
        self.memset("v", KT[:, KE_WQC:KE_WQC + 512], 0.0)
        self.memset("v", KT[:, KE_BQC:KE_BQC + 1024], 0.0)
        self.memset("v", KT[:, KE_BFIN:KE_BFIN + 1024], 0.0)
        self.memset("v", self.VT[:, 2048:6144], 0.0)
        self.dma("g", bass.AP(self.WBa, 0, [[8192, 128], [1024, 8], [1, 1024]]),
                 bass.AP(H["w_o"], 0, [[1024, 128], [131072, 8], [1, 1024]]))
        for nm, col in (("w_uqT", KE_UQT), ("w_ukT", KE_UKT), ("w_uvT", KE_UVT)):
            self.dma("g", bass.AP(KT, col, [[32768, 128], [32, 8], [1, 32]]),
                     bass.AP(H[nm], 0, [[32, 128], [4096, 8], [1, 32]]))
        self.dma("g", KT[0:32, KE_WQR:KE_WQR + 256], H["w_qr"][:, :])
        self.dma("g", bass.AP(self.STAT, 0, [[16, 128], [1, 8]]),
                 bass.AP(H["b_uq"], 0, [[1, 128], [128, 8]]),
                 allow_slow_non_contiguous=True)
        self.dma("g", bass.AP(self.STAT, 8, [[16, 128], [1, 8]]),
                 bass.AP(H["b_uv"], 0, [[1, 128], [128, 8]]),
                 allow_slow_non_contiguous=True)
        self.dma("g", self.SROW[0:1, SR_STAGE:SR_STAGE + 256], H["b_qr"][:, :])
        self.dma("g", self.SM1[0:1, 1024:2048], H["b_o"][:, :])
        gd = self.g_gd()
        self.vcopy(self.SM1[:, 0:16], self.STAT[:, 0:16], w=(gd,))
        self.vcopy(bass.AP(KT, KE_BQC + 32, [[32768, 1], [64, 8], [1, 32]]),
                   bass.AP(self.SROW, SR_STAGE, [[576, 1], [32, 8], [1, 32]]))
        for h8 in range(8):
            uq = KT[:, KE_UQT + h8 * 32:KE_UQT + (h8 + 1) * 32]
            uk = KT[:, KE_UKT + h8 * 32:KE_UKT + (h8 + 1) * 32]
            self.mm(self.PSA[0:32, h8 * 64:h8 * 64 + 32], uq, uk,
                    w=(gd,) if h8 == 0 else ())
            self.mm(self.PSA[0:1, h8 * 64 + 32:h8 * 64 + 64],
                    self.SM1[:, h8:h8 + 1], uk,
                    w=(self.g_v(),) if h8 == 0 else ())
        tg = self.g_t()
        for h8 in range(8):
            self.vcopy(KT[0:32, KE_WQC + h8 * 64:KE_WQC + h8 * 64 + 32],
                       self.PSA[0:32, h8 * 64:h8 * 64 + 32], w=(tg,) if h8 == 0 else ())
            self.vcopy(KT[0:1, KE_BQC + h8 * 64:KE_BQC + h8 * 64 + 32],
                       self.PSA[0:1, h8 * 64 + 32:h8 * 64 + 64])
            self.vcopy(KT[0:32, KE_WQC + h8 * 64 + 32:KE_WQC + (h8 + 1) * 64],
                       KT[0:32, KE_WQR + h8 * 32:KE_WQR + (h8 + 1) * 32])
        vg = self.g_v()
        for h8 in range(8):
            for half in range(2):
                self.mm(self.PSB[0:32, half * 512:(half + 1) * 512],
                        KT[:, KE_UVT + h8 * 32:KE_UVT + (h8 + 1) * 32],
                        self.WBa[:, h8 * 1024 + half * 512:h8 * 1024 + (half + 1) * 512],
                        w=(vg,) if h8 == 0 and half == 0 else
                        (self.g_s(),) if half == 0 else ())
            self.act(KT[(h8 % 4) * 32:(h8 % 4) * 32 + 32,
                        KE_UVO + (h8 // 4) * 1024:KE_UVO + (h8 // 4 + 1) * 1024],
                     self.PSB[0:32, 0:1024], AF.Copy, w=(self.g_t(),))
        for kc in range(8):
            for half in range(2):
                self.mm(self.PSA[0:1, 1024 + half * 512:1024 + (half + 1) * 512],
                        self.SM1[:, 8 + kc:9 + kc],
                        self.WBa[:, kc * 1024 + half * 512:kc * 1024 + (half + 1) * 512],
                        start=(kc == 0), stop=False,
                        w=(self.g_v(), gd) if kc == 0 and half == 0 else ())
        for half in range(2):
            self.mm(self.PSA[0:1, 1024 + half * 512:1024 + (half + 1) * 512],
                    self.ONESR[0:1, 0:1],
                    self.SM1[0:1, 1024 + half * 512:1024 + (half + 1) * 512],
                    start=False, stop=True)
        self.vcopy(KT[0:1, KE_BFIN:KE_BFIN + 1024], self.PSA[0:1, 1024:2048],
                   w=(self.g_t(),))
        self.dma("g", bass.AP(self.WBb, 0, [[8192, 128], [32, 8], [1, 32]]),
                 bass.AP(H["w_dq"], 0, [[32, 128], [4096, 8], [1, 32]]))
        self.dma("g", bass.AP(self.WBb, 256, [[8192, 128], [64, 8], [1, 64]]),
                 bass.AP(H["w_kv"], 0, [[64, 128], [8192, 8], [1, 64]]))
        self.dma("g", self.ROWS[0:1, 0:32], H["b_dq"][:, :])
        self.dma("g", self.ROWS[0:1, 32:96], H["b_kv"][:, :])
        wg = self.g_gd()
        self.memset("v", KT[64:65, KE_KV:KE_KV + S], 1.0, w=(self.g_t(),))
        EIDF = KT[:, 14848:15104].bitcast(F32)
        self.vcopy(EIDF, self.IDN[:, :])
        onk = self.g_v()
        lnb = self.LEB
        sc_f32 = self.LEB[:, 2048:4096].bitcast(F32)
        xs_free = None
        lnb_free = ((self.st, 0),)
        for s_ in range(NSLAB):
            w = [xs_free] if xs_free else []
            if s_ == 0:
                w.append(wg)
            self.dma("y", bass.AP(self.XS, 0, [[2048, 128], [1024, 2], [1, 1024]]),
                     bass.AP(H["x2"], s_ * SLAB * DM, [[1024, 128], [131072, 2], [1, 1024]]),
                     w=w)
            xdma = self.g_y()
            for u in range(2):
                self.ln_tile(u * 1024, u, lnb[:, u * 1024:(u + 1) * 1024], sc_f32,
                             w_x=(xdma,) if u == 0 else (), w_apply=lnb_free)
            xs_free = self.g_s()
            self.transpose_slab(lnb, 0, w_first=(self.g_v(),))
            lnb_free = (self.g_t(),)
            tev = self.g_s()
            for kc in range(8):
                self.mm(self.PSA[0:32, 0:256], self.WBb[:, kc * 32:(kc + 1) * 32],
                        self.TSL[:, kc * 256:(kc + 1) * 256],
                        start=(kc == 0), stop=False,
                        w=(tev, self.g_v()) if kc == 0 else ())
            self.mm(self.PSA[0:32, 0:256], self.ROWS[:, 0:32], self.ONESR[:, 0:256],
                    start=False, stop=True)
            for kc in range(8):
                self.mm(self.PSA[0:64, 256:512],
                        self.WBb[:, 256 + kc * 64:256 + (kc + 1) * 64],
                        self.TSL[:, kc * 256:(kc + 1) * 256],
                        start=(kc == 0), stop=False)
            self.mm(self.PSA[0:64, 256:512], self.ROWS[:, 32:96], self.ONESR[:, 0:256],
                    start=False, stop=True)
            self.vcopy(KT[0:32, KE_CQ + s_ * 256:KE_CQ + (s_ + 1) * 256],
                       self.PSA[0:32, 0:256], w=(self.g_t(),))
            self.vcopy(KT[0:64, KE_KV + s_ * 256:KE_KV + (s_ + 1) * 256],
                       self.PSA[0:64, 256:512])
        prj = self.g_v()
        self.memset("v", bass.AP(self.VT, 32, [[32768, 128], [33, 32], [1, 1]]), 1.0,
                    w=(self.g_t(),))
        for mq4 in range(8):
            for k4 in range(4):
                m = mq4 * 4 + k4
                w = ()
                if m == 0:
                    w = (prj, onk, self.g_v())
                elif k4 == 0:
                    w = (self.g_v(),)
                self.mm(self.PSB[:, k4 * 128:k4 * 128 + 65],
                        KT[0:65, KE_KV + m * 128:KE_KV + (m + 1) * 128],
                        self.IDN[0:65, 0:65], w=w)
            self.vcopy(bass.AP(self.VT, mq4 * 4 * 33, [[32768, 128], [33, 4], [1, 32]]),
                       bass.AP(self.PSB, 0, [[2048, 128], [128, 4], [1, 32]]),
                       w=(self.g_t(),))
        QSQ = self.SM1
        for piece in range(8):
            self.tt("v", QSQ[0:64, 0:512],
                    KT[0:64, KE_KV + piece * 512:KE_KV + (piece + 1) * 512],
                    KT[0:64, KE_KV + piece * 512:KE_KV + (piece + 1) * 512], ALU.mult)
            self.mm(self.PSA[0:1, 0:512], self.ONESC[0:64, 0:1], QSQ[0:64, 0:512],
                    w=(self.g_v(),))
            self.act(self.SROW[0:1, SR_STAGE:SR_STAGE + 512], self.PSA[0:1, 0:512],
                     AF.Sqrt, w=(self.g_t(),))
            if piece == 0:
                self.vred(self.SROW[0:1, SR_KMAX:SR_KMAX + 1],
                          self.SROW[0:1, SR_STAGE:SR_STAGE + 512], ALU.max,
                          w=(self.g_s(),))
            else:
                self.vred(self.SROW[0:1, SR_KMAX + 1:SR_KMAX + 2],
                          self.SROW[0:1, SR_STAGE:SR_STAGE + 512], ALU.max,
                          w=(self.g_s(),))
                self.tt("v", self.SROW[0:1, SR_KMAX:SR_KMAX + 1],
                        self.SROW[0:1, SR_KMAX:SR_KMAX + 1],
                        self.SROW[0:1, SR_KMAX + 1:SR_KMAX + 2], ALU.max)
        self.ts("v", self.SROW[0:1, SR_KNEG:SR_KNEG + 1],
                self.SROW[0:1, SR_KMAX:SR_KMAX + 1], -1.0, None, ALU.mult)
        kstar = self.g_v()
        scale = 1.0 / math.sqrt(160.0)
        exp_free = [None, None]
        leb_free = [None, None, None, None]
        expd_all = [None, None, None, None]
        norm_done = [None, None]
        rc_last = None
        xs_gate = None
        xo_gate = None
        QA = self.VT
        BCV = KT[0:32, 15360:16384].bitcast(F32)
        for nb in range(8):
            for h8 in range(8):
                pq = 1024 + (h8 % 2) * 512
                wq = (self.g_s(),)
                if h8 <= 1:
                    wq = (kstar,) + ((exp_free[1],) if exp_free[1] else (self.g_s(),))
                    if rc_last is not None:
                        wq = wq + (rc_last,)
                self.mm(self.PSA[0:64, pq:pq + 512],
                        KT[:, KE_WQC + h8 * 64:KE_WQC + (h8 + 1) * 64],
                        KT[:, KE_CQ + nb * 512:KE_CQ + (nb + 1) * 512],
                        start=True, stop=False, w=wq)
                self.mm(self.PSA[0:64, pq:pq + 512],
                        KT[:, KE_BQC + h8 * 64:KE_BQC + (h8 + 1) * 64],
                        self.ONESR[:, 0:512], start=False, stop=True)
                tq = self.g_t()
                self.act(QSQ[0:64, 0:512], self.PSA[0:64, pq:pq + 512], AF.Square,
                         w=(tq,))
                self.act(QA[0:64, 2048 + h8 * 512:2048 + (h8 + 1) * 512],
                         self.PSA[0:64, pq:pq + 512], AF.Copy)
                sq = self.g_s()
                self.mm(self.PSB[0:1, 1536:2048], self.ONESC[0:64, 0:1], QSQ[0:64, 0:512],
                        w=(sq, rc_last) if (h8 == 0 and rc_last is not None) else (sq,))
                self.act(self.SROW[0:1, SR_STAGE:SR_STAGE + 512], self.PSB[0:1, 1536:2048],
                         AF.Sqrt, w=(self.g_t(),))
                self.ts("v", QA[64:65, 2048 + h8 * 512:2048 + (h8 + 1) * 512],
                        self.SROW[0:1, SR_STAGE:SR_STAGE + 512],
                        self.SROW[0:1, SR_KNEG:SR_KNEG + 1], None, ALU.mult,
                        w=(self.g_s(),))
            qaug = self.g_v()
            for h8 in range(8):
                hv = h8 % 2
                for pair in range(17):
                    if pair < 16:
                        pc = (pair % 2) * 1024
                        sl = pair % 4
                        wl = []
                        if exp_free[pair % 2] is not None:
                            wl.append(exp_free[pair % 2])
                        if h8 == 0 and pair <= 1:
                            wl.append(qaug)
                        for mi in range(2):
                            m = pair * 2 + mi
                            self.mm(self.PSA[:, pc + mi * 512:pc + (mi + 1) * 512],
                                    KT[:, KE_KV + m * 128:KE_KV + (m + 1) * 128],
                                    QA[:, 2048 + h8 * 512:2048 + (h8 + 1) * 512],
                                    w=wl if mi == 0 else ())
                        sc = self.g_t()
                        wexp = [sc]
                        if leb_free[sl] is not None:
                            wexp.append(leb_free[sl])
                        self.act(self.LEB[:, sl * 1024:(sl + 1) * 1024],
                                 self.PSA[:, pc:pc + 1024], AF.Exp, scale=scale, w=wexp)
                        expd_all[sl] = self.g_s()
                        exp_free[pair % 2] = expd_all[sl]
                    if pair >= 1:
                        pp = pair - 1
                        psl = pp % 4
                        expd = expd_all[psl]
                        for mi in range(2):
                            m = pp * 2 + mi
                            wl = []
                            if mi == 0:
                                wl.append(expd)
                                if pp == 0 and norm_done[hv] is not None:
                                    wl.append(norm_done[hv])
                            self.mm(self.PSB[0:33, hv * 512:hv * 512 + 512],
                                    bass.AP(self.VT, m * 33, [[32768, 128], [1, 33]]),
                                    self.LEB[:, psl * 1024 + mi * 512:
                                             psl * 1024 + (mi + 1) * 512],
                                    start=(m == 0), stop=(m == 31), w=wl)
                        leb_free[psl] = self.g_t()
                ctxh = self.g_t()
                self.act(self.SROW[0:1, SR_STAGE:SR_STAGE + 512],
                         self.PSB[32:33, hv * 512:hv * 512 + 512], AF.Copy, w=(ctxh,))
                drow = self.g_s()
                for u in range(4):
                    self.mm(self.PSB[0:128, 1540 + u:1541 + u],
                            self.SROW[0:1, SR_STAGE + u * 128:SR_STAGE + (u + 1) * 128],
                            self.SROW[0:1, SR_ONES:SR_ONES + 1],
                            w=(drow, self.g_v()) if u == 0 else ())
                self.vrecip(self.FQ4[:, 0:4], self.PSB[:, 1540:1544], w=(self.g_t(),))
                rc = self.g_v()
                rc_last = rc
                for u in range(4):
                    self.mm(self.PSB[0:1, 1024 + u * 128:1024 + (u + 1) * 128],
                            self.FQ4[:, u:u + 1], EIDF,
                            w=(rc,) if u == 0 else ())
                self.act(self.SROW[0:1, SR_STAGE:SR_STAGE + 512],
                         self.PSB[0:1, 1024:1536], AF.Copy, w=(self.g_t(),))
                rrow = self.g_s()
                self.mm(self.PSB[0:32, 1024:1536], self.SROW[0:1, SR_ONES:SR_ONES + 32],
                        self.SROW[0:1, SR_STAGE:SR_STAGE + 512], w=(rrow,))
                self.vcopy(BCV, self.PSB[0:32, 1024:1536], w=(self.g_t(),))
                self.tt("v", self.TSL[(h8 % 4) * 32:(h8 % 4) * 32 + 32,
                                      (h8 // 4) * 512:(h8 // 4) * 512 + 512],
                        self.PSB[0:32, hv * 512:hv * 512 + 512], BCV, ALU.mult)
                norm_done[hv] = self.g_v()
            ctxn = self.g_v()
            for ns in range(4):
                wdma_l = []
                if ns < 2:
                    if xs_gate is not None:
                        wdma_l.append(xs_gate)
                else:
                    wdma_l.append(self.g_v())
                self.dma("y", bass.AP(self.XS, (ns % 2) * 1024, [[2048, 128], [1, 1024]]),
                         bass.AP(H["x2"], (nb * 512 + ns * 128) * DM,
                                 [[1024, 128], [1, 1024]]),
                         w=wdma_l)
                xdma = self.g_y()
                wl = [ctxn, xdma] if ns == 0 else [self.g_v(), xdma]
                if ns in (0, 2) and xo_gate is not None:
                    wl.append(xo_gate)
                for j in range(2):
                    for half in range(2):
                        self.mm(self.PSB[:, 1024 + half * 512:1024 + (half + 1) * 512],
                                self.TSL[:, j * 512 + ns * 128:j * 512 + ns * 128 + 128],
                                KT[:, KE_UVO + j * 1024 + half * 512:
                                   KE_UVO + j * 1024 + (half + 1) * 512],
                                start=(j == 0), stop=False,
                                w=wl if (j == 0 and half == 0) else ())
                for half in range(2):
                    self.mm(self.PSB[:, 1024 + half * 512:1024 + (half + 1) * 512],
                            self.ONESR[:, 0:128],
                            KT[:, KE_BFIN + half * 512:KE_BFIN + (half + 1) * 512],
                            start=False, stop=True)
                self.stt("v", self.XO[:, (ns % 2) * 1024:(ns % 2 + 1) * 1024],
                         self.PSB[:, 1024:2048], 1.0,
                         self.XS[:, (ns % 2) * 1024:(ns % 2 + 1) * 1024],
                         ALU.mult, ALU.add, w=(self.g_t(),))
                if ns % 2 == 1:
                    self.dma("y", bass.AP(H["x3"], (nb * 512 + (ns - 1) * 128) * DM,
                                          [[1024, 128], [131072, 2], [1, 1024]]),
                             bass.AP(self.XO, 0, [[2048, 128], [1024, 2], [1, 1024]]),
                             w=(self.g_v(),))
                    xo_gate = self.g_y()
            xs_gate = self.g_v()

    def quant_tile(self, xin, xout, dqp, dqn, perblock=False, w0=(), w_out=()):
        e = "v"
        F1 = self.SM1[:, 0:2048].bitcast(F32)
        F2 = self.LEB[:, 0:2048].bitcast(F32)
        F3 = self.KT[:, 0:2048].bitcast(F32)
        F4 = self.FQ4
        AM = F4[:, 0:32]
        EB = F4[:, 32:64]
        RS = F4[:, 64:96]
        SC = F4[:, 96:128]
        x3d = xin.rearrange("p (a b) -> p a b", b=32)
        self.vred(AM, x3d, ALU.max, apply_absolute_value=True, w=w0)
        self.ts(e, EB.bitcast(I32), AM.bitcast(I32), 23, None, ALU.logical_shift_right)
        self.ts(e, EB.bitcast(I32), EB.bitcast(I32), 1, None, ALU.max)
        self.ts(e, RS.bitcast(I32), EB.bitcast(I32), -1, 256, ALU.mult, ALU.add)
        self.ts(e, RS.bitcast(I32), RS.bitcast(I32), 23, None, ALU.logical_shift_left)
        self.ts(e, SC.bitcast(I32), EB.bitcast(I32), -2, None, ALU.add)
        self.ts(e, SC.bitcast(I32), SC.bitcast(I32), 23, None, ALU.logical_shift_left)
        if perblock:
            self.tt(e, RS, RS, dqn, ALU.mult)
            self.tt(e, SC, SC, dqp, ALU.mult)
        else:
            self.ts(e, RS, RS, dqn, None, ALU.mult)
            self.ts(e, SC, SC, dqp, None, ALU.mult)
        rs_b = bass.AP(self.FQ4, 64, [[128, 128], [1, 32], [0, 32]])
        sc_b = bass.AP(self.FQ4, 96, [[128, 128], [1, 32], [0, 32]])
        self.tt(e, F1.rearrange("p (a b) -> p a b", b=32), x3d, rs_b, ALU.mult)
        self.stt(e, F2, F1, -1.0, F1, ALU.mult, ALU.max)
        self.ts(e, F2, F2, 6.0, None, ALU.min)
        self.ts(e, F3.bitcast(I32), F2.bitcast(I32), 23, None, ALU.logical_shift_right)
        self.ts(e, F3.bitcast(I32), F3.bitcast(I32), 127, 129, ALU.max, ALU.min)
        self.ts(e, F1.bitcast(I32), F3.bitcast(I32), -1, 255, ALU.mult, ALU.add)
        self.ts(e, F1.bitcast(I32), F1.bitcast(I32), 23, None, ALU.logical_shift_left)
        self.tt(e, F2, F2, F1, ALU.mult)
        self.ts(e, F2, F2, 8388608.0, None, ALU.add)
        self.ts(e, F2, F2, 8388608.0, None, ALU.subtract)
        self.ts(e, F3.bitcast(I32), F3.bitcast(I32), -1, None, ALU.add)
        self.ts(e, F3.bitcast(I32), F3.bitcast(I32), 23, None, ALU.logical_shift_left)
        self.tt(e, F2, F2, F3, ALU.mult)
        self.ts(e, F3.bitcast(I32), xin.bitcast(I32), -2147483648, None, ALU.bitwise_and)
        self.r.op(e, "tensor_copy", F1.rearrange("p (a b) -> p a b", b=32), sc_b,
                  i=((self.sv, 1),))
        self.tt(e, F1.bitcast(I32), F1.bitcast(I32), F3.bitcast(I32), ALU.bitwise_or)
        self.tt(e, xout, F2, F1, ALU.mult, w=w_out)

    def pW(self):
        H = self.h
        XQB = self.LEB[:, 2048:4096]
        DQW = self.RSC
        for oc in range(8):
            self.dma("y", self.XO[:, 0:1024],
                     bass.AP(H["w_f"], oc * 128 * DM, [[1024, 128], [1, 1024]]),
                     w=(self.g_v(), self.g_t()))
            self.dma("y", DQW[:, 0:32],
                     bass.AP(H["delta_w"], oc * 128 * 32, [[32, 128], [1, 32]]))
            wdma = self.g_y()
            self.act(DQW[:, 0:32], DQW[:, 0:32], AF.Tanh, scale=4.0, w=(wdma,))
            self.act(DQW[:, 32:64], DQW[:, 0:32], AF.Exp, scale=-0.5 * LN2)
            self.act(DQW[:, 0:32], DQW[:, 0:32], AF.Exp, scale=0.5 * LN2)
            dq = self.g_s()
            self.quant_tile(self.XO[:, 0:1024], XQB[:, 0:1024],
                            DQW[:, 0:32], DQW[:, 32:64],
                            perblock=True, w0=(dq, wdma), w_out=(self.g_t(),))
            qd = self.g_v()
            for dc in range(8):
                w = ()
                if dc == 0:
                    w = (qd, self.g_s())
                elif dc == 4:
                    w = (self.g_s(),)
                self.mm(self.PSA[:, 1024 + (dc % 4) * 128:1024 + (dc % 4) * 128 + 128],
                        XQB[:, dc * 128:(dc + 1) * 128], self.IDN[:, :], w=w)
                if dc % 4 == 3:
                    self.act(bass.AP(self.WBa, (dc - 3) * 1024 + oc * 128,
                                     [[8192, 128], [1024, 4], [1, 128]]),
                             self.PSA[:, 1024:1536], AF.Copy, w=(self.g_t(),))
        self.dma("g", self.ROWS[0:1, :], H["b_f"][:, :])

    def pF(self):
        H = self.h
        dqp = self.SCL[:, 0:1]
        dqn = self.SCL[:, 1:2]
        XQB = self.LEB[:, 2048:4096]
        lnf = self.VT[:, 0:2048].bitcast(F32)
        lnscr = self.KT[:, 2048:4096].bitcast(F32)
        xs_free = None
        xo_gate = None
        rowg = self.g_gd()
        for s_ in range(NSLAB):
            p2 = s_ % 2
            w = [xs_free] if xs_free else []
            if s_ == 0:
                w.append(rowg)
            self.dma("y", bass.AP(self.XS, 0, [[2048, 128], [1024, 2], [1, 1024]]),
                     bass.AP(H["x3"], s_ * SLAB * DM, [[1024, 128], [131072, 2], [1, 1024]]),
                     w=w)
            xdma = self.g_y()
            for u in range(2):
                self.ln_tile(u * 1024, u, lnf,
                             lnscr, w_x=(xdma, self.g_v()) if u == 0 else (self.g_v(),))
                lng = self.g_s()
                self.quant_tile(lnf, XQB[:, u * 1024:(u + 1) * 1024], dqp, dqn,
                                w0=(lng, self.g_t()), w_out=(self.g_t(),))
                qd = self.g_v()
                for c in range(8):
                    w2 = ()
                    if c == 0:
                        w2 = (qd, self.g_s())
                    elif c == 4:
                        w2 = (self.g_s(),)
                    self.mm(self.PSA[:, 1024 + (c % 4) * 128:1024 + (c % 4) * 128 + 128],
                            XQB[:, u * 1024 + c * 128:u * 1024 + (c + 1) * 128],
                            self.IDN[:, :], w=w2)
                    if c % 4 == 3:
                        self.act(bass.AP(self.TSL, p2 * 2048 + (c - 3) * 256 + u * 128,
                                         [[4096, 128], [256, 4], [1, 128]]),
                                 self.PSA[:, 1024:1536], AF.Copy, w=(self.g_t(),))
            tq = self.g_s()
            for u in range(2):
                po = u * 1024
                for kc in range(8):
                    for half in range(2):
                        self.mm(self.PSB[:, po + half * 512:po + (half + 1) * 512],
                                self.TSL[:, p2 * 2048 + kc * 256 + u * 128:
                                         p2 * 2048 + kc * 256 + u * 128 + 128],
                                self.WBa[:, kc * 1024 + half * 512:
                                         kc * 1024 + (half + 1) * 512],
                                start=(kc == 0), stop=False,
                                w=(tq, self.g_v()) if (u == 0 and kc == 0 and half == 0)
                                else ())
                for half in range(2):
                    self.mm(self.PSB[:, po + half * 512:po + (half + 1) * 512],
                            self.ONESR[:, 0:128],
                            self.ROWS[:, half * 512:(half + 1) * 512],
                            start=False, stop=True)
                wst = [self.g_t()]
                if u == 0 and xo_gate is not None:
                    wst.append(xo_gate)
                self.stt("v", self.XO[:, u * 1024:(u + 1) * 1024],
                         self.PSB[:, po:po + 1024], 1.0,
                         self.XS[:, u * 1024:(u + 1) * 1024],
                         ALU.mult, ALU.add, w=wst)
            xs_free = self.g_v()
            self.dma("y", bass.AP(H["out"], s_ * SLAB * DM, [[1024, 128], [131072, 2], [1, 1024]]),
                     bass.AP(self.XO, 0, [[2048, 128], [1024, 2], [1, 1024]]),
                     w=(self.g_v(),))
            xo_gate = self.g_y()

    def build(self, phases):
        table = {"0": self.p0, "A1": self.pA1, "A2": self.pA2, "B": self.pB,
                 "C1": self.pC1, "C1b": self.pC1b, "C2": self.pC2, "D": self.pD,
                 "E": self.pE, "W": self.pW, "F": self.pF}
        for ph in phases:
            table[ph]()
            self.fence()
        for nm in self.dbg.get("dump_sbuf", ()):
            t = getattr(self, nm)
            self.dma("y", self.h["dbg_" + nm][:, :], t[:, :])
        self.r.op("y", None, w=((self.dy, self.dy.n), (self.dg, self.dg.n)))
        rec = self.r
        with self.nc.Block() as blk:
            @blk.tensor
            def _(eng):
                rec.replay("t", eng)

            @blk.vector
            def _(eng):
                rec.replay("v", eng)

            @blk.scalar
            def _(eng):
                rec.replay("s", eng)

            @blk.gpsimd
            def _(eng):
                rec.replay("g", eng)

            @blk.sync
            def _(eng):
                rec.replay("y", eng)


ALL_PHASES = ("0", "A1", "A2", "B", "C1", "C1b", "C2", "D", "E", "W", "F")


def build_nc(phases=ALL_PHASES, dbg=None):
    nc = bass.Bass("TRN2", target_bir_lowering=False)
    mk = MK(nc, dbg=dbg)
    mk.build(phases)
    return nc, mk


def make_inputs(inp, core):
    x = np.asarray(inp["L0_attn_hidden_states"])
    lay = host_layout(inp)
    cst = host_constants()
    m = {"x": np.ascontiguousarray(x[core], dtype=np.float32)}
    m.update(lay)
    m.update(cst)
    return m


def kernel(**inputs):
    nc, mk = build_nc()
    in_maps = [make_inputs(inputs, c) for c in range(8)]
    res = run_bass_kernel_spmd(nc, in_maps, core_ids=list(range(8)))
    out = np.stack([np.asarray(r["out"]) for r in res.results], axis=0)
    return out.astype(np.float32)


# version 14
# speedup vs baseline: 1.0338x; 1.0338x over previous
import math
import numpy as np
import ml_dtypes

import concourse.bass as bass
import concourse.mybir as mybir
from concourse.bass_utils import run_bass_kernel_spmd

F32 = mybir.dt.float32
BF16 = mybir.dt.bfloat16
I32 = mybir.dt.int32
I16 = mybir.dt.int16
AF = mybir.ActivationFunctionType
ALU = mybir.AluOpType
AX = mybir.AxisListType

S = 4096
DM = 1024
NSLAB = 16
SLAB = 256

BFD = ml_dtypes.bfloat16
LN2 = math.log(2.0)

SR_STAGE = 0
SR_KMAX = 512
SR_KNEG = 514
SR_DELTA = 516
SR_TANH = 518
SR_DQ2P = 520
SR_DQ2N = 522
SR_ONES = 528

KE_CQ = 0
KE_KV = 4096
KE_UVO = 8192
KE_WQC = 10240
KE_BQC = 10752
KE_BFIN = 11776
KE_UQT = 12800
KE_UKT = 13056
KE_UVT = 13312
KE_WQR = 13568


class Sem:
    __slots__ = ("h", "n")

    def __init__(self, h):
        self.h = h
        self.n = 0


def _ap_span(ap):
    if not isinstance(ap, bass.AP):
        return None
    es = mybir.dt.size(ap.dtype)
    pat = list(ap.ap)
    reach = 0
    fel = 1
    for k, (stp, cnt) in enumerate(pat):
        if k == 0:
            continue
        reach += abs(int(stp)) * (int(cnt) - 1)
        fel *= int(cnt)
    lo = int(ap.offset) * es
    return (ap.tensor.name, lo, lo + (reach + 1) * es, fel)


def _overlap(a, b):
    return a[0] == b[0] and a[1] < b[2] and b[1] < a[2]


class Rec:
    ENGS = ("t", "v", "s", "g", "y")
    OWN = {"t": "st", "v": "sv", "s": "ss", "g": "sg"}

    def __init__(self, nc):
        self.nc = nc
        self.q = {e: [] for e in self.ENGS}
        self.sems = {}
        self._hist = {e: [] for e in self.ENGS}

    def sem(self, name):
        s = Sem(self.nc.alloc_semaphore(name=name))
        self.sems[name] = s
        return s

    def op(self, e, meth, *a, w=(), i=(), **kw):
        waits = [(s.h, v) for (s, v) in w if s is not None and v > 0]
        if e in self.OWN and meth not in (None,):
            aps = [x for x in a if isinstance(x, bass.AP)]
            aps += [x for x in kw.values() if isinstance(x, bass.AP)]
            wr = _ap_span(aps[0]) if aps else None
            rds = [s2 for s2 in (_ap_span(x) for x in aps[1:]) if s2 is not None]
            own = self.sems[self.OWN[e]]
            need = 0
            for pw, cnt in self._hist[e]:
                for rd in rds:
                    if _overlap(rd, pw) and (rd[3] < 512 or pw[3] < 512):
                        need = max(need, cnt)
            if need:
                waits.append((own.h, need))
            self._hist[e] = self._hist[e][-1:] + ([(wr, own.n + 1)] if wr else [])
        self.q[e].append((meth, a, kw, waits,
                          [(s.h, d) for (s, d) in i] if i else None))
        for s, d in i:
            s.n += d

    def replay(self, e, eng):
        for meth, a, kw, waits, incs in self.q[e]:
            for h, v in waits:
                eng.wait_ge(h, v)
            if meth is None:
                continue
            try:
                ins = getattr(eng, meth)(*a, **kw)
            except Exception:
                import sys as _sys
                print(f"REPLAY-FAIL eng={e} meth={meth} "
                      f"args={[getattr(x, 'shape', x) for x in a]}", file=_sys.stderr)
                raise
            if incs:
                for h, d in incs:
                    ins.then_inc(h, d)


def host_constants():
    c = {}
    c["c_ident"] = np.eye(128, dtype=BFD)
    c["c_iota16"] = np.tile(np.arange(16, dtype=np.float32)[None, :], (128, 1)).astype(BFD)
    j = np.arange(1024)
    e16 = np.zeros((128, 1024), dtype=np.float32)
    e16[:16] = (j[None, :] // 64 == np.arange(16)[:, None])
    c["c_e16"] = e16.astype(BFD)
    return c


def host_layout(inp):
    d = {}
    f32 = np.float32
    asc = np.ascontiguousarray
    d["w_q"] = asc(inp["L0_attn_Wq"], dtype=f32)
    d["w_k"] = asc(inp["L0_attn_Wk"], dtype=f32)
    d["w_v"] = asc(inp["L0_attn_Wv"], dtype=f32)
    for nm, key in (("b_q", "L0_attn_bq"), ("b_k", "L0_attn_bk"), ("b_v", "L0_attn_bv")):
        d[nm] = asc(np.asarray(inp[key]).reshape(1, 1024), dtype=f32)
    for f in ("q", "k", "v"):
        w1 = np.asarray(inp[f"L0_ffn_{f}_w1"])
        d[f"w1_{f}"] = asc(w1.transpose(1, 0, 2).reshape(1024, 1024), dtype=f32)
        d[f"w2_{f}"] = asc(np.asarray(inp[f"L0_ffn_{f}_w2"]).reshape(1024, 1024), dtype=f32)
        d[f"b1_{f}"] = asc(np.asarray(inp[f"L0_ffn_{f}_b1"]).reshape(1, 1024), dtype=f32)
        d[f"b2_{f}"] = asc(inp[f"L0_ffn_{f}_b2"], dtype=f32)
    nwt = np.concatenate([np.asarray(inp[f"L0_ffn_{f}_node_w"]).T for f in ("q", "k", "v")],
                         axis=1)
    d["node_wt"] = asc(nwt, dtype=f32)
    nb_ = np.concatenate([np.asarray(inp[f"L0_ffn_{f}_node_b"]) for f in ("q", "k", "v")])
    d["node_b"] = asc(nb_.reshape(1, 45), dtype=f32)
    d["w_dq"] = asc(inp["L1_attn_W_DQ"], dtype=f32)
    d["b_dq"] = asc(np.asarray(inp["L1_attn_b_DQ"]).reshape(1, 32), dtype=f32)
    d["w_kv"] = asc(np.concatenate([inp["L1_attn_W_DKV"], inp["L1_attn_W_KR"]], axis=1),
                    dtype=f32)
    d["b_kv"] = asc(np.concatenate([np.asarray(inp["L1_attn_b_DKV"]),
                                    np.asarray(inp["L1_attn_b_KR"])]).reshape(1, 64),
                    dtype=f32)
    d["w_uqT"] = asc(np.asarray(inp["L1_attn_W_UQ"]).T.reshape(8, 128, 32), dtype=f32)
    d["w_ukT"] = asc(np.asarray(inp["L1_attn_W_UK"]).T.reshape(8, 128, 32), dtype=f32)
    d["w_uvT"] = asc(np.asarray(inp["L1_attn_W_UV"]).T.reshape(8, 128, 32), dtype=f32)
    d["w_qr"] = asc(inp["L1_attn_W_QR"], dtype=f32)
    d["b_qr"] = asc(np.asarray(inp["L1_attn_b_QR"]).reshape(1, 256), dtype=f32)
    d["b_uq"] = asc(np.asarray(inp["L1_attn_b_UQ"]).reshape(8, 128), dtype=f32)
    d["b_uv"] = asc(np.asarray(inp["L1_attn_b_UV"]).reshape(8, 128), dtype=f32)
    d["w_o"] = asc(inp["L1_attn_W_O"], dtype=f32)
    d["b_o"] = asc(np.asarray(inp["L1_attn_b_O"]).reshape(1, 1024), dtype=f32)
    d["w_f"] = asc(inp["L1_ffn_weight"], dtype=f32)
    d["b_f"] = asc(np.asarray(inp["L1_ffn_bias"]).reshape(1, 1024), dtype=f32)
    d["delta_w"] = asc(inp["L1_ffn_delta_w"], dtype=f32)
    d["delta_act"] = asc(np.asarray(inp["L1_ffn_delta_act"]).reshape(1, 1), dtype=f32)
    return d


class MK:
    def __init__(self, nc, dbg=None):
        self.nc = nc
        self.dbg = dbg or {}
        self.r = Rec(nc)
        r = self.r
        self.st = r.sem("st")
        self.sv = r.sem("sv")
        self.ss = r.sem("ss")
        self.sg = r.sem("sg")
        self.dy = r.sem("dy")
        self.dg = r.sem("dg")
        self.dq2 = r.sem("dq2")
        self._declare_io()
        self._alloc()

    def _declare_io(self):
        nc = self.nc
        T = nc.dram_tensor
        self.h = {}
        H = self.h
        H["x"] = T("x", [S, DM], F32, kind="ExternalInput")
        H["out"] = T("out", [S, DM], F32, kind="ExternalOutput")
        for nm, sh, dt in (("c_ident", [128, 128], BF16),
                           ("c_iota16", [128, 16], BF16), ("c_e16", [128, 1024], BF16)):
            H[nm] = T(nm, sh, dt, kind="ExternalInput")
        for nm, sh in (
            ("w_q", [DM, DM]), ("w_k", [DM, DM]), ("w_v", [DM, DM]),
            ("b_q", [1, DM]), ("b_k", [1, DM]), ("b_v", [1, DM]),
            ("w1_q", [DM, DM]), ("w1_k", [DM, DM]), ("w1_v", [DM, DM]),
            ("w2_q", [DM, DM]), ("w2_k", [DM, DM]), ("w2_v", [DM, DM]),
            ("b1_q", [1, DM]), ("b1_k", [1, DM]), ("b1_v", [1, DM]),
            ("b2_q", [16, DM]), ("b2_k", [16, DM]), ("b2_v", [16, DM]),
            ("node_wt", [DM, 45]), ("node_b", [1, 45]),
            ("w_dq", [DM, 32]), ("b_dq", [1, 32]),
            ("w_kv", [DM, 64]), ("b_kv", [1, 64]),
            ("w_uqT", [8, 128, 32]), ("w_ukT", [8, 128, 32]), ("w_uvT", [8, 128, 32]),
            ("w_qr", [32, 256]), ("b_qr", [1, 256]),
            ("b_uq", [8, 128]), ("b_uv", [8, 128]),
            ("w_o", [DM, DM]), ("b_o", [1, DM]),
            ("w_f", [DM, DM]), ("b_f", [1, DM]),
            ("delta_w", [DM, 32]), ("delta_act", [1, 1]),
        ):
            H[nm] = T(nm, sh, F32, kind="ExternalInput")
        dump = set(self.dbg.get("dump", ()))
        pre = set(self.dbg.get("preload", ()))

        def scratch(nm, sh, dt):
            kind = "Internal"
            if nm in dump:
                kind = "ExternalOutput"
            elif nm in pre:
                kind = "ExternalInput"
            H[nm] = T(nm, sh, dt, kind=kind)

        scratch("x1", [S, DM], F32)
        scratch("x2", [S, DM], F32)
        scratch("x3", [S, DM], F32)
        scratch("xlnT", [16, 128, 2048], BF16)
        scratch("qTs", [16, 128, 2048], BF16)
        for nm in self.dbg.get("dump_sbuf", ()):
            H["dbg_" + nm] = T("dbg_" + nm, [128, self._dbg_cols(nm)], BF16,
                               kind="ExternalOutput")

    def _dbg_cols(self, nm):
        return {"KT": 8 * S, "VT": 8 * S, "WBa": 8192, "WBb": 8192,
                "TSL": 4096, "LEB": 4096, "SM1": 2048}.get(nm, 4096)

    def _alloc(self):
        nc = self.nc
        A = nc.alloc_sbuf_tensor
        self.KT = A("KT", [128, 8 * S], BF16)
        self.VT = A("VT", [128, 8 * S], BF16)
        self.WBa = A("WBa", [128, 8192], BF16)
        self.WBb = A("WBb", [128, 8192], BF16)
        self.XS = A("XS", [128, 2048], F32)
        self.XO = A("XO", [128, 2048], F32)
        self.TSL = A("TSL", [128, 4096], BF16)
        self.LEB = A("LEB", [128, 4096], BF16)
        self.SM1 = A("SM1", [128, 2048], BF16)
        self.IDN = A("IDN", [128, 128], BF16)
        self.IOTA = A("IOTA", [128, 16], BF16)
        self.E16 = A("E16", [128, 1024], BF16)
        self.ONESC = A("ONESC", [128, 1], BF16)
        self.ONESR = A("ONESR", [128, 512], BF16)
        self.ROWS = A("ROWS", [128, 1024], BF16)
        self.B2C = A("B2C", [128, 1024], BF16)
        self.NWT = A("NWT", [128, 360], BF16)
        self.NBR = A("NBR", [128, 45], BF16)
        self.ROUTE = A("ROUTE", [128, 32], F32)
        self.RSC = A("RSC", [128, 64], F32)
        self.OHT = A("OHT", [128, 256], BF16)
        self.STAT = A("STAT", [128, 16], F32)
        self.SROW = A("SROW", [1, 576], F32)
        self.SCL = A("SCL", [128, 8], F32)
        self.FQ4 = A("FQ4", [128, 128], F32)
        Ap = nc.alloc_psum_tensor
        self.PSA = Ap("PSA", [128, 2048], F32)
        self.PSB = Ap("PSB", [128, 2048], F32)

    def dma(self, q, out, in_, w=(), **kw):
        sem = self.dy if q == "y" else (self.dq2 if q == "s" else self.dg)
        self.r.op(q, "dma_start", w=w, i=((sem, 16),), out=out, in_=in_, **kw)

    def mm(self, out, lhsT, rhs, start=True, stop=True, w=()):
        self.r.op("t", "matmul", out, lhsT, rhs, start=start, stop=stop,
                  w=w, i=((self.st, 1),))

    def mmT(self, out_psum, in_sb, w=()):
        self.mm(out_psum, in_sb, self.IDN[:, :], w=w)

    def act(self, out, in_, func, w=(), **kw):
        self.r.op("s", "activation", out, in_, func, w=w, i=((self.ss, 1),), **kw)

    def ts(self, e, out, in0, s1, s2, op0, op1=None, w=()):
        sem = {"v": self.sv, "g": self.sg}[e]
        if op1 is None:
            self.r.op(e, "tensor_scalar", out, in0, s1, None, op0, w=w, i=((sem, 1),))
        else:
            self.r.op(e, "tensor_scalar", out, in0, s1, s2, op0, op1, w=w, i=((sem, 1),))

    def tt(self, e, out, in0, in1, op, w=()):
        sem = {"v": self.sv, "g": self.sg}[e]
        self.r.op(e, "tensor_tensor", out, in0, in1, op, w=w, i=((sem, 1),))

    def stt(self, e, out, in0, scalar, in1, op0, op1, w=()):
        sem = {"v": self.sv, "g": self.sg}[e]
        self.r.op(e, "scalar_tensor_tensor", out, in0, scalar, in1, op0, op1,
                  w=w, i=((sem, 1),))

    def vcopy(self, out, in_, w=()):
        self.r.op("v", "tensor_copy", out, in_, w=w, i=((self.sv, 1),))

    def vred(self, out, in_, op, w=(), **kw):
        self.r.op("v", "tensor_reduce", out, in_, AX.X, op, w=w, i=((self.sv, 1),), **kw)

    def vrecip(self, out, in_, w=()):
        self.r.op("v", "reciprocal", out, in_, w=w, i=((self.sv, 1),))

    def memset(self, e, ap, val, w=()):
        sem = {"v": self.sv, "g": self.sg}[e]
        self.r.op(e, "memset", ap, val, w=w, i=((sem, 1),))

    def g_t(self):
        return (self.st, self.st.n)

    def g_v(self):
        return (self.sv, self.sv.n)

    def g_s(self):
        return (self.ss, self.ss.n)

    def g_g(self):
        return (self.sg, self.sg.n)

    def g_y(self):
        return (self.dy, self.dy.n)

    def g_gd(self):
        return (self.dg, self.dg.n)

    def fence(self):
        counts = {nm: s.n for nm, s in self.r.sems.items()}
        sems = self.r.sems
        for e in Rec.ENGS:
            own = {"t": "st", "v": "sv", "s": "ss", "g": "sg"}.get(e)
            w = [(sems[nm], v) for nm, v in counts.items() if nm != own and v > 0]
            self.r.op(e, None, w=w)

    def ln_tile(self, xoff, col, out_ap, scratch, w_x=(), w_apply=None):
        ST = self.STAT
        c = 8 * col
        xs = self.XS[:, xoff:xoff + 1024]
        self.vred(ST[:, c + 0:c + 1], xs, ALU.add, w=w_x)
        self.stt("v", scratch, xs, 1.0, xs, ALU.mult, ALU.mult)
        self.vred(ST[:, c + 1:c + 2], scratch, ALU.add)
        self.ts("v", ST[:, c + 2:c + 3], ST[:, c + 0:c + 1], 1.0 / DM, None, ALU.mult)
        self.ts("v", ST[:, c + 3:c + 4], ST[:, c + 1:c + 2], 1.0 / DM, None, ALU.mult)
        self.tt("v", ST[:, c + 4:c + 5], ST[:, c + 2:c + 3], ST[:, c + 2:c + 3], ALU.mult)
        self.tt("v", ST[:, c + 3:c + 4], ST[:, c + 3:c + 4], ST[:, c + 4:c + 5],
                ALU.subtract)
        self.ts("v", ST[:, c + 3:c + 4], ST[:, c + 3:c + 4], 1e-5, None, ALU.add)
        v_ = ST[:, c + 3:c + 4]
        y = ST[:, c + 5:c + 6]
        t = ST[:, c + 6:c + 7]
        self.ts("v", t.bitcast(I32), v_.bitcast(I32), 1, None, ALU.logical_shift_right)
        self.ts("v", y.bitcast(I32), t.bitcast(I32), -1, 0x5F3759DF, ALU.mult, ALU.add)
        for _ in range(2):
            self.tt("v", t, y, y, ALU.mult)
            self.tt("v", t, t, v_, ALU.mult)
            self.ts("v", t, t, -0.5, 1.5, ALU.mult, ALU.add)
            self.tt("v", y, y, t, ALU.mult)
        mb = ST[:, c + 7:c + 8]
        self.stt("v", mb, ST[:, c + 2:c + 3], -1.0, y, ALU.mult, ALU.mult)
        wa = (self.g_v(),) + (tuple(w_apply) if w_apply is not None else (self.g_t(),))
        self.act(out_ap, xs, AF.Identity, bias=mb, scale=y, w=wa)

    def transpose_slab(self, src, base, w_first=()):
        for u in range(2):
            for c in range(8):
                w = ()
                if u == 0 and c == 0:
                    w = tuple(w_first) + (self.g_s(),)
                elif u == 1 and c == 0:
                    w = (self.g_s(),)
                self.mmT(self.PSA[:, 1024 + c * 128:1024 + (c + 1) * 128],
                         src[:, base + u * 1024 + c * 128:base + u * 1024 + (c + 1) * 128],
                         w=w)
            out = bass.AP(self.TSL, u * 128, [[4096, 128], [256, 8], [1, 128]])
            self.act(out, self.PSA[:, 1024:2048], AF.Copy, w=(self.g_t(),))

    def p0(self):
        H = self.h
        self.dma("y", self.IDN[:, :], H["c_ident"][:, :])
        self.dma("y", self.IOTA[:, :], H["c_iota16"][:, :])
        self.dma("y", self.E16[:, :], H["c_e16"][:, :])
        self.memset("v", self.ONESC[:, :], 1.0)
        self.memset("v", self.ONESR[:, :], 0.0)
        self.memset("v", self.ONESR[0:1, :], 1.0)
        self.memset("v", self.ROWS[:, :], 0.0)
        self.memset("v", self.B2C[:, :], 0.0)
        self.memset("v", self.NBR[:, :], 0.0)
        self.memset("v", self.OHT[:, :], 0.0)
        self.memset("v", self.SROW[0:1, SR_STAGE:SR_STAGE + 128], 1.0)
        self.memset("v", self.SROW[0:1, SR_ONES:SR_ONES + 32], 1.0)
        self.dma("g", self.SROW[0:1, SR_DELTA:SR_DELTA + 1], H["delta_act"][:, :])
        gd = self.g_gd()
        self.act(self.SROW[0:1, SR_TANH:SR_TANH + 1],
                 self.SROW[0:1, SR_DELTA:SR_DELTA + 1], AF.Tanh, scale=4.0, w=(gd,))
        self.act(self.SROW[0:1, SR_DQ2P:SR_DQ2P + 1],
                 self.SROW[0:1, SR_TANH:SR_TANH + 1], AF.Exp, scale=0.5 * LN2)
        self.act(self.SROW[0:1, SR_DQ2N:SR_DQ2N + 1],
                 self.SROW[0:1, SR_TANH:SR_TANH + 1], AF.Exp, scale=-0.5 * LN2)
        sg_ = self.g_s()
        self.mm(self.PSB[0:128, 1024:1025], self.SROW[0:1, SR_STAGE:SR_STAGE + 128],
                self.SROW[0:1, SR_DQ2P:SR_DQ2P + 1], w=(sg_, self.g_v()))
        self.mm(self.PSB[0:128, 1025:1026], self.SROW[0:1, SR_STAGE:SR_STAGE + 128],
                self.SROW[0:1, SR_DQ2N:SR_DQ2N + 1])
        self.vcopy(self.SCL[:, 0:2], self.PSB[:, 1024:1026], w=(self.g_t(),))

    def pA1(self):
        H = self.h
        self.dma("g", bass.AP(self.WBa, 0, [[8192, 128], [1024, 8], [1, 1024]]),
                 bass.AP(H["w_q"], 0, [[1024, 128], [131072, 8], [1, 1024]]))
        self.dma("g", bass.AP(self.WBb, 0, [[8192, 128], [1024, 8], [1, 1024]]),
                 bass.AP(H["w_k"], 0, [[1024, 128], [131072, 8], [1, 1024]]))
        self.dma("g", self.ROWS[0:1, :], H["b_q"][:, :])
        self.dma("g", self.B2C[0:1, :], H["b_k"][:, :])
        wg = self.g_gd()
        lnb = self.LEB
        sc_f32 = self.LEB[:, 2048:4096].bitcast(F32)
        xs_free = None
        lnb_free = ((self.st, 0),)
        for s_ in range(NSLAB):
            p2 = s_ % 2
            w = [xs_free] if xs_free else []
            self.dma("y", bass.AP(self.XS, 0, [[2048, 128], [1024, 2], [1, 1024]]),
                     bass.AP(H["x"], s_ * SLAB * DM, [[1024, 128], [131072, 2], [1, 1024]]),
                     w=w)
            xdma = self.g_y()
            for u in range(2):
                self.ln_tile(u * 1024, u, lnb[:, u * 1024:(u + 1) * 1024], sc_f32,
                             w_x=(xdma,) if u == 0 else (), w_apply=lnb_free)
            xs_free = self.g_s()
            self.transpose_slab(lnb, 0)
            lnb_free = (self.g_t(),)
            tev = self.g_s()
            qstage = self.XO[:, p2 * 1024:(p2 + 1) * 1024].bitcast(BF16)
            for oc in range(8):
                for kc in range(8):
                    self.mm(self.PSB[:, oc * 256:(oc + 1) * 256],
                            self.WBa[:, kc * 1024 + oc * 128:kc * 1024 + oc * 128 + 128],
                            self.TSL[:, kc * 256:(kc + 1) * 256],
                            start=(kc == 0), stop=False,
                            w=(tev, wg) if (oc == 0 and kc == 0) else ())
                self.mm(self.PSB[:, oc * 256:(oc + 1) * 256],
                        self.ROWS[:, oc * 128:(oc + 1) * 128], self.ONESR[:, 0:256],
                        start=False, stop=True)
            self.act(qstage, self.PSB[:, :], AF.Copy, w=(self.g_t(), self.g_y()))
            qev = self.g_s()
            for oc in range(8):
                for kc in range(8):
                    self.mm(self.PSB[:, oc * 256:(oc + 1) * 256],
                            self.WBb[:, kc * 1024 + oc * 128:kc * 1024 + oc * 128 + 128],
                            self.TSL[:, kc * 256:(kc + 1) * 256],
                            start=(kc == 0), stop=False,
                            w=(qev,) if (oc == 0 and kc == 0) else ())
                self.mm(self.PSB[:, oc * 256:(oc + 1) * 256],
                        self.B2C[:, oc * 128:(oc + 1) * 128], self.ONESR[:, 0:256],
                        start=False, stop=True)
            self.act(bass.AP(self.KT, s_ * 256, [[32768, 128], [4096, 8], [1, 256]]),
                     self.PSB[:, :], AF.Copy, w=(self.g_t(),))
            self.dma("y", bass.AP(H["xlnT"], s_ * 262144, [[2048, 128], [1, 2048]]),
                     bass.AP(self.TSL, 0, [[4096, 128], [1, 2048]]),
                     w=(tev,))
            self.dma("y", bass.AP(H["qTs"], s_ * 262144, [[2048, 128], [1, 2048]]),
                     qstage,
                     w=(qev,))

    def pA2(self):
        H = self.h
        self.dma("g", bass.AP(self.WBa, 0, [[8192, 128], [1024, 8], [1, 1024]]),
                 bass.AP(H["w_v"], 0, [[1024, 128], [131072, 8], [1, 1024]]))
        self.dma("g", self.ROWS[0:1, :], H["b_v"][:, :])
        wg = self.g_gd()
        for half in range(2):
            self.mm(self.PSA[:, half * 512:(half + 1) * 512], self.ONESR[:, 0:128],
                    self.ROWS[:, half * 512:(half + 1) * 512],
                    w=(wg,) if half == 0 else ())
        self.vcopy(self.SM1[:, 0:1024], self.PSA[:, 0:1024], w=(self.g_t(),))
        bvt = self.g_v()
        tsl_free = None
        for s_ in range(NSLAB):
            w = [tsl_free] if tsl_free else []
            self.dma("y", bass.AP(self.TSL, 0, [[4096, 128], [1, 2048]]),
                     bass.AP(H["xlnT"], s_ * 262144, [[2048, 128], [1, 2048]]),
                     w=w)
            xdma = self.g_y()
            for u in range(2):
                po = u * 1024
                for kc in range(8):
                    for half in range(2):
                        self.mm(self.PSA[:, po + half * 512:po + (half + 1) * 512],
                                self.TSL[:, kc * 256 + u * 128:kc * 256 + u * 128 + 128],
                                self.WBa[:, kc * 1024 + half * 512:
                                         kc * 1024 + (half + 1) * 512],
                                start=(kc == 0), stop=(kc == 7),
                                w=(xdma, self.g_v()) if kc == 0 and half == 0 else ())
                j = 2 * s_ + u
                self.tt("v", self.VT[:, j * 1024:(j + 1) * 1024],
                        self.PSA[:, po:po + 1024], self.SM1[:, 0:1024], ALU.add,
                        w=(self.g_t(), bvt))
            tsl_free = self.g_t()

    def pB(self):
        H = self.h
        exp_free = [None, None]
        qts_free = {}
        xs_free = None
        xo_gate = None
        leb_h0 = [None, None]
        leb_all = [None, None]
        vg = {}
        SM2 = self.SM1[:, 1536:2048]

        def tmax(a, b):
            if a is None:
                return b
            if b is None:
                return a
            return (a[0], max(a[1], b[1]))

        for nb in range(16):
            p = nb % 2
            self.dma("s", bass.AP(self.TSL, p * 2048, [[4096, 128], [1, 2048]]),
                     bass.AP(H["qTs"], nb * 262144, [[2048, 128], [1, 2048]]),
                     w=[qts_free[p]] if p in qts_free else [])
            qdma = (self.dq2, self.dq2.n)
            self.dma("y", bass.AP(self.XS, 0, [[2048, 128], [1024, 2], [1, 1024]]),
                     bass.AP(H["x"], nb * 256 * DM, [[1024, 128], [131072, 2], [1, 1024]]),
                     w=[xs_free] if xs_free else [])
            xsdma = self.g_y()
            for m in range(34):
                if m < 32:
                    mp = m % 2
                    sc_half = [None, None]
                    for halfg in range(2):
                        wlist = []
                        if exp_free[halfg] is not None:
                            wlist.append(exp_free[halfg])
                        if m == 0 and halfg == 0:
                            wlist.append(qdma)
                        for gg in range(4):
                            g = halfg * 4 + gg
                            self.mm(self.PSA[:, halfg * 1024 + gg * 256:
                                             halfg * 1024 + (gg + 1) * 256],
                                    self.KT[:, g * S + m * 128:g * S + (m + 1) * 128],
                                    self.TSL[:, p * 2048 + g * 256:
                                             p * 2048 + (g + 1) * 256],
                                    w=wlist if gg == 0 else ())
                        sc_half[halfg] = self.g_t()
                if m >= 2:
                    mc = m - 2
                    mpc = mc % 2
                    vn, gn = vg[mc]
                    for g in range(8):
                        for ns in range(2):
                            wlist = []
                            if g == 0 and ns == 0:
                                wlist = [vn]
                                if mc == 0 and xo_gate is not None:
                                    wlist.append(self.g_v())
                            elif g == 6 and ns == 0:
                                wlist = [gn]
                            self.mm(self.PSB[:, ns * 1024 + g * 128:
                                             ns * 1024 + (g + 1) * 128],
                                    self.LEB[:, mpc * 2048 + g * 256 + ns * 128:
                                             mpc * 2048 + g * 256 + ns * 128 + 128],
                                    self.VT[:, mc * 1024 + g * 128:
                                            mc * 1024 + (g + 1) * 128],
                                    start=(mc == 0 and g % 4 == 0), stop=(mc == 31),
                                    w=wlist)
                        if g == 3:
                            leb_h0[mpc] = self.g_t()
                    leb_all[mpc] = self.g_t()
                if m < 32:
                    e_half = [None, None]
                    for halfg in range(2):
                        if halfg == 0:
                            wexp = [tmax(sc_half[1], leb_h0[mp])]
                        else:
                            wexp = [tmax(sc_half[1], leb_all[mp])]
                        self.act(self.LEB[:, mp * 2048 + halfg * 1024:
                                          mp * 2048 + (halfg + 1) * 1024],
                                 self.PSA[:, halfg * 1024:(halfg + 1) * 1024], AF.Exp,
                                 scale=1.0 / 32.0, w=wexp)
                        exp_free[halfg] = self.g_s()
                        e_half[halfg] = self.g_s()
                    eb = mp * 2048
                    so = mp * 256
                    self.tt("v", bass.AP(self.SM1, 0, [[2048, 128], [256, 2], [1, 256]]),
                            bass.AP(self.LEB, eb, [[4096, 128], [512, 2], [1, 256]]),
                            bass.AP(self.LEB, eb + 256, [[4096, 128], [512, 2], [1, 256]]),
                            ALU.add, w=(e_half[0],))
                    self.tt("v", bass.AP(self.SM1, 512, [[2048, 128], [256, 2], [1, 256]]),
                            bass.AP(self.LEB, eb + 1024, [[4096, 128], [512, 2], [1, 256]]),
                            bass.AP(self.LEB, eb + 1280, [[4096, 128], [512, 2], [1, 256]]),
                            ALU.add, w=(e_half[1],))
                    self.tt("v", bass.AP(self.SM1, 1024, [[2048, 128], [256, 2], [1, 256]]),
                            bass.AP(self.SM1, 0, [[2048, 128], [512, 2], [1, 256]]),
                            bass.AP(self.SM1, 256, [[2048, 128], [512, 2], [1, 256]]),
                            ALU.add)
                    self.tt("v", SM2[:, so:so + 256], self.SM1[:, 1024:1280],
                            self.SM1[:, 1280:1536], ALU.add)
                    D = SM2[:, so:so + 256]
                    seed = self.SM1[:, 0:256].bitcast(I16)
                    t = self.SM1[:, 256:512]
                    self.ts("v", seed, D.bitcast(I16), -1, 0x7EF4, ALU.mult, ALU.add)
                    r0 = seed.bitcast(BF16)
                    self.tt("v", t, D, r0, ALU.mult)
                    self.ts("v", t, t, -1.0, 2.0, ALU.mult, ALU.add)
                    self.tt("v", SM2[:, so:so + 256], r0, t, ALU.mult)
                    rn = self.g_v()
                    rd6 = bass.AP(self.SM1, 1536 + so, [[2048, 128], [0, 6], [1, 256]])
                    self.tt("v", bass.AP(self.LEB, eb, [[4096, 128], [256, 6], [1, 256]]),
                            bass.AP(self.LEB, eb, [[4096, 128], [256, 6], [1, 256]]),
                            rd6, ALU.mult)
                    vn = self.g_v()
                    rd2 = bass.AP(self.SM1, 1536 + so, [[2048, 128], [0, 2], [1, 256]])
                    self.tt("g", bass.AP(self.LEB, eb + 1536, [[4096, 128], [256, 2], [1, 256]]),
                            bass.AP(self.LEB, eb + 1536, [[4096, 128], [256, 2], [1, 256]]),
                            rd2, ALU.mult, w=(rn,))
                    gn = self.g_g()
                    vg[m] = (vn, gn)
            ctx_done = self.g_t()
            qts_free[p] = ctx_done
            for u in range(2):
                wl = [ctx_done, xsdma] if u == 0 else []
                if u == 0 and xo_gate is not None:
                    wl.append(xo_gate)
                self.stt("v", self.XO[:, u * 1024:(u + 1) * 1024],
                         self.PSB[:, u * 1024:(u + 1) * 1024], 1.0,
                         self.XS[:, u * 1024:(u + 1) * 1024],
                         ALU.mult, ALU.add, w=wl)
            xs_free = self.g_v()
            self.dma("y", bass.AP(H["x1"], nb * 256 * DM, [[1024, 128], [131072, 2], [1, 1024]]),
                     bass.AP(self.XO, 0, [[2048, 128], [1024, 2], [1, 1024]]),
                     w=(self.g_v(),))
            xo_gate = self.g_y()

    def fff(self, f, s_, p2, dest, tbase=0, w_tsl=()):
        H = self.h
        fo = {"q": 0, "k": 1, "v": 2}[f]
        for u in range(2):
            po = 1024 + u * 256
            for kc in range(8):
                self.mm(self.PSA[:, po:po + 15],
                        self.TSL[:, tbase + kc * 256 + u * 128:
                                 tbase + kc * 256 + u * 128 + 128],
                        self.NWT[:, kc * 45 + fo * 15:kc * 45 + fo * 15 + 15],
                        start=(kc == 0), stop=False,
                        w=(self.g_s(), self.g_v()) + tuple(w_tsl)
                        if (u == 0 and kc == 0) else ())
            self.mm(self.PSA[:, po:po + 15], self.ONESR[:, 0:128],
                    self.NBR[:, fo * 15:fo * 15 + 15], start=False, stop=True)
        tr = self.g_t()
        for u in range(2):
            self.vcopy(self.ROUTE[:, u * 15:u * 15 + 15],
                       self.PSA[:, 1024 + u * 256:1024 + u * 256 + 15],
                       w=(tr,) if u == 0 else ())
        R = self.ROUTE
        C = self.RSC

        def Sv(j):
            return bass.AP(R, j, [[32, 128], [15, 2], [1, 1]])

        def c2(a):
            return C[:, a:a + 2]

        b0, b1, b2, b3 = c2(32), c2(34), c2(36), c2(38)
        c0, c1, cc2 = c2(40), c2(42), c2(44)
        sel, d, t = c2(46), c2(48), c2(50)
        n = [c2(52 + 2 * k) for k in range(4)]
        leaf = c2(60)
        mq = [c2(0 + 2 * k) for k in range(8)]
        Z = self.RSC[:, 16:18]
        O1 = self.RSC[:, 18:20]
        self.tt("v", b0, Sv(0), Z, ALU.is_ge, w=(self.g_v(), self.g_g()))
        self.tt("v", d, Sv(2), Sv(1), ALU.subtract)
        self.tt("v", sel, d, b0, ALU.mult)
        self.tt("v", sel, sel, Sv(1), ALU.add)
        self.tt("v", b1, sel, Z, ALU.is_ge)
        self.tt("v", c0, O1, b0, ALU.subtract)
        self.tt("v", c1, O1, b1, ALU.subtract)
        self.tt("v", n[0], c0, c1, ALU.mult)
        self.tt("v", n[1], c0, b1, ALU.mult)
        self.tt("v", n[2], b0, c1, ALU.mult)
        self.tt("v", n[3], b0, b1, ALU.mult)
        self.tt("v", sel, Sv(3), n[0], ALU.mult)
        for k, jj in enumerate((4, 5, 6)):
            self.tt("v", t, Sv(jj), n[k + 1], ALU.mult)
            self.tt("v", sel, sel, t, ALU.add)
        self.tt("v", b2, sel, Z, ALU.is_ge)
        self.tt("v", cc2, O1, b2, ALU.subtract)
        for k in range(4):
            self.tt("v", mq[2 * k], n[k], cc2, ALU.mult)
            self.tt("v", mq[2 * k + 1], n[k], b2, ALU.mult)
        self.tt("v", sel, Sv(7), mq[0], ALU.mult)
        for k in range(1, 8):
            self.tt("v", t, Sv(7 + k), mq[k], ALU.mult)
            self.tt("v", sel, sel, t, ALU.add)
        self.tt("v", b3, sel, Z, ALU.is_ge)
        self.tt("v", leaf, b0, b0, ALU.add)
        self.tt("v", leaf, leaf, b1, ALU.add)
        self.tt("v", leaf, leaf, leaf, ALU.add)
        self.tt("v", leaf, leaf, b2, ALU.add)
        self.tt("v", leaf, leaf, leaf, ALU.add)
        self.tt("v", leaf, leaf, b3, ALU.add)
        for u in range(2):
            self.ts("v", self.SM1[:, 2016 + u * 16:2016 + (u + 1) * 16], self.IOTA[:, :],
                    leaf[:, u:u + 1], None, ALU.is_equal,
                    w=(self.g_t(),) if u == 0 else ())
        ohv = self.g_v()
        w1_half = None
        for oc in range(8):
            for kc in range(8):
                self.mm(self.PSB[:, oc * 256:(oc + 1) * 256],
                        self.WBa[:, kc * 1024 + oc * 128:kc * 1024 + oc * 128 + 128],
                        self.TSL[:, tbase + kc * 256:tbase + (kc + 1) * 256],
                        start=(kc == 0), stop=False,
                        w=(self.g_s(),) if (oc == 0 and kc == 0) else ())
            self.mm(self.PSB[:, oc * 256:(oc + 1) * 256],
                    self.ROWS[:, oc * 128:(oc + 1) * 128],
                    self.ONESR[:, 0:256], start=False, stop=True)
            if oc == 3:
                w1_half = self.g_t()
        w1_done = self.g_t()
        for u in range(2):
            self.mm(self.PSA[0:16, 1536 + u * 128:1536 + (u + 1) * 128],
                    self.SM1[:, 2016 + u * 16:2016 + (u + 1) * 16], self.IDN[:, :],
                    w=(ohv,) if u == 0 else ())
        ohmm = self.g_t()
        self.vcopy(self.OHT[0:16, 0:256], self.PSA[0:16, 1536:1792], w=(ohmm,))
        oht = self.g_v()
        for c in range(8):
            dst = c * 256 if c < 4 else 1024 + (c - 4) * 256
            self.mm(self.PSA[:, dst:dst + 256],
                    self.E16[:, c * 128:(c + 1) * 128], self.OHT[:, 0:256],
                    w=(oht, self.g_v()) if c == 0 else ())
        maskmm = self.g_t()
        relu_half = [None, None]
        for half in range(2):
            wre = [w1_half if half == 0 else ohmm]
            if half == 0:
                wre.append(self.g_v())
            self.act(self.SM1[:, half * 1024:(half + 1) * 1024],
                     self.PSB[:, half * 1024:(half + 1) * 1024], AF.Relu, w=wre)
            relu_half[half] = self.g_s()
        hm = [None, None]
        for half in range(2):
            self.tt("v", self.SM1[:, half * 1024:(half + 1) * 1024],
                    self.SM1[:, half * 1024:(half + 1) * 1024],
                    self.PSA[:, half * 1024:(half + 1) * 1024], ALU.mult,
                    w=(maskmm, relu_half[half]))
            hm[half] = self.g_v()
        if dest in ("qT", "kT"):
            for kc in range(8):
                for oc in range(8):
                    wlist = ()
                    if oc == 0 and kc == 0:
                        wlist = (hm[0],)
                    elif oc == 0 and kc == 4:
                        wlist = (hm[1],)
                    self.mm(self.PSB[:, oc * 256:(oc + 1) * 256],
                            self.WBb[:, kc * 1024 + oc * 128:kc * 1024 + oc * 128 + 128],
                            self.SM1[:, kc * 256:(kc + 1) * 256],
                            start=(kc == 0), stop=False, w=wlist)
            for oc in range(8):
                self.mm(self.PSB[:, oc * 256:(oc + 1) * 256],
                        self.B2C[:, oc * 128:(oc + 1) * 128], self.OHT[:, 0:256],
                        start=False, stop=True)
            if dest == "qT":
                qstage = self.XO[:, p2 * 1024:(p2 + 1) * 1024].bitcast(BF16)
                self.act(qstage, self.PSB[:, :], AF.Copy, w=(self.g_t(), self.g_y()))
                qev = self.g_s()
                self.dma("y", bass.AP(H["qTs"], s_ * 262144, [[2048, 128], [1, 2048]]),
                         qstage,
                         w=(qev,))
            else:
                self.act(bass.AP(self.KT, s_ * 256, [[32768, 128], [4096, 8], [1, 256]]),
                         self.PSB[:, :], AF.Copy, w=(self.g_t(),))
        else:
            for u in range(2):
                po = u * 1024
                for kc in range(8):
                    for half in range(2):
                        wlist = ()
                        if u == 0 and half == 0 and kc == 0:
                            wlist = (hm[0],)
                        elif u == 0 and half == 0 and kc == 4:
                            wlist = (hm[1],)
                        self.mm(self.PSB[:, po + half * 512:po + (half + 1) * 512],
                                self.SM1[:, kc * 256 + u * 128:kc * 256 + u * 128 + 128],
                                self.WBb[:, kc * 1024 + half * 512:
                                         kc * 1024 + (half + 1) * 512],
                                start=(kc == 0), stop=False, w=wlist)
                for half in range(2):
                    self.mm(self.PSB[:, po + half * 512:po + (half + 1) * 512],
                            self.OHT[:, u * 128:(u + 1) * 128],
                            self.B2C[:, half * 512:(half + 1) * 512],
                            start=False, stop=True)
                j = 2 * s_ + u
                self.act(self.VT[:, j * 1024:(j + 1) * 1024],
                         self.PSB[:, po:po + 1024], AF.Copy, w=(self.g_t(),))
        return w1_done

    def _load_fff_weights(self, f):
        H = self.h
        self.memset("v", self.RSC[:, 16:18], 0.0, w=(self.g_t(),))
        self.memset("v", self.RSC[:, 18:20], 1.0)
        self.dma("g", bass.AP(self.WBa, 0, [[8192, 128], [1024, 8], [1, 1024]]),
                 bass.AP(H[f"w1_{f}"], 0, [[1024, 128], [131072, 8], [1, 1024]]))
        self.dma("g", bass.AP(self.WBb, 0, [[8192, 128], [1024, 8], [1, 1024]]),
                 bass.AP(H[f"w2_{f}"], 0, [[1024, 128], [131072, 8], [1, 1024]]))
        self.dma("g", self.ROWS[0:1, :], H[f"b1_{f}"][:, :])
        self.dma("g", self.B2C[0:16, :], H[f"b2_{f}"][:, :])
        return self.g_gd()

    def pC1(self):
        H = self.h
        self.memset("v", self.OHT[:, :], 0.0)
        wg = self._load_fff_weights("q")
        self.dma("g", bass.AP(self.NWT, 0, [[360, 128], [45, 8], [1, 45]]),
                 bass.AP(H["node_wt"], 0, [[45, 128], [45 * 128, 8], [1, 45]]))
        self.dma("g", self.NBR[0:1, 0:45], H["node_b"][:, :])
        lnb = self.LEB
        sc_f32 = self.LEB[:, 2048:4096].bitcast(F32)
        xs_free = [None, None]
        lnb_free = ((self.st, 0),)
        for s_ in range(NSLAB):
            p2 = s_ % 2
            xdma = [None, None]
            for u in range(2):
                w = [xs_free[u]] if xs_free[u] else []
                if s_ == 0 and u == 0:
                    w.append(wg)
                self.dma("s", bass.AP(self.XS, u * 1024, [[2048, 128], [1, 1024]]),
                         bass.AP(H["x1"], s_ * SLAB * DM + u * 131072,
                                 [[1024, 128], [1, 1024]]),
                         w=w)
                xdma[u] = (self.dq2, self.dq2.n)
            for u in range(2):
                self.ln_tile(u * 1024, u, lnb[:, u * 1024:(u + 1) * 1024], sc_f32,
                             w_x=(xdma[u],), w_apply=lnb_free)
                xs_free[u] = self.g_s()
            self.transpose_slab(lnb, 0, w_first=(self.g_v(),))
            lnb_free = (self.g_t(),)
            tev = self.g_s()
            self.dma("y", bass.AP(H["xlnT"], s_ * 262144, [[2048, 128], [1, 2048]]),
                     bass.AP(self.TSL, 0, [[4096, 128], [1, 2048]]),
                     w=(tev,))
            self.fff("q", s_, p2, "qT")

    def pC1b(self):
        H = self.h
        wg = self._load_fff_weights("k")
        tsl_free = {}
        for s_ in range(NSLAB):
            pr = s_ % 2
            w = [tsl_free[pr]] if pr in tsl_free else []
            if s_ == 0:
                w.append(wg)
            self.dma("y", bass.AP(self.TSL, pr * 2048, [[4096, 128], [1, 2048]]),
                     bass.AP(H["xlnT"], s_ * 262144, [[2048, 128], [1, 2048]]), w=w)
            tdma = self.g_y()
            tsl_free[pr] = self.fff("k", s_, pr, "kT", tbase=pr * 2048,
                                    w_tsl=(tdma,))

    def pC2(self):
        H = self.h
        wg = self._load_fff_weights("v")
        tsl_free = {}
        for s_ in range(NSLAB):
            pr = s_ % 2
            w = [tsl_free[pr]] if pr in tsl_free else []
            if s_ == 0:
                w.append(wg)
            self.dma("y", bass.AP(self.TSL, pr * 2048, [[4096, 128], [1, 2048]]),
                     bass.AP(H["xlnT"], s_ * 262144, [[2048, 128], [1, 2048]]), w=w)
            tdma = self.g_y()
            tsl_free[pr] = self.fff("v", s_, pr, "v", tbase=pr * 2048,
                                    w_tsl=(tdma,))

    def pD(self):
        H = self.h
        QSQ = self.SM1
        for piece in range(8):
            for c in range(8):
                self.tt("v", QSQ[:, 0:512],
                        self.KT[:, c * S + piece * 512:c * S + (piece + 1) * 512],
                        self.KT[:, c * S + piece * 512:c * S + (piece + 1) * 512],
                        ALU.mult, w=(self.g_t(),))
                self.mm(self.PSA[0:1, 0:512], self.ONESC[:, 0:1], QSQ[:, 0:512],
                        start=(c == 0), stop=(c == 7), w=(self.g_v(),))
            self.act(self.SROW[0:1, SR_STAGE:SR_STAGE + 512], self.PSA[0:1, 0:512],
                     AF.Sqrt, w=(self.g_t(),))
            if piece == 0:
                self.vred(self.SROW[0:1, SR_KMAX:SR_KMAX + 1],
                          self.SROW[0:1, SR_STAGE:SR_STAGE + 512], ALU.max,
                          w=(self.g_s(),))
            else:
                self.vred(self.SROW[0:1, SR_KMAX + 1:SR_KMAX + 2],
                          self.SROW[0:1, SR_STAGE:SR_STAGE + 512], ALU.max,
                          w=(self.g_s(),))
                self.tt("v", self.SROW[0:1, SR_KMAX:SR_KMAX + 1],
                        self.SROW[0:1, SR_KMAX:SR_KMAX + 1],
                        self.SROW[0:1, SR_KMAX + 1:SR_KMAX + 2], ALU.max)
        self.ts("v", self.SROW[0:1, SR_KNEG:SR_KNEG + 1],
                self.SROW[0:1, SR_KMAX:SR_KMAX + 1], -1.0, None, ALU.mult)
        kstar = self.g_v()
        exp_free = [None, None]
        leb_free = [None, None]
        qts_free = {}
        xs_free = None
        xo_gate = None
        for nb in range(16):
            p = nb % 2
            self.dma("y", bass.AP(self.TSL, p * 2048, [[4096, 128], [1, 2048]]),
                     bass.AP(H["qTs"], nb * 262144, [[2048, 128], [1, 2048]]),
                     w=[qts_free[p]] if p in qts_free else [])
            self.dma("y", bass.AP(self.XS, 0, [[2048, 128], [1024, 2], [1, 1024]]),
                     bass.AP(H["x1"], nb * 256 * DM, [[1024, 128], [131072, 2], [1, 1024]]),
                     w=[xs_free] if xs_free else [])
            qdma = self.g_y()
            self.tt("v", QSQ[:, 0:2048], self.TSL[:, p * 2048:p * 2048 + 2048],
                    self.TSL[:, p * 2048:p * 2048 + 2048], ALU.mult, w=(qdma, kstar))
            qsqv = self.g_v()
            for c in range(8):
                self.mm(self.PSA[0:1, 1280:1536], self.ONESC[:, 0:1],
                        QSQ[:, c * 256:(c + 1) * 256], start=(c == 0), stop=(c == 7),
                        w=(qsqv, self.g_s()) if c == 0 else ())
            self.act(self.SROW[0:1, SR_STAGE:SR_STAGE + 256], self.PSA[0:1, 1280:1536],
                     AF.Sqrt, w=(self.g_t(),))
            self.ts("v", self.ROWS[0:1, 0:256], self.SROW[0:1, SR_STAGE:SR_STAGE + 256],
                    self.SROW[0:1, SR_KNEG:SR_KNEG + 1], None, ALU.mult, w=(self.g_s(),))
            negb = self.g_v()
            expd_p = [None, None]
            for pair in range(17):
                if pair < 16:
                    for mi in range(2):
                        m = pair * 2 + mi
                        pcol = (pair % 2) * 512 + mi * 256
                        wlist = []
                        if mi == 0 and exp_free[pair % 2] is not None:
                            wlist.append(exp_free[pair % 2])
                        if m == 0:
                            wlist.append(negb)
                        for kc in range(8):
                            self.mm(self.PSA[:, pcol:pcol + 256],
                                    self.KT[:, kc * S + m * 128:kc * S + (m + 1) * 128],
                                    self.TSL[:, p * 2048 + kc * 256:
                                             p * 2048 + (kc + 1) * 256],
                                    start=(kc == 0), stop=False,
                                    w=wlist if kc == 0 else ())
                        self.mm(self.PSA[:, pcol:pcol + 256], self.ONESR[:, 0:128],
                                self.ROWS[:, 0:256], start=False, stop=True)
                    sc_done = self.g_t()
                    wexp = [sc_done]
                    if leb_free[pair % 2] is not None:
                        wexp.append(leb_free[pair % 2])
                    self.act(self.LEB[:, (pair % 2) * 512:(pair % 2) * 512 + 512],
                             self.PSA[:, (pair % 2) * 512:(pair % 2) * 512 + 512],
                             AF.Exp, scale=1.0 / 32.0, w=wexp)
                    expd_p[pair % 2] = self.g_s()
                    exp_free[pair % 2] = expd_p[pair % 2]
                if pair >= 1:
                    pp = pair - 1
                    expd = expd_p[pp % 2]
                    for mi in range(2):
                        mm_ = pp * 2 + mi
                        lb = (pp % 2) * 512 + mi * 256
                        for ns in range(2):
                            wl = []
                            if mi == 0 and ns == 0:
                                wl.append(expd)
                                if mm_ == 0 and xo_gate is not None:
                                    wl.append(self.g_v())
                            for half in range(2):
                                self.mm(self.PSB[:, ns * 1024 + half * 512:
                                                 ns * 1024 + (half + 1) * 512],
                                        self.LEB[:, lb + ns * 128:lb + ns * 128 + 128],
                                        self.VT[:, mm_ * 1024 + half * 512:
                                                mm_ * 1024 + (half + 1) * 512],
                                        start=(mm_ == 0), stop=(mm_ == 31),
                                        w=wl if half == 0 else ())
                        self.mm(self.PSA[0:1, 1024:1280], self.ONESC[:, 0:1],
                                self.LEB[:, lb:lb + 256],
                                start=(mm_ == 0), stop=(mm_ == 31))
                    leb_free[pp % 2] = self.g_t()
            ctx_done = self.g_t()
            qts_free[p] = ctx_done
            self.act(self.ROWS[0:1, 256:512], self.PSA[0:1, 1024:1280], AF.Copy,
                     w=(ctx_done,))
            drow = self.g_s()
            for u in range(2):
                self.mm(self.PSA[0:128, 1536 + u:1537 + u],
                        self.ROWS[:, 256 + u * 128:256 + (u + 1) * 128],
                        self.ONESR[:, 0:1], w=(drow,) if u == 0 else ())
            self.vrecip(self.SCL[:, 2:4], self.PSA[:, 1536:1538], w=(self.g_t(),))
            for u in range(2):
                wl = []
                if u == 0 and xo_gate is not None:
                    wl.append(xo_gate)
                self.stt("v", self.XO[:, u * 1024:(u + 1) * 1024],
                         self.PSB[:, u * 1024:(u + 1) * 1024], self.SCL[:, 2 + u:3 + u],
                         self.XS[:, u * 1024:(u + 1) * 1024],
                         ALU.mult, ALU.add, w=wl)
            xs_free = self.g_v()
            self.dma("y", bass.AP(H["x2"], nb * 256 * DM, [[1024, 128], [131072, 2], [1, 1024]]),
                     bass.AP(self.XO, 0, [[2048, 128], [1024, 2], [1, 1024]]),
                     w=(self.g_v(),))
            xo_gate = self.g_y()

    def pE(self):
        H = self.h
        KT = self.KT
        self.memset("v", KT[:, KE_CQ:KE_CQ + 4096], 0.0, w=(self.g_t(),))
        self.memset("v", KT[:, KE_KV:KE_KV + S], 0.0)
        self.memset("v", KT[:, KE_WQC:KE_WQC + 512], 0.0)
        self.memset("v", KT[:, KE_BQC:KE_BQC + 1024], 0.0)
        self.memset("v", KT[:, KE_BFIN:KE_BFIN + 1024], 0.0)
        self.memset("v", self.VT[:, 2048:6144], 0.0)
        self.dma("g", bass.AP(self.WBa, 0, [[8192, 128], [1024, 8], [1, 1024]]),
                 bass.AP(H["w_o"], 0, [[1024, 128], [131072, 8], [1, 1024]]))
        for nm, col in (("w_uqT", KE_UQT), ("w_ukT", KE_UKT), ("w_uvT", KE_UVT)):
            self.dma("g", bass.AP(KT, col, [[32768, 128], [32, 8], [1, 32]]),
                     bass.AP(H[nm], 0, [[32, 128], [4096, 8], [1, 32]]))
        self.dma("g", KT[0:32, KE_WQR:KE_WQR + 256], H["w_qr"][:, :])
        self.dma("g", bass.AP(self.STAT, 0, [[16, 128], [1, 8]]),
                 bass.AP(H["b_uq"], 0, [[1, 128], [128, 8]]),
                 allow_slow_non_contiguous=True)
        self.dma("g", bass.AP(self.STAT, 8, [[16, 128], [1, 8]]),
                 bass.AP(H["b_uv"], 0, [[1, 128], [128, 8]]),
                 allow_slow_non_contiguous=True)
        self.dma("g", self.SROW[0:1, SR_STAGE:SR_STAGE + 256], H["b_qr"][:, :])
        self.dma("g", self.SM1[0:1, 1024:2048], H["b_o"][:, :])
        gd = self.g_gd()
        self.vcopy(self.SM1[:, 0:16], self.STAT[:, 0:16], w=(gd,))
        self.vcopy(bass.AP(KT, KE_BQC + 32, [[32768, 1], [64, 8], [1, 32]]),
                   bass.AP(self.SROW, SR_STAGE, [[576, 1], [32, 8], [1, 32]]))
        for h8 in range(8):
            uq = KT[:, KE_UQT + h8 * 32:KE_UQT + (h8 + 1) * 32]
            uk = KT[:, KE_UKT + h8 * 32:KE_UKT + (h8 + 1) * 32]
            self.mm(self.PSA[0:32, h8 * 64:h8 * 64 + 32], uq, uk,
                    w=(gd,) if h8 == 0 else ())
            self.mm(self.PSA[0:1, h8 * 64 + 32:h8 * 64 + 64],
                    self.SM1[:, h8:h8 + 1], uk,
                    w=(self.g_v(),) if h8 == 0 else ())
        tg = self.g_t()
        for h8 in range(8):
            self.vcopy(KT[0:32, KE_WQC + h8 * 64:KE_WQC + h8 * 64 + 32],
                       self.PSA[0:32, h8 * 64:h8 * 64 + 32], w=(tg,) if h8 == 0 else ())
            self.vcopy(KT[0:1, KE_BQC + h8 * 64:KE_BQC + h8 * 64 + 32],
                       self.PSA[0:1, h8 * 64 + 32:h8 * 64 + 64])
            self.vcopy(KT[0:32, KE_WQC + h8 * 64 + 32:KE_WQC + (h8 + 1) * 64],
                       KT[0:32, KE_WQR + h8 * 32:KE_WQR + (h8 + 1) * 32])
        vg = self.g_v()
        for h8 in range(8):
            for half in range(2):
                self.mm(self.PSB[0:32, half * 512:(half + 1) * 512],
                        KT[:, KE_UVT + h8 * 32:KE_UVT + (h8 + 1) * 32],
                        self.WBa[:, h8 * 1024 + half * 512:h8 * 1024 + (half + 1) * 512],
                        w=(vg,) if h8 == 0 and half == 0 else
                        (self.g_s(),) if half == 0 else ())
            self.act(KT[(h8 % 4) * 32:(h8 % 4) * 32 + 32,
                        KE_UVO + (h8 // 4) * 1024:KE_UVO + (h8 // 4 + 1) * 1024],
                     self.PSB[0:32, 0:1024], AF.Copy, w=(self.g_t(),))
        for kc in range(8):
            for half in range(2):
                self.mm(self.PSA[0:1, 1024 + half * 512:1024 + (half + 1) * 512],
                        self.SM1[:, 8 + kc:9 + kc],
                        self.WBa[:, kc * 1024 + half * 512:kc * 1024 + (half + 1) * 512],
                        start=(kc == 0), stop=False,
                        w=(self.g_v(), gd) if kc == 0 and half == 0 else ())
        for half in range(2):
            self.mm(self.PSA[0:1, 1024 + half * 512:1024 + (half + 1) * 512],
                    self.ONESR[0:1, 0:1],
                    self.SM1[0:1, 1024 + half * 512:1024 + (half + 1) * 512],
                    start=False, stop=True)
        self.vcopy(KT[0:1, KE_BFIN:KE_BFIN + 1024], self.PSA[0:1, 1024:2048],
                   w=(self.g_t(),))
        self.dma("g", bass.AP(self.WBb, 0, [[8192, 128], [32, 8], [1, 32]]),
                 bass.AP(H["w_dq"], 0, [[32, 128], [4096, 8], [1, 32]]))
        self.dma("g", bass.AP(self.WBb, 256, [[8192, 128], [64, 8], [1, 64]]),
                 bass.AP(H["w_kv"], 0, [[64, 128], [8192, 8], [1, 64]]))
        self.dma("g", self.ROWS[0:1, 0:32], H["b_dq"][:, :])
        self.dma("g", self.ROWS[0:1, 32:96], H["b_kv"][:, :])
        wg = self.g_gd()
        self.memset("v", KT[64:65, KE_KV:KE_KV + S], 1.0, w=(self.g_t(),))
        EIDF = KT[:, 14848:15104].bitcast(F32)
        self.vcopy(EIDF, self.IDN[:, :])
        onk = self.g_v()
        lnb = self.LEB
        sc_f32 = self.LEB[:, 2048:4096].bitcast(F32)
        xs_free = None
        lnb_free = ((self.st, 0),)
        for s_ in range(NSLAB):
            w = [xs_free] if xs_free else []
            if s_ == 0:
                w.append(wg)
            self.dma("y", bass.AP(self.XS, 0, [[2048, 128], [1024, 2], [1, 1024]]),
                     bass.AP(H["x2"], s_ * SLAB * DM, [[1024, 128], [131072, 2], [1, 1024]]),
                     w=w)
            xdma = self.g_y()
            for u in range(2):
                self.ln_tile(u * 1024, u, lnb[:, u * 1024:(u + 1) * 1024], sc_f32,
                             w_x=(xdma,) if u == 0 else (), w_apply=lnb_free)
            xs_free = self.g_s()
            self.transpose_slab(lnb, 0, w_first=(self.g_v(),))
            lnb_free = (self.g_t(),)
            tev = self.g_s()
            for kc in range(8):
                self.mm(self.PSA[0:32, 0:256], self.WBb[:, kc * 32:(kc + 1) * 32],
                        self.TSL[:, kc * 256:(kc + 1) * 256],
                        start=(kc == 0), stop=False,
                        w=(tev, self.g_v()) if kc == 0 else ())
            self.mm(self.PSA[0:32, 0:256], self.ROWS[:, 0:32], self.ONESR[:, 0:256],
                    start=False, stop=True)
            for kc in range(8):
                self.mm(self.PSA[0:64, 256:512],
                        self.WBb[:, 256 + kc * 64:256 + (kc + 1) * 64],
                        self.TSL[:, kc * 256:(kc + 1) * 256],
                        start=(kc == 0), stop=False)
            self.mm(self.PSA[0:64, 256:512], self.ROWS[:, 32:96], self.ONESR[:, 0:256],
                    start=False, stop=True)
            self.vcopy(KT[0:32, KE_CQ + s_ * 256:KE_CQ + (s_ + 1) * 256],
                       self.PSA[0:32, 0:256], w=(self.g_t(),))
            self.vcopy(KT[0:64, KE_KV + s_ * 256:KE_KV + (s_ + 1) * 256],
                       self.PSA[0:64, 256:512])
        prj = self.g_v()
        self.memset("v", bass.AP(self.VT, 32, [[32768, 128], [33, 32], [1, 1]]), 1.0,
                    w=(self.g_t(),))
        for mq4 in range(8):
            for k4 in range(4):
                m = mq4 * 4 + k4
                w = ()
                if m == 0:
                    w = (prj, onk, self.g_v())
                elif k4 == 0:
                    w = (self.g_v(),)
                self.mm(self.PSB[:, k4 * 128:k4 * 128 + 65],
                        KT[0:65, KE_KV + m * 128:KE_KV + (m + 1) * 128],
                        self.IDN[0:65, 0:65], w=w)
            self.vcopy(bass.AP(self.VT, mq4 * 4 * 33, [[32768, 128], [33, 4], [1, 32]]),
                       bass.AP(self.PSB, 0, [[2048, 128], [128, 4], [1, 32]]),
                       w=(self.g_t(),))
        QSQ = self.SM1
        for piece in range(8):
            self.tt("v", QSQ[0:64, 0:512],
                    KT[0:64, KE_KV + piece * 512:KE_KV + (piece + 1) * 512],
                    KT[0:64, KE_KV + piece * 512:KE_KV + (piece + 1) * 512], ALU.mult)
            self.mm(self.PSA[0:1, 0:512], self.ONESC[0:64, 0:1], QSQ[0:64, 0:512],
                    w=(self.g_v(),))
            self.act(self.SROW[0:1, SR_STAGE:SR_STAGE + 512], self.PSA[0:1, 0:512],
                     AF.Sqrt, w=(self.g_t(),))
            if piece == 0:
                self.vred(self.SROW[0:1, SR_KMAX:SR_KMAX + 1],
                          self.SROW[0:1, SR_STAGE:SR_STAGE + 512], ALU.max,
                          w=(self.g_s(),))
            else:
                self.vred(self.SROW[0:1, SR_KMAX + 1:SR_KMAX + 2],
                          self.SROW[0:1, SR_STAGE:SR_STAGE + 512], ALU.max,
                          w=(self.g_s(),))
                self.tt("v", self.SROW[0:1, SR_KMAX:SR_KMAX + 1],
                        self.SROW[0:1, SR_KMAX:SR_KMAX + 1],
                        self.SROW[0:1, SR_KMAX + 1:SR_KMAX + 2], ALU.max)
        self.ts("v", self.SROW[0:1, SR_KNEG:SR_KNEG + 1],
                self.SROW[0:1, SR_KMAX:SR_KMAX + 1], -1.0, None, ALU.mult)
        kstar = self.g_v()
        scale = 1.0 / math.sqrt(160.0)
        exp_free = [None, None]
        leb_free = [None, None, None, None]
        expd_all = [None, None, None, None]
        norm_done = [None, None]
        rc_last = None
        xs_gate = None
        xo_gate = None
        QA = self.VT
        BCV = KT[0:32, 15360:16384].bitcast(F32)
        for nb in range(8):
            for h8 in range(8):
                pq = 1024 + (h8 % 2) * 512
                wq = (self.g_s(),)
                if h8 <= 1:
                    wq = (kstar,) + ((exp_free[1],) if exp_free[1] else (self.g_s(),))
                    if rc_last is not None:
                        wq = wq + (rc_last,)
                self.mm(self.PSA[0:64, pq:pq + 512],
                        KT[:, KE_WQC + h8 * 64:KE_WQC + (h8 + 1) * 64],
                        KT[:, KE_CQ + nb * 512:KE_CQ + (nb + 1) * 512],
                        start=True, stop=False, w=wq)
                self.mm(self.PSA[0:64, pq:pq + 512],
                        KT[:, KE_BQC + h8 * 64:KE_BQC + (h8 + 1) * 64],
                        self.ONESR[:, 0:512], start=False, stop=True)
                tq = self.g_t()
                self.act(QSQ[0:64, 0:512], self.PSA[0:64, pq:pq + 512], AF.Square,
                         w=(tq,))
                self.act(QA[0:64, 2048 + h8 * 512:2048 + (h8 + 1) * 512],
                         self.PSA[0:64, pq:pq + 512], AF.Copy)
                sq = self.g_s()
                self.mm(self.PSB[0:1, 1536:2048], self.ONESC[0:64, 0:1], QSQ[0:64, 0:512],
                        w=(sq, rc_last) if (h8 == 0 and rc_last is not None) else (sq,))
                self.act(self.SROW[0:1, SR_STAGE:SR_STAGE + 512], self.PSB[0:1, 1536:2048],
                         AF.Sqrt, w=(self.g_t(),))
                self.ts("v", QA[64:65, 2048 + h8 * 512:2048 + (h8 + 1) * 512],
                        self.SROW[0:1, SR_STAGE:SR_STAGE + 512],
                        self.SROW[0:1, SR_KNEG:SR_KNEG + 1], None, ALU.mult,
                        w=(self.g_s(),))
            qaug = self.g_v()

            def dance_stages(hprev, ctxh_gate):
                hv1 = hprev % 2
                st = {}

                def st0():
                    self.act(self.SROW[0:1, SR_STAGE:SR_STAGE + 512],
                             self.PSB[32:33, hv1 * 512:hv1 * 512 + 512], AF.Copy,
                             w=(ctxh_gate,))
                    st["drow"] = self.g_s()

                def st1():
                    for u in range(4):
                        wl = ()
                        if u == 0:
                            wl = (st["drow"],)
                            if rc_last is not None:
                                wl = wl + (rc_last,)
                        self.mm(self.PSB[0:128, 1540 + u:1541 + u],
                                self.SROW[0:1, SR_STAGE + u * 128:
                                          SR_STAGE + (u + 1) * 128],
                                self.SROW[0:1, SR_ONES:SR_ONES + 1], w=wl)
                    self.vrecip(self.FQ4[:, 0:4], self.PSB[:, 1540:1544],
                                w=(self.g_t(),))
                    st["rc"] = self.g_v()

                def st2():
                    for u in range(4):
                        self.mm(self.PSB[0:1, 1024 + u * 128:1024 + (u + 1) * 128],
                                self.FQ4[:, u:u + 1], EIDF,
                                w=(st["rc"],) if u == 0 else ())
                    self.act(self.SROW[0:1, SR_STAGE:SR_STAGE + 512],
                             self.PSB[0:1, 1024:1536], AF.Copy, w=(self.g_t(),))
                    st["rrow"] = self.g_s()

                def st3():
                    self.mm(self.PSB[0:32, 1024:1536],
                            self.SROW[0:1, SR_ONES:SR_ONES + 32],
                            self.SROW[0:1, SR_STAGE:SR_STAGE + 512], w=(st["rrow"],))
                    self.vcopy(BCV, self.PSB[0:32, 1024:1536], w=(self.g_t(),))
                    self.tt("v", self.TSL[(hprev % 4) * 32:(hprev % 4) * 32 + 32,
                                          (hprev // 4) * 512:(hprev // 4) * 512 + 512],
                            self.PSB[0:32, hv1 * 512:hv1 * 512 + 512], BCV, ALU.mult)
                    norm_done[hv1] = self.g_v()
                return [st0, st1, st2, st3], st

            pend = None
            ctxh = None
            for h8 in range(8):
                hv = h8 % 2
                if ctxh is not None:
                    pend = dance_stages(h8 - 1, ctxh)
                for pair in range(17):
                    if pair < 16:
                        pc = (pair % 2) * 1024
                        sl = pair % 4
                        wl = []
                        if exp_free[pair % 2] is not None:
                            wl.append(exp_free[pair % 2])
                        if h8 == 0 and pair <= 1:
                            wl.append(qaug)
                        for mi in range(2):
                            m = pair * 2 + mi
                            self.mm(self.PSA[:, pc + mi * 512:pc + (mi + 1) * 512],
                                    KT[:, KE_KV + m * 128:KE_KV + (m + 1) * 128],
                                    QA[:, 2048 + h8 * 512:2048 + (h8 + 1) * 512],
                                    w=wl if mi == 0 else ())
                        sc = self.g_t()
                        wexp = [sc]
                        if leb_free[sl] is not None:
                            wexp.append(leb_free[sl])
                        self.act(self.LEB[:, sl * 1024:(sl + 1) * 1024],
                                 self.PSA[:, pc:pc + 1024], AF.Exp, scale=scale, w=wexp)
                        expd_all[sl] = self.g_s()
                        exp_free[pair % 2] = expd_all[sl]
                    if pair >= 1:
                        pp = pair - 1
                        psl = pp % 4
                        expd = expd_all[psl]
                        for mi in range(2):
                            m = pp * 2 + mi
                            wl = []
                            if mi == 0:
                                wl.append(expd)
                                if pp == 0 and norm_done[hv] is not None:
                                    wl.append(norm_done[hv])
                            self.mm(self.PSB[0:33, hv * 512:hv * 512 + 512],
                                    bass.AP(self.VT, m * 33, [[32768, 128], [1, 33]]),
                                    self.LEB[:, psl * 1024 + mi * 512:
                                             psl * 1024 + (mi + 1) * 512],
                                    start=(m == 0), stop=(m == 31), w=wl)
                        leb_free[psl] = self.g_t()
                    if pend is not None and pair < 4:
                        pend[0][pair]()
                        if pair == 1:
                            rc_last = pend[1]["rc"]
                ctxh = self.g_t()
            tail = dance_stages(7, ctxh)
            for k_, f_ in enumerate(tail[0]):
                f_()
                if k_ == 1:
                    rc_last = tail[1]["rc"]
            ctxn = self.g_v()
            for ns in range(4):
                wdma_l = []
                if ns < 2:
                    if xs_gate is not None:
                        wdma_l.append(xs_gate)
                else:
                    wdma_l.append(self.g_v())
                self.dma("s", bass.AP(self.XS, (ns % 2) * 1024, [[2048, 128], [1, 1024]]),
                         bass.AP(H["x2"], (nb * 512 + ns * 128) * DM,
                                 [[1024, 128], [1, 1024]]),
                         w=wdma_l)
                xdma = (self.dq2, self.dq2.n)
                wl = [ctxn, xdma] if ns == 0 else [self.g_v(), xdma]
                if ns in (0, 2) and xo_gate is not None:
                    wl.append(xo_gate)
                for j in range(2):
                    for half in range(2):
                        self.mm(self.PSB[:, 1024 + half * 512:1024 + (half + 1) * 512],
                                self.TSL[:, j * 512 + ns * 128:j * 512 + ns * 128 + 128],
                                KT[:, KE_UVO + j * 1024 + half * 512:
                                   KE_UVO + j * 1024 + (half + 1) * 512],
                                start=(j == 0), stop=False,
                                w=wl if (j == 0 and half == 0) else ())
                for half in range(2):
                    self.mm(self.PSB[:, 1024 + half * 512:1024 + (half + 1) * 512],
                            self.ONESR[:, 0:128],
                            KT[:, KE_BFIN + half * 512:KE_BFIN + (half + 1) * 512],
                            start=False, stop=True)
                self.stt("v", self.XO[:, (ns % 2) * 1024:(ns % 2 + 1) * 1024],
                         self.PSB[:, 1024:2048], 1.0,
                         self.XS[:, (ns % 2) * 1024:(ns % 2 + 1) * 1024],
                         ALU.mult, ALU.add, w=(self.g_t(),))
                if ns % 2 == 1:
                    self.dma("y", bass.AP(H["x3"], (nb * 512 + (ns - 1) * 128) * DM,
                                          [[1024, 128], [131072, 2], [1, 1024]]),
                             bass.AP(self.XO, 0, [[2048, 128], [1024, 2], [1, 1024]]),
                             w=(self.g_v(),))
                    xo_gate = self.g_y()
            xs_gate = self.g_v()

    def quant_tile(self, xin, xout, dqp, dqn, perblock=False, w0=(), w_out=()):
        e = "v"
        F1 = self.SM1[:, 0:2048].bitcast(F32)
        F2 = self.LEB[:, 0:2048].bitcast(F32)
        F3 = self.KT[:, 0:2048].bitcast(F32)
        F4 = self.FQ4
        AM = F4[:, 0:32]
        EB = F4[:, 32:64]
        RS = F4[:, 64:96]
        SC = F4[:, 96:128]
        x3d = xin.rearrange("p (a b) -> p a b", b=32)
        self.vred(AM, x3d, ALU.max, apply_absolute_value=True, w=w0)
        self.ts(e, EB.bitcast(I32), AM.bitcast(I32), 23, None, ALU.logical_shift_right)
        self.ts(e, EB.bitcast(I32), EB.bitcast(I32), 1, None, ALU.max)
        self.ts(e, RS.bitcast(I32), EB.bitcast(I32), -1, 256, ALU.mult, ALU.add)
        self.ts(e, RS.bitcast(I32), RS.bitcast(I32), 23, None, ALU.logical_shift_left)
        self.ts(e, SC.bitcast(I32), EB.bitcast(I32), -2, None, ALU.add)
        self.ts(e, SC.bitcast(I32), SC.bitcast(I32), 23, None, ALU.logical_shift_left)
        if perblock:
            self.tt(e, RS, RS, dqn, ALU.mult)
            self.tt(e, SC, SC, dqp, ALU.mult)
        else:
            self.ts(e, RS, RS, dqn, None, ALU.mult)
            self.ts(e, SC, SC, dqp, None, ALU.mult)
        rs_b = bass.AP(self.FQ4, 64, [[128, 128], [1, 32], [0, 32]])
        sc_b = bass.AP(self.FQ4, 96, [[128, 128], [1, 32], [0, 32]])
        self.tt(e, F1.rearrange("p (a b) -> p a b", b=32), x3d, rs_b, ALU.mult)
        self.stt(e, F2, F1, -1.0, F1, ALU.mult, ALU.max)
        self.ts(e, F2, F2, 6.0, None, ALU.min)
        self.ts(e, F3.bitcast(I32), F2.bitcast(I32), 23, None, ALU.logical_shift_right)
        self.ts(e, F3.bitcast(I32), F3.bitcast(I32), 127, 129, ALU.max, ALU.min)
        self.ts(e, F1.bitcast(I32), F3.bitcast(I32), -1, 255, ALU.mult, ALU.add)
        self.ts(e, F1.bitcast(I32), F1.bitcast(I32), 23, None, ALU.logical_shift_left)
        self.tt(e, F2, F2, F1, ALU.mult)
        self.ts(e, F2, F2, 8388608.0, None, ALU.add)
        self.ts(e, F2, F2, 8388608.0, None, ALU.subtract)
        self.ts(e, F3.bitcast(I32), F3.bitcast(I32), -1, None, ALU.add)
        self.ts(e, F3.bitcast(I32), F3.bitcast(I32), 23, None, ALU.logical_shift_left)
        self.tt(e, F2, F2, F3, ALU.mult)
        self.ts(e, F3.bitcast(I32), xin.bitcast(I32), -2147483648, None, ALU.bitwise_and)
        self.r.op(e, "tensor_copy", F1.rearrange("p (a b) -> p a b", b=32), sc_b,
                  i=((self.sv, 1),))
        self.tt(e, F1.bitcast(I32), F1.bitcast(I32), F3.bitcast(I32), ALU.bitwise_or)
        self.tt(e, xout, F2, F1, ALU.mult, w=w_out)

    def pW(self):
        H = self.h
        XQB = self.LEB[:, 2048:4096]
        DQW = self.RSC
        for oc in range(8):
            self.dma("y", self.XO[:, 0:1024],
                     bass.AP(H["w_f"], oc * 128 * DM, [[1024, 128], [1, 1024]]),
                     w=(self.g_v(), self.g_t()))
            self.dma("y", DQW[:, 0:32],
                     bass.AP(H["delta_w"], oc * 128 * 32, [[32, 128], [1, 32]]))
            wdma = self.g_y()
            self.act(DQW[:, 0:32], DQW[:, 0:32], AF.Tanh, scale=4.0, w=(wdma,))
            self.act(DQW[:, 32:64], DQW[:, 0:32], AF.Exp, scale=-0.5 * LN2)
            self.act(DQW[:, 0:32], DQW[:, 0:32], AF.Exp, scale=0.5 * LN2)
            dq = self.g_s()
            self.quant_tile(self.XO[:, 0:1024], XQB[:, 0:1024],
                            DQW[:, 0:32], DQW[:, 32:64],
                            perblock=True, w0=(dq, wdma), w_out=(self.g_t(),))
            qd = self.g_v()
            for dc in range(8):
                w = ()
                if dc == 0:
                    w = (qd, self.g_s())
                elif dc == 4:
                    w = (self.g_s(),)
                self.mm(self.PSA[:, 1024 + (dc % 4) * 128:1024 + (dc % 4) * 128 + 128],
                        XQB[:, dc * 128:(dc + 1) * 128], self.IDN[:, :], w=w)
                if dc % 4 == 3:
                    self.act(bass.AP(self.WBa, (dc - 3) * 1024 + oc * 128,
                                     [[8192, 128], [1024, 4], [1, 128]]),
                             self.PSA[:, 1024:1536], AF.Copy, w=(self.g_t(),))
        self.dma("g", self.ROWS[0:1, :], H["b_f"][:, :])

    def pF(self):
        H = self.h
        dqp = self.SCL[:, 0:1]
        dqn = self.SCL[:, 1:2]
        XQB = self.LEB[:, 2048:4096]
        lnf = self.VT[:, 0:2048].bitcast(F32)
        lnscr = self.KT[:, 2048:4096].bitcast(F32)
        xs_free = None
        xo_gate = None
        rowg = self.g_gd()
        for s_ in range(NSLAB):
            p2 = s_ % 2
            w = [xs_free] if xs_free else []
            if s_ == 0:
                w.append(rowg)
            self.dma("y", bass.AP(self.XS, 0, [[2048, 128], [1024, 2], [1, 1024]]),
                     bass.AP(H["x3"], s_ * SLAB * DM, [[1024, 128], [131072, 2], [1, 1024]]),
                     w=w)
            xdma = self.g_y()
            for u in range(2):
                self.ln_tile(u * 1024, u, lnf,
                             lnscr, w_x=(xdma, self.g_v()) if u == 0 else (self.g_v(),))
                lng = self.g_s()
                self.quant_tile(lnf, XQB[:, u * 1024:(u + 1) * 1024], dqp, dqn,
                                w0=(lng, self.g_t()), w_out=(self.g_t(),))
                qd = self.g_v()
                for c in range(8):
                    w2 = ()
                    if c == 0:
                        w2 = (qd, self.g_s())
                    elif c == 4:
                        w2 = (self.g_s(),)
                    self.mm(self.PSA[:, 1024 + (c % 4) * 128:1024 + (c % 4) * 128 + 128],
                            XQB[:, u * 1024 + c * 128:u * 1024 + (c + 1) * 128],
                            self.IDN[:, :], w=w2)
                    if c % 4 == 3:
                        self.act(bass.AP(self.TSL, p2 * 2048 + (c - 3) * 256 + u * 128,
                                         [[4096, 128], [256, 4], [1, 128]]),
                                 self.PSA[:, 1024:1536], AF.Copy, w=(self.g_t(),))
            tq = self.g_s()
            for u in range(2):
                po = u * 1024
                for kc in range(8):
                    for half in range(2):
                        self.mm(self.PSB[:, po + half * 512:po + (half + 1) * 512],
                                self.TSL[:, p2 * 2048 + kc * 256 + u * 128:
                                         p2 * 2048 + kc * 256 + u * 128 + 128],
                                self.WBa[:, kc * 1024 + half * 512:
                                         kc * 1024 + (half + 1) * 512],
                                start=(kc == 0), stop=False,
                                w=(tq, self.g_v()) if (u == 0 and kc == 0 and half == 0)
                                else ())
                for half in range(2):
                    self.mm(self.PSB[:, po + half * 512:po + (half + 1) * 512],
                            self.ONESR[:, 0:128],
                            self.ROWS[:, half * 512:(half + 1) * 512],
                            start=False, stop=True)
                wst = [self.g_t()]
                if u == 0 and xo_gate is not None:
                    wst.append(xo_gate)
                self.stt("v", self.XO[:, u * 1024:(u + 1) * 1024],
                         self.PSB[:, po:po + 1024], 1.0,
                         self.XS[:, u * 1024:(u + 1) * 1024],
                         ALU.mult, ALU.add, w=wst)
            xs_free = self.g_v()
            self.dma("y", bass.AP(H["out"], s_ * SLAB * DM, [[1024, 128], [131072, 2], [1, 1024]]),
                     bass.AP(self.XO, 0, [[2048, 128], [1024, 2], [1, 1024]]),
                     w=(self.g_v(),))
            xo_gate = self.g_y()

    def build(self, phases):
        table = {"0": self.p0, "A1": self.pA1, "A2": self.pA2, "B": self.pB,
                 "C1": self.pC1, "C1b": self.pC1b, "C2": self.pC2, "D": self.pD,
                 "E": self.pE, "W": self.pW, "F": self.pF}
        for ph in phases:
            table[ph]()
            self.fence()
        for nm in self.dbg.get("dump_sbuf", ()):
            t = getattr(self, nm)
            self.dma("y", self.h["dbg_" + nm][:, :], t[:, :])
        self.r.op("y", None, w=((self.dy, self.dy.n), (self.dg, self.dg.n)))
        rec = self.r
        with self.nc.Block() as blk:
            @blk.tensor
            def _(eng):
                rec.replay("t", eng)

            @blk.vector
            def _(eng):
                rec.replay("v", eng)

            @blk.scalar
            def _(eng):
                rec.replay("s", eng)

            @blk.gpsimd
            def _(eng):
                rec.replay("g", eng)

            @blk.sync
            def _(eng):
                rec.replay("y", eng)


ALL_PHASES = ("0", "A1", "A2", "B", "C1", "C1b", "C2", "D", "E", "W", "F")


def build_nc(phases=ALL_PHASES, dbg=None):
    nc = bass.Bass("TRN2", target_bir_lowering=False)
    mk = MK(nc, dbg=dbg)
    mk.build(phases)
    return nc, mk


def make_inputs(inp, core):
    x = np.asarray(inp["L0_attn_hidden_states"])
    lay = host_layout(inp)
    cst = host_constants()
    m = {"x": np.ascontiguousarray(x[core], dtype=np.float32)}
    m.update(lay)
    m.update(cst)
    return m


def kernel(**inputs):
    nc, mk = build_nc()
    in_maps = [make_inputs(inputs, c) for c in range(8)]
    res = run_bass_kernel_spmd(nc, in_maps, core_ids=list(range(8)))
    out = np.stack([np.asarray(r["out"]) for r in res.results], axis=0)
    return out.astype(np.float32)


# version 15
# speedup vs baseline: 1.0389x; 1.0049x over previous
import math
import numpy as np
import ml_dtypes

import concourse.bass as bass
import concourse.mybir as mybir
from concourse.bass_utils import run_bass_kernel_spmd

F32 = mybir.dt.float32
BF16 = mybir.dt.bfloat16
I32 = mybir.dt.int32
I16 = mybir.dt.int16
AF = mybir.ActivationFunctionType
ALU = mybir.AluOpType
AX = mybir.AxisListType

S = 4096
DM = 1024
NSLAB = 16
SLAB = 256

BFD = ml_dtypes.bfloat16
LN2 = math.log(2.0)

SR_STAGE = 0
SR_KMAX = 512
SR_KNEG = 514
SR_DELTA = 516
SR_TANH = 518
SR_DQ2P = 520
SR_DQ2N = 522
SR_ONES = 528

KE_CQ = 0
KE_KV = 4096
KE_UVO = 8192
KE_WQC = 10240
KE_BQC = 10752
KE_BFIN = 11776
KE_UQT = 12800
KE_UKT = 13056
KE_UVT = 13312
KE_WQR = 13568


class Sem:
    __slots__ = ("h", "n")

    def __init__(self, h):
        self.h = h
        self.n = 0


def _ap_span(ap):
    if not isinstance(ap, bass.AP):
        return None
    es = mybir.dt.size(ap.dtype)
    pat = list(ap.ap)
    reach = 0
    fel = 1
    for k, (stp, cnt) in enumerate(pat):
        if k == 0:
            continue
        reach += abs(int(stp)) * (int(cnt) - 1)
        fel *= int(cnt)
    lo = int(ap.offset) * es
    return (ap.tensor.name, lo, lo + (reach + 1) * es, fel)


def _overlap(a, b):
    return a[0] == b[0] and a[1] < b[2] and b[1] < a[2]


class Rec:
    ENGS = ("t", "v", "s", "g", "y")
    OWN = {"t": "st", "v": "sv", "s": "ss", "g": "sg"}

    def __init__(self, nc):
        self.nc = nc
        self.q = {e: [] for e in self.ENGS}
        self.sems = {}
        self._hist = {e: [] for e in self.ENGS}

    def sem(self, name):
        s = Sem(self.nc.alloc_semaphore(name=name))
        self.sems[name] = s
        return s

    def op(self, e, meth, *a, w=(), i=(), **kw):
        waits = [(s.h, v) for (s, v) in w if s is not None and v > 0]
        if e in self.OWN and meth not in (None,):
            aps = [x for x in a if isinstance(x, bass.AP)]
            aps += [x for x in kw.values() if isinstance(x, bass.AP)]
            wr = _ap_span(aps[0]) if aps else None
            rds = [s2 for s2 in (_ap_span(x) for x in aps[1:]) if s2 is not None]
            own = self.sems[self.OWN[e]]
            need = 0
            for pw, cnt in self._hist[e]:
                for rd in rds:
                    if _overlap(rd, pw) and (rd[3] < 512 or pw[3] < 512):
                        need = max(need, cnt)
            if need:
                waits.append((own.h, need))
            self._hist[e] = self._hist[e][-1:] + ([(wr, own.n + 1)] if wr else [])
        self.q[e].append((meth, a, kw, waits,
                          [(s.h, d) for (s, d) in i] if i else None))
        for s, d in i:
            s.n += d

    def replay(self, e, eng):
        for meth, a, kw, waits, incs in self.q[e]:
            for h, v in waits:
                eng.wait_ge(h, v)
            if meth is None:
                continue
            try:
                ins = getattr(eng, meth)(*a, **kw)
            except Exception:
                import sys as _sys
                print(f"REPLAY-FAIL eng={e} meth={meth} "
                      f"args={[getattr(x, 'shape', x) for x in a]}", file=_sys.stderr)
                raise
            if incs:
                for h, d in incs:
                    ins.then_inc(h, d)


def host_constants():
    c = {}
    c["c_ident"] = np.eye(128, dtype=BFD)
    c["c_iota16"] = np.tile(np.arange(16, dtype=np.float32)[None, :], (128, 1)).astype(BFD)
    j = np.arange(1024)
    e16 = np.zeros((128, 1024), dtype=np.float32)
    e16[:16] = (j[None, :] // 64 == np.arange(16)[:, None])
    c["c_e16"] = e16.astype(BFD)
    return c


def host_layout(inp):
    d = {}
    f32 = np.float32
    asc = np.ascontiguousarray
    d["w_q"] = asc(inp["L0_attn_Wq"], dtype=f32)
    d["w_k"] = asc(inp["L0_attn_Wk"], dtype=f32)
    d["w_v"] = asc(inp["L0_attn_Wv"], dtype=f32)
    for nm, key in (("b_q", "L0_attn_bq"), ("b_k", "L0_attn_bk"), ("b_v", "L0_attn_bv")):
        d[nm] = asc(np.asarray(inp[key]).reshape(1, 1024), dtype=f32)
    for f in ("q", "k", "v"):
        w1 = np.asarray(inp[f"L0_ffn_{f}_w1"])
        d[f"w1_{f}"] = asc(w1.transpose(1, 0, 2).reshape(1024, 1024), dtype=f32)
        d[f"w2_{f}"] = asc(np.asarray(inp[f"L0_ffn_{f}_w2"]).reshape(1024, 1024), dtype=f32)
        d[f"b1_{f}"] = asc(np.asarray(inp[f"L0_ffn_{f}_b1"]).reshape(1, 1024), dtype=f32)
        d[f"b2_{f}"] = asc(inp[f"L0_ffn_{f}_b2"], dtype=f32)
    nwt = np.concatenate([np.asarray(inp[f"L0_ffn_{f}_node_w"]).T for f in ("q", "k", "v")],
                         axis=1)
    d["node_wt"] = asc(nwt, dtype=f32)
    nb_ = np.concatenate([np.asarray(inp[f"L0_ffn_{f}_node_b"]) for f in ("q", "k", "v")])
    d["node_b"] = asc(nb_.reshape(1, 45), dtype=f32)
    d["w_dq"] = asc(inp["L1_attn_W_DQ"], dtype=f32)
    d["b_dq"] = asc(np.asarray(inp["L1_attn_b_DQ"]).reshape(1, 32), dtype=f32)
    d["w_kv"] = asc(np.concatenate([inp["L1_attn_W_DKV"], inp["L1_attn_W_KR"]], axis=1),
                    dtype=f32)
    d["b_kv"] = asc(np.concatenate([np.asarray(inp["L1_attn_b_DKV"]),
                                    np.asarray(inp["L1_attn_b_KR"])]).reshape(1, 64),
                    dtype=f32)
    d["w_uqT"] = asc(np.asarray(inp["L1_attn_W_UQ"]).T.reshape(8, 128, 32), dtype=f32)
    d["w_ukT"] = asc(np.asarray(inp["L1_attn_W_UK"]).T.reshape(8, 128, 32), dtype=f32)
    d["w_uvT"] = asc(np.asarray(inp["L1_attn_W_UV"]).T.reshape(8, 128, 32), dtype=f32)
    d["w_qr"] = asc(inp["L1_attn_W_QR"], dtype=f32)
    d["b_qr"] = asc(np.asarray(inp["L1_attn_b_QR"]).reshape(1, 256), dtype=f32)
    d["b_uq"] = asc(np.asarray(inp["L1_attn_b_UQ"]).reshape(8, 128), dtype=f32)
    d["b_uv"] = asc(np.asarray(inp["L1_attn_b_UV"]).reshape(8, 128), dtype=f32)
    d["w_o"] = asc(inp["L1_attn_W_O"], dtype=f32)
    d["b_o"] = asc(np.asarray(inp["L1_attn_b_O"]).reshape(1, 1024), dtype=f32)
    d["w_f"] = asc(inp["L1_ffn_weight"], dtype=f32)
    d["b_f"] = asc(np.asarray(inp["L1_ffn_bias"]).reshape(1, 1024), dtype=f32)
    d["delta_w"] = asc(inp["L1_ffn_delta_w"], dtype=f32)
    d["delta_act"] = asc(np.asarray(inp["L1_ffn_delta_act"]).reshape(1, 1), dtype=f32)
    return d


class MK:
    def __init__(self, nc, dbg=None):
        self.nc = nc
        self.dbg = dbg or {}
        self.r = Rec(nc)
        r = self.r
        self.st = r.sem("st")
        self.sv = r.sem("sv")
        self.ss = r.sem("ss")
        self.sg = r.sem("sg")
        self.dy = r.sem("dy")
        self.dg = r.sem("dg")
        self.dq2 = r.sem("dq2")
        self._declare_io()
        self._alloc()

    def _declare_io(self):
        nc = self.nc
        T = nc.dram_tensor
        self.h = {}
        H = self.h
        H["x"] = T("x", [S, DM], F32, kind="ExternalInput")
        H["out"] = T("out", [S, DM], F32, kind="ExternalOutput")
        for nm, sh, dt in (("c_ident", [128, 128], BF16),
                           ("c_iota16", [128, 16], BF16), ("c_e16", [128, 1024], BF16)):
            H[nm] = T(nm, sh, dt, kind="ExternalInput")
        for nm, sh in (
            ("w_q", [DM, DM]), ("w_k", [DM, DM]), ("w_v", [DM, DM]),
            ("b_q", [1, DM]), ("b_k", [1, DM]), ("b_v", [1, DM]),
            ("w1_q", [DM, DM]), ("w1_k", [DM, DM]), ("w1_v", [DM, DM]),
            ("w2_q", [DM, DM]), ("w2_k", [DM, DM]), ("w2_v", [DM, DM]),
            ("b1_q", [1, DM]), ("b1_k", [1, DM]), ("b1_v", [1, DM]),
            ("b2_q", [16, DM]), ("b2_k", [16, DM]), ("b2_v", [16, DM]),
            ("node_wt", [DM, 45]), ("node_b", [1, 45]),
            ("w_dq", [DM, 32]), ("b_dq", [1, 32]),
            ("w_kv", [DM, 64]), ("b_kv", [1, 64]),
            ("w_uqT", [8, 128, 32]), ("w_ukT", [8, 128, 32]), ("w_uvT", [8, 128, 32]),
            ("w_qr", [32, 256]), ("b_qr", [1, 256]),
            ("b_uq", [8, 128]), ("b_uv", [8, 128]),
            ("w_o", [DM, DM]), ("b_o", [1, DM]),
            ("w_f", [DM, DM]), ("b_f", [1, DM]),
            ("delta_w", [DM, 32]), ("delta_act", [1, 1]),
        ):
            H[nm] = T(nm, sh, F32, kind="ExternalInput")
        dump = set(self.dbg.get("dump", ()))
        pre = set(self.dbg.get("preload", ()))

        def scratch(nm, sh, dt):
            kind = "Internal"
            if nm in dump:
                kind = "ExternalOutput"
            elif nm in pre:
                kind = "ExternalInput"
            H[nm] = T(nm, sh, dt, kind=kind)

        scratch("x1", [S, DM], F32)
        scratch("x2", [S, DM], F32)
        scratch("x3", [S, DM], F32)
        scratch("xlnT", [16, 128, 2048], BF16)
        scratch("qTs", [16, 128, 2048], BF16)
        for nm in self.dbg.get("dump_sbuf", ()):
            H["dbg_" + nm] = T("dbg_" + nm, [128, self._dbg_cols(nm)], BF16,
                               kind="ExternalOutput")

    def _dbg_cols(self, nm):
        return {"KT": 8 * S, "VT": 8 * S, "WBa": 8192, "WBb": 8192,
                "TSL": 4096, "LEB": 4096, "SM1": 2048}.get(nm, 4096)

    def _alloc(self):
        nc = self.nc
        A = nc.alloc_sbuf_tensor
        self.KT = A("KT", [128, 8 * S], BF16)
        self.VT = A("VT", [128, 8 * S], BF16)
        self.WBa = A("WBa", [128, 8192], BF16)
        self.WBb = A("WBb", [128, 8192], BF16)
        self.XS = A("XS", [128, 2048], F32)
        self.XO = A("XO", [128, 2048], F32)
        self.TSL = A("TSL", [128, 4096], BF16)
        self.LEB = A("LEB", [128, 4096], BF16)
        self.SM1 = A("SM1", [128, 2048], BF16)
        self.IDN = A("IDN", [128, 128], BF16)
        self.IOTA = A("IOTA", [128, 16], BF16)
        self.E16 = A("E16", [128, 1024], BF16)
        self.ONESC = A("ONESC", [128, 1], BF16)
        self.ONESR = A("ONESR", [128, 512], BF16)
        self.ROWS = A("ROWS", [128, 1024], BF16)
        self.B2C = A("B2C", [128, 1024], BF16)
        self.NWT = A("NWT", [128, 360], BF16)
        self.NBR = A("NBR", [128, 45], BF16)
        self.ROUTE = A("ROUTE", [128, 32], F32)
        self.RSC = A("RSC", [128, 64], F32)
        self.OHT = A("OHT", [128, 256], BF16)
        self.STAT = A("STAT", [128, 16], F32)
        self.SROW = A("SROW", [1, 576], F32)
        self.SCL = A("SCL", [128, 8], F32)
        self.FQ4 = A("FQ4", [128, 128], F32)
        Ap = nc.alloc_psum_tensor
        self.PSA = Ap("PSA", [128, 2048], F32)
        self.PSB = Ap("PSB", [128, 2048], F32)

    def dma(self, q, out, in_, w=(), **kw):
        sem = self.dy if q == "y" else (self.dq2 if q == "s" else self.dg)
        self.r.op(q, "dma_start", w=w, i=((sem, 16),), out=out, in_=in_, **kw)

    def mm(self, out, lhsT, rhs, start=True, stop=True, w=()):
        self.r.op("t", "matmul", out, lhsT, rhs, start=start, stop=stop,
                  w=w, i=((self.st, 1),))

    def mmT(self, out_psum, in_sb, w=()):
        self.mm(out_psum, in_sb, self.IDN[:, :], w=w)

    def act(self, out, in_, func, w=(), **kw):
        self.r.op("s", "activation", out, in_, func, w=w, i=((self.ss, 1),), **kw)

    def ts(self, e, out, in0, s1, s2, op0, op1=None, w=()):
        sem = {"v": self.sv, "g": self.sg}[e]
        if op1 is None:
            self.r.op(e, "tensor_scalar", out, in0, s1, None, op0, w=w, i=((sem, 1),))
        else:
            self.r.op(e, "tensor_scalar", out, in0, s1, s2, op0, op1, w=w, i=((sem, 1),))

    def tt(self, e, out, in0, in1, op, w=()):
        sem = {"v": self.sv, "g": self.sg}[e]
        self.r.op(e, "tensor_tensor", out, in0, in1, op, w=w, i=((sem, 1),))

    def stt(self, e, out, in0, scalar, in1, op0, op1, w=()):
        sem = {"v": self.sv, "g": self.sg}[e]
        self.r.op(e, "scalar_tensor_tensor", out, in0, scalar, in1, op0, op1,
                  w=w, i=((sem, 1),))

    def vcopy(self, out, in_, w=()):
        self.r.op("v", "tensor_copy", out, in_, w=w, i=((self.sv, 1),))

    def vred(self, out, in_, op, w=(), **kw):
        self.r.op("v", "tensor_reduce", out, in_, AX.X, op, w=w, i=((self.sv, 1),), **kw)

    def vrecip(self, out, in_, w=()):
        self.r.op("v", "reciprocal", out, in_, w=w, i=((self.sv, 1),))

    def memset(self, e, ap, val, w=()):
        sem = {"v": self.sv, "g": self.sg}[e]
        self.r.op(e, "memset", ap, val, w=w, i=((sem, 1),))

    def g_t(self):
        return (self.st, self.st.n)

    def g_v(self):
        return (self.sv, self.sv.n)

    def g_s(self):
        return (self.ss, self.ss.n)

    def g_g(self):
        return (self.sg, self.sg.n)

    def g_y(self):
        return (self.dy, self.dy.n)

    def g_gd(self):
        return (self.dg, self.dg.n)

    def fence(self):
        counts = {nm: s.n for nm, s in self.r.sems.items()}
        sems = self.r.sems
        for e in Rec.ENGS:
            own = {"t": "st", "v": "sv", "s": "ss", "g": "sg"}.get(e)
            w = [(sems[nm], v) for nm, v in counts.items() if nm != own and v > 0]
            self.r.op(e, None, w=w)

    def ln_tile(self, xoff, col, out_ap, scratch, w_x=(), w_apply=None):
        ST = self.STAT
        c = 8 * col
        xs = self.XS[:, xoff:xoff + 1024]
        self.vred(ST[:, c + 0:c + 1], xs, ALU.add, w=w_x)
        self.stt("v", scratch, xs, 1.0, xs, ALU.mult, ALU.mult)
        self.vred(ST[:, c + 1:c + 2], scratch, ALU.add)
        self.ts("v", ST[:, c + 2:c + 3], ST[:, c + 0:c + 1], 1.0 / DM, None, ALU.mult)
        self.ts("v", ST[:, c + 3:c + 4], ST[:, c + 1:c + 2], 1.0 / DM, None, ALU.mult)
        self.tt("v", ST[:, c + 4:c + 5], ST[:, c + 2:c + 3], ST[:, c + 2:c + 3], ALU.mult)
        self.tt("v", ST[:, c + 3:c + 4], ST[:, c + 3:c + 4], ST[:, c + 4:c + 5],
                ALU.subtract)
        self.ts("v", ST[:, c + 3:c + 4], ST[:, c + 3:c + 4], 1e-5, None, ALU.add)
        v_ = ST[:, c + 3:c + 4]
        y = ST[:, c + 5:c + 6]
        t = ST[:, c + 6:c + 7]
        self.ts("v", t.bitcast(I32), v_.bitcast(I32), 1, None, ALU.logical_shift_right)
        self.ts("v", y.bitcast(I32), t.bitcast(I32), -1, 0x5F3759DF, ALU.mult, ALU.add)
        for _ in range(2):
            self.tt("v", t, y, y, ALU.mult)
            self.tt("v", t, t, v_, ALU.mult)
            self.ts("v", t, t, -0.5, 1.5, ALU.mult, ALU.add)
            self.tt("v", y, y, t, ALU.mult)
        mb = ST[:, c + 7:c + 8]
        self.stt("v", mb, ST[:, c + 2:c + 3], -1.0, y, ALU.mult, ALU.mult)
        wa = (self.g_v(),) + (tuple(w_apply) if w_apply is not None else (self.g_t(),))
        self.act(out_ap, xs, AF.Identity, bias=mb, scale=y, w=wa)

    def transpose_slab(self, src, base, w_first=()):
        for u in range(2):
            for c in range(8):
                w = ()
                if u == 0 and c == 0:
                    w = tuple(w_first) + (self.g_s(),)
                elif u == 1 and c == 0:
                    w = (self.g_s(),)
                self.mmT(self.PSA[:, 1024 + c * 128:1024 + (c + 1) * 128],
                         src[:, base + u * 1024 + c * 128:base + u * 1024 + (c + 1) * 128],
                         w=w)
            out = bass.AP(self.TSL, u * 128, [[4096, 128], [256, 8], [1, 128]])
            self.act(out, self.PSA[:, 1024:2048], AF.Copy, w=(self.g_t(),))

    def p0(self):
        H = self.h
        self.dma("y", self.IDN[:, :], H["c_ident"][:, :])
        self.dma("y", self.IOTA[:, :], H["c_iota16"][:, :])
        self.dma("y", self.E16[:, :], H["c_e16"][:, :])
        self.memset("v", self.ONESC[:, :], 1.0)
        self.memset("v", self.ONESR[:, :], 0.0)
        self.memset("v", self.ONESR[0:1, :], 1.0)
        self.memset("v", self.ROWS[:, :], 0.0)
        self.memset("v", self.B2C[:, :], 0.0)
        self.memset("v", self.NBR[:, :], 0.0)
        self.memset("v", self.OHT[:, :], 0.0)
        self.memset("v", self.SROW[0:1, SR_STAGE:SR_STAGE + 128], 1.0)
        self.memset("v", self.SROW[0:1, SR_ONES:SR_ONES + 32], 1.0)
        self.dma("g", self.SROW[0:1, SR_DELTA:SR_DELTA + 1], H["delta_act"][:, :])
        gd = self.g_gd()
        self.act(self.SROW[0:1, SR_TANH:SR_TANH + 1],
                 self.SROW[0:1, SR_DELTA:SR_DELTA + 1], AF.Tanh, scale=4.0, w=(gd,))
        self.act(self.SROW[0:1, SR_DQ2P:SR_DQ2P + 1],
                 self.SROW[0:1, SR_TANH:SR_TANH + 1], AF.Exp, scale=0.5 * LN2)
        self.act(self.SROW[0:1, SR_DQ2N:SR_DQ2N + 1],
                 self.SROW[0:1, SR_TANH:SR_TANH + 1], AF.Exp, scale=-0.5 * LN2)
        sg_ = self.g_s()
        self.mm(self.PSB[0:128, 1024:1025], self.SROW[0:1, SR_STAGE:SR_STAGE + 128],
                self.SROW[0:1, SR_DQ2P:SR_DQ2P + 1], w=(sg_, self.g_v()))
        self.mm(self.PSB[0:128, 1025:1026], self.SROW[0:1, SR_STAGE:SR_STAGE + 128],
                self.SROW[0:1, SR_DQ2N:SR_DQ2N + 1])
        self.vcopy(self.SCL[:, 0:2], self.PSB[:, 1024:1026], w=(self.g_t(),))

    def pA1(self):
        H = self.h
        self.dma("g", bass.AP(self.WBa, 0, [[8192, 128], [1024, 8], [1, 1024]]),
                 bass.AP(H["w_q"], 0, [[1024, 128], [131072, 8], [1, 1024]]))
        self.dma("g", bass.AP(self.WBb, 0, [[8192, 128], [1024, 8], [1, 1024]]),
                 bass.AP(H["w_k"], 0, [[1024, 128], [131072, 8], [1, 1024]]))
        self.dma("g", self.ROWS[0:1, :], H["b_q"][:, :])
        self.dma("g", self.B2C[0:1, :], H["b_k"][:, :])
        wg = self.g_gd()
        lnb = self.LEB
        sc_f32 = self.LEB[:, 2048:4096].bitcast(F32)
        xs_free = None
        lnb_free = ((self.st, 0),)
        for s_ in range(NSLAB):
            p2 = s_ % 2
            w = [xs_free] if xs_free else []
            self.dma("y", bass.AP(self.XS, 0, [[2048, 128], [1024, 2], [1, 1024]]),
                     bass.AP(H["x"], s_ * SLAB * DM, [[1024, 128], [131072, 2], [1, 1024]]),
                     w=w)
            xdma = self.g_y()
            for u in range(2):
                self.ln_tile(u * 1024, u, lnb[:, u * 1024:(u + 1) * 1024], sc_f32,
                             w_x=(xdma,) if u == 0 else (), w_apply=lnb_free)
            xs_free = self.g_s()
            self.transpose_slab(lnb, 0)
            lnb_free = (self.g_t(),)
            tev = self.g_s()
            qstage = self.XO[:, p2 * 1024:(p2 + 1) * 1024].bitcast(BF16)
            for oc in range(8):
                for kc in range(8):
                    self.mm(self.PSB[:, oc * 256:(oc + 1) * 256],
                            self.WBa[:, kc * 1024 + oc * 128:kc * 1024 + oc * 128 + 128],
                            self.TSL[:, kc * 256:(kc + 1) * 256],
                            start=(kc == 0), stop=False,
                            w=(tev, wg) if (oc == 0 and kc == 0) else ())
                self.mm(self.PSB[:, oc * 256:(oc + 1) * 256],
                        self.ROWS[:, oc * 128:(oc + 1) * 128], self.ONESR[:, 0:256],
                        start=False, stop=True)
            self.act(qstage, self.PSB[:, :], AF.Copy, w=(self.g_t(), self.g_y()))
            qev = self.g_s()
            for oc in range(8):
                for kc in range(8):
                    self.mm(self.PSB[:, oc * 256:(oc + 1) * 256],
                            self.WBb[:, kc * 1024 + oc * 128:kc * 1024 + oc * 128 + 128],
                            self.TSL[:, kc * 256:(kc + 1) * 256],
                            start=(kc == 0), stop=False,
                            w=(qev,) if (oc == 0 and kc == 0) else ())
                self.mm(self.PSB[:, oc * 256:(oc + 1) * 256],
                        self.B2C[:, oc * 128:(oc + 1) * 128], self.ONESR[:, 0:256],
                        start=False, stop=True)
            self.act(bass.AP(self.KT, s_ * 256, [[32768, 128], [4096, 8], [1, 256]]),
                     self.PSB[:, :], AF.Copy, w=(self.g_t(),))
            self.dma("y", bass.AP(H["xlnT"], s_ * 262144, [[2048, 128], [1, 2048]]),
                     bass.AP(self.TSL, 0, [[4096, 128], [1, 2048]]),
                     w=(tev,))
            self.dma("y", bass.AP(H["qTs"], s_ * 262144, [[2048, 128], [1, 2048]]),
                     qstage,
                     w=(qev,))

    def pA2(self):
        H = self.h
        self.dma("g", bass.AP(self.WBa, 0, [[8192, 128], [1024, 8], [1, 1024]]),
                 bass.AP(H["w_v"], 0, [[1024, 128], [131072, 8], [1, 1024]]))
        self.dma("g", self.ROWS[0:1, :], H["b_v"][:, :])
        wg = self.g_gd()
        for half in range(2):
            self.mm(self.PSA[:, half * 512:(half + 1) * 512], self.ONESR[:, 0:128],
                    self.ROWS[:, half * 512:(half + 1) * 512],
                    w=(wg,) if half == 0 else ())
        self.vcopy(self.SM1[:, 0:1024], self.PSA[:, 0:1024], w=(self.g_t(),))
        bvt = self.g_v()
        tsl_free = None
        for s_ in range(NSLAB):
            w = [tsl_free] if tsl_free else []
            self.dma("y", bass.AP(self.TSL, 0, [[4096, 128], [1, 2048]]),
                     bass.AP(H["xlnT"], s_ * 262144, [[2048, 128], [1, 2048]]),
                     w=w)
            xdma = self.g_y()
            for u in range(2):
                po = u * 1024
                for kc in range(8):
                    for half in range(2):
                        self.mm(self.PSA[:, po + half * 512:po + (half + 1) * 512],
                                self.TSL[:, kc * 256 + u * 128:kc * 256 + u * 128 + 128],
                                self.WBa[:, kc * 1024 + half * 512:
                                         kc * 1024 + (half + 1) * 512],
                                start=(kc == 0), stop=(kc == 7),
                                w=(xdma, self.g_v()) if kc == 0 and half == 0 else ())
                j = 2 * s_ + u
                self.tt("v", self.VT[:, j * 1024:(j + 1) * 1024],
                        self.PSA[:, po:po + 1024], self.SM1[:, 0:1024], ALU.add,
                        w=(self.g_t(), bvt))
            tsl_free = self.g_t()

    def pB(self):
        H = self.h
        exp_free = [None, None]
        qts_free = {}
        xs_free = None
        xo_gate = None
        leb_h0 = [None, None]
        leb_all = [None, None]
        vg = {}
        SM2 = self.SM1[:, 1536:2048]

        def tmax(a, b):
            if a is None:
                return b
            if b is None:
                return a
            return (a[0], max(a[1], b[1]))

        for nb in range(16):
            p = nb % 2
            self.dma("s", bass.AP(self.TSL, p * 2048, [[4096, 128], [1, 2048]]),
                     bass.AP(H["qTs"], nb * 262144, [[2048, 128], [1, 2048]]),
                     w=[qts_free[p]] if p in qts_free else [])
            qdma = (self.dq2, self.dq2.n)
            self.dma("y", bass.AP(self.XS, 0, [[2048, 128], [1024, 2], [1, 1024]]),
                     bass.AP(H["x"], nb * 256 * DM, [[1024, 128], [131072, 2], [1, 1024]]),
                     w=[xs_free] if xs_free else [])
            xsdma = self.g_y()
            for m in range(34):
                if m < 32:
                    mp = m % 2
                    sc_half = [None, None]
                    for halfg in range(2):
                        wlist = []
                        if exp_free[halfg] is not None:
                            wlist.append(exp_free[halfg])
                        if m == 0 and halfg == 0:
                            wlist.append(qdma)
                        for gg in range(4):
                            g = halfg * 4 + gg
                            self.mm(self.PSA[:, halfg * 1024 + gg * 256:
                                             halfg * 1024 + (gg + 1) * 256],
                                    self.KT[:, g * S + m * 128:g * S + (m + 1) * 128],
                                    self.TSL[:, p * 2048 + g * 256:
                                             p * 2048 + (g + 1) * 256],
                                    w=wlist if gg == 0 else ())
                        sc_half[halfg] = self.g_t()
                if m >= 2:
                    mc = m - 2
                    mpc = mc % 2
                    vn, gn = vg[mc]
                    for g in range(8):
                        for ns in range(2):
                            wlist = []
                            if g == 0 and ns == 0:
                                wlist = [vn]
                                if mc == 0 and xo_gate is not None:
                                    wlist.append(self.g_v())
                            elif g == 6 and ns == 0:
                                wlist = [gn]
                            self.mm(self.PSB[:, ns * 1024 + g * 128:
                                             ns * 1024 + (g + 1) * 128],
                                    self.LEB[:, mpc * 2048 + g * 256 + ns * 128:
                                             mpc * 2048 + g * 256 + ns * 128 + 128],
                                    self.VT[:, mc * 1024 + g * 128:
                                            mc * 1024 + (g + 1) * 128],
                                    start=(mc == 0 and g % 4 == 0), stop=(mc == 31),
                                    w=wlist)
                        if g == 3:
                            leb_h0[mpc] = self.g_t()
                    leb_all[mpc] = self.g_t()
                if m < 32:
                    e_half = [None, None]
                    for halfg in range(2):
                        if halfg == 0:
                            wexp = [tmax(sc_half[1], leb_h0[mp])]
                        else:
                            wexp = [tmax(sc_half[1], leb_all[mp])]
                        self.act(self.LEB[:, mp * 2048 + halfg * 1024:
                                          mp * 2048 + (halfg + 1) * 1024],
                                 self.PSA[:, halfg * 1024:(halfg + 1) * 1024], AF.Exp,
                                 scale=1.0 / 32.0, w=wexp)
                        exp_free[halfg] = self.g_s()
                        e_half[halfg] = self.g_s()
                    eb = mp * 2048
                    so = mp * 256
                    self.tt("v", bass.AP(self.SM1, 0, [[2048, 128], [256, 2], [1, 256]]),
                            bass.AP(self.LEB, eb, [[4096, 128], [512, 2], [1, 256]]),
                            bass.AP(self.LEB, eb + 256, [[4096, 128], [512, 2], [1, 256]]),
                            ALU.add, w=(e_half[0],))
                    self.tt("v", bass.AP(self.SM1, 512, [[2048, 128], [256, 2], [1, 256]]),
                            bass.AP(self.LEB, eb + 1024, [[4096, 128], [512, 2], [1, 256]]),
                            bass.AP(self.LEB, eb + 1280, [[4096, 128], [512, 2], [1, 256]]),
                            ALU.add, w=(e_half[1],))
                    self.tt("v", bass.AP(self.SM1, 1024, [[2048, 128], [256, 2], [1, 256]]),
                            bass.AP(self.SM1, 0, [[2048, 128], [512, 2], [1, 256]]),
                            bass.AP(self.SM1, 256, [[2048, 128], [512, 2], [1, 256]]),
                            ALU.add)
                    self.tt("v", SM2[:, so:so + 256], self.SM1[:, 1024:1280],
                            self.SM1[:, 1280:1536], ALU.add)
                    D = SM2[:, so:so + 256]
                    seed = self.SM1[:, 0:256].bitcast(I16)
                    t = self.SM1[:, 256:512]
                    self.ts("v", seed, D.bitcast(I16), -1, 0x7EF4, ALU.mult, ALU.add)
                    r0 = seed.bitcast(BF16)
                    self.tt("v", t, D, r0, ALU.mult)
                    self.ts("v", t, t, -1.0, 2.0, ALU.mult, ALU.add)
                    self.tt("v", SM2[:, so:so + 256], r0, t, ALU.mult)
                    rn = self.g_v()
                    rd6 = bass.AP(self.SM1, 1536 + so, [[2048, 128], [0, 6], [1, 256]])
                    self.tt("v", bass.AP(self.LEB, eb, [[4096, 128], [256, 6], [1, 256]]),
                            bass.AP(self.LEB, eb, [[4096, 128], [256, 6], [1, 256]]),
                            rd6, ALU.mult)
                    vn = self.g_v()
                    rd2 = bass.AP(self.SM1, 1536 + so, [[2048, 128], [0, 2], [1, 256]])
                    self.tt("g", bass.AP(self.LEB, eb + 1536, [[4096, 128], [256, 2], [1, 256]]),
                            bass.AP(self.LEB, eb + 1536, [[4096, 128], [256, 2], [1, 256]]),
                            rd2, ALU.mult, w=(rn,))
                    gn = self.g_g()
                    vg[m] = (vn, gn)
            ctx_done = self.g_t()
            qts_free[p] = ctx_done
            for u in range(2):
                wl = [ctx_done, xsdma] if u == 0 else []
                if u == 0 and xo_gate is not None:
                    wl.append(xo_gate)
                self.stt("v", self.XO[:, u * 1024:(u + 1) * 1024],
                         self.PSB[:, u * 1024:(u + 1) * 1024], 1.0,
                         self.XS[:, u * 1024:(u + 1) * 1024],
                         ALU.mult, ALU.add, w=wl)
            xs_free = self.g_v()
            self.dma("y", bass.AP(H["x1"], nb * 256 * DM, [[1024, 128], [131072, 2], [1, 1024]]),
                     bass.AP(self.XO, 0, [[2048, 128], [1024, 2], [1, 1024]]),
                     w=(self.g_v(),))
            xo_gate = self.g_y()

    def fff(self, f, s_, p2, dest, tbase=0, w_tsl=()):
        H = self.h
        fo = {"q": 0, "k": 1, "v": 2}[f]
        for u in range(2):
            po = 1024 + u * 256
            for kc in range(8):
                self.mm(self.PSA[:, po:po + 15],
                        self.TSL[:, tbase + kc * 256 + u * 128:
                                 tbase + kc * 256 + u * 128 + 128],
                        self.NWT[:, kc * 45 + fo * 15:kc * 45 + fo * 15 + 15],
                        start=(kc == 0), stop=False,
                        w=(self.g_s(), self.g_v()) + tuple(w_tsl)
                        if (u == 0 and kc == 0) else ())
            self.mm(self.PSA[:, po:po + 15], self.ONESR[:, 0:128],
                    self.NBR[:, fo * 15:fo * 15 + 15], start=False, stop=True)
        tr = self.g_t()
        for u in range(2):
            self.vcopy(self.ROUTE[:, u * 15:u * 15 + 15],
                       self.PSA[:, 1024 + u * 256:1024 + u * 256 + 15],
                       w=(tr,) if u == 0 else ())
        R = self.ROUTE
        C = self.RSC

        def Sv(j):
            return bass.AP(R, j, [[32, 128], [15, 2], [1, 1]])

        def c2(a):
            return C[:, a:a + 2]

        b0, b1, b2, b3 = c2(32), c2(34), c2(36), c2(38)
        c0, c1, cc2 = c2(40), c2(42), c2(44)
        sel, d, t = c2(46), c2(48), c2(50)
        n = [c2(52 + 2 * k) for k in range(4)]
        leaf = c2(60)
        mq = [c2(0 + 2 * k) for k in range(8)]
        Z = self.RSC[:, 16:18]
        O1 = self.RSC[:, 18:20]
        self.tt("v", b0, Sv(0), Z, ALU.is_ge, w=(self.g_v(), self.g_g()))
        self.tt("v", d, Sv(2), Sv(1), ALU.subtract)
        self.tt("v", sel, d, b0, ALU.mult)
        self.tt("v", sel, sel, Sv(1), ALU.add)
        self.tt("v", b1, sel, Z, ALU.is_ge)
        self.tt("v", c0, O1, b0, ALU.subtract)
        self.tt("v", c1, O1, b1, ALU.subtract)
        self.tt("v", n[0], c0, c1, ALU.mult)
        self.tt("v", n[1], c0, b1, ALU.mult)
        self.tt("v", n[2], b0, c1, ALU.mult)
        self.tt("v", n[3], b0, b1, ALU.mult)
        self.tt("v", sel, Sv(3), n[0], ALU.mult)
        for k, jj in enumerate((4, 5, 6)):
            self.tt("v", t, Sv(jj), n[k + 1], ALU.mult)
            self.tt("v", sel, sel, t, ALU.add)
        self.tt("v", b2, sel, Z, ALU.is_ge)
        self.tt("v", cc2, O1, b2, ALU.subtract)
        for k in range(4):
            self.tt("v", mq[2 * k], n[k], cc2, ALU.mult)
            self.tt("v", mq[2 * k + 1], n[k], b2, ALU.mult)
        self.tt("v", sel, Sv(7), mq[0], ALU.mult)
        for k in range(1, 8):
            self.tt("v", t, Sv(7 + k), mq[k], ALU.mult)
            self.tt("v", sel, sel, t, ALU.add)
        self.tt("v", b3, sel, Z, ALU.is_ge)
        self.tt("v", leaf, b0, b0, ALU.add)
        self.tt("v", leaf, leaf, b1, ALU.add)
        self.tt("v", leaf, leaf, leaf, ALU.add)
        self.tt("v", leaf, leaf, b2, ALU.add)
        self.tt("v", leaf, leaf, leaf, ALU.add)
        self.tt("v", leaf, leaf, b3, ALU.add)
        for u in range(2):
            self.ts("v", self.SM1[:, 2016 + u * 16:2016 + (u + 1) * 16], self.IOTA[:, :],
                    leaf[:, u:u + 1], None, ALU.is_equal,
                    w=(self.g_t(),) if u == 0 else ())
        ohv = self.g_v()
        w1_half = None
        for oc in range(8):
            for kc in range(8):
                self.mm(self.PSB[:, oc * 256:(oc + 1) * 256],
                        self.WBa[:, kc * 1024 + oc * 128:kc * 1024 + oc * 128 + 128],
                        self.TSL[:, tbase + kc * 256:tbase + (kc + 1) * 256],
                        start=(kc == 0), stop=False,
                        w=(self.g_s(),) if (oc == 0 and kc == 0) else ())
            self.mm(self.PSB[:, oc * 256:(oc + 1) * 256],
                    self.ROWS[:, oc * 128:(oc + 1) * 128],
                    self.ONESR[:, 0:256], start=False, stop=True)
            if oc == 3:
                w1_half = self.g_t()
        w1_done = self.g_t()
        for u in range(2):
            self.mm(self.PSA[0:16, 1536 + u * 128:1536 + (u + 1) * 128],
                    self.SM1[:, 2016 + u * 16:2016 + (u + 1) * 16], self.IDN[:, :],
                    w=(ohv,) if u == 0 else ())
        ohmm = self.g_t()
        self.vcopy(self.OHT[0:16, 0:256], self.PSA[0:16, 1536:1792], w=(ohmm,))
        oht = self.g_v()
        for c in range(8):
            dst = c * 256 if c < 4 else 1024 + (c - 4) * 256
            self.mm(self.PSA[:, dst:dst + 256],
                    self.E16[:, c * 128:(c + 1) * 128], self.OHT[:, 0:256],
                    w=(oht, self.g_v()) if c == 0 else ())
        maskmm = self.g_t()
        relu_half = [None, None]
        for half in range(2):
            wre = [w1_half if half == 0 else ohmm]
            if half == 0:
                wre.append(self.g_v())
            self.act(self.SM1[:, half * 1024:(half + 1) * 1024],
                     self.PSB[:, half * 1024:(half + 1) * 1024], AF.Relu, w=wre)
            relu_half[half] = self.g_s()
        hm = [None, None]
        for half in range(2):
            self.tt("v", self.SM1[:, half * 1024:(half + 1) * 1024],
                    self.SM1[:, half * 1024:(half + 1) * 1024],
                    self.PSA[:, half * 1024:(half + 1) * 1024], ALU.mult,
                    w=(maskmm, relu_half[half]))
            hm[half] = self.g_v()
        if dest in ("qT", "kT"):
            for kc in range(8):
                for oc in range(8):
                    wlist = ()
                    if oc == 0 and kc == 0:
                        wlist = (hm[0],)
                    elif oc == 0 and kc == 4:
                        wlist = (hm[1],)
                    self.mm(self.PSB[:, oc * 256:(oc + 1) * 256],
                            self.WBb[:, kc * 1024 + oc * 128:kc * 1024 + oc * 128 + 128],
                            self.SM1[:, kc * 256:(kc + 1) * 256],
                            start=(kc == 0), stop=False, w=wlist)
            for oc in range(8):
                self.mm(self.PSB[:, oc * 256:(oc + 1) * 256],
                        self.B2C[:, oc * 128:(oc + 1) * 128], self.OHT[:, 0:256],
                        start=False, stop=True)
            if dest == "qT":
                qstage = self.XO[:, p2 * 1024:(p2 + 1) * 1024].bitcast(BF16)
                self.act(qstage, self.PSB[:, :], AF.Copy, w=(self.g_t(), self.g_y()))
                qev = self.g_s()
                self.dma("y", bass.AP(H["qTs"], s_ * 262144, [[2048, 128], [1, 2048]]),
                         qstage,
                         w=(qev,))
            else:
                self.act(bass.AP(self.KT, s_ * 256, [[32768, 128], [4096, 8], [1, 256]]),
                         self.PSB[:, :], AF.Copy, w=(self.g_t(),))
        else:
            for u in range(2):
                po = u * 1024
                for kc in range(8):
                    for half in range(2):
                        wlist = ()
                        if u == 0 and half == 0 and kc == 0:
                            wlist = (hm[0],)
                        elif u == 0 and half == 0 and kc == 4:
                            wlist = (hm[1],)
                        self.mm(self.PSB[:, po + half * 512:po + (half + 1) * 512],
                                self.SM1[:, kc * 256 + u * 128:kc * 256 + u * 128 + 128],
                                self.WBb[:, kc * 1024 + half * 512:
                                         kc * 1024 + (half + 1) * 512],
                                start=(kc == 0), stop=False, w=wlist)
                for half in range(2):
                    self.mm(self.PSB[:, po + half * 512:po + (half + 1) * 512],
                            self.OHT[:, u * 128:(u + 1) * 128],
                            self.B2C[:, half * 512:(half + 1) * 512],
                            start=False, stop=True)
                j = 2 * s_ + u
                self.act(self.VT[:, j * 1024:(j + 1) * 1024],
                         self.PSB[:, po:po + 1024], AF.Copy, w=(self.g_t(),))
        return w1_done

    def _load_fff_weights(self, f):
        H = self.h
        self.memset("v", self.RSC[:, 16:18], 0.0, w=(self.g_t(),))
        self.memset("v", self.RSC[:, 18:20], 1.0)
        self.dma("g", bass.AP(self.WBa, 0, [[8192, 128], [1024, 8], [1, 1024]]),
                 bass.AP(H[f"w1_{f}"], 0, [[1024, 128], [131072, 8], [1, 1024]]))
        self.dma("g", bass.AP(self.WBb, 0, [[8192, 128], [1024, 8], [1, 1024]]),
                 bass.AP(H[f"w2_{f}"], 0, [[1024, 128], [131072, 8], [1, 1024]]))
        self.dma("g", self.ROWS[0:1, :], H[f"b1_{f}"][:, :])
        self.dma("g", self.B2C[0:16, :], H[f"b2_{f}"][:, :])
        return self.g_gd()

    def pC1(self):
        H = self.h
        self.memset("v", self.OHT[:, :], 0.0)
        wg = self._load_fff_weights("q")
        self.dma("g", bass.AP(self.NWT, 0, [[360, 128], [45, 8], [1, 45]]),
                 bass.AP(H["node_wt"], 0, [[45, 128], [45 * 128, 8], [1, 45]]))
        self.dma("g", self.NBR[0:1, 0:45], H["node_b"][:, :])
        lnb = self.LEB
        sc_f32 = self.LEB[:, 2048:4096].bitcast(F32)
        xs_free = [None, None]
        lnb_free = ((self.st, 0),)
        for s_ in range(NSLAB):
            p2 = s_ % 2
            xdma = [None, None]
            for u in range(2):
                w = [xs_free[u]] if xs_free[u] else []
                if s_ == 0 and u == 0:
                    w.append(wg)
                self.dma("s", bass.AP(self.XS, u * 1024, [[2048, 128], [1, 1024]]),
                         bass.AP(H["x1"], s_ * SLAB * DM + u * 131072,
                                 [[1024, 128], [1, 1024]]),
                         w=w)
                xdma[u] = (self.dq2, self.dq2.n)
            for u in range(2):
                self.ln_tile(u * 1024, u, lnb[:, u * 1024:(u + 1) * 1024], sc_f32,
                             w_x=(xdma[u],), w_apply=lnb_free)
                xs_free[u] = self.g_s()
            self.transpose_slab(lnb, 0, w_first=(self.g_v(),))
            lnb_free = (self.g_t(),)
            tev = self.g_s()
            self.dma("y", bass.AP(H["xlnT"], s_ * 262144, [[2048, 128], [1, 2048]]),
                     bass.AP(self.TSL, 0, [[4096, 128], [1, 2048]]),
                     w=(tev,))
            self.fff("q", s_, p2, "qT")

    def pC1b(self):
        H = self.h
        wg = self._load_fff_weights("k")
        tsl_free = {}
        for s_ in range(NSLAB):
            pr = s_ % 2
            w = [tsl_free[pr]] if pr in tsl_free else []
            if s_ == 0:
                w.append(wg)
            self.dma("y", bass.AP(self.TSL, pr * 2048, [[4096, 128], [1, 2048]]),
                     bass.AP(H["xlnT"], s_ * 262144, [[2048, 128], [1, 2048]]), w=w)
            tdma = self.g_y()
            tsl_free[pr] = self.fff("k", s_, pr, "kT", tbase=pr * 2048,
                                    w_tsl=(tdma,))

    def pC2(self):
        H = self.h
        wg = self._load_fff_weights("v")
        tsl_free = {}
        for s_ in range(NSLAB):
            pr = s_ % 2
            w = [tsl_free[pr]] if pr in tsl_free else []
            if s_ == 0:
                w.append(wg)
            self.dma("y", bass.AP(self.TSL, pr * 2048, [[4096, 128], [1, 2048]]),
                     bass.AP(H["xlnT"], s_ * 262144, [[2048, 128], [1, 2048]]), w=w)
            tdma = self.g_y()
            tsl_free[pr] = self.fff("v", s_, pr, "v", tbase=pr * 2048,
                                    w_tsl=(tdma,))

    def pD(self):
        H = self.h
        QSQ = self.SM1
        for piece in range(8):
            for c in range(8):
                self.tt("v", QSQ[:, 0:512],
                        self.KT[:, c * S + piece * 512:c * S + (piece + 1) * 512],
                        self.KT[:, c * S + piece * 512:c * S + (piece + 1) * 512],
                        ALU.mult, w=(self.g_t(),))
                self.mm(self.PSA[0:1, 0:512], self.ONESC[:, 0:1], QSQ[:, 0:512],
                        start=(c == 0), stop=(c == 7), w=(self.g_v(),))
            self.act(self.SROW[0:1, SR_STAGE:SR_STAGE + 512], self.PSA[0:1, 0:512],
                     AF.Sqrt, w=(self.g_t(),))
            if piece == 0:
                self.vred(self.SROW[0:1, SR_KMAX:SR_KMAX + 1],
                          self.SROW[0:1, SR_STAGE:SR_STAGE + 512], ALU.max,
                          w=(self.g_s(),))
            else:
                self.vred(self.SROW[0:1, SR_KMAX + 1:SR_KMAX + 2],
                          self.SROW[0:1, SR_STAGE:SR_STAGE + 512], ALU.max,
                          w=(self.g_s(),))
                self.tt("v", self.SROW[0:1, SR_KMAX:SR_KMAX + 1],
                        self.SROW[0:1, SR_KMAX:SR_KMAX + 1],
                        self.SROW[0:1, SR_KMAX + 1:SR_KMAX + 2], ALU.max)
        self.ts("v", self.SROW[0:1, SR_KNEG:SR_KNEG + 1],
                self.SROW[0:1, SR_KMAX:SR_KMAX + 1], -1.0, None, ALU.mult)
        kstar = self.g_v()
        exp_free = [None, None]
        leb_free = [None, None]
        qts_free = {}
        xs_free = None
        xo_gate = None
        for nb in range(16):
            p = nb % 2
            self.dma("y", bass.AP(self.TSL, p * 2048, [[4096, 128], [1, 2048]]),
                     bass.AP(H["qTs"], nb * 262144, [[2048, 128], [1, 2048]]),
                     w=[qts_free[p]] if p in qts_free else [])
            self.dma("y", bass.AP(self.XS, 0, [[2048, 128], [1024, 2], [1, 1024]]),
                     bass.AP(H["x1"], nb * 256 * DM, [[1024, 128], [131072, 2], [1, 1024]]),
                     w=[xs_free] if xs_free else [])
            qdma = self.g_y()
            self.tt("v", QSQ[:, 0:2048], self.TSL[:, p * 2048:p * 2048 + 2048],
                    self.TSL[:, p * 2048:p * 2048 + 2048], ALU.mult, w=(qdma, kstar))
            qsqv = self.g_v()
            for c in range(8):
                self.mm(self.PSA[0:1, 1280:1536], self.ONESC[:, 0:1],
                        QSQ[:, c * 256:(c + 1) * 256], start=(c == 0), stop=(c == 7),
                        w=(qsqv, self.g_s()) if c == 0 else ())
            self.act(self.SROW[0:1, SR_STAGE:SR_STAGE + 256], self.PSA[0:1, 1280:1536],
                     AF.Sqrt, w=(self.g_t(),))
            self.ts("v", self.ROWS[0:1, 0:256], self.SROW[0:1, SR_STAGE:SR_STAGE + 256],
                    self.SROW[0:1, SR_KNEG:SR_KNEG + 1], None, ALU.mult, w=(self.g_s(),))
            negb = self.g_v()
            expd_p = [None, None]
            for pair in range(17):
                if pair < 16:
                    for mi in range(2):
                        m = pair * 2 + mi
                        pcol = (pair % 2) * 512 + mi * 256
                        wlist = []
                        if mi == 0 and exp_free[pair % 2] is not None:
                            wlist.append(exp_free[pair % 2])
                        if m == 0:
                            wlist.append(negb)
                        for kc in range(8):
                            self.mm(self.PSA[:, pcol:pcol + 256],
                                    self.KT[:, kc * S + m * 128:kc * S + (m + 1) * 128],
                                    self.TSL[:, p * 2048 + kc * 256:
                                             p * 2048 + (kc + 1) * 256],
                                    start=(kc == 0), stop=False,
                                    w=wlist if kc == 0 else ())
                        self.mm(self.PSA[:, pcol:pcol + 256], self.ONESR[:, 0:128],
                                self.ROWS[:, 0:256], start=False, stop=True)
                    sc_done = self.g_t()
                    wexp = [sc_done]
                    if leb_free[pair % 2] is not None:
                        wexp.append(leb_free[pair % 2])
                    self.act(self.LEB[:, (pair % 2) * 512:(pair % 2) * 512 + 512],
                             self.PSA[:, (pair % 2) * 512:(pair % 2) * 512 + 512],
                             AF.Exp, scale=1.0 / 32.0, w=wexp)
                    expd_p[pair % 2] = self.g_s()
                    exp_free[pair % 2] = expd_p[pair % 2]
                if pair >= 1:
                    pp = pair - 1
                    expd = expd_p[pp % 2]
                    for mi in range(2):
                        mm_ = pp * 2 + mi
                        lb = (pp % 2) * 512 + mi * 256
                        for ns in range(2):
                            wl = []
                            if mi == 0 and ns == 0:
                                wl.append(expd)
                                if mm_ == 0 and xo_gate is not None:
                                    wl.append(self.g_v())
                            for half in range(2):
                                self.mm(self.PSB[:, ns * 1024 + half * 512:
                                                 ns * 1024 + (half + 1) * 512],
                                        self.LEB[:, lb + ns * 128:lb + ns * 128 + 128],
                                        self.VT[:, mm_ * 1024 + half * 512:
                                                mm_ * 1024 + (half + 1) * 512],
                                        start=(mm_ == 0), stop=(mm_ == 31),
                                        w=wl if half == 0 else ())
                        self.mm(self.PSA[0:1, 1024:1280], self.ONESC[:, 0:1],
                                self.LEB[:, lb:lb + 256],
                                start=(mm_ == 0), stop=(mm_ == 31))
                    leb_free[pp % 2] = self.g_t()
            ctx_done = self.g_t()
            qts_free[p] = ctx_done
            self.act(self.ROWS[0:1, 256:512], self.PSA[0:1, 1024:1280], AF.Copy,
                     w=(ctx_done,))
            drow = self.g_s()
            for u in range(2):
                self.mm(self.PSA[0:128, 1536 + u:1537 + u],
                        self.ROWS[:, 256 + u * 128:256 + (u + 1) * 128],
                        self.ONESR[:, 0:1], w=(drow,) if u == 0 else ())
            self.vrecip(self.SCL[:, 2:4], self.PSA[:, 1536:1538], w=(self.g_t(),))
            for u in range(2):
                wl = []
                if u == 0 and xo_gate is not None:
                    wl.append(xo_gate)
                self.stt("v", self.XO[:, u * 1024:(u + 1) * 1024],
                         self.PSB[:, u * 1024:(u + 1) * 1024], self.SCL[:, 2 + u:3 + u],
                         self.XS[:, u * 1024:(u + 1) * 1024],
                         ALU.mult, ALU.add, w=wl)
            xs_free = self.g_v()
            self.dma("y", bass.AP(H["x2"], nb * 256 * DM, [[1024, 128], [131072, 2], [1, 1024]]),
                     bass.AP(self.XO, 0, [[2048, 128], [1024, 2], [1, 1024]]),
                     w=(self.g_v(),))
            xo_gate = self.g_y()

    def pE(self):
        H = self.h
        KT = self.KT
        self.memset("v", KT[:, KE_CQ:KE_CQ + 4096], 0.0, w=(self.g_t(),))
        self.memset("v", KT[:, KE_KV:KE_KV + S], 0.0)
        self.memset("v", KT[:, KE_WQC:KE_WQC + 512], 0.0)
        self.memset("v", KT[:, KE_BQC:KE_BQC + 1024], 0.0)
        self.memset("v", KT[:, KE_BFIN:KE_BFIN + 1024], 0.0)
        self.memset("v", self.VT[:, 2048:6144], 0.0)
        self.dma("g", bass.AP(self.WBa, 0, [[8192, 128], [1024, 8], [1, 1024]]),
                 bass.AP(H["w_o"], 0, [[1024, 128], [131072, 8], [1, 1024]]))
        for nm, col in (("w_uqT", KE_UQT), ("w_ukT", KE_UKT), ("w_uvT", KE_UVT)):
            self.dma("g", bass.AP(KT, col, [[32768, 128], [32, 8], [1, 32]]),
                     bass.AP(H[nm], 0, [[32, 128], [4096, 8], [1, 32]]))
        self.dma("g", KT[0:32, KE_WQR:KE_WQR + 256], H["w_qr"][:, :])
        self.dma("g", bass.AP(self.STAT, 0, [[16, 128], [1, 8]]),
                 bass.AP(H["b_uq"], 0, [[1, 128], [128, 8]]),
                 allow_slow_non_contiguous=True)
        self.dma("g", bass.AP(self.STAT, 8, [[16, 128], [1, 8]]),
                 bass.AP(H["b_uv"], 0, [[1, 128], [128, 8]]),
                 allow_slow_non_contiguous=True)
        self.dma("g", self.SROW[0:1, SR_STAGE:SR_STAGE + 256], H["b_qr"][:, :])
        self.dma("g", self.SM1[0:1, 1024:2048], H["b_o"][:, :])
        gd = self.g_gd()
        self.vcopy(self.SM1[:, 0:16], self.STAT[:, 0:16], w=(gd,))
        self.vcopy(bass.AP(KT, KE_BQC + 32, [[32768, 1], [64, 8], [1, 32]]),
                   bass.AP(self.SROW, SR_STAGE, [[576, 1], [32, 8], [1, 32]]))
        for h8 in range(8):
            uq = KT[:, KE_UQT + h8 * 32:KE_UQT + (h8 + 1) * 32]
            uk = KT[:, KE_UKT + h8 * 32:KE_UKT + (h8 + 1) * 32]
            self.mm(self.PSA[0:32, h8 * 64:h8 * 64 + 32], uq, uk,
                    w=(gd,) if h8 == 0 else ())
            self.mm(self.PSA[0:1, h8 * 64 + 32:h8 * 64 + 64],
                    self.SM1[:, h8:h8 + 1], uk,
                    w=(self.g_v(),) if h8 == 0 else ())
        tg = self.g_t()
        for h8 in range(8):
            self.vcopy(KT[0:32, KE_WQC + h8 * 64:KE_WQC + h8 * 64 + 32],
                       self.PSA[0:32, h8 * 64:h8 * 64 + 32], w=(tg,) if h8 == 0 else ())
            self.vcopy(KT[0:1, KE_BQC + h8 * 64:KE_BQC + h8 * 64 + 32],
                       self.PSA[0:1, h8 * 64 + 32:h8 * 64 + 64])
            self.vcopy(KT[0:32, KE_WQC + h8 * 64 + 32:KE_WQC + (h8 + 1) * 64],
                       KT[0:32, KE_WQR + h8 * 32:KE_WQR + (h8 + 1) * 32])
        vg = self.g_v()
        for h8 in range(8):
            for half in range(2):
                self.mm(self.PSB[0:32, half * 512:(half + 1) * 512],
                        KT[:, KE_UVT + h8 * 32:KE_UVT + (h8 + 1) * 32],
                        self.WBa[:, h8 * 1024 + half * 512:h8 * 1024 + (half + 1) * 512],
                        w=(vg,) if h8 == 0 and half == 0 else
                        (self.g_s(),) if half == 0 else ())
            self.act(KT[(h8 % 4) * 32:(h8 % 4) * 32 + 32,
                        KE_UVO + (h8 // 4) * 1024:KE_UVO + (h8 // 4 + 1) * 1024],
                     self.PSB[0:32, 0:1024], AF.Copy, w=(self.g_t(),))
        for kc in range(8):
            for half in range(2):
                self.mm(self.PSA[0:1, 1024 + half * 512:1024 + (half + 1) * 512],
                        self.SM1[:, 8 + kc:9 + kc],
                        self.WBa[:, kc * 1024 + half * 512:kc * 1024 + (half + 1) * 512],
                        start=(kc == 0), stop=False,
                        w=(self.g_v(), gd) if kc == 0 and half == 0 else ())
        for half in range(2):
            self.mm(self.PSA[0:1, 1024 + half * 512:1024 + (half + 1) * 512],
                    self.ONESR[0:1, 0:1],
                    self.SM1[0:1, 1024 + half * 512:1024 + (half + 1) * 512],
                    start=False, stop=True)
        self.vcopy(KT[0:1, KE_BFIN:KE_BFIN + 1024], self.PSA[0:1, 1024:2048],
                   w=(self.g_t(),))
        self.dma("g", bass.AP(self.WBb, 0, [[8192, 128], [32, 8], [1, 32]]),
                 bass.AP(H["w_dq"], 0, [[32, 128], [4096, 8], [1, 32]]))
        self.dma("g", bass.AP(self.WBb, 256, [[8192, 128], [64, 8], [1, 64]]),
                 bass.AP(H["w_kv"], 0, [[64, 128], [8192, 8], [1, 64]]))
        self.dma("g", self.ROWS[0:1, 0:32], H["b_dq"][:, :])
        self.dma("g", self.ROWS[0:1, 32:96], H["b_kv"][:, :])
        wg = self.g_gd()
        self.memset("v", KT[64:65, KE_KV:KE_KV + S], 1.0, w=(self.g_t(),))
        EIDF = KT[:, 14848:15104].bitcast(F32)
        self.vcopy(EIDF, self.IDN[:, :])
        onk = self.g_v()
        lnb = self.LEB
        sc_f32 = self.LEB[:, 2048:4096].bitcast(F32)
        xs_free = None
        lnb_free = ((self.st, 0),)
        for s_ in range(NSLAB):
            w = [xs_free] if xs_free else []
            if s_ == 0:
                w.append(wg)
            self.dma("y", bass.AP(self.XS, 0, [[2048, 128], [1024, 2], [1, 1024]]),
                     bass.AP(H["x2"], s_ * SLAB * DM, [[1024, 128], [131072, 2], [1, 1024]]),
                     w=w)
            xdma = self.g_y()
            for u in range(2):
                self.ln_tile(u * 1024, u, lnb[:, u * 1024:(u + 1) * 1024], sc_f32,
                             w_x=(xdma,) if u == 0 else (), w_apply=lnb_free)
            xs_free = self.g_s()
            self.transpose_slab(lnb, 0, w_first=(self.g_v(),))
            lnb_free = (self.g_t(),)
            tev = self.g_s()
            for kc in range(8):
                self.mm(self.PSA[0:32, 0:256], self.WBb[:, kc * 32:(kc + 1) * 32],
                        self.TSL[:, kc * 256:(kc + 1) * 256],
                        start=(kc == 0), stop=False,
                        w=(tev, self.g_v()) if kc == 0 else ())
            self.mm(self.PSA[0:32, 0:256], self.ROWS[:, 0:32], self.ONESR[:, 0:256],
                    start=False, stop=True)
            for kc in range(8):
                self.mm(self.PSA[0:64, 256:512],
                        self.WBb[:, 256 + kc * 64:256 + (kc + 1) * 64],
                        self.TSL[:, kc * 256:(kc + 1) * 256],
                        start=(kc == 0), stop=False)
            self.mm(self.PSA[0:64, 256:512], self.ROWS[:, 32:96], self.ONESR[:, 0:256],
                    start=False, stop=True)
            self.vcopy(KT[0:32, KE_CQ + s_ * 256:KE_CQ + (s_ + 1) * 256],
                       self.PSA[0:32, 0:256], w=(self.g_t(),))
            self.vcopy(KT[0:64, KE_KV + s_ * 256:KE_KV + (s_ + 1) * 256],
                       self.PSA[0:64, 256:512])
        prj = self.g_v()
        self.memset("v", bass.AP(self.VT, 32, [[32768, 128], [33, 32], [1, 1]]), 1.0,
                    w=(self.g_t(),))
        for mq4 in range(8):
            for k4 in range(4):
                m = mq4 * 4 + k4
                w = ()
                if m == 0:
                    w = (prj, onk, self.g_v())
                elif k4 == 0:
                    w = (self.g_v(),)
                self.mm(self.PSB[:, k4 * 128:k4 * 128 + 65],
                        KT[0:65, KE_KV + m * 128:KE_KV + (m + 1) * 128],
                        self.IDN[0:65, 0:65], w=w)
            self.vcopy(bass.AP(self.VT, mq4 * 4 * 33, [[32768, 128], [33, 4], [1, 32]]),
                       bass.AP(self.PSB, 0, [[2048, 128], [128, 4], [1, 32]]),
                       w=(self.g_t(),))
        QSQ = self.SM1
        for piece in range(8):
            self.tt("v", QSQ[0:64, 0:512],
                    KT[0:64, KE_KV + piece * 512:KE_KV + (piece + 1) * 512],
                    KT[0:64, KE_KV + piece * 512:KE_KV + (piece + 1) * 512], ALU.mult)
            self.mm(self.PSA[0:1, 0:512], self.ONESC[0:64, 0:1], QSQ[0:64, 0:512],
                    w=(self.g_v(),))
            self.act(self.SROW[0:1, SR_STAGE:SR_STAGE + 512], self.PSA[0:1, 0:512],
                     AF.Sqrt, w=(self.g_t(),))
            if piece == 0:
                self.vred(self.SROW[0:1, SR_KMAX:SR_KMAX + 1],
                          self.SROW[0:1, SR_STAGE:SR_STAGE + 512], ALU.max,
                          w=(self.g_s(),))
            else:
                self.vred(self.SROW[0:1, SR_KMAX + 1:SR_KMAX + 2],
                          self.SROW[0:1, SR_STAGE:SR_STAGE + 512], ALU.max,
                          w=(self.g_s(),))
                self.tt("v", self.SROW[0:1, SR_KMAX:SR_KMAX + 1],
                        self.SROW[0:1, SR_KMAX:SR_KMAX + 1],
                        self.SROW[0:1, SR_KMAX + 1:SR_KMAX + 2], ALU.max)
        self.ts("v", self.SROW[0:1, SR_KNEG:SR_KNEG + 1],
                self.SROW[0:1, SR_KMAX:SR_KMAX + 1], -1.0, None, ALU.mult)
        kstar = self.g_v()
        scale = 1.0 / math.sqrt(160.0)
        exp_free = [None, None]
        leb_free = [None, None, None, None]
        expd_all = [None, None, None, None]
        norm_done = [None, None]
        rc_last = None
        xs_gate = None
        xo_gate = None
        QA = self.VT
        BCV = KT[0:32, 15360:16384].bitcast(F32)
        qa_free = [None, None]
        XIN = [KT[:, 16384 + k * 4096:16384 + k * 4096 + 2048].bitcast(F32)
               for k in range(4)]
        xin_free = [None, None, None, None]
        for nb in range(8):
            xin_dma = [None] * 4
            for k in range(4):
                wdma_l = [xin_free[k]] if xin_free[k] else []
                self.dma("s", XIN[k],
                         bass.AP(H["x2"], (nb * 512 + k * 128) * DM,
                                 [[1024, 128], [1, 1024]]),
                         w=wdma_l)
                xin_dma[k] = (self.dq2, self.dq2.n)
            for h8 in range(8):
                pq = 1024 + (h8 % 2) * 512
                wq = ()
                if h8 <= 1:
                    wq = (kstar,) + ((exp_free[1],) if exp_free[1] else ())
                    if rc_last is not None:
                        wq = wq + (rc_last,)
                elif qa_free[h8 % 2] is not None:
                    wq = (qa_free[h8 % 2],)
                self.mm(self.PSA[0:64, pq:pq + 512],
                        KT[:, KE_WQC + h8 * 64:KE_WQC + (h8 + 1) * 64],
                        KT[:, KE_CQ + nb * 512:KE_CQ + (nb + 1) * 512],
                        start=True, stop=False, w=wq)
                self.mm(self.PSA[0:64, pq:pq + 512],
                        KT[:, KE_BQC + h8 * 64:KE_BQC + (h8 + 1) * 64],
                        self.ONESR[:, 0:512], start=False, stop=True)
                tq = self.g_t()
                self.act(QSQ[0:64, 0:512], self.PSA[0:64, pq:pq + 512], AF.Square,
                         w=(tq,))
                self.act(QA[0:64, 2048 + h8 * 512:2048 + (h8 + 1) * 512],
                         self.PSA[0:64, pq:pq + 512], AF.Copy)
                sq = self.g_s()
                qa_free[h8 % 2] = sq
                self.mm(self.PSB[0:1, 0:512], self.ONESC[0:64, 0:1], QSQ[0:64, 0:512],
                        w=(sq, rc_last) if (h8 == 0 and rc_last is not None) else (sq,))
                self.act(self.SROW[0:1, SR_STAGE:SR_STAGE + 512], self.PSB[0:1, 0:512],
                         AF.Sqrt, w=(self.g_t(),))
                self.ts("v", QA[64:65, 2048 + h8 * 512:2048 + (h8 + 1) * 512],
                        self.SROW[0:1, SR_STAGE:SR_STAGE + 512],
                        self.SROW[0:1, SR_KNEG:SR_KNEG + 1], None, ALU.mult,
                        w=(self.g_s(),))
            qaug = self.g_v()

            def dance_stages(hprev, ctxh_gate):
                hv1 = hprev % 2
                st = {}

                def st0():
                    self.act(self.SROW[0:1, SR_STAGE:SR_STAGE + 512],
                             self.PSB[32:33, hv1 * 512:hv1 * 512 + 512], AF.Copy,
                             w=(ctxh_gate,))
                    st["drow"] = self.g_s()

                def st1():
                    for u in range(4):
                        wl = ()
                        if u == 0:
                            wl = (st["drow"],)
                            if rc_last is not None:
                                wl = wl + (rc_last,)
                        self.mm(self.PSB[0:128, 1540 + u:1541 + u],
                                self.SROW[0:1, SR_STAGE + u * 128:
                                          SR_STAGE + (u + 1) * 128],
                                self.SROW[0:1, SR_ONES:SR_ONES + 1], w=wl)
                    self.vrecip(self.FQ4[:, 0:4], self.PSB[:, 1540:1544],
                                w=(self.g_t(),))
                    st["rc"] = self.g_v()

                def st2():
                    for u in range(4):
                        self.mm(self.PSB[0:1, 1024 + u * 128:1024 + (u + 1) * 128],
                                self.FQ4[:, u:u + 1], EIDF,
                                w=(st["rc"],) if u == 0 else ())
                    self.act(self.SROW[0:1, SR_STAGE:SR_STAGE + 512],
                             self.PSB[0:1, 1024:1536], AF.Copy, w=(self.g_t(),))
                    st["rrow"] = self.g_s()

                def st3():
                    self.mm(self.PSB[0:32, 1024:1536],
                            self.SROW[0:1, SR_ONES:SR_ONES + 32],
                            self.SROW[0:1, SR_STAGE:SR_STAGE + 512], w=(st["rrow"],))
                    self.vcopy(BCV, self.PSB[0:32, 1024:1536], w=(self.g_t(),))
                    self.tt("v", self.TSL[(hprev % 4) * 32:(hprev % 4) * 32 + 32,
                                          (hprev // 4) * 512:(hprev // 4) * 512 + 512],
                            self.PSB[0:32, hv1 * 512:hv1 * 512 + 512], BCV, ALU.mult)
                    norm_done[hv1] = self.g_v()
                return [st0, st1, st2, st3], st

            pend = None
            ctxh = None
            for h8 in range(8):
                hv = h8 % 2
                if ctxh is not None:
                    pend = dance_stages(h8 - 1, ctxh)
                for pair in range(17):
                    if pair < 16:
                        pc = (pair % 2) * 1024
                        sl = pair % 4
                        wl = []
                        if exp_free[pair % 2] is not None:
                            wl.append(exp_free[pair % 2])
                        if h8 == 0 and pair <= 1:
                            wl.append(qaug)
                        for mi in range(2):
                            m = pair * 2 + mi
                            self.mm(self.PSA[:, pc + mi * 512:pc + (mi + 1) * 512],
                                    KT[:, KE_KV + m * 128:KE_KV + (m + 1) * 128],
                                    QA[:, 2048 + h8 * 512:2048 + (h8 + 1) * 512],
                                    w=wl if mi == 0 else ())
                        sc = self.g_t()
                        wexp = [sc]
                        if leb_free[sl] is not None:
                            wexp.append(leb_free[sl])
                        self.act(self.LEB[:, sl * 1024:(sl + 1) * 1024],
                                 self.PSA[:, pc:pc + 1024], AF.Exp, scale=scale, w=wexp)
                        expd_all[sl] = self.g_s()
                        exp_free[pair % 2] = expd_all[sl]
                    if pair >= 1:
                        pp = pair - 1
                        psl = pp % 4
                        expd = expd_all[psl]
                        for mi in range(2):
                            m = pp * 2 + mi
                            wl = []
                            if mi == 0:
                                wl.append(expd)
                                if pp == 0 and norm_done[hv] is not None:
                                    wl.append(norm_done[hv])
                            self.mm(self.PSB[0:33, hv * 512:hv * 512 + 512],
                                    bass.AP(self.VT, m * 33, [[32768, 128], [1, 33]]),
                                    self.LEB[:, psl * 1024 + mi * 512:
                                             psl * 1024 + (mi + 1) * 512],
                                    start=(m == 0), stop=(m == 31), w=wl)
                        leb_free[psl] = self.g_t()
                    if pend is not None and pair < 4:
                        pend[0][pair]()
                        if pair == 1:
                            rc_last = pend[1]["rc"]
                ctxh = self.g_t()
            tail = dance_stages(7, ctxh)
            for k_, f_ in enumerate(tail[0]):
                f_()
                if k_ == 1:
                    rc_last = tail[1]["rc"]
            ctxn = self.g_v()
            for ns in range(4):
                wl = [ctxn, xin_dma[ns]] if ns == 0 else [self.g_v(), xin_dma[ns]]
                if ns in (0, 2) and xo_gate is not None:
                    wl.append(xo_gate)
                for j in range(2):
                    for half in range(2):
                        self.mm(self.PSB[:, 1024 + half * 512:1024 + (half + 1) * 512],
                                self.TSL[:, j * 512 + ns * 128:j * 512 + ns * 128 + 128],
                                KT[:, KE_UVO + j * 1024 + half * 512:
                                   KE_UVO + j * 1024 + (half + 1) * 512],
                                start=(j == 0), stop=False,
                                w=wl if (j == 0 and half == 0) else ())
                for half in range(2):
                    self.mm(self.PSB[:, 1024 + half * 512:1024 + (half + 1) * 512],
                            self.ONESR[:, 0:128],
                            KT[:, KE_BFIN + half * 512:KE_BFIN + (half + 1) * 512],
                            start=False, stop=True)
                self.stt("v", self.XO[:, (ns % 2) * 1024:(ns % 2 + 1) * 1024],
                         self.PSB[:, 1024:2048], 1.0, XIN[ns],
                         ALU.mult, ALU.add, w=(self.g_t(),))
                xin_free[ns] = self.g_v()
                if ns % 2 == 1:
                    self.dma("y", bass.AP(H["x3"], (nb * 512 + (ns - 1) * 128) * DM,
                                          [[1024, 128], [131072, 2], [1, 1024]]),
                             bass.AP(self.XO, 0, [[2048, 128], [1024, 2], [1, 1024]]),
                             w=(self.g_v(),))
                    xo_gate = self.g_y()

    def quant_tile(self, xin, xout, dqp, dqn, perblock=False, w0=(), w_out=()):
        e = "v"
        F1 = self.SM1[:, 0:2048].bitcast(F32)
        F2 = self.LEB[:, 0:2048].bitcast(F32)
        F3 = self.KT[:, 0:2048].bitcast(F32)
        F4 = self.FQ4
        AM = F4[:, 0:32]
        EB = F4[:, 32:64]
        RS = F4[:, 64:96]
        SC = F4[:, 96:128]
        x3d = xin.rearrange("p (a b) -> p a b", b=32)
        self.vred(AM, x3d, ALU.max, apply_absolute_value=True, w=w0)
        self.ts(e, EB.bitcast(I32), AM.bitcast(I32), 23, None, ALU.logical_shift_right)
        self.ts(e, EB.bitcast(I32), EB.bitcast(I32), 1, None, ALU.max)
        self.ts(e, RS.bitcast(I32), EB.bitcast(I32), -1, 256, ALU.mult, ALU.add)
        self.ts(e, RS.bitcast(I32), RS.bitcast(I32), 23, None, ALU.logical_shift_left)
        self.ts(e, SC.bitcast(I32), EB.bitcast(I32), -2, None, ALU.add)
        self.ts(e, SC.bitcast(I32), SC.bitcast(I32), 23, None, ALU.logical_shift_left)
        if perblock:
            self.tt(e, RS, RS, dqn, ALU.mult)
            self.tt(e, SC, SC, dqp, ALU.mult)
        else:
            self.ts(e, RS, RS, dqn, None, ALU.mult)
            self.ts(e, SC, SC, dqp, None, ALU.mult)
        rs_b = bass.AP(self.FQ4, 64, [[128, 128], [1, 32], [0, 32]])
        sc_b = bass.AP(self.FQ4, 96, [[128, 128], [1, 32], [0, 32]])
        self.tt(e, F1.rearrange("p (a b) -> p a b", b=32), x3d, rs_b, ALU.mult)
        self.stt(e, F2, F1, -1.0, F1, ALU.mult, ALU.max)
        self.ts(e, F2, F2, 6.0, None, ALU.min)
        self.ts(e, F3.bitcast(I32), F2.bitcast(I32), 23, None, ALU.logical_shift_right)
        self.ts(e, F3.bitcast(I32), F3.bitcast(I32), 127, 129, ALU.max, ALU.min)
        self.ts(e, F1.bitcast(I32), F3.bitcast(I32), -1, 255, ALU.mult, ALU.add)
        self.ts(e, F1.bitcast(I32), F1.bitcast(I32), 23, None, ALU.logical_shift_left)
        self.tt(e, F2, F2, F1, ALU.mult)
        self.ts(e, F2, F2, 8388608.0, None, ALU.add)
        self.ts(e, F2, F2, 8388608.0, None, ALU.subtract)
        self.ts(e, F3.bitcast(I32), F3.bitcast(I32), -1, None, ALU.add)
        self.ts(e, F3.bitcast(I32), F3.bitcast(I32), 23, None, ALU.logical_shift_left)
        self.tt(e, F2, F2, F3, ALU.mult)
        self.ts(e, F3.bitcast(I32), xin.bitcast(I32), -2147483648, None, ALU.bitwise_and)
        self.r.op(e, "tensor_copy", F1.rearrange("p (a b) -> p a b", b=32), sc_b,
                  i=((self.sv, 1),))
        self.tt(e, F1.bitcast(I32), F1.bitcast(I32), F3.bitcast(I32), ALU.bitwise_or)
        self.tt(e, xout, F2, F1, ALU.mult, w=w_out)

    def pW(self):
        H = self.h
        XQB = self.LEB[:, 2048:4096]
        DQW = self.RSC
        for oc in range(8):
            self.dma("y", self.XO[:, 0:1024],
                     bass.AP(H["w_f"], oc * 128 * DM, [[1024, 128], [1, 1024]]),
                     w=(self.g_v(), self.g_t()))
            self.dma("y", DQW[:, 0:32],
                     bass.AP(H["delta_w"], oc * 128 * 32, [[32, 128], [1, 32]]))
            wdma = self.g_y()
            self.act(DQW[:, 0:32], DQW[:, 0:32], AF.Tanh, scale=4.0, w=(wdma,))
            self.act(DQW[:, 32:64], DQW[:, 0:32], AF.Exp, scale=-0.5 * LN2)
            self.act(DQW[:, 0:32], DQW[:, 0:32], AF.Exp, scale=0.5 * LN2)
            dq = self.g_s()
            self.quant_tile(self.XO[:, 0:1024], XQB[:, 0:1024],
                            DQW[:, 0:32], DQW[:, 32:64],
                            perblock=True, w0=(dq, wdma), w_out=(self.g_t(),))
            qd = self.g_v()
            for dc in range(8):
                w = ()
                if dc == 0:
                    w = (qd, self.g_s())
                elif dc == 4:
                    w = (self.g_s(),)
                self.mm(self.PSA[:, 1024 + (dc % 4) * 128:1024 + (dc % 4) * 128 + 128],
                        XQB[:, dc * 128:(dc + 1) * 128], self.IDN[:, :], w=w)
                if dc % 4 == 3:
                    self.act(bass.AP(self.WBa, (dc - 3) * 1024 + oc * 128,
                                     [[8192, 128], [1024, 4], [1, 128]]),
                             self.PSA[:, 1024:1536], AF.Copy, w=(self.g_t(),))
        self.dma("g", self.ROWS[0:1, :], H["b_f"][:, :])

    def pF(self):
        H = self.h
        dqp = self.SCL[:, 0:1]
        dqn = self.SCL[:, 1:2]
        XQB = self.LEB[:, 2048:4096]
        lnf = self.VT[:, 0:2048].bitcast(F32)
        lnscr = self.KT[:, 2048:4096].bitcast(F32)
        xs_free = None
        xo_gate = None
        rowg = self.g_gd()
        for s_ in range(NSLAB):
            p2 = s_ % 2
            w = [xs_free] if xs_free else []
            if s_ == 0:
                w.append(rowg)
            self.dma("y", bass.AP(self.XS, 0, [[2048, 128], [1024, 2], [1, 1024]]),
                     bass.AP(H["x3"], s_ * SLAB * DM, [[1024, 128], [131072, 2], [1, 1024]]),
                     w=w)
            xdma = self.g_y()
            for u in range(2):
                self.ln_tile(u * 1024, u, lnf,
                             lnscr, w_x=(xdma, self.g_v()) if u == 0 else (self.g_v(),))
                lng = self.g_s()
                self.quant_tile(lnf, XQB[:, u * 1024:(u + 1) * 1024], dqp, dqn,
                                w0=(lng, self.g_t()), w_out=(self.g_t(),))
                qd = self.g_v()
                for c in range(8):
                    w2 = ()
                    if c == 0:
                        w2 = (qd, self.g_s())
                    elif c == 4:
                        w2 = (self.g_s(),)
                    self.mm(self.PSA[:, 1024 + (c % 4) * 128:1024 + (c % 4) * 128 + 128],
                            XQB[:, u * 1024 + c * 128:u * 1024 + (c + 1) * 128],
                            self.IDN[:, :], w=w2)
                    if c % 4 == 3:
                        self.act(bass.AP(self.TSL, p2 * 2048 + (c - 3) * 256 + u * 128,
                                         [[4096, 128], [256, 4], [1, 128]]),
                                 self.PSA[:, 1024:1536], AF.Copy, w=(self.g_t(),))
            tq = self.g_s()
            for u in range(2):
                po = u * 1024
                for kc in range(8):
                    for half in range(2):
                        self.mm(self.PSB[:, po + half * 512:po + (half + 1) * 512],
                                self.TSL[:, p2 * 2048 + kc * 256 + u * 128:
                                         p2 * 2048 + kc * 256 + u * 128 + 128],
                                self.WBa[:, kc * 1024 + half * 512:
                                         kc * 1024 + (half + 1) * 512],
                                start=(kc == 0), stop=False,
                                w=(tq, self.g_v()) if (u == 0 and kc == 0 and half == 0)
                                else ())
                for half in range(2):
                    self.mm(self.PSB[:, po + half * 512:po + (half + 1) * 512],
                            self.ONESR[:, 0:128],
                            self.ROWS[:, half * 512:(half + 1) * 512],
                            start=False, stop=True)
                wst = [self.g_t()]
                if u == 0 and xo_gate is not None:
                    wst.append(xo_gate)
                self.stt("v", self.XO[:, u * 1024:(u + 1) * 1024],
                         self.PSB[:, po:po + 1024], 1.0,
                         self.XS[:, u * 1024:(u + 1) * 1024],
                         ALU.mult, ALU.add, w=wst)
            xs_free = self.g_v()
            self.dma("y", bass.AP(H["out"], s_ * SLAB * DM, [[1024, 128], [131072, 2], [1, 1024]]),
                     bass.AP(self.XO, 0, [[2048, 128], [1024, 2], [1, 1024]]),
                     w=(self.g_v(),))
            xo_gate = self.g_y()

    def build(self, phases):
        table = {"0": self.p0, "A1": self.pA1, "A2": self.pA2, "B": self.pB,
                 "C1": self.pC1, "C1b": self.pC1b, "C2": self.pC2, "D": self.pD,
                 "E": self.pE, "W": self.pW, "F": self.pF}
        for ph in phases:
            table[ph]()
            self.fence()
        for nm in self.dbg.get("dump_sbuf", ()):
            t = getattr(self, nm)
            self.dma("y", self.h["dbg_" + nm][:, :], t[:, :])
        self.r.op("y", None, w=((self.dy, self.dy.n), (self.dg, self.dg.n)))
        rec = self.r
        with self.nc.Block() as blk:
            @blk.tensor
            def _(eng):
                rec.replay("t", eng)

            @blk.vector
            def _(eng):
                rec.replay("v", eng)

            @blk.scalar
            def _(eng):
                rec.replay("s", eng)

            @blk.gpsimd
            def _(eng):
                rec.replay("g", eng)

            @blk.sync
            def _(eng):
                rec.replay("y", eng)


ALL_PHASES = ("0", "A1", "A2", "B", "C1", "C1b", "C2", "D", "E", "W", "F")


def build_nc(phases=ALL_PHASES, dbg=None):
    nc = bass.Bass("TRN2", target_bir_lowering=False)
    mk = MK(nc, dbg=dbg)
    mk.build(phases)
    return nc, mk


def make_inputs(inp, core):
    x = np.asarray(inp["L0_attn_hidden_states"])
    lay = host_layout(inp)
    cst = host_constants()
    m = {"x": np.ascontiguousarray(x[core], dtype=np.float32)}
    m.update(lay)
    m.update(cst)
    return m


def kernel(**inputs):
    nc, mk = build_nc()
    in_maps = [make_inputs(inputs, c) for c in range(8)]
    res = run_bass_kernel_spmd(nc, in_maps, core_ids=list(range(8)))
    out = np.stack([np.asarray(r["out"]) for r in res.results], axis=0)
    return out.astype(np.float32)


# version 16
# speedup vs baseline: 1.0521x; 1.0127x over previous
import math
import numpy as np
import ml_dtypes

import concourse.bass as bass
import concourse.mybir as mybir
from concourse.bass_utils import run_bass_kernel_spmd

F32 = mybir.dt.float32
BF16 = mybir.dt.bfloat16
I32 = mybir.dt.int32
I16 = mybir.dt.int16
AF = mybir.ActivationFunctionType
ALU = mybir.AluOpType
AX = mybir.AxisListType

S = 4096
DM = 1024
NSLAB = 16
SLAB = 256

BFD = ml_dtypes.bfloat16
LN2 = math.log(2.0)

SR_STAGE = 0
SR_KMAX = 512
SR_KNEG = 514
SR_DELTA = 516
SR_TANH = 518
SR_DQ2P = 520
SR_DQ2N = 522
SR_ONES = 528

KE_CQ = 0
KE_KV = 4096
KE_UVO = 8192
KE_WQC = 10240
KE_BQC = 10752
KE_BFIN = 11776
KE_UQT = 12800
KE_UKT = 13056
KE_UVT = 13312
KE_WQR = 13568


class Sem:
    __slots__ = ("h", "n")

    def __init__(self, h):
        self.h = h
        self.n = 0


def _ap_span(ap):
    if not isinstance(ap, bass.AP):
        return None
    es = mybir.dt.size(ap.dtype)
    pat = list(ap.ap)
    reach = 0
    fel = 1
    for k, (stp, cnt) in enumerate(pat):
        if k == 0:
            continue
        reach += abs(int(stp)) * (int(cnt) - 1)
        fel *= int(cnt)
    lo = int(ap.offset) * es
    return (ap.tensor.name, lo, lo + (reach + 1) * es, fel)


def _overlap(a, b):
    return a[0] == b[0] and a[1] < b[2] and b[1] < a[2]


class Rec:
    ENGS = ("t", "v", "s", "g", "y")
    OWN = {"t": "st", "v": "sv", "s": "ss", "g": "sg"}

    def __init__(self, nc):
        self.nc = nc
        self.q = {e: [] for e in self.ENGS}
        self.sems = {}
        self._hist = {e: [] for e in self.ENGS}

    def sem(self, name):
        s = Sem(self.nc.alloc_semaphore(name=name))
        self.sems[name] = s
        return s

    def op(self, e, meth, *a, w=(), i=(), **kw):
        waits = [(s.h, v) for (s, v) in w if s is not None and v > 0]
        if e in self.OWN and meth not in (None,):
            aps = [x for x in a if isinstance(x, bass.AP)]
            aps += [x for x in kw.values() if isinstance(x, bass.AP)]
            wr = _ap_span(aps[0]) if aps else None
            rds = [s2 for s2 in (_ap_span(x) for x in aps[1:]) if s2 is not None]
            own = self.sems[self.OWN[e]]
            need = 0
            for pw, cnt in self._hist[e]:
                for rd in rds:
                    if _overlap(rd, pw) and (rd[3] < 512 or pw[3] < 512):
                        need = max(need, cnt)
            if need:
                waits.append((own.h, need))
            self._hist[e] = self._hist[e][-1:] + ([(wr, own.n + 1)] if wr else [])
        self.q[e].append((meth, a, kw, waits,
                          [(s.h, d) for (s, d) in i] if i else None))
        for s, d in i:
            s.n += d

    def replay(self, e, eng):
        for meth, a, kw, waits, incs in self.q[e]:
            for h, v in waits:
                eng.wait_ge(h, v)
            if meth is None:
                continue
            try:
                ins = getattr(eng, meth)(*a, **kw)
            except Exception:
                import sys as _sys
                print(f"REPLAY-FAIL eng={e} meth={meth} "
                      f"args={[getattr(x, 'shape', x) for x in a]}", file=_sys.stderr)
                raise
            if incs:
                for h, d in incs:
                    ins.then_inc(h, d)


def host_constants():
    c = {}
    c["c_ident"] = np.eye(128, dtype=BFD)
    c["c_iota16"] = np.tile(np.arange(16, dtype=np.float32)[None, :], (128, 1)).astype(BFD)
    j = np.arange(1024)
    e16 = np.zeros((128, 1024), dtype=np.float32)
    e16[:16] = (j[None, :] // 64 == np.arange(16)[:, None])
    c["c_e16"] = e16.astype(BFD)
    return c


def host_layout(inp):
    d = {}
    f32 = np.float32
    asc = np.ascontiguousarray
    d["w_q"] = asc(inp["L0_attn_Wq"], dtype=f32)
    d["w_k"] = asc(inp["L0_attn_Wk"], dtype=f32)
    d["w_v"] = asc(inp["L0_attn_Wv"], dtype=f32)
    for nm, key in (("b_q", "L0_attn_bq"), ("b_k", "L0_attn_bk"), ("b_v", "L0_attn_bv")):
        d[nm] = asc(np.asarray(inp[key]).reshape(1, 1024), dtype=f32)
    for f in ("q", "k", "v"):
        w1 = np.asarray(inp[f"L0_ffn_{f}_w1"])
        d[f"w1_{f}"] = asc(w1.transpose(1, 0, 2).reshape(1024, 1024), dtype=f32)
        d[f"w2_{f}"] = asc(np.asarray(inp[f"L0_ffn_{f}_w2"]).reshape(1024, 1024), dtype=f32)
        d[f"b1_{f}"] = asc(np.asarray(inp[f"L0_ffn_{f}_b1"]).reshape(1, 1024), dtype=f32)
        d[f"b2_{f}"] = asc(inp[f"L0_ffn_{f}_b2"], dtype=f32)
    nwt = np.concatenate([np.asarray(inp[f"L0_ffn_{f}_node_w"]).T for f in ("q", "k", "v")],
                         axis=1)
    d["node_wt"] = asc(nwt, dtype=f32)
    nb_ = np.concatenate([np.asarray(inp[f"L0_ffn_{f}_node_b"]) for f in ("q", "k", "v")])
    d["node_b"] = asc(nb_.reshape(1, 45), dtype=f32)
    d["w_dq"] = asc(inp["L1_attn_W_DQ"], dtype=f32)
    d["b_dq"] = asc(np.asarray(inp["L1_attn_b_DQ"]).reshape(1, 32), dtype=f32)
    d["w_kv"] = asc(np.concatenate([inp["L1_attn_W_DKV"], inp["L1_attn_W_KR"]], axis=1),
                    dtype=f32)
    d["b_kv"] = asc(np.concatenate([np.asarray(inp["L1_attn_b_DKV"]),
                                    np.asarray(inp["L1_attn_b_KR"])]).reshape(1, 64),
                    dtype=f32)
    d["w_uqT"] = asc(np.asarray(inp["L1_attn_W_UQ"]).T.reshape(8, 128, 32), dtype=f32)
    d["w_ukT"] = asc(np.asarray(inp["L1_attn_W_UK"]).T.reshape(8, 128, 32), dtype=f32)
    d["w_uvT"] = asc(np.asarray(inp["L1_attn_W_UV"]).T.reshape(8, 128, 32), dtype=f32)
    d["w_qr"] = asc(inp["L1_attn_W_QR"], dtype=f32)
    d["b_qr"] = asc(np.asarray(inp["L1_attn_b_QR"]).reshape(1, 256), dtype=f32)
    d["b_uq"] = asc(np.asarray(inp["L1_attn_b_UQ"]).reshape(8, 128), dtype=f32)
    d["b_uv"] = asc(np.asarray(inp["L1_attn_b_UV"]).reshape(8, 128), dtype=f32)
    d["w_o"] = asc(inp["L1_attn_W_O"], dtype=f32)
    d["b_o"] = asc(np.asarray(inp["L1_attn_b_O"]).reshape(1, 1024), dtype=f32)
    d["w_f"] = asc(inp["L1_ffn_weight"], dtype=f32)
    d["b_f"] = asc(np.asarray(inp["L1_ffn_bias"]).reshape(1, 1024), dtype=f32)
    d["delta_w"] = asc(inp["L1_ffn_delta_w"], dtype=f32)
    d["delta_act"] = asc(np.asarray(inp["L1_ffn_delta_act"]).reshape(1, 1), dtype=f32)
    return d


class MK:
    def __init__(self, nc, dbg=None):
        self.nc = nc
        self.dbg = dbg or {}
        self.r = Rec(nc)
        r = self.r
        self.st = r.sem("st")
        self.sv = r.sem("sv")
        self.ss = r.sem("ss")
        self.sg = r.sem("sg")
        self.dy = r.sem("dy")
        self.dg = r.sem("dg")
        self.dq2 = r.sem("dq2")
        self._declare_io()
        self._alloc()

    def _declare_io(self):
        nc = self.nc
        T = nc.dram_tensor
        self.h = {}
        H = self.h
        H["x"] = T("x", [S, DM], F32, kind="ExternalInput")
        H["out"] = T("out", [S, DM], F32, kind="ExternalOutput")
        for nm, sh, dt in (("c_ident", [128, 128], BF16),
                           ("c_iota16", [128, 16], BF16), ("c_e16", [128, 1024], BF16)):
            H[nm] = T(nm, sh, dt, kind="ExternalInput")
        for nm, sh in (
            ("w_q", [DM, DM]), ("w_k", [DM, DM]), ("w_v", [DM, DM]),
            ("b_q", [1, DM]), ("b_k", [1, DM]), ("b_v", [1, DM]),
            ("w1_q", [DM, DM]), ("w1_k", [DM, DM]), ("w1_v", [DM, DM]),
            ("w2_q", [DM, DM]), ("w2_k", [DM, DM]), ("w2_v", [DM, DM]),
            ("b1_q", [1, DM]), ("b1_k", [1, DM]), ("b1_v", [1, DM]),
            ("b2_q", [16, DM]), ("b2_k", [16, DM]), ("b2_v", [16, DM]),
            ("node_wt", [DM, 45]), ("node_b", [1, 45]),
            ("w_dq", [DM, 32]), ("b_dq", [1, 32]),
            ("w_kv", [DM, 64]), ("b_kv", [1, 64]),
            ("w_uqT", [8, 128, 32]), ("w_ukT", [8, 128, 32]), ("w_uvT", [8, 128, 32]),
            ("w_qr", [32, 256]), ("b_qr", [1, 256]),
            ("b_uq", [8, 128]), ("b_uv", [8, 128]),
            ("w_o", [DM, DM]), ("b_o", [1, DM]),
            ("w_f", [DM, DM]), ("b_f", [1, DM]),
            ("delta_w", [DM, 32]), ("delta_act", [1, 1]),
        ):
            H[nm] = T(nm, sh, F32, kind="ExternalInput")
        dump = set(self.dbg.get("dump", ()))
        pre = set(self.dbg.get("preload", ()))

        def scratch(nm, sh, dt):
            kind = "Internal"
            if nm in dump:
                kind = "ExternalOutput"
            elif nm in pre:
                kind = "ExternalInput"
            H[nm] = T(nm, sh, dt, kind=kind)

        scratch("x1", [S, DM], F32)
        scratch("x2", [S, DM], F32)
        scratch("x3", [S, DM], F32)
        scratch("xlnT", [16, 128, 2048], BF16)
        scratch("qTs", [16, 128, 2048], BF16)
        for nm in self.dbg.get("dump_sbuf", ()):
            H["dbg_" + nm] = T("dbg_" + nm, [128, self._dbg_cols(nm)], BF16,
                               kind="ExternalOutput")

    def _dbg_cols(self, nm):
        return {"KT": 8 * S, "VT": 8 * S, "WBa": 8192, "WBb": 8192,
                "TSL": 4096, "LEB": 4096, "SM1": 2048}.get(nm, 4096)

    def _alloc(self):
        nc = self.nc
        A = nc.alloc_sbuf_tensor
        self.KT = A("KT", [128, 8 * S], BF16)
        self.VT = A("VT", [128, 8 * S], BF16)
        self.WBa = A("WBa", [128, 8192], BF16)
        self.WBb = A("WBb", [128, 8192], BF16)
        self.XS = A("XS", [128, 2048], F32)
        self.XO = A("XO", [128, 2048], F32)
        self.TSL = A("TSL", [128, 4096], BF16)
        self.LEB = A("LEB", [128, 4096], BF16)
        self.SM1 = A("SM1", [128, 2048], BF16)
        self.IDN = A("IDN", [128, 128], BF16)
        self.IOTA = A("IOTA", [128, 16], BF16)
        self.E16 = A("E16", [128, 1024], BF16)
        self.ONESC = A("ONESC", [128, 1], BF16)
        self.ONESR = A("ONESR", [128, 512], BF16)
        self.ROWS = A("ROWS", [128, 1024], BF16)
        self.B2C = A("B2C", [128, 1024], BF16)
        self.NWT = A("NWT", [128, 360], BF16)
        self.NBR = A("NBR", [128, 45], BF16)
        self.ROUTE = A("ROUTE", [128, 32], F32)
        self.RSC = A("RSC", [128, 64], F32)
        self.OHT = A("OHT", [128, 256], BF16)
        self.STAT = A("STAT", [128, 16], F32)
        self.SROW = A("SROW", [1, 576], F32)
        self.SCL = A("SCL", [128, 8], F32)
        self.FQ4 = A("FQ4", [128, 128], F32)
        Ap = nc.alloc_psum_tensor
        self.PSA = Ap("PSA", [128, 2048], F32)
        self.PSB = Ap("PSB", [128, 2048], F32)

    def dma(self, q, out, in_, w=(), **kw):
        sem = self.dy if q == "y" else (self.dq2 if q == "s" else self.dg)
        self.r.op(q, "dma_start", w=w, i=((sem, 16),), out=out, in_=in_, **kw)

    def mm(self, out, lhsT, rhs, start=True, stop=True, w=()):
        self.r.op("t", "matmul", out, lhsT, rhs, start=start, stop=stop,
                  w=w, i=((self.st, 1),))

    def mmT(self, out_psum, in_sb, w=()):
        self.mm(out_psum, in_sb, self.IDN[:, :], w=w)

    def act(self, out, in_, func, w=(), **kw):
        self.r.op("s", "activation", out, in_, func, w=w, i=((self.ss, 1),), **kw)

    def ts(self, e, out, in0, s1, s2, op0, op1=None, w=()):
        sem = {"v": self.sv, "g": self.sg}[e]
        if op1 is None:
            self.r.op(e, "tensor_scalar", out, in0, s1, None, op0, w=w, i=((sem, 1),))
        else:
            self.r.op(e, "tensor_scalar", out, in0, s1, s2, op0, op1, w=w, i=((sem, 1),))

    def tt(self, e, out, in0, in1, op, w=()):
        sem = {"v": self.sv, "g": self.sg}[e]
        self.r.op(e, "tensor_tensor", out, in0, in1, op, w=w, i=((sem, 1),))

    def stt(self, e, out, in0, scalar, in1, op0, op1, w=()):
        sem = {"v": self.sv, "g": self.sg}[e]
        self.r.op(e, "scalar_tensor_tensor", out, in0, scalar, in1, op0, op1,
                  w=w, i=((sem, 1),))

    def vcopy(self, out, in_, w=()):
        self.r.op("v", "tensor_copy", out, in_, w=w, i=((self.sv, 1),))

    def vred(self, out, in_, op, w=(), **kw):
        self.r.op("v", "tensor_reduce", out, in_, AX.X, op, w=w, i=((self.sv, 1),), **kw)

    def vrecip(self, out, in_, w=()):
        self.r.op("v", "reciprocal", out, in_, w=w, i=((self.sv, 1),))

    def memset(self, e, ap, val, w=()):
        sem = {"v": self.sv, "g": self.sg}[e]
        self.r.op(e, "memset", ap, val, w=w, i=((sem, 1),))

    def g_t(self):
        return (self.st, self.st.n)

    def g_v(self):
        return (self.sv, self.sv.n)

    def g_s(self):
        return (self.ss, self.ss.n)

    def g_g(self):
        return (self.sg, self.sg.n)

    def g_y(self):
        return (self.dy, self.dy.n)

    def g_gd(self):
        return (self.dg, self.dg.n)

    def fence(self):
        counts = {nm: s.n for nm, s in self.r.sems.items()}
        sems = self.r.sems
        for e in Rec.ENGS:
            own = {"t": "st", "v": "sv", "s": "ss", "g": "sg"}.get(e)
            w = [(sems[nm], v) for nm, v in counts.items() if nm != own and v > 0]
            self.r.op(e, None, w=w)

    def ln_tile(self, xoff, col, out_ap, scratch, w_x=(), w_apply=None):
        ST = self.STAT
        c = 8 * col
        xs = self.XS[:, xoff:xoff + 1024]
        self.vred(ST[:, c + 0:c + 1], xs, ALU.add, w=w_x)
        self.stt("v", scratch, xs, 1.0, xs, ALU.mult, ALU.mult)
        self.vred(ST[:, c + 1:c + 2], scratch, ALU.add)
        self.ts("v", ST[:, c + 2:c + 3], ST[:, c + 0:c + 1], 1.0 / DM, None, ALU.mult)
        self.ts("v", ST[:, c + 3:c + 4], ST[:, c + 1:c + 2], 1.0 / DM, None, ALU.mult)
        self.tt("v", ST[:, c + 4:c + 5], ST[:, c + 2:c + 3], ST[:, c + 2:c + 3], ALU.mult)
        self.tt("v", ST[:, c + 3:c + 4], ST[:, c + 3:c + 4], ST[:, c + 4:c + 5],
                ALU.subtract)
        self.ts("v", ST[:, c + 3:c + 4], ST[:, c + 3:c + 4], 1e-5, None, ALU.add)
        v_ = ST[:, c + 3:c + 4]
        y = ST[:, c + 5:c + 6]
        t = ST[:, c + 6:c + 7]
        self.ts("v", t.bitcast(I32), v_.bitcast(I32), 1, None, ALU.logical_shift_right)
        self.ts("v", y.bitcast(I32), t.bitcast(I32), -1, 0x5F3759DF, ALU.mult, ALU.add)
        for _ in range(2):
            self.tt("v", t, y, y, ALU.mult)
            self.tt("v", t, t, v_, ALU.mult)
            self.ts("v", t, t, -0.5, 1.5, ALU.mult, ALU.add)
            self.tt("v", y, y, t, ALU.mult)
        mb = ST[:, c + 7:c + 8]
        self.stt("v", mb, ST[:, c + 2:c + 3], -1.0, y, ALU.mult, ALU.mult)
        wa = (self.g_v(),) + (tuple(w_apply) if w_apply is not None else (self.g_t(),))
        self.act(out_ap, xs, AF.Identity, bias=mb, scale=y, w=wa)

    def transpose_slab(self, src, base, w_first=()):
        for u in range(2):
            for c in range(8):
                w = ()
                if u == 0 and c == 0:
                    w = tuple(w_first) + (self.g_s(),)
                elif u == 1 and c == 0:
                    w = (self.g_s(),)
                self.mmT(self.PSA[:, 1024 + c * 128:1024 + (c + 1) * 128],
                         src[:, base + u * 1024 + c * 128:base + u * 1024 + (c + 1) * 128],
                         w=w)
            out = bass.AP(self.TSL, u * 128, [[4096, 128], [256, 8], [1, 128]])
            self.act(out, self.PSA[:, 1024:2048], AF.Copy, w=(self.g_t(),))

    def p0(self):
        H = self.h
        self.dma("y", self.IDN[:, :], H["c_ident"][:, :])
        self.dma("y", self.IOTA[:, :], H["c_iota16"][:, :])
        self.dma("y", self.E16[:, :], H["c_e16"][:, :])
        self.memset("v", self.ONESC[:, :], 1.0)
        self.memset("v", self.ONESR[:, :], 0.0)
        self.memset("v", self.ONESR[0:1, :], 1.0)
        self.memset("v", self.ROWS[:, :], 0.0)
        self.memset("v", self.B2C[:, :], 0.0)
        self.memset("v", self.NBR[:, :], 0.0)
        self.memset("v", self.OHT[:, :], 0.0)
        self.memset("v", self.SROW[0:1, SR_STAGE:SR_STAGE + 128], 1.0)
        self.memset("v", self.SROW[0:1, SR_ONES:SR_ONES + 32], 1.0)
        self.dma("g", self.SROW[0:1, SR_DELTA:SR_DELTA + 1], H["delta_act"][:, :])
        gd = self.g_gd()
        self.act(self.SROW[0:1, SR_TANH:SR_TANH + 1],
                 self.SROW[0:1, SR_DELTA:SR_DELTA + 1], AF.Tanh, scale=4.0, w=(gd,))
        self.act(self.SROW[0:1, SR_DQ2P:SR_DQ2P + 1],
                 self.SROW[0:1, SR_TANH:SR_TANH + 1], AF.Exp, scale=0.5 * LN2)
        self.act(self.SROW[0:1, SR_DQ2N:SR_DQ2N + 1],
                 self.SROW[0:1, SR_TANH:SR_TANH + 1], AF.Exp, scale=-0.5 * LN2)
        sg_ = self.g_s()
        self.mm(self.PSB[0:128, 1024:1025], self.SROW[0:1, SR_STAGE:SR_STAGE + 128],
                self.SROW[0:1, SR_DQ2P:SR_DQ2P + 1], w=(sg_, self.g_v()))
        self.mm(self.PSB[0:128, 1025:1026], self.SROW[0:1, SR_STAGE:SR_STAGE + 128],
                self.SROW[0:1, SR_DQ2N:SR_DQ2N + 1])
        self.vcopy(self.SCL[:, 0:2], self.PSB[:, 1024:1026], w=(self.g_t(),))

    def pA1(self):
        H = self.h
        self.dma("g", bass.AP(self.WBa, 0, [[8192, 128], [1024, 8], [1, 1024]]),
                 bass.AP(H["w_q"], 0, [[1024, 128], [131072, 8], [1, 1024]]))
        self.dma("g", bass.AP(self.WBb, 0, [[8192, 128], [1024, 8], [1, 1024]]),
                 bass.AP(H["w_k"], 0, [[1024, 128], [131072, 8], [1, 1024]]))
        self.dma("g", self.ROWS[0:1, :], H["b_q"][:, :])
        self.dma("g", self.B2C[0:1, :], H["b_k"][:, :])
        wg = self.g_gd()
        lnb = self.LEB
        sc_f32 = self.LEB[:, 2048:4096].bitcast(F32)
        xs_free = None
        lnb_free = ((self.st, 0),)
        for s_ in range(NSLAB):
            p2 = s_ % 2
            w = [xs_free] if xs_free else []
            self.dma("y", bass.AP(self.XS, 0, [[2048, 128], [1024, 2], [1, 1024]]),
                     bass.AP(H["x"], s_ * SLAB * DM, [[1024, 128], [131072, 2], [1, 1024]]),
                     w=w)
            xdma = self.g_y()
            for u in range(2):
                self.ln_tile(u * 1024, u, lnb[:, u * 1024:(u + 1) * 1024], sc_f32,
                             w_x=(xdma,) if u == 0 else (), w_apply=lnb_free)
            xs_free = self.g_s()
            self.transpose_slab(lnb, 0)
            lnb_free = (self.g_t(),)
            tev = self.g_s()
            qstage = self.XO[:, p2 * 1024:(p2 + 1) * 1024].bitcast(BF16)
            for oc in range(8):
                for kc in range(8):
                    self.mm(self.PSB[:, oc * 256:(oc + 1) * 256],
                            self.WBa[:, kc * 1024 + oc * 128:kc * 1024 + oc * 128 + 128],
                            self.TSL[:, kc * 256:(kc + 1) * 256],
                            start=(kc == 0), stop=False,
                            w=(tev, wg) if (oc == 0 and kc == 0) else ())
                self.mm(self.PSB[:, oc * 256:(oc + 1) * 256],
                        self.ROWS[:, oc * 128:(oc + 1) * 128], self.ONESR[:, 0:256],
                        start=False, stop=True)
            self.act(qstage, self.PSB[:, :], AF.Copy, w=(self.g_t(), self.g_y()))
            qev = self.g_s()
            for oc in range(8):
                for kc in range(8):
                    self.mm(self.PSB[:, oc * 256:(oc + 1) * 256],
                            self.WBb[:, kc * 1024 + oc * 128:kc * 1024 + oc * 128 + 128],
                            self.TSL[:, kc * 256:(kc + 1) * 256],
                            start=(kc == 0), stop=False,
                            w=(qev,) if (oc == 0 and kc == 0) else ())
                self.mm(self.PSB[:, oc * 256:(oc + 1) * 256],
                        self.B2C[:, oc * 128:(oc + 1) * 128], self.ONESR[:, 0:256],
                        start=False, stop=True)
            self.act(bass.AP(self.KT, s_ * 256, [[32768, 128], [4096, 8], [1, 256]]),
                     self.PSB[:, :], AF.Copy, w=(self.g_t(),))
            self.dma("y", bass.AP(H["xlnT"], s_ * 262144, [[2048, 128], [1, 2048]]),
                     bass.AP(self.TSL, 0, [[4096, 128], [1, 2048]]),
                     w=(tev,))
            self.dma("y", bass.AP(H["qTs"], s_ * 262144, [[2048, 128], [1, 2048]]),
                     qstage,
                     w=(qev,))

    def pA2(self):
        H = self.h
        self.dma("g", bass.AP(self.WBa, 0, [[8192, 128], [1024, 8], [1, 1024]]),
                 bass.AP(H["w_v"], 0, [[1024, 128], [131072, 8], [1, 1024]]))
        self.dma("g", self.ROWS[0:1, :], H["b_v"][:, :])
        wg = self.g_gd()
        for half in range(2):
            self.mm(self.PSA[:, half * 512:(half + 1) * 512], self.ONESR[:, 0:128],
                    self.ROWS[:, half * 512:(half + 1) * 512],
                    w=(wg,) if half == 0 else ())
        self.vcopy(self.SM1[:, 0:1024], self.PSA[:, 0:1024], w=(self.g_t(),))
        bvt = self.g_v()
        tsl_free = None
        for s_ in range(NSLAB):
            w = [tsl_free] if tsl_free else []
            self.dma("y", bass.AP(self.TSL, 0, [[4096, 128], [1, 2048]]),
                     bass.AP(H["xlnT"], s_ * 262144, [[2048, 128], [1, 2048]]),
                     w=w)
            xdma = self.g_y()
            for u in range(2):
                po = u * 1024
                for kc in range(8):
                    for half in range(2):
                        self.mm(self.PSA[:, po + half * 512:po + (half + 1) * 512],
                                self.TSL[:, kc * 256 + u * 128:kc * 256 + u * 128 + 128],
                                self.WBa[:, kc * 1024 + half * 512:
                                         kc * 1024 + (half + 1) * 512],
                                start=(kc == 0), stop=(kc == 7),
                                w=(xdma, self.g_v()) if kc == 0 and half == 0 else ())
                j = 2 * s_ + u
                self.tt("v", self.VT[:, j * 1024:(j + 1) * 1024],
                        self.PSA[:, po:po + 1024], self.SM1[:, 0:1024], ALU.add,
                        w=(self.g_t(), bvt))
            tsl_free = self.g_t()

    def pB(self):
        H = self.h
        exp_free = [None, None]
        qts_free = {}
        xs_free = None
        xo_gate = None
        leb_h0 = [None, None]
        leb_all = [None, None]
        vg = {}
        SM2 = self.SM1[:, 1536:2048]

        def tmax(a, b):
            if a is None:
                return b
            if b is None:
                return a
            return (a[0], max(a[1], b[1]))

        for nb in range(16):
            p = nb % 2
            self.dma("s", bass.AP(self.TSL, p * 2048, [[4096, 128], [1, 2048]]),
                     bass.AP(H["qTs"], nb * 262144, [[2048, 128], [1, 2048]]),
                     w=[qts_free[p]] if p in qts_free else [])
            qdma = (self.dq2, self.dq2.n)
            self.dma("y", bass.AP(self.XS, 0, [[2048, 128], [1024, 2], [1, 1024]]),
                     bass.AP(H["x"], nb * 256 * DM, [[1024, 128], [131072, 2], [1, 1024]]),
                     w=[xs_free] if xs_free else [])
            xsdma = self.g_y()
            for m in range(34):
                if m < 32:
                    mp = m % 2
                    sc_half = [None, None]
                    for halfg in range(2):
                        wlist = []
                        if exp_free[halfg] is not None:
                            wlist.append(exp_free[halfg])
                        if m == 0 and halfg == 0:
                            wlist.append(qdma)
                        for gg in range(4):
                            g = halfg * 4 + gg
                            self.mm(self.PSA[:, halfg * 1024 + gg * 256:
                                             halfg * 1024 + (gg + 1) * 256],
                                    self.KT[:, g * S + m * 128:g * S + (m + 1) * 128],
                                    self.TSL[:, p * 2048 + g * 256:
                                             p * 2048 + (g + 1) * 256],
                                    w=wlist if gg == 0 else ())
                        sc_half[halfg] = self.g_t()
                if m >= 2:
                    mc = m - 2
                    mpc = mc % 2
                    vn, gn = vg[mc]
                    for g in range(8):
                        for ns in range(2):
                            wlist = []
                            if g == 0 and ns == 0:
                                wlist = [vn]
                                if mc == 0 and xo_gate is not None:
                                    wlist.append(self.g_v())
                            elif g == 6 and ns == 0:
                                wlist = [gn]
                            self.mm(self.PSB[:, ns * 1024 + g * 128:
                                             ns * 1024 + (g + 1) * 128],
                                    self.LEB[:, mpc * 2048 + g * 256 + ns * 128:
                                             mpc * 2048 + g * 256 + ns * 128 + 128],
                                    self.VT[:, mc * 1024 + g * 128:
                                            mc * 1024 + (g + 1) * 128],
                                    start=(mc == 0 and g % 4 == 0), stop=(mc == 31),
                                    w=wlist)
                        if g == 3:
                            leb_h0[mpc] = self.g_t()
                    leb_all[mpc] = self.g_t()
                if m < 32:
                    e_half = [None, None]
                    for halfg in range(2):
                        if halfg == 0:
                            wexp = [tmax(sc_half[1], leb_h0[mp])]
                        else:
                            wexp = [tmax(sc_half[1], leb_all[mp])]
                        self.act(self.LEB[:, mp * 2048 + halfg * 1024:
                                          mp * 2048 + (halfg + 1) * 1024],
                                 self.PSA[:, halfg * 1024:(halfg + 1) * 1024], AF.Exp,
                                 scale=1.0 / 32.0, w=wexp)
                        exp_free[halfg] = self.g_s()
                        e_half[halfg] = self.g_s()
                    eb = mp * 2048
                    so = mp * 256
                    self.tt("v", bass.AP(self.SM1, 0, [[2048, 128], [256, 2], [1, 256]]),
                            bass.AP(self.LEB, eb, [[4096, 128], [512, 2], [1, 256]]),
                            bass.AP(self.LEB, eb + 256, [[4096, 128], [512, 2], [1, 256]]),
                            ALU.add, w=(e_half[0],))
                    self.tt("v", bass.AP(self.SM1, 512, [[2048, 128], [256, 2], [1, 256]]),
                            bass.AP(self.LEB, eb + 1024, [[4096, 128], [512, 2], [1, 256]]),
                            bass.AP(self.LEB, eb + 1280, [[4096, 128], [512, 2], [1, 256]]),
                            ALU.add, w=(e_half[1],))
                    self.tt("v", bass.AP(self.SM1, 1024, [[2048, 128], [256, 2], [1, 256]]),
                            bass.AP(self.SM1, 0, [[2048, 128], [512, 2], [1, 256]]),
                            bass.AP(self.SM1, 256, [[2048, 128], [512, 2], [1, 256]]),
                            ALU.add)
                    self.tt("v", SM2[:, so:so + 256], self.SM1[:, 1024:1280],
                            self.SM1[:, 1280:1536], ALU.add)
                    D = SM2[:, so:so + 256]
                    seed = self.SM1[:, 0:256].bitcast(I16)
                    t = self.SM1[:, 256:512]
                    self.ts("v", seed, D.bitcast(I16), -1, 0x7EF4, ALU.mult, ALU.add)
                    r0 = seed.bitcast(BF16)
                    self.tt("v", t, D, r0, ALU.mult)
                    self.ts("v", t, t, -1.0, 2.0, ALU.mult, ALU.add)
                    self.tt("v", SM2[:, so:so + 256], r0, t, ALU.mult)
                    rn = self.g_v()
                    rd6 = bass.AP(self.SM1, 1536 + so, [[2048, 128], [0, 6], [1, 256]])
                    self.tt("v", bass.AP(self.LEB, eb, [[4096, 128], [256, 6], [1, 256]]),
                            bass.AP(self.LEB, eb, [[4096, 128], [256, 6], [1, 256]]),
                            rd6, ALU.mult)
                    vn = self.g_v()
                    rd2 = bass.AP(self.SM1, 1536 + so, [[2048, 128], [0, 2], [1, 256]])
                    self.tt("g", bass.AP(self.LEB, eb + 1536, [[4096, 128], [256, 2], [1, 256]]),
                            bass.AP(self.LEB, eb + 1536, [[4096, 128], [256, 2], [1, 256]]),
                            rd2, ALU.mult, w=(rn,))
                    gn = self.g_g()
                    vg[m] = (vn, gn)
            ctx_done = self.g_t()
            qts_free[p] = ctx_done
            for u in range(2):
                wl = [ctx_done, xsdma] if u == 0 else []
                if u == 0 and xo_gate is not None:
                    wl.append(xo_gate)
                self.stt("v", self.XO[:, u * 1024:(u + 1) * 1024],
                         self.PSB[:, u * 1024:(u + 1) * 1024], 1.0,
                         self.XS[:, u * 1024:(u + 1) * 1024],
                         ALU.mult, ALU.add, w=wl)
            xs_free = self.g_v()
            self.dma("y", bass.AP(H["x1"], nb * 256 * DM, [[1024, 128], [131072, 2], [1, 1024]]),
                     bass.AP(self.XO, 0, [[2048, 128], [1024, 2], [1, 1024]]),
                     w=(self.g_v(),))
            xo_gate = self.g_y()

    def fff(self, f, s_, p2, dest, tbase=0, w_tsl=()):
        H = self.h
        fo = {"q": 0, "k": 1, "v": 2}[f]
        for u in range(2):
            po = 1024 + u * 256
            for kc in range(8):
                self.mm(self.PSA[:, po:po + 15],
                        self.TSL[:, tbase + kc * 256 + u * 128:
                                 tbase + kc * 256 + u * 128 + 128],
                        self.NWT[:, kc * 45 + fo * 15:kc * 45 + fo * 15 + 15],
                        start=(kc == 0), stop=False,
                        w=(self.g_s(), self.g_v()) + tuple(w_tsl)
                        if (u == 0 and kc == 0) else ())
            self.mm(self.PSA[:, po:po + 15], self.ONESR[:, 0:128],
                    self.NBR[:, fo * 15:fo * 15 + 15], start=False, stop=True)
        tr = self.g_t()
        for u in range(2):
            self.vcopy(self.ROUTE[:, u * 15:u * 15 + 15],
                       self.PSA[:, 1024 + u * 256:1024 + u * 256 + 15],
                       w=(tr,) if u == 0 else ())
        R = self.ROUTE
        C = self.RSC

        def Sv(j):
            return bass.AP(R, j, [[32, 128], [15, 2], [1, 1]])

        def c2(a):
            return C[:, a:a + 2]

        b0, b1, b2, b3 = c2(32), c2(34), c2(36), c2(38)
        c0, c1, cc2 = c2(40), c2(42), c2(44)
        sel, d, t = c2(46), c2(48), c2(50)
        n = [c2(52 + 2 * k) for k in range(4)]
        leaf = c2(60)
        mq = [c2(0 + 2 * k) for k in range(8)]
        Z = self.RSC[:, 16:18]
        O1 = self.RSC[:, 18:20]
        self.tt("v", b0, Sv(0), Z, ALU.is_ge, w=(self.g_v(), self.g_g()))
        self.tt("v", d, Sv(2), Sv(1), ALU.subtract)
        self.tt("v", sel, d, b0, ALU.mult)
        self.tt("v", sel, sel, Sv(1), ALU.add)
        self.tt("v", b1, sel, Z, ALU.is_ge)
        self.tt("v", c0, O1, b0, ALU.subtract)
        self.tt("v", c1, O1, b1, ALU.subtract)
        self.tt("v", n[0], c0, c1, ALU.mult)
        self.tt("v", n[1], c0, b1, ALU.mult)
        self.tt("v", n[2], b0, c1, ALU.mult)
        self.tt("v", n[3], b0, b1, ALU.mult)
        self.tt("v", sel, Sv(3), n[0], ALU.mult)
        for k, jj in enumerate((4, 5, 6)):
            self.tt("v", t, Sv(jj), n[k + 1], ALU.mult)
            self.tt("v", sel, sel, t, ALU.add)
        self.tt("v", b2, sel, Z, ALU.is_ge)
        self.tt("v", cc2, O1, b2, ALU.subtract)
        for k in range(4):
            self.tt("v", mq[2 * k], n[k], cc2, ALU.mult)
            self.tt("v", mq[2 * k + 1], n[k], b2, ALU.mult)
        self.tt("v", sel, Sv(7), mq[0], ALU.mult)
        for k in range(1, 8):
            self.tt("v", t, Sv(7 + k), mq[k], ALU.mult)
            self.tt("v", sel, sel, t, ALU.add)
        self.tt("v", b3, sel, Z, ALU.is_ge)
        self.tt("v", leaf, b0, b0, ALU.add)
        self.tt("v", leaf, leaf, b1, ALU.add)
        self.tt("v", leaf, leaf, leaf, ALU.add)
        self.tt("v", leaf, leaf, b2, ALU.add)
        self.tt("v", leaf, leaf, leaf, ALU.add)
        self.tt("v", leaf, leaf, b3, ALU.add)
        for u in range(2):
            self.ts("v", self.SM1[:, 2016 + u * 16:2016 + (u + 1) * 16], self.IOTA[:, :],
                    leaf[:, u:u + 1], None, ALU.is_equal,
                    w=(self.g_t(),) if u == 0 else ())
        ohv = self.g_v()
        w1_half = None
        for oc in range(8):
            for kc in range(8):
                self.mm(self.PSB[:, oc * 256:(oc + 1) * 256],
                        self.WBa[:, kc * 1024 + oc * 128:kc * 1024 + oc * 128 + 128],
                        self.TSL[:, tbase + kc * 256:tbase + (kc + 1) * 256],
                        start=(kc == 0), stop=False,
                        w=(self.g_s(),) if (oc == 0 and kc == 0) else ())
            self.mm(self.PSB[:, oc * 256:(oc + 1) * 256],
                    self.ROWS[:, oc * 128:(oc + 1) * 128],
                    self.ONESR[:, 0:256], start=False, stop=True)
            if oc == 3:
                w1_half = self.g_t()
        w1_done = self.g_t()
        for u in range(2):
            self.mm(self.PSA[0:16, 1536 + u * 128:1536 + (u + 1) * 128],
                    self.SM1[:, 2016 + u * 16:2016 + (u + 1) * 16], self.IDN[:, :],
                    w=(ohv,) if u == 0 else ())
        ohmm = self.g_t()
        self.vcopy(self.OHT[0:16, 0:256], self.PSA[0:16, 1536:1792], w=(ohmm,))
        oht = self.g_v()
        for c in range(8):
            dst = c * 256 if c < 4 else 1024 + (c - 4) * 256
            self.mm(self.PSA[:, dst:dst + 256],
                    self.E16[:, c * 128:(c + 1) * 128], self.OHT[:, 0:256],
                    w=(oht, self.g_v()) if c == 0 else ())
        maskmm = self.g_t()
        relu_half = [None, None]
        for half in range(2):
            wre = [w1_half if half == 0 else ohmm]
            if half == 0:
                wre.append(self.g_v())
            self.act(self.SM1[:, half * 1024:(half + 1) * 1024],
                     self.PSB[:, half * 1024:(half + 1) * 1024], AF.Relu, w=wre)
            relu_half[half] = self.g_s()
        hm = [None, None]
        for half in range(2):
            self.tt("v", self.SM1[:, half * 1024:(half + 1) * 1024],
                    self.SM1[:, half * 1024:(half + 1) * 1024],
                    self.PSA[:, half * 1024:(half + 1) * 1024], ALU.mult,
                    w=(maskmm, relu_half[half]))
            hm[half] = self.g_v()
        if dest in ("qT", "kT"):
            for kc in range(8):
                for oc in range(8):
                    wlist = ()
                    if oc == 0 and kc == 0:
                        wlist = (hm[0],)
                    elif oc == 0 and kc == 4:
                        wlist = (hm[1],)
                    self.mm(self.PSB[:, oc * 256:(oc + 1) * 256],
                            self.WBb[:, kc * 1024 + oc * 128:kc * 1024 + oc * 128 + 128],
                            self.SM1[:, kc * 256:(kc + 1) * 256],
                            start=(kc == 0), stop=False, w=wlist)
            for oc in range(8):
                self.mm(self.PSB[:, oc * 256:(oc + 1) * 256],
                        self.B2C[:, oc * 128:(oc + 1) * 128], self.OHT[:, 0:256],
                        start=False, stop=True)
            if dest == "qT":
                qstage = self.XO[:, p2 * 1024:(p2 + 1) * 1024].bitcast(BF16)
                self.act(qstage, self.PSB[:, :], AF.Copy, w=(self.g_t(), self.g_y()))
                qev = self.g_s()
                self.dma("y", bass.AP(H["qTs"], s_ * 262144, [[2048, 128], [1, 2048]]),
                         qstage,
                         w=(qev,))
            else:
                self.act(bass.AP(self.KT, s_ * 256, [[32768, 128], [4096, 8], [1, 256]]),
                         self.PSB[:, :], AF.Copy, w=(self.g_t(),))
        else:
            for u in range(2):
                po = u * 1024
                for kc in range(8):
                    for half in range(2):
                        wlist = ()
                        if u == 0 and half == 0 and kc == 0:
                            wlist = (hm[0],)
                        elif u == 0 and half == 0 and kc == 4:
                            wlist = (hm[1],)
                        self.mm(self.PSB[:, po + half * 512:po + (half + 1) * 512],
                                self.SM1[:, kc * 256 + u * 128:kc * 256 + u * 128 + 128],
                                self.WBb[:, kc * 1024 + half * 512:
                                         kc * 1024 + (half + 1) * 512],
                                start=(kc == 0), stop=False, w=wlist)
                for half in range(2):
                    self.mm(self.PSB[:, po + half * 512:po + (half + 1) * 512],
                            self.OHT[:, u * 128:(u + 1) * 128],
                            self.B2C[:, half * 512:(half + 1) * 512],
                            start=False, stop=True)
                j = 2 * s_ + u
                self.act(self.VT[:, j * 1024:(j + 1) * 1024],
                         self.PSB[:, po:po + 1024], AF.Copy, w=(self.g_t(),))
        return w1_done

    def _load_fff_weights(self, f):
        H = self.h
        self.memset("v", self.RSC[:, 16:18], 0.0, w=(self.g_t(),))
        self.memset("v", self.RSC[:, 18:20], 1.0)
        self.dma("g", bass.AP(self.WBa, 0, [[8192, 128], [1024, 8], [1, 1024]]),
                 bass.AP(H[f"w1_{f}"], 0, [[1024, 128], [131072, 8], [1, 1024]]))
        self.dma("g", bass.AP(self.WBb, 0, [[8192, 128], [1024, 8], [1, 1024]]),
                 bass.AP(H[f"w2_{f}"], 0, [[1024, 128], [131072, 8], [1, 1024]]))
        self.dma("g", self.ROWS[0:1, :], H[f"b1_{f}"][:, :])
        self.dma("g", self.B2C[0:16, :], H[f"b2_{f}"][:, :])
        return self.g_gd()

    def pC1(self):
        H = self.h
        self.memset("v", self.OHT[:, :], 0.0)
        wg = self._load_fff_weights("q")
        self.dma("g", bass.AP(self.NWT, 0, [[360, 128], [45, 8], [1, 45]]),
                 bass.AP(H["node_wt"], 0, [[45, 128], [45 * 128, 8], [1, 45]]))
        self.dma("g", self.NBR[0:1, 0:45], H["node_b"][:, :])
        lnb = self.LEB
        sc_f32 = self.LEB[:, 2048:4096].bitcast(F32)
        xs_free = [None, None]
        lnb_free = ((self.st, 0),)
        for s_ in range(NSLAB):
            p2 = s_ % 2
            xdma = [None, None]
            for u in range(2):
                w = [xs_free[u]] if xs_free[u] else []
                if s_ == 0 and u == 0:
                    w.append(wg)
                self.dma("s", bass.AP(self.XS, u * 1024, [[2048, 128], [1, 1024]]),
                         bass.AP(H["x1"], s_ * SLAB * DM + u * 131072,
                                 [[1024, 128], [1, 1024]]),
                         w=w)
                xdma[u] = (self.dq2, self.dq2.n)
            for u in range(2):
                self.ln_tile(u * 1024, u, lnb[:, u * 1024:(u + 1) * 1024], sc_f32,
                             w_x=(xdma[u],), w_apply=lnb_free)
                xs_free[u] = self.g_s()
            self.transpose_slab(lnb, 0, w_first=(self.g_v(),))
            lnb_free = (self.g_t(),)
            tev = self.g_s()
            self.dma("y", bass.AP(H["xlnT"], s_ * 262144, [[2048, 128], [1, 2048]]),
                     bass.AP(self.TSL, 0, [[4096, 128], [1, 2048]]),
                     w=(tev,))
            self.fff("q", s_, p2, "qT")

    def pC1b(self):
        H = self.h
        wg = self._load_fff_weights("k")
        tsl_free = {}
        for s_ in range(NSLAB):
            pr = s_ % 2
            w = [tsl_free[pr]] if pr in tsl_free else []
            if s_ == 0:
                w.append(wg)
            self.dma("y", bass.AP(self.TSL, pr * 2048, [[4096, 128], [1, 2048]]),
                     bass.AP(H["xlnT"], s_ * 262144, [[2048, 128], [1, 2048]]), w=w)
            tdma = self.g_y()
            tsl_free[pr] = self.fff("k", s_, pr, "kT", tbase=pr * 2048,
                                    w_tsl=(tdma,))

    def pC2(self):
        H = self.h
        wg = self._load_fff_weights("v")
        tsl_free = {}
        for s_ in range(NSLAB):
            pr = s_ % 2
            w = [tsl_free[pr]] if pr in tsl_free else []
            if s_ == 0:
                w.append(wg)
            self.dma("y", bass.AP(self.TSL, pr * 2048, [[4096, 128], [1, 2048]]),
                     bass.AP(H["xlnT"], s_ * 262144, [[2048, 128], [1, 2048]]), w=w)
            tdma = self.g_y()
            tsl_free[pr] = self.fff("v", s_, pr, "v", tbase=pr * 2048,
                                    w_tsl=(tdma,))

    def pD(self):
        H = self.h
        QSQ = self.SM1
        for piece in range(8):
            for c in range(8):
                self.tt("v", QSQ[:, 0:512],
                        self.KT[:, c * S + piece * 512:c * S + (piece + 1) * 512],
                        self.KT[:, c * S + piece * 512:c * S + (piece + 1) * 512],
                        ALU.mult, w=(self.g_t(),))
                self.mm(self.PSA[0:1, 0:512], self.ONESC[:, 0:1], QSQ[:, 0:512],
                        start=(c == 0), stop=(c == 7), w=(self.g_v(),))
            self.act(self.SROW[0:1, SR_STAGE:SR_STAGE + 512], self.PSA[0:1, 0:512],
                     AF.Sqrt, w=(self.g_t(),))
            if piece == 0:
                self.vred(self.SROW[0:1, SR_KMAX:SR_KMAX + 1],
                          self.SROW[0:1, SR_STAGE:SR_STAGE + 512], ALU.max,
                          w=(self.g_s(),))
            else:
                self.vred(self.SROW[0:1, SR_KMAX + 1:SR_KMAX + 2],
                          self.SROW[0:1, SR_STAGE:SR_STAGE + 512], ALU.max,
                          w=(self.g_s(),))
                self.tt("v", self.SROW[0:1, SR_KMAX:SR_KMAX + 1],
                        self.SROW[0:1, SR_KMAX:SR_KMAX + 1],
                        self.SROW[0:1, SR_KMAX + 1:SR_KMAX + 2], ALU.max)
        self.ts("v", self.SROW[0:1, SR_KNEG:SR_KNEG + 1],
                self.SROW[0:1, SR_KMAX:SR_KMAX + 1], -1.0, None, ALU.mult)
        kstar = self.g_v()
        exp_free = [None, None]
        leb_free = [None, None]
        qts_free = {}
        xs_free = None
        xo_gate = None
        for nb in range(16):
            p = nb % 2
            self.dma("y", bass.AP(self.TSL, p * 2048, [[4096, 128], [1, 2048]]),
                     bass.AP(H["qTs"], nb * 262144, [[2048, 128], [1, 2048]]),
                     w=[qts_free[p]] if p in qts_free else [])
            self.dma("y", bass.AP(self.XS, 0, [[2048, 128], [1024, 2], [1, 1024]]),
                     bass.AP(H["x1"], nb * 256 * DM, [[1024, 128], [131072, 2], [1, 1024]]),
                     w=[xs_free] if xs_free else [])
            qdma = self.g_y()
            self.tt("v", QSQ[:, 0:2048], self.TSL[:, p * 2048:p * 2048 + 2048],
                    self.TSL[:, p * 2048:p * 2048 + 2048], ALU.mult, w=(qdma, kstar))
            qsqv = self.g_v()
            for c in range(8):
                self.mm(self.PSA[0:1, 1280:1536], self.ONESC[:, 0:1],
                        QSQ[:, c * 256:(c + 1) * 256], start=(c == 0), stop=(c == 7),
                        w=(qsqv, self.g_s()) if c == 0 else ())
            self.act(self.SROW[0:1, SR_STAGE:SR_STAGE + 256], self.PSA[0:1, 1280:1536],
                     AF.Sqrt, w=(self.g_t(),))
            self.ts("v", self.ROWS[0:1, 0:256], self.SROW[0:1, SR_STAGE:SR_STAGE + 256],
                    self.SROW[0:1, SR_KNEG:SR_KNEG + 1], None, ALU.mult, w=(self.g_s(),))
            negb = self.g_v()
            expd_p = [None, None]
            for pair in range(17):
                if pair < 16:
                    for mi in range(2):
                        m = pair * 2 + mi
                        pcol = (pair % 2) * 512 + mi * 256
                        wlist = []
                        if mi == 0 and exp_free[pair % 2] is not None:
                            wlist.append(exp_free[pair % 2])
                        if m == 0:
                            wlist.append(negb)
                        for kc in range(8):
                            self.mm(self.PSA[:, pcol:pcol + 256],
                                    self.KT[:, kc * S + m * 128:kc * S + (m + 1) * 128],
                                    self.TSL[:, p * 2048 + kc * 256:
                                             p * 2048 + (kc + 1) * 256],
                                    start=(kc == 0), stop=False,
                                    w=wlist if kc == 0 else ())
                        self.mm(self.PSA[:, pcol:pcol + 256], self.ONESR[:, 0:128],
                                self.ROWS[:, 0:256], start=False, stop=True)
                    sc_done = self.g_t()
                    wexp = [sc_done]
                    if leb_free[pair % 2] is not None:
                        wexp.append(leb_free[pair % 2])
                    self.act(self.LEB[:, (pair % 2) * 512:(pair % 2) * 512 + 512],
                             self.PSA[:, (pair % 2) * 512:(pair % 2) * 512 + 512],
                             AF.Exp, scale=1.0 / 32.0, w=wexp)
                    expd_p[pair % 2] = self.g_s()
                    exp_free[pair % 2] = expd_p[pair % 2]
                if pair >= 1:
                    pp = pair - 1
                    expd = expd_p[pp % 2]
                    for mi in range(2):
                        mm_ = pp * 2 + mi
                        lb = (pp % 2) * 512 + mi * 256
                        for ns in range(2):
                            wl = []
                            if mi == 0 and ns == 0:
                                wl.append(expd)
                                if mm_ == 0 and xo_gate is not None:
                                    wl.append(self.g_v())
                            for half in range(2):
                                self.mm(self.PSB[:, ns * 1024 + half * 512:
                                                 ns * 1024 + (half + 1) * 512],
                                        self.LEB[:, lb + ns * 128:lb + ns * 128 + 128],
                                        self.VT[:, mm_ * 1024 + half * 512:
                                                mm_ * 1024 + (half + 1) * 512],
                                        start=(mm_ == 0), stop=(mm_ == 31),
                                        w=wl if half == 0 else ())
                        self.mm(self.PSA[0:1, 1024:1280], self.ONESC[:, 0:1],
                                self.LEB[:, lb:lb + 256],
                                start=(mm_ == 0), stop=(mm_ == 31))
                    leb_free[pp % 2] = self.g_t()
            ctx_done = self.g_t()
            qts_free[p] = ctx_done
            self.act(self.ROWS[0:1, 256:512], self.PSA[0:1, 1024:1280], AF.Copy,
                     w=(ctx_done,))
            drow = self.g_s()
            for u in range(2):
                self.mm(self.PSA[0:128, 1536 + u:1537 + u],
                        self.ROWS[:, 256 + u * 128:256 + (u + 1) * 128],
                        self.ONESR[:, 0:1], w=(drow,) if u == 0 else ())
            self.vrecip(self.SCL[:, 2:4], self.PSA[:, 1536:1538], w=(self.g_t(),))
            for u in range(2):
                wl = []
                if u == 0 and xo_gate is not None:
                    wl.append(xo_gate)
                self.stt("v", self.XO[:, u * 1024:(u + 1) * 1024],
                         self.PSB[:, u * 1024:(u + 1) * 1024], self.SCL[:, 2 + u:3 + u],
                         self.XS[:, u * 1024:(u + 1) * 1024],
                         ALU.mult, ALU.add, w=wl)
            xs_free = self.g_v()
            self.dma("y", bass.AP(H["x2"], nb * 256 * DM, [[1024, 128], [131072, 2], [1, 1024]]),
                     bass.AP(self.XO, 0, [[2048, 128], [1024, 2], [1, 1024]]),
                     w=(self.g_v(),))
            xo_gate = self.g_y()

    def pE(self):
        H = self.h
        KT = self.KT
        self.memset("v", KT[:, KE_CQ:KE_CQ + 4096], 0.0, w=(self.g_t(),))
        self.memset("v", KT[:, KE_KV:KE_KV + S], 0.0)
        self.memset("v", KT[:, KE_WQC:KE_WQC + 512], 0.0)
        self.memset("v", KT[:, KE_BQC:KE_BQC + 1024], 0.0)
        self.memset("v", KT[:, KE_BFIN:KE_BFIN + 1024], 0.0)
        self.memset("v", self.VT[:, 2048:6144], 0.0)
        self.dma("g", bass.AP(self.WBa, 0, [[8192, 128], [1024, 8], [1, 1024]]),
                 bass.AP(H["w_o"], 0, [[1024, 128], [131072, 8], [1, 1024]]))
        for nm, col in (("w_uqT", KE_UQT), ("w_ukT", KE_UKT), ("w_uvT", KE_UVT)):
            self.dma("g", bass.AP(KT, col, [[32768, 128], [32, 8], [1, 32]]),
                     bass.AP(H[nm], 0, [[32, 128], [4096, 8], [1, 32]]))
        self.dma("g", KT[0:32, KE_WQR:KE_WQR + 256], H["w_qr"][:, :])
        self.dma("g", bass.AP(self.STAT, 0, [[16, 128], [1, 8]]),
                 bass.AP(H["b_uq"], 0, [[1, 128], [128, 8]]),
                 allow_slow_non_contiguous=True)
        self.dma("g", bass.AP(self.STAT, 8, [[16, 128], [1, 8]]),
                 bass.AP(H["b_uv"], 0, [[1, 128], [128, 8]]),
                 allow_slow_non_contiguous=True)
        self.dma("g", self.SROW[0:1, SR_STAGE:SR_STAGE + 256], H["b_qr"][:, :])
        self.dma("g", self.SM1[0:1, 1024:2048], H["b_o"][:, :])
        gd = self.g_gd()
        self.vcopy(self.SM1[:, 0:16], self.STAT[:, 0:16], w=(gd,))
        self.vcopy(bass.AP(KT, KE_BQC + 32, [[32768, 1], [64, 8], [1, 32]]),
                   bass.AP(self.SROW, SR_STAGE, [[576, 1], [32, 8], [1, 32]]))
        for h8 in range(8):
            uq = KT[:, KE_UQT + h8 * 32:KE_UQT + (h8 + 1) * 32]
            uk = KT[:, KE_UKT + h8 * 32:KE_UKT + (h8 + 1) * 32]
            self.mm(self.PSA[0:32, h8 * 64:h8 * 64 + 32], uq, uk,
                    w=(gd,) if h8 == 0 else ())
            self.mm(self.PSA[0:1, h8 * 64 + 32:h8 * 64 + 64],
                    self.SM1[:, h8:h8 + 1], uk,
                    w=(self.g_v(),) if h8 == 0 else ())
        tg = self.g_t()
        for h8 in range(8):
            self.vcopy(KT[0:32, KE_WQC + h8 * 64:KE_WQC + h8 * 64 + 32],
                       self.PSA[0:32, h8 * 64:h8 * 64 + 32], w=(tg,) if h8 == 0 else ())
            self.vcopy(KT[0:1, KE_BQC + h8 * 64:KE_BQC + h8 * 64 + 32],
                       self.PSA[0:1, h8 * 64 + 32:h8 * 64 + 64])
            self.vcopy(KT[0:32, KE_WQC + h8 * 64 + 32:KE_WQC + (h8 + 1) * 64],
                       KT[0:32, KE_WQR + h8 * 32:KE_WQR + (h8 + 1) * 32])
        vg = self.g_v()
        for h8 in range(8):
            for half in range(2):
                self.mm(self.PSB[0:32, half * 512:(half + 1) * 512],
                        KT[:, KE_UVT + h8 * 32:KE_UVT + (h8 + 1) * 32],
                        self.WBa[:, h8 * 1024 + half * 512:h8 * 1024 + (half + 1) * 512],
                        w=(vg,) if h8 == 0 and half == 0 else
                        (self.g_s(),) if half == 0 else ())
            self.act(KT[(h8 % 4) * 32:(h8 % 4) * 32 + 32,
                        KE_UVO + (h8 // 4) * 1024:KE_UVO + (h8 // 4 + 1) * 1024],
                     self.PSB[0:32, 0:1024], AF.Copy, w=(self.g_t(),))
        for kc in range(8):
            for half in range(2):
                self.mm(self.PSA[0:1, 1024 + half * 512:1024 + (half + 1) * 512],
                        self.SM1[:, 8 + kc:9 + kc],
                        self.WBa[:, kc * 1024 + half * 512:kc * 1024 + (half + 1) * 512],
                        start=(kc == 0), stop=False,
                        w=(self.g_v(), gd) if kc == 0 and half == 0 else ())
        for half in range(2):
            self.mm(self.PSA[0:1, 1024 + half * 512:1024 + (half + 1) * 512],
                    self.ONESR[0:1, 0:1],
                    self.SM1[0:1, 1024 + half * 512:1024 + (half + 1) * 512],
                    start=False, stop=True)
        self.vcopy(KT[0:1, KE_BFIN:KE_BFIN + 1024], self.PSA[0:1, 1024:2048],
                   w=(self.g_t(),))
        self.dma("g", bass.AP(self.WBb, 0, [[8192, 128], [32, 8], [1, 32]]),
                 bass.AP(H["w_dq"], 0, [[32, 128], [4096, 8], [1, 32]]))
        self.dma("g", bass.AP(self.WBb, 256, [[8192, 128], [64, 8], [1, 64]]),
                 bass.AP(H["w_kv"], 0, [[64, 128], [8192, 8], [1, 64]]))
        self.dma("g", self.ROWS[0:1, 0:32], H["b_dq"][:, :])
        self.dma("g", self.ROWS[0:1, 32:96], H["b_kv"][:, :])
        wg = self.g_gd()
        self.memset("v", KT[64:65, KE_KV:KE_KV + S], 1.0, w=(self.g_t(),))
        EIDF = KT[:, 14848:15104].bitcast(F32)
        self.vcopy(EIDF, self.IDN[:, :])
        onk = self.g_v()
        lnb = self.LEB
        sc_f32 = self.LEB[:, 2048:4096].bitcast(F32)
        xs_free = None
        lnb_free = ((self.st, 0),)
        for s_ in range(NSLAB):
            w = [xs_free] if xs_free else []
            if s_ == 0:
                w.append(wg)
            self.dma("y", bass.AP(self.XS, 0, [[2048, 128], [1024, 2], [1, 1024]]),
                     bass.AP(H["x2"], s_ * SLAB * DM, [[1024, 128], [131072, 2], [1, 1024]]),
                     w=w)
            xdma = self.g_y()
            for u in range(2):
                self.ln_tile(u * 1024, u, lnb[:, u * 1024:(u + 1) * 1024], sc_f32,
                             w_x=(xdma,) if u == 0 else (), w_apply=lnb_free)
            xs_free = self.g_s()
            self.transpose_slab(lnb, 0, w_first=(self.g_v(),))
            lnb_free = (self.g_t(),)
            tev = self.g_s()
            for kc in range(8):
                self.mm(self.PSA[0:32, 0:256], self.WBb[:, kc * 32:(kc + 1) * 32],
                        self.TSL[:, kc * 256:(kc + 1) * 256],
                        start=(kc == 0), stop=False,
                        w=(tev, self.g_v()) if kc == 0 else ())
            self.mm(self.PSA[0:32, 0:256], self.ROWS[:, 0:32], self.ONESR[:, 0:256],
                    start=False, stop=True)
            for kc in range(8):
                self.mm(self.PSA[0:64, 256:512],
                        self.WBb[:, 256 + kc * 64:256 + (kc + 1) * 64],
                        self.TSL[:, kc * 256:(kc + 1) * 256],
                        start=(kc == 0), stop=False)
            self.mm(self.PSA[0:64, 256:512], self.ROWS[:, 32:96], self.ONESR[:, 0:256],
                    start=False, stop=True)
            self.vcopy(KT[0:32, KE_CQ + s_ * 256:KE_CQ + (s_ + 1) * 256],
                       self.PSA[0:32, 0:256], w=(self.g_t(),))
            self.vcopy(KT[0:64, KE_KV + s_ * 256:KE_KV + (s_ + 1) * 256],
                       self.PSA[0:64, 256:512])
        prj = self.g_v()
        self.memset("v", bass.AP(self.VT, 32, [[32768, 128], [33, 32], [1, 1]]), 1.0,
                    w=(self.g_t(),))
        for mq4 in range(8):
            for k4 in range(4):
                m = mq4 * 4 + k4
                w = ()
                if m == 0:
                    w = (prj, onk, self.g_v())
                elif k4 == 0:
                    w = (self.g_v(),)
                self.mm(self.PSB[:, k4 * 128:k4 * 128 + 65],
                        KT[0:65, KE_KV + m * 128:KE_KV + (m + 1) * 128],
                        self.IDN[0:65, 0:65], w=w)
            self.vcopy(bass.AP(self.VT, mq4 * 4 * 33, [[32768, 128], [33, 4], [1, 32]]),
                       bass.AP(self.PSB, 0, [[2048, 128], [128, 4], [1, 32]]),
                       w=(self.g_t(),))
        QSQ = self.SM1
        for piece in range(8):
            self.tt("v", QSQ[0:64, 0:512],
                    KT[0:64, KE_KV + piece * 512:KE_KV + (piece + 1) * 512],
                    KT[0:64, KE_KV + piece * 512:KE_KV + (piece + 1) * 512], ALU.mult)
            self.mm(self.PSA[0:1, 0:512], self.ONESC[0:64, 0:1], QSQ[0:64, 0:512],
                    w=(self.g_v(),))
            self.act(self.SROW[0:1, SR_STAGE:SR_STAGE + 512], self.PSA[0:1, 0:512],
                     AF.Sqrt, w=(self.g_t(),))
            if piece == 0:
                self.vred(self.SROW[0:1, SR_KMAX:SR_KMAX + 1],
                          self.SROW[0:1, SR_STAGE:SR_STAGE + 512], ALU.max,
                          w=(self.g_s(),))
            else:
                self.vred(self.SROW[0:1, SR_KMAX + 1:SR_KMAX + 2],
                          self.SROW[0:1, SR_STAGE:SR_STAGE + 512], ALU.max,
                          w=(self.g_s(),))
                self.tt("v", self.SROW[0:1, SR_KMAX:SR_KMAX + 1],
                        self.SROW[0:1, SR_KMAX:SR_KMAX + 1],
                        self.SROW[0:1, SR_KMAX + 1:SR_KMAX + 2], ALU.max)
        self.ts("v", self.SROW[0:1, SR_KNEG:SR_KNEG + 1],
                self.SROW[0:1, SR_KMAX:SR_KMAX + 1], -1.0, None, ALU.mult)
        kstar = self.g_v()
        scale = 1.0 / math.sqrt(160.0)
        exp_free = [None, None]
        leb_free = [None, None, None, None]
        expd_all = [None, None, None, None]
        norm_done = [None, None]
        rc_last = None
        xs_gate = None
        xo_gate = None
        xs2_gate = None
        QA = self.VT
        BCV = KT[0:32, 15360:16384].bitcast(F32)
        qa_free = [None, None]
        XIN = [KT[:, 16384 + k * 4096:16384 + k * 4096 + 2048].bitcast(F32)
               for k in range(4)]
        xin_free = [None, None, None, None]
        for nb in range(8):
            xin_dma = [None] * 4
            for k in range(4):
                wdma_l = [xin_free[k]] if xin_free[k] else []
                self.dma("s", XIN[k],
                         bass.AP(H["x2"], (nb * 512 + k * 128) * DM,
                                 [[1024, 128], [1, 1024]]),
                         w=wdma_l)
                xin_dma[k] = (self.dq2, self.dq2.n)
            SQR = self.TSL[0:1, 1024:3072].bitcast(F32)
            for hp in range(4):
                h8 = 2 * hp
                pq = 1024 + (hp % 2) * 512
                wq = ()
                if hp == 0:
                    wq = (kstar,) + ((exp_free[1],) if exp_free[1] else ())
                    if rc_last is not None:
                        wq = wq + (rc_last,)
                elif hp == 1:
                    wq = (kstar,)
                elif qa_free[hp % 2] is not None:
                    wq = (qa_free[hp % 2],)
                self.mm(self.PSA[0:128, pq:pq + 512],
                        KT[:, KE_WQC + h8 * 64:KE_WQC + (h8 + 2) * 64],
                        KT[:, KE_CQ + nb * 512:KE_CQ + (nb + 1) * 512],
                        start=True, stop=False, w=wq)
                self.mm(self.PSA[0:128, pq:pq + 512],
                        KT[:, KE_BQC + h8 * 64:KE_BQC + (h8 + 2) * 64],
                        self.ONESR[:, 0:512], start=False, stop=True)
                tq = self.g_t()
                self.act(QSQ[0:128, 0:512], self.PSA[0:128, pq:pq + 512], AF.Square,
                         w=(tq,))
                for g2 in range(2):
                    self.act(QA[0:64, 2048 + (h8 + g2) * 512:
                                2048 + (h8 + g2 + 1) * 512],
                             self.PSA[g2 * 64:(g2 + 1) * 64, pq:pq + 512], AF.Copy)
                sq = self.g_s()
                qa_free[hp % 2] = sq
                for g2 in range(2):
                    self.mm(self.PSB[0:1, g2 * 512:(g2 + 1) * 512],
                            self.ONESC[g2 * 64:(g2 + 1) * 64, 0:1],
                            QSQ[g2 * 64:(g2 + 1) * 64, 0:512],
                            w=(sq, rc_last) if (hp == 0 and g2 == 0 and
                                                rc_last is not None)
                            else (sq,) if g2 == 0 else ())
                self.act(SQR, self.PSB[0:1, 0:1024], AF.Sqrt, w=(self.g_t(),))
                self.ts("v", QA[64:65, 2048 + h8 * 512:2048 + (h8 + 2) * 512],
                        SQR, self.SROW[0:1, SR_KNEG:SR_KNEG + 1], None, ALU.mult,
                        w=(self.g_s(),))
            qaug = self.g_v()

            def dance_stages(hprev, ctxh_gate):
                hv1 = hprev % 2
                st = {}

                def st0():
                    self.act(self.SROW[0:1, SR_STAGE:SR_STAGE + 512],
                             self.PSB[32:33, hv1 * 512:hv1 * 512 + 512], AF.Copy,
                             w=(ctxh_gate,))
                    st["drow"] = self.g_s()

                def st1():
                    for u in range(4):
                        wl = ()
                        if u == 0:
                            wl = (st["drow"],)
                            if rc_last is not None:
                                wl = wl + (rc_last,)
                        self.mm(self.PSB[0:128, 1540 + u:1541 + u],
                                self.SROW[0:1, SR_STAGE + u * 128:
                                          SR_STAGE + (u + 1) * 128],
                                self.SROW[0:1, SR_ONES:SR_ONES + 1], w=wl)
                    self.vrecip(self.FQ4[:, 0:4], self.PSB[:, 1540:1544],
                                w=(self.g_t(),))
                    st["rc"] = self.g_v()

                def st2():
                    for u in range(4):
                        self.mm(self.PSB[0:1, 1024 + u * 128:1024 + (u + 1) * 128],
                                self.FQ4[:, u:u + 1], EIDF,
                                w=(st["rc"],) if u == 0 else ())
                    self.act(self.SROW[0:1, SR_STAGE:SR_STAGE + 512],
                             self.PSB[0:1, 1024:1536], AF.Copy, w=(self.g_t(),))
                    st["rrow"] = self.g_s()

                def st3():
                    self.mm(self.PSB[0:32, 1024:1536],
                            self.SROW[0:1, SR_ONES:SR_ONES + 32],
                            self.SROW[0:1, SR_STAGE:SR_STAGE + 512], w=(st["rrow"],))
                    self.vcopy(BCV, self.PSB[0:32, 1024:1536], w=(self.g_t(),))
                    self.tt("v", self.TSL[(hprev % 4) * 32:(hprev % 4) * 32 + 32,
                                          (hprev // 4) * 512:(hprev // 4) * 512 + 512],
                            self.PSB[0:32, hv1 * 512:hv1 * 512 + 512], BCV, ALU.mult)
                    norm_done[hv1] = self.g_v()
                return [st0, st1, st2, st3], st

            pend = None
            ctxh = None
            for h8 in range(8):
                hv = h8 % 2
                if ctxh is not None:
                    pend = dance_stages(h8 - 1, ctxh)
                for pair in range(17):
                    if pair < 16:
                        pc = (pair % 2) * 1024
                        sl = pair % 4
                        wl = []
                        if exp_free[pair % 2] is not None:
                            wl.append(exp_free[pair % 2])
                        if h8 == 0 and pair <= 1:
                            wl.append(qaug)
                        for mi in range(2):
                            m = pair * 2 + mi
                            self.mm(self.PSA[:, pc + mi * 512:pc + (mi + 1) * 512],
                                    KT[:, KE_KV + m * 128:KE_KV + (m + 1) * 128],
                                    QA[:, 2048 + h8 * 512:2048 + (h8 + 1) * 512],
                                    w=wl if mi == 0 else ())
                        sc = self.g_t()
                        wexp = [sc]
                        if leb_free[sl] is not None:
                            wexp.append(leb_free[sl])
                        self.act(self.LEB[:, sl * 1024:(sl + 1) * 1024],
                                 self.PSA[:, pc:pc + 1024], AF.Exp, scale=scale, w=wexp)
                        expd_all[sl] = self.g_s()
                        exp_free[pair % 2] = expd_all[sl]
                    if pair >= 1:
                        pp = pair - 1
                        psl = pp % 4
                        expd = expd_all[psl]
                        for mi in range(2):
                            m = pp * 2 + mi
                            wl = []
                            if mi == 0:
                                wl.append(expd)
                                if pp == 0 and norm_done[hv] is not None:
                                    wl.append(norm_done[hv])
                            self.mm(self.PSB[0:33, hv * 512:hv * 512 + 512],
                                    bass.AP(self.VT, m * 33, [[32768, 128], [1, 33]]),
                                    self.LEB[:, psl * 1024 + mi * 512:
                                             psl * 1024 + (mi + 1) * 512],
                                    start=(m == 0), stop=(m == 31), w=wl)
                        leb_free[psl] = self.g_t()
                    if pend is not None and pair < 4:
                        pend[0][pair]()
                        if pair == 1:
                            rc_last = pend[1]["rc"]
                ctxh = self.g_t()
            tail = dance_stages(7, ctxh)
            for k_, f_ in enumerate(tail[0]):
                f_()
                if k_ == 1:
                    rc_last = tail[1]["rc"]
            ctxn = self.g_v()
            for ns in range(4):
                obuf = self.XO if ns < 2 else self.XS
                wl = [ctxn, xin_dma[ns]] if ns == 0 else [self.g_v(), xin_dma[ns]]
                if ns == 0 and xo_gate is not None:
                    wl.append(xo_gate)
                if ns == 2 and xs2_gate is not None:
                    wl.append(xs2_gate)
                for j in range(2):
                    for half in range(2):
                        self.mm(self.PSB[:, 1024 + half * 512:1024 + (half + 1) * 512],
                                self.TSL[:, j * 512 + ns * 128:j * 512 + ns * 128 + 128],
                                KT[:, KE_UVO + j * 1024 + half * 512:
                                   KE_UVO + j * 1024 + (half + 1) * 512],
                                start=(j == 0), stop=False,
                                w=wl if (j == 0 and half == 0) else ())
                for half in range(2):
                    self.mm(self.PSB[:, 1024 + half * 512:1024 + (half + 1) * 512],
                            self.ONESR[:, 0:128],
                            KT[:, KE_BFIN + half * 512:KE_BFIN + (half + 1) * 512],
                            start=False, stop=True)
                self.stt("v", obuf[:, (ns % 2) * 1024:(ns % 2 + 1) * 1024],
                         self.PSB[:, 1024:2048], 1.0, XIN[ns],
                         ALU.mult, ALU.add, w=(self.g_t(),))
                xin_free[ns] = self.g_v()
                if ns % 2 == 1:
                    self.dma("y", bass.AP(H["x3"], (nb * 512 + (ns - 1) * 128) * DM,
                                          [[1024, 128], [131072, 2], [1, 1024]]),
                             bass.AP(obuf, 0, [[2048, 128], [1024, 2], [1, 1024]]),
                             w=(self.g_v(),))
                    if ns == 1:
                        xo_gate = self.g_y()
                    else:
                        xs2_gate = self.g_y()

    def quant_tile(self, xin, xout, dqp, dqn, perblock=False, w0=(), w_out=()):
        e = "v"
        F1 = self.SM1[:, 0:2048].bitcast(F32)
        F2 = self.LEB[:, 0:2048].bitcast(F32)
        F3 = self.KT[:, 0:2048].bitcast(F32)
        F4 = self.FQ4
        AM = F4[:, 0:32]
        EB = F4[:, 32:64]
        RS = F4[:, 64:96]
        SC = F4[:, 96:128]
        x3d = xin.rearrange("p (a b) -> p a b", b=32)
        self.vred(AM, x3d, ALU.max, apply_absolute_value=True, w=w0)
        self.ts(e, EB.bitcast(I32), AM.bitcast(I32), 23, None, ALU.logical_shift_right)
        self.ts(e, EB.bitcast(I32), EB.bitcast(I32), 1, None, ALU.max)
        self.ts(e, RS.bitcast(I32), EB.bitcast(I32), -1, 256, ALU.mult, ALU.add)
        self.ts(e, RS.bitcast(I32), RS.bitcast(I32), 23, None, ALU.logical_shift_left)
        self.ts(e, SC.bitcast(I32), EB.bitcast(I32), -2, None, ALU.add)
        self.ts(e, SC.bitcast(I32), SC.bitcast(I32), 23, None, ALU.logical_shift_left)
        if perblock:
            self.tt(e, RS, RS, dqn, ALU.mult)
            self.tt(e, SC, SC, dqp, ALU.mult)
        else:
            self.ts(e, RS, RS, dqn, None, ALU.mult)
            self.ts(e, SC, SC, dqp, None, ALU.mult)
        rs_b = bass.AP(self.FQ4, 64, [[128, 128], [1, 32], [0, 32]])
        sc_b = bass.AP(self.FQ4, 96, [[128, 128], [1, 32], [0, 32]])
        self.tt(e, F1.rearrange("p (a b) -> p a b", b=32), x3d, rs_b, ALU.mult)
        self.stt(e, F2, F1, -1.0, F1, ALU.mult, ALU.max)
        self.ts(e, F2, F2, 6.0, None, ALU.min)
        self.ts(e, F3.bitcast(I32), F2.bitcast(I32), 23, None, ALU.logical_shift_right)
        self.ts(e, F3.bitcast(I32), F3.bitcast(I32), 127, 129, ALU.max, ALU.min)
        self.ts(e, F1.bitcast(I32), F3.bitcast(I32), -1, 255, ALU.mult, ALU.add)
        self.ts(e, F1.bitcast(I32), F1.bitcast(I32), 23, None, ALU.logical_shift_left)
        self.tt(e, F2, F2, F1, ALU.mult)
        self.ts(e, F2, F2, 8388608.0, None, ALU.add)
        self.ts(e, F2, F2, 8388608.0, None, ALU.subtract)
        self.ts(e, F3.bitcast(I32), F3.bitcast(I32), -1, None, ALU.add)
        self.ts(e, F3.bitcast(I32), F3.bitcast(I32), 23, None, ALU.logical_shift_left)
        self.tt(e, F2, F2, F3, ALU.mult)
        self.ts(e, F3.bitcast(I32), xin.bitcast(I32), -2147483648, None, ALU.bitwise_and)
        self.r.op(e, "tensor_copy", F1.rearrange("p (a b) -> p a b", b=32), sc_b,
                  i=((self.sv, 1),))
        self.tt(e, F1.bitcast(I32), F1.bitcast(I32), F3.bitcast(I32), ALU.bitwise_or)
        self.tt(e, xout, F2, F1, ALU.mult, w=w_out)

    def pW(self):
        H = self.h
        XQB = self.LEB[:, 2048:4096]
        DQW = self.RSC
        for oc in range(8):
            self.dma("y", self.XO[:, 0:1024],
                     bass.AP(H["w_f"], oc * 128 * DM, [[1024, 128], [1, 1024]]),
                     w=(self.g_v(), self.g_t()))
            self.dma("y", DQW[:, 0:32],
                     bass.AP(H["delta_w"], oc * 128 * 32, [[32, 128], [1, 32]]))
            wdma = self.g_y()
            self.act(DQW[:, 0:32], DQW[:, 0:32], AF.Tanh, scale=4.0, w=(wdma,))
            self.act(DQW[:, 32:64], DQW[:, 0:32], AF.Exp, scale=-0.5 * LN2)
            self.act(DQW[:, 0:32], DQW[:, 0:32], AF.Exp, scale=0.5 * LN2)
            dq = self.g_s()
            self.quant_tile(self.XO[:, 0:1024], XQB[:, 0:1024],
                            DQW[:, 0:32], DQW[:, 32:64],
                            perblock=True, w0=(dq, wdma), w_out=(self.g_t(),))
            qd = self.g_v()
            for dc in range(8):
                w = ()
                if dc == 0:
                    w = (qd, self.g_s())
                elif dc == 4:
                    w = (self.g_s(),)
                self.mm(self.PSA[:, 1024 + (dc % 4) * 128:1024 + (dc % 4) * 128 + 128],
                        XQB[:, dc * 128:(dc + 1) * 128], self.IDN[:, :], w=w)
                if dc % 4 == 3:
                    self.act(bass.AP(self.WBa, (dc - 3) * 1024 + oc * 128,
                                     [[8192, 128], [1024, 4], [1, 128]]),
                             self.PSA[:, 1024:1536], AF.Copy, w=(self.g_t(),))
        self.dma("g", self.ROWS[0:1, :], H["b_f"][:, :])

    def pF(self):
        H = self.h
        dqp = self.SCL[:, 0:1]
        dqn = self.SCL[:, 1:2]
        XQB = self.LEB[:, 2048:4096]
        lnf = self.VT[:, 0:2048].bitcast(F32)
        lnscr = self.KT[:, 2048:4096].bitcast(F32)
        xs_free = None
        xo_gate = None
        rowg = self.g_gd()
        for s_ in range(NSLAB):
            p2 = s_ % 2
            w = [xs_free] if xs_free else []
            if s_ == 0:
                w.append(rowg)
            self.dma("y", bass.AP(self.XS, 0, [[2048, 128], [1024, 2], [1, 1024]]),
                     bass.AP(H["x3"], s_ * SLAB * DM, [[1024, 128], [131072, 2], [1, 1024]]),
                     w=w)
            xdma = self.g_y()
            for u in range(2):
                self.ln_tile(u * 1024, u, lnf,
                             lnscr, w_x=(xdma, self.g_v()) if u == 0 else (self.g_v(),))
                lng = self.g_s()
                self.quant_tile(lnf, XQB[:, u * 1024:(u + 1) * 1024], dqp, dqn,
                                w0=(lng, self.g_t()), w_out=(self.g_t(),))
                qd = self.g_v()
                for c in range(8):
                    w2 = ()
                    if c == 0:
                        w2 = (qd, self.g_s())
                    elif c == 4:
                        w2 = (self.g_s(),)
                    self.mm(self.PSA[:, 1024 + (c % 4) * 128:1024 + (c % 4) * 128 + 128],
                            XQB[:, u * 1024 + c * 128:u * 1024 + (c + 1) * 128],
                            self.IDN[:, :], w=w2)
                    if c % 4 == 3:
                        self.act(bass.AP(self.TSL, p2 * 2048 + (c - 3) * 256 + u * 128,
                                         [[4096, 128], [256, 4], [1, 128]]),
                                 self.PSA[:, 1024:1536], AF.Copy, w=(self.g_t(),))
            tq = self.g_s()
            for u in range(2):
                po = u * 1024
                for kc in range(8):
                    for half in range(2):
                        self.mm(self.PSB[:, po + half * 512:po + (half + 1) * 512],
                                self.TSL[:, p2 * 2048 + kc * 256 + u * 128:
                                         p2 * 2048 + kc * 256 + u * 128 + 128],
                                self.WBa[:, kc * 1024 + half * 512:
                                         kc * 1024 + (half + 1) * 512],
                                start=(kc == 0), stop=False,
                                w=(tq, self.g_v()) if (u == 0 and kc == 0 and half == 0)
                                else ())
                for half in range(2):
                    self.mm(self.PSB[:, po + half * 512:po + (half + 1) * 512],
                            self.ONESR[:, 0:128],
                            self.ROWS[:, half * 512:(half + 1) * 512],
                            start=False, stop=True)
                wst = [self.g_t()]
                if u == 0 and xo_gate is not None:
                    wst.append(xo_gate)
                self.stt("v", self.XO[:, u * 1024:(u + 1) * 1024],
                         self.PSB[:, po:po + 1024], 1.0,
                         self.XS[:, u * 1024:(u + 1) * 1024],
                         ALU.mult, ALU.add, w=wst)
            xs_free = self.g_v()
            self.dma("y", bass.AP(H["out"], s_ * SLAB * DM, [[1024, 128], [131072, 2], [1, 1024]]),
                     bass.AP(self.XO, 0, [[2048, 128], [1024, 2], [1, 1024]]),
                     w=(self.g_v(),))
            xo_gate = self.g_y()

    def build(self, phases):
        table = {"0": self.p0, "A1": self.pA1, "A2": self.pA2, "B": self.pB,
                 "C1": self.pC1, "C1b": self.pC1b, "C2": self.pC2, "D": self.pD,
                 "E": self.pE, "W": self.pW, "F": self.pF}
        for ph in phases:
            table[ph]()
            self.fence()
        for nm in self.dbg.get("dump_sbuf", ()):
            t = getattr(self, nm)
            self.dma("y", self.h["dbg_" + nm][:, :], t[:, :])
        self.r.op("y", None, w=((self.dy, self.dy.n), (self.dg, self.dg.n)))
        rec = self.r
        with self.nc.Block() as blk:
            @blk.tensor
            def _(eng):
                rec.replay("t", eng)

            @blk.vector
            def _(eng):
                rec.replay("v", eng)

            @blk.scalar
            def _(eng):
                rec.replay("s", eng)

            @blk.gpsimd
            def _(eng):
                rec.replay("g", eng)

            @blk.sync
            def _(eng):
                rec.replay("y", eng)


ALL_PHASES = ("0", "A1", "A2", "B", "C1", "C1b", "C2", "D", "E", "W", "F")


def build_nc(phases=ALL_PHASES, dbg=None):
    nc = bass.Bass("TRN2", target_bir_lowering=False)
    mk = MK(nc, dbg=dbg)
    mk.build(phases)
    return nc, mk


def make_inputs(inp, core):
    x = np.asarray(inp["L0_attn_hidden_states"])
    lay = host_layout(inp)
    cst = host_constants()
    m = {"x": np.ascontiguousarray(x[core], dtype=np.float32)}
    m.update(lay)
    m.update(cst)
    return m


def kernel(**inputs):
    nc, mk = build_nc()
    in_maps = [make_inputs(inputs, c) for c in range(8)]
    res = run_bass_kernel_spmd(nc, in_maps, core_ids=list(range(8)))
    out = np.stack([np.asarray(r["out"]) for r in res.results], axis=0)
    return out.astype(np.float32)


# version 17
# speedup vs baseline: 1.0562x; 1.0039x over previous
import math
import numpy as np
import ml_dtypes

import concourse.bass as bass
import concourse.mybir as mybir
from concourse.bass_utils import run_bass_kernel_spmd

F32 = mybir.dt.float32
BF16 = mybir.dt.bfloat16
I32 = mybir.dt.int32
I16 = mybir.dt.int16
AF = mybir.ActivationFunctionType
ALU = mybir.AluOpType
AX = mybir.AxisListType

S = 4096
DM = 1024
NSLAB = 16
SLAB = 256

BFD = ml_dtypes.bfloat16
LN2 = math.log(2.0)

SR_STAGE = 0
SR_KMAX = 512
SR_KNEG = 514
SR_DELTA = 516
SR_TANH = 518
SR_DQ2P = 520
SR_DQ2N = 522
SR_ONES = 528

KE_CQ = 0
KE_KV = 4096
KE_UVO = 8192
KE_WQC = 10240
KE_BQC = 10752
KE_BFIN = 11776
KE_UQT = 12800
KE_UKT = 13056
KE_UVT = 13312
KE_WQR = 13568


class Sem:
    __slots__ = ("h", "n")

    def __init__(self, h):
        self.h = h
        self.n = 0


def _ap_span(ap):
    if not isinstance(ap, bass.AP):
        return None
    es = mybir.dt.size(ap.dtype)
    pat = list(ap.ap)
    reach = 0
    fel = 1
    for k, (stp, cnt) in enumerate(pat):
        if k == 0:
            continue
        reach += abs(int(stp)) * (int(cnt) - 1)
        fel *= int(cnt)
    lo = int(ap.offset) * es
    return (ap.tensor.name, lo, lo + (reach + 1) * es, fel)


def _overlap(a, b):
    return a[0] == b[0] and a[1] < b[2] and b[1] < a[2]


class Rec:
    ENGS = ("t", "v", "s", "g", "y")
    OWN = {"t": "st", "v": "sv", "s": "ss", "g": "sg"}

    def __init__(self, nc):
        self.nc = nc
        self.q = {e: [] for e in self.ENGS}
        self.sems = {}
        self._hist = {e: [] for e in self.ENGS}

    def sem(self, name):
        s = Sem(self.nc.alloc_semaphore(name=name))
        self.sems[name] = s
        return s

    def op(self, e, meth, *a, w=(), i=(), **kw):
        waits = [(s.h, v) for (s, v) in w if s is not None and v > 0]
        if e in self.OWN and meth not in (None,):
            aps = [x for x in a if isinstance(x, bass.AP)]
            aps += [x for x in kw.values() if isinstance(x, bass.AP)]
            wr = _ap_span(aps[0]) if aps else None
            rds = [s2 for s2 in (_ap_span(x) for x in aps[1:]) if s2 is not None]
            own = self.sems[self.OWN[e]]
            need = 0
            for pw, cnt in self._hist[e]:
                for rd in rds:
                    if _overlap(rd, pw) and (rd[3] < 512 or pw[3] < 512):
                        need = max(need, cnt)
            if need:
                waits.append((own.h, need))
            self._hist[e] = self._hist[e][-1:] + ([(wr, own.n + 1)] if wr else [])
        self.q[e].append((meth, a, kw, waits,
                          [(s.h, d) for (s, d) in i] if i else None))
        for s, d in i:
            s.n += d

    def replay(self, e, eng):
        for meth, a, kw, waits, incs in self.q[e]:
            for h, v in waits:
                eng.wait_ge(h, v)
            if meth is None:
                continue
            try:
                ins = getattr(eng, meth)(*a, **kw)
            except Exception:
                import sys as _sys
                print(f"REPLAY-FAIL eng={e} meth={meth} "
                      f"args={[getattr(x, 'shape', x) for x in a]}", file=_sys.stderr)
                raise
            if incs:
                for h, d in incs:
                    ins.then_inc(h, d)


def host_constants():
    c = {}
    c["c_ident"] = np.eye(128, dtype=BFD)
    c["c_iota16"] = np.tile(np.arange(16, dtype=np.float32)[None, :], (128, 1)).astype(BFD)
    j = np.arange(1024)
    e16 = np.zeros((128, 1024), dtype=np.float32)
    e16[:16] = (j[None, :] // 64 == np.arange(16)[:, None])
    c["c_e16"] = e16.astype(BFD)
    return c


def host_layout(inp):
    d = {}
    f32 = np.float32
    asc = np.ascontiguousarray
    d["w_q"] = asc(inp["L0_attn_Wq"], dtype=f32)
    d["w_k"] = asc(inp["L0_attn_Wk"], dtype=f32)
    d["w_v"] = asc(inp["L0_attn_Wv"], dtype=f32)
    for nm, key in (("b_q", "L0_attn_bq"), ("b_k", "L0_attn_bk"), ("b_v", "L0_attn_bv")):
        d[nm] = asc(np.asarray(inp[key]).reshape(1, 1024), dtype=f32)
    for f in ("q", "k", "v"):
        w1 = np.asarray(inp[f"L0_ffn_{f}_w1"])
        d[f"w1_{f}"] = asc(w1.transpose(1, 0, 2).reshape(1024, 1024), dtype=f32)
        d[f"w2_{f}"] = asc(np.asarray(inp[f"L0_ffn_{f}_w2"]).reshape(1024, 1024), dtype=f32)
        d[f"b1_{f}"] = asc(np.asarray(inp[f"L0_ffn_{f}_b1"]).reshape(1, 1024), dtype=f32)
        d[f"b2_{f}"] = asc(inp[f"L0_ffn_{f}_b2"], dtype=f32)
    nwt = np.concatenate([np.asarray(inp[f"L0_ffn_{f}_node_w"]).T for f in ("q", "k", "v")],
                         axis=1)
    d["node_wt"] = asc(nwt, dtype=f32)
    nb_ = np.concatenate([np.asarray(inp[f"L0_ffn_{f}_node_b"]) for f in ("q", "k", "v")])
    d["node_b"] = asc(nb_.reshape(1, 45), dtype=f32)
    d["w_dq"] = asc(inp["L1_attn_W_DQ"], dtype=f32)
    d["b_dq"] = asc(np.asarray(inp["L1_attn_b_DQ"]).reshape(1, 32), dtype=f32)
    d["w_kv"] = asc(np.concatenate([inp["L1_attn_W_DKV"], inp["L1_attn_W_KR"]], axis=1),
                    dtype=f32)
    d["b_kv"] = asc(np.concatenate([np.asarray(inp["L1_attn_b_DKV"]),
                                    np.asarray(inp["L1_attn_b_KR"])]).reshape(1, 64),
                    dtype=f32)
    d["w_uqT"] = asc(np.asarray(inp["L1_attn_W_UQ"]).T.reshape(8, 128, 32), dtype=f32)
    d["w_ukT"] = asc(np.asarray(inp["L1_attn_W_UK"]).T.reshape(8, 128, 32), dtype=f32)
    d["w_uvT"] = asc(np.asarray(inp["L1_attn_W_UV"]).T.reshape(8, 128, 32), dtype=f32)
    d["w_qr"] = asc(inp["L1_attn_W_QR"], dtype=f32)
    d["b_qr"] = asc(np.asarray(inp["L1_attn_b_QR"]).reshape(1, 256), dtype=f32)
    d["b_uq"] = asc(np.asarray(inp["L1_attn_b_UQ"]).reshape(8, 128), dtype=f32)
    d["b_uv"] = asc(np.asarray(inp["L1_attn_b_UV"]).reshape(8, 128), dtype=f32)
    d["w_o"] = asc(inp["L1_attn_W_O"], dtype=f32)
    d["b_o"] = asc(np.asarray(inp["L1_attn_b_O"]).reshape(1, 1024), dtype=f32)
    d["w_f"] = asc(inp["L1_ffn_weight"], dtype=f32)
    d["b_f"] = asc(np.asarray(inp["L1_ffn_bias"]).reshape(1, 1024), dtype=f32)
    d["delta_w"] = asc(inp["L1_ffn_delta_w"], dtype=f32)
    d["delta_act"] = asc(np.asarray(inp["L1_ffn_delta_act"]).reshape(1, 1), dtype=f32)
    return d


class MK:
    def __init__(self, nc, dbg=None):
        self.nc = nc
        self.dbg = dbg or {}
        self.r = Rec(nc)
        r = self.r
        self.st = r.sem("st")
        self.sv = r.sem("sv")
        self.ss = r.sem("ss")
        self.sg = r.sem("sg")
        self.dy = r.sem("dy")
        self.dg = r.sem("dg")
        self.dq2 = r.sem("dq2")
        self._declare_io()
        self._alloc()

    def _declare_io(self):
        nc = self.nc
        T = nc.dram_tensor
        self.h = {}
        H = self.h
        H["x"] = T("x", [S, DM], F32, kind="ExternalInput")
        H["out"] = T("out", [S, DM], F32, kind="ExternalOutput")
        for nm, sh, dt in (("c_ident", [128, 128], BF16),
                           ("c_iota16", [128, 16], BF16), ("c_e16", [128, 1024], BF16)):
            H[nm] = T(nm, sh, dt, kind="ExternalInput")
        for nm, sh in (
            ("w_q", [DM, DM]), ("w_k", [DM, DM]), ("w_v", [DM, DM]),
            ("b_q", [1, DM]), ("b_k", [1, DM]), ("b_v", [1, DM]),
            ("w1_q", [DM, DM]), ("w1_k", [DM, DM]), ("w1_v", [DM, DM]),
            ("w2_q", [DM, DM]), ("w2_k", [DM, DM]), ("w2_v", [DM, DM]),
            ("b1_q", [1, DM]), ("b1_k", [1, DM]), ("b1_v", [1, DM]),
            ("b2_q", [16, DM]), ("b2_k", [16, DM]), ("b2_v", [16, DM]),
            ("node_wt", [DM, 45]), ("node_b", [1, 45]),
            ("w_dq", [DM, 32]), ("b_dq", [1, 32]),
            ("w_kv", [DM, 64]), ("b_kv", [1, 64]),
            ("w_uqT", [8, 128, 32]), ("w_ukT", [8, 128, 32]), ("w_uvT", [8, 128, 32]),
            ("w_qr", [32, 256]), ("b_qr", [1, 256]),
            ("b_uq", [8, 128]), ("b_uv", [8, 128]),
            ("w_o", [DM, DM]), ("b_o", [1, DM]),
            ("w_f", [DM, DM]), ("b_f", [1, DM]),
            ("delta_w", [DM, 32]), ("delta_act", [1, 1]),
        ):
            H[nm] = T(nm, sh, F32, kind="ExternalInput")
        dump = set(self.dbg.get("dump", ()))
        pre = set(self.dbg.get("preload", ()))

        def scratch(nm, sh, dt):
            kind = "Internal"
            if nm in dump:
                kind = "ExternalOutput"
            elif nm in pre:
                kind = "ExternalInput"
            H[nm] = T(nm, sh, dt, kind=kind)

        scratch("x1", [S, DM], F32)
        scratch("x2", [S, DM], F32)
        scratch("x3", [S, DM], F32)
        scratch("xlnT", [16, 128, 2048], BF16)
        scratch("qTs", [16, 128, 2048], BF16)
        for nm in self.dbg.get("dump_sbuf", ()):
            H["dbg_" + nm] = T("dbg_" + nm, [128, self._dbg_cols(nm)], BF16,
                               kind="ExternalOutput")

    def _dbg_cols(self, nm):
        return {"KT": 8 * S, "VT": 8 * S, "WBa": 8192, "WBb": 8192,
                "TSL": 4096, "LEB": 4096, "SM1": 2048}.get(nm, 4096)

    def _alloc(self):
        nc = self.nc
        A = nc.alloc_sbuf_tensor
        self.KT = A("KT", [128, 8 * S], BF16)
        self.VT = A("VT", [128, 8 * S], BF16)
        self.WBa = A("WBa", [128, 8192], BF16)
        self.WBb = A("WBb", [128, 8192], BF16)
        self.XS = A("XS", [128, 2048], F32)
        self.XO = A("XO", [128, 2048], F32)
        self.TSL = A("TSL", [128, 4096], BF16)
        self.LEB = A("LEB", [128, 4096], BF16)
        self.SM1 = A("SM1", [128, 2048], BF16)
        self.IDN = A("IDN", [128, 128], BF16)
        self.IOTA = A("IOTA", [128, 16], BF16)
        self.E16 = A("E16", [128, 1024], BF16)
        self.ONESC = A("ONESC", [128, 1], BF16)
        self.ONESR = A("ONESR", [128, 512], BF16)
        self.ROWS = A("ROWS", [128, 1024], BF16)
        self.B2C = A("B2C", [128, 1024], BF16)
        self.NWT = A("NWT", [128, 360], BF16)
        self.NBR = A("NBR", [128, 45], BF16)
        self.ROUTE = A("ROUTE", [128, 32], F32)
        self.RSC = A("RSC", [128, 64], F32)
        self.OHT = A("OHT", [128, 256], BF16)
        self.STAT = A("STAT", [128, 16], F32)
        self.SROW = A("SROW", [1, 576], F32)
        self.SCL = A("SCL", [128, 8], F32)
        self.FQ4 = A("FQ4", [128, 128], F32)
        Ap = nc.alloc_psum_tensor
        self.PSA = Ap("PSA", [128, 2048], F32)
        self.PSB = Ap("PSB", [128, 2048], F32)

    def dma(self, q, out, in_, w=(), **kw):
        sem = self.dy if q == "y" else (self.dq2 if q == "s" else self.dg)
        self.r.op(q, "dma_start", w=w, i=((sem, 16),), out=out, in_=in_, **kw)

    def mm(self, out, lhsT, rhs, start=True, stop=True, w=()):
        self.r.op("t", "matmul", out, lhsT, rhs, start=start, stop=stop,
                  w=w, i=((self.st, 1),))

    def mmT(self, out_psum, in_sb, w=()):
        self.mm(out_psum, in_sb, self.IDN[:, :], w=w)

    def act(self, out, in_, func, w=(), **kw):
        self.r.op("s", "activation", out, in_, func, w=w, i=((self.ss, 1),), **kw)

    def ts(self, e, out, in0, s1, s2, op0, op1=None, w=()):
        sem = {"v": self.sv, "g": self.sg}[e]
        if op1 is None:
            self.r.op(e, "tensor_scalar", out, in0, s1, None, op0, w=w, i=((sem, 1),))
        else:
            self.r.op(e, "tensor_scalar", out, in0, s1, s2, op0, op1, w=w, i=((sem, 1),))

    def tt(self, e, out, in0, in1, op, w=()):
        sem = {"v": self.sv, "g": self.sg}[e]
        self.r.op(e, "tensor_tensor", out, in0, in1, op, w=w, i=((sem, 1),))

    def stt(self, e, out, in0, scalar, in1, op0, op1, w=()):
        sem = {"v": self.sv, "g": self.sg}[e]
        self.r.op(e, "scalar_tensor_tensor", out, in0, scalar, in1, op0, op1,
                  w=w, i=((sem, 1),))

    def vcopy(self, out, in_, w=()):
        self.r.op("v", "tensor_copy", out, in_, w=w, i=((self.sv, 1),))

    def vred(self, out, in_, op, w=(), **kw):
        self.r.op("v", "tensor_reduce", out, in_, AX.X, op, w=w, i=((self.sv, 1),), **kw)

    def vrecip(self, out, in_, w=()):
        self.r.op("v", "reciprocal", out, in_, w=w, i=((self.sv, 1),))

    def memset(self, e, ap, val, w=()):
        sem = {"v": self.sv, "g": self.sg}[e]
        self.r.op(e, "memset", ap, val, w=w, i=((sem, 1),))

    def g_t(self):
        return (self.st, self.st.n)

    def g_v(self):
        return (self.sv, self.sv.n)

    def g_s(self):
        return (self.ss, self.ss.n)

    def g_g(self):
        return (self.sg, self.sg.n)

    def g_y(self):
        return (self.dy, self.dy.n)

    def g_gd(self):
        return (self.dg, self.dg.n)

    def fence(self):
        counts = {nm: s.n for nm, s in self.r.sems.items()}
        sems = self.r.sems
        for e in Rec.ENGS:
            own = {"t": "st", "v": "sv", "s": "ss", "g": "sg"}.get(e)
            w = [(sems[nm], v) for nm, v in counts.items() if nm != own and v > 0]
            self.r.op(e, None, w=w)

    def ln_tile(self, xoff, col, out_ap, scratch, w_x=(), w_apply=None):
        ST = self.STAT
        c = 8 * col
        xs = self.XS[:, xoff:xoff + 1024]
        self.vred(ST[:, c + 0:c + 1], xs, ALU.add, w=w_x)
        self.stt("v", scratch, xs, 1.0, xs, ALU.mult, ALU.mult)
        self.vred(ST[:, c + 1:c + 2], scratch, ALU.add)
        self.ts("v", ST[:, c + 2:c + 3], ST[:, c + 0:c + 1], 1.0 / DM, None, ALU.mult)
        self.ts("v", ST[:, c + 3:c + 4], ST[:, c + 1:c + 2], 1.0 / DM, None, ALU.mult)
        self.tt("v", ST[:, c + 4:c + 5], ST[:, c + 2:c + 3], ST[:, c + 2:c + 3], ALU.mult)
        self.tt("v", ST[:, c + 3:c + 4], ST[:, c + 3:c + 4], ST[:, c + 4:c + 5],
                ALU.subtract)
        self.ts("v", ST[:, c + 3:c + 4], ST[:, c + 3:c + 4], 1e-5, None, ALU.add)
        v_ = ST[:, c + 3:c + 4]
        y = ST[:, c + 5:c + 6]
        t = ST[:, c + 6:c + 7]
        self.ts("v", t.bitcast(I32), v_.bitcast(I32), 1, None, ALU.logical_shift_right)
        self.ts("v", y.bitcast(I32), t.bitcast(I32), -1, 0x5F3759DF, ALU.mult, ALU.add)
        for _ in range(2):
            self.tt("v", t, y, y, ALU.mult)
            self.tt("v", t, t, v_, ALU.mult)
            self.ts("v", t, t, -0.5, 1.5, ALU.mult, ALU.add)
            self.tt("v", y, y, t, ALU.mult)
        mb = ST[:, c + 7:c + 8]
        self.stt("v", mb, ST[:, c + 2:c + 3], -1.0, y, ALU.mult, ALU.mult)
        wa = (self.g_v(),) + (tuple(w_apply) if w_apply is not None else (self.g_t(),))
        self.act(out_ap, xs, AF.Identity, bias=mb, scale=y, w=wa)

    def transpose_slab(self, src, base, w_first=()):
        for u in range(2):
            for c in range(8):
                w = ()
                if u == 0 and c == 0:
                    w = tuple(w_first) + (self.g_s(),)
                elif u == 1 and c == 0:
                    w = (self.g_s(),)
                self.mmT(self.PSA[:, 1024 + c * 128:1024 + (c + 1) * 128],
                         src[:, base + u * 1024 + c * 128:base + u * 1024 + (c + 1) * 128],
                         w=w)
            out = bass.AP(self.TSL, u * 128, [[4096, 128], [256, 8], [1, 128]])
            self.act(out, self.PSA[:, 1024:2048], AF.Copy, w=(self.g_t(),))

    def p0(self):
        H = self.h
        self.dma("y", self.IDN[:, :], H["c_ident"][:, :])
        self.dma("y", self.IOTA[:, :], H["c_iota16"][:, :])
        self.dma("y", self.E16[:, :], H["c_e16"][:, :])
        self.memset("v", self.ONESC[:, :], 1.0)
        self.memset("v", self.ONESR[:, :], 0.0)
        self.memset("v", self.ONESR[0:1, :], 1.0)
        self.memset("v", self.ROWS[:, :], 0.0)
        self.memset("v", self.B2C[:, :], 0.0)
        self.memset("v", self.NBR[:, :], 0.0)
        self.memset("v", self.OHT[:, :], 0.0)
        self.memset("v", self.SROW[0:1, SR_STAGE:SR_STAGE + 128], 1.0)
        self.memset("v", self.RSC[:, 20:22], 0.0)
        self.ts("v", self.RSC[:, 20:21].bitcast(I32), self.RSC[:, 20:21].bitcast(I32),
                1258291200, None, ALU.add)
        self.ts("v", self.RSC[:, 21:22].bitcast(I32), self.RSC[:, 21:22].bitcast(I32),
                -889192448, None, ALU.add)
        self.memset("v", self.SROW[0:1, SR_ONES:SR_ONES + 32], 1.0)
        self.dma("g", self.SROW[0:1, SR_DELTA:SR_DELTA + 1], H["delta_act"][:, :])
        gd = self.g_gd()
        self.act(self.SROW[0:1, SR_TANH:SR_TANH + 1],
                 self.SROW[0:1, SR_DELTA:SR_DELTA + 1], AF.Tanh, scale=4.0, w=(gd,))
        self.act(self.SROW[0:1, SR_DQ2P:SR_DQ2P + 1],
                 self.SROW[0:1, SR_TANH:SR_TANH + 1], AF.Exp, scale=0.5 * LN2)
        self.act(self.SROW[0:1, SR_DQ2N:SR_DQ2N + 1],
                 self.SROW[0:1, SR_TANH:SR_TANH + 1], AF.Exp, scale=-0.5 * LN2)
        sg_ = self.g_s()
        self.mm(self.PSB[0:128, 1024:1025], self.SROW[0:1, SR_STAGE:SR_STAGE + 128],
                self.SROW[0:1, SR_DQ2P:SR_DQ2P + 1], w=(sg_, self.g_v()))
        self.mm(self.PSB[0:128, 1025:1026], self.SROW[0:1, SR_STAGE:SR_STAGE + 128],
                self.SROW[0:1, SR_DQ2N:SR_DQ2N + 1])
        self.vcopy(self.SCL[:, 0:2], self.PSB[:, 1024:1026], w=(self.g_t(),))

    def pA1(self):
        H = self.h
        self.dma("g", bass.AP(self.WBa, 0, [[8192, 128], [1024, 8], [1, 1024]]),
                 bass.AP(H["w_q"], 0, [[1024, 128], [131072, 8], [1, 1024]]))
        self.dma("g", bass.AP(self.WBb, 0, [[8192, 128], [1024, 8], [1, 1024]]),
                 bass.AP(H["w_k"], 0, [[1024, 128], [131072, 8], [1, 1024]]))
        self.dma("g", self.ROWS[0:1, :], H["b_q"][:, :])
        self.dma("g", self.B2C[0:1, :], H["b_k"][:, :])
        wg = self.g_gd()
        lnb = self.LEB
        sc_f32 = self.LEB[:, 2048:4096].bitcast(F32)
        xs_free = None
        lnb_free = ((self.st, 0),)
        for s_ in range(NSLAB):
            p2 = s_ % 2
            w = [xs_free] if xs_free else []
            self.dma("y", bass.AP(self.XS, 0, [[2048, 128], [1024, 2], [1, 1024]]),
                     bass.AP(H["x"], s_ * SLAB * DM, [[1024, 128], [131072, 2], [1, 1024]]),
                     w=w)
            xdma = self.g_y()
            for u in range(2):
                self.ln_tile(u * 1024, u, lnb[:, u * 1024:(u + 1) * 1024], sc_f32,
                             w_x=(xdma,) if u == 0 else (), w_apply=lnb_free)
            xs_free = self.g_s()
            self.transpose_slab(lnb, 0)
            lnb_free = (self.g_t(),)
            tev = self.g_s()
            qstage = self.XO[:, p2 * 1024:(p2 + 1) * 1024].bitcast(BF16)
            for oc in range(8):
                for kc in range(8):
                    self.mm(self.PSB[:, oc * 256:(oc + 1) * 256],
                            self.WBa[:, kc * 1024 + oc * 128:kc * 1024 + oc * 128 + 128],
                            self.TSL[:, kc * 256:(kc + 1) * 256],
                            start=(kc == 0), stop=False,
                            w=(tev, wg) if (oc == 0 and kc == 0) else ())
                self.mm(self.PSB[:, oc * 256:(oc + 1) * 256],
                        self.ROWS[:, oc * 128:(oc + 1) * 128], self.ONESR[:, 0:256],
                        start=False, stop=True)
            self.act(qstage, self.PSB[:, :], AF.Copy, w=(self.g_t(), self.g_y()))
            qev = self.g_s()
            for oc in range(8):
                for kc in range(8):
                    self.mm(self.PSB[:, oc * 256:(oc + 1) * 256],
                            self.WBb[:, kc * 1024 + oc * 128:kc * 1024 + oc * 128 + 128],
                            self.TSL[:, kc * 256:(kc + 1) * 256],
                            start=(kc == 0), stop=False,
                            w=(qev,) if (oc == 0 and kc == 0) else ())
                self.mm(self.PSB[:, oc * 256:(oc + 1) * 256],
                        self.B2C[:, oc * 128:(oc + 1) * 128], self.ONESR[:, 0:256],
                        start=False, stop=True)
            self.act(bass.AP(self.KT, s_ * 256, [[32768, 128], [4096, 8], [1, 256]]),
                     self.PSB[:, :], AF.Copy, w=(self.g_t(),))
            self.dma("y", bass.AP(H["xlnT"], s_ * 262144, [[2048, 128], [1, 2048]]),
                     bass.AP(self.TSL, 0, [[4096, 128], [1, 2048]]),
                     w=(tev,))
            self.dma("y", bass.AP(H["qTs"], s_ * 262144, [[2048, 128], [1, 2048]]),
                     qstage,
                     w=(qev,))

    def pA2(self):
        H = self.h
        self.dma("g", bass.AP(self.WBa, 0, [[8192, 128], [1024, 8], [1, 1024]]),
                 bass.AP(H["w_v"], 0, [[1024, 128], [131072, 8], [1, 1024]]))
        self.dma("g", self.ROWS[0:1, :], H["b_v"][:, :])
        wg = self.g_gd()
        for half in range(2):
            self.mm(self.PSA[:, half * 512:(half + 1) * 512], self.ONESR[:, 0:128],
                    self.ROWS[:, half * 512:(half + 1) * 512],
                    w=(wg,) if half == 0 else ())
        self.vcopy(self.SM1[:, 0:1024], self.PSA[:, 0:1024], w=(self.g_t(),))
        bvt = self.g_v()
        tsl_free = None
        for s_ in range(NSLAB):
            w = [tsl_free] if tsl_free else []
            self.dma("y", bass.AP(self.TSL, 0, [[4096, 128], [1, 2048]]),
                     bass.AP(H["xlnT"], s_ * 262144, [[2048, 128], [1, 2048]]),
                     w=w)
            xdma = self.g_y()
            for u in range(2):
                po = u * 1024
                for kc in range(8):
                    for half in range(2):
                        self.mm(self.PSA[:, po + half * 512:po + (half + 1) * 512],
                                self.TSL[:, kc * 256 + u * 128:kc * 256 + u * 128 + 128],
                                self.WBa[:, kc * 1024 + half * 512:
                                         kc * 1024 + (half + 1) * 512],
                                start=(kc == 0), stop=(kc == 7),
                                w=(xdma, self.g_v()) if kc == 0 and half == 0 else ())
                j = 2 * s_ + u
                self.tt("v", self.VT[:, j * 1024:(j + 1) * 1024],
                        self.PSA[:, po:po + 1024], self.SM1[:, 0:1024], ALU.add,
                        w=(self.g_t(), bvt))
            tsl_free = self.g_t()

    def pB(self):
        H = self.h
        exp_free = [None, None]
        qts_free = {}
        xs_free = None
        xo_gate = None
        leb_h0 = [None, None]
        leb_all = [None, None]
        vg = {}
        SM2 = self.SM1[:, 1536:2048]

        def tmax(a, b):
            if a is None:
                return b
            if b is None:
                return a
            return (a[0], max(a[1], b[1]))

        for nb in range(16):
            p = nb % 2
            self.dma("s", bass.AP(self.TSL, p * 2048, [[4096, 128], [1, 2048]]),
                     bass.AP(H["qTs"], nb * 262144, [[2048, 128], [1, 2048]]),
                     w=[qts_free[p]] if p in qts_free else [])
            qdma = (self.dq2, self.dq2.n)
            self.dma("y", bass.AP(self.XS, 0, [[2048, 128], [1024, 2], [1, 1024]]),
                     bass.AP(H["x"], nb * 256 * DM, [[1024, 128], [131072, 2], [1, 1024]]),
                     w=[xs_free] if xs_free else [])
            xsdma = self.g_y()
            for m in range(34):
                if m < 32:
                    mp = m % 2
                    sc_half = [None, None]
                    for halfg in range(2):
                        wlist = []
                        if exp_free[halfg] is not None:
                            wlist.append(exp_free[halfg])
                        if m == 0 and halfg == 0:
                            wlist.append(qdma)
                        for gg in range(4):
                            g = halfg * 4 + gg
                            self.mm(self.PSA[:, halfg * 1024 + gg * 256:
                                             halfg * 1024 + (gg + 1) * 256],
                                    self.KT[:, g * S + m * 128:g * S + (m + 1) * 128],
                                    self.TSL[:, p * 2048 + g * 256:
                                             p * 2048 + (g + 1) * 256],
                                    w=wlist if gg == 0 else ())
                        sc_half[halfg] = self.g_t()
                if m >= 2:
                    mc = m - 2
                    mpc = mc % 2
                    vn, gn = vg[mc]
                    for g in range(8):
                        for ns in range(2):
                            wlist = []
                            if g == 0 and ns == 0:
                                wlist = [vn]
                                if mc == 0 and xo_gate is not None:
                                    wlist.append(self.g_v())
                            elif g == 6 and ns == 0:
                                wlist = [gn]
                            self.mm(self.PSB[:, ns * 1024 + g * 128:
                                             ns * 1024 + (g + 1) * 128],
                                    self.LEB[:, mpc * 2048 + g * 256 + ns * 128:
                                             mpc * 2048 + g * 256 + ns * 128 + 128],
                                    self.VT[:, mc * 1024 + g * 128:
                                            mc * 1024 + (g + 1) * 128],
                                    start=(mc == 0 and g % 4 == 0), stop=(mc == 31),
                                    w=wlist)
                        if g == 3:
                            leb_h0[mpc] = self.g_t()
                    leb_all[mpc] = self.g_t()
                if m < 32:
                    e_half = [None, None]
                    for halfg in range(2):
                        if halfg == 0:
                            wexp = [tmax(sc_half[1], leb_h0[mp])]
                        else:
                            wexp = [tmax(sc_half[1], leb_all[mp])]
                        self.act(self.LEB[:, mp * 2048 + halfg * 1024:
                                          mp * 2048 + (halfg + 1) * 1024],
                                 self.PSA[:, halfg * 1024:(halfg + 1) * 1024], AF.Exp,
                                 scale=1.0 / 32.0, w=wexp)
                        exp_free[halfg] = self.g_s()
                        e_half[halfg] = self.g_s()
                    eb = mp * 2048
                    so = mp * 256
                    self.tt("v", bass.AP(self.SM1, 0, [[2048, 128], [256, 2], [1, 256]]),
                            bass.AP(self.LEB, eb, [[4096, 128], [512, 2], [1, 256]]),
                            bass.AP(self.LEB, eb + 256, [[4096, 128], [512, 2], [1, 256]]),
                            ALU.add, w=(e_half[0],))
                    self.tt("v", bass.AP(self.SM1, 512, [[2048, 128], [256, 2], [1, 256]]),
                            bass.AP(self.LEB, eb + 1024, [[4096, 128], [512, 2], [1, 256]]),
                            bass.AP(self.LEB, eb + 1280, [[4096, 128], [512, 2], [1, 256]]),
                            ALU.add, w=(e_half[1],))
                    self.tt("v", bass.AP(self.SM1, 1024, [[2048, 128], [256, 2], [1, 256]]),
                            bass.AP(self.SM1, 0, [[2048, 128], [512, 2], [1, 256]]),
                            bass.AP(self.SM1, 256, [[2048, 128], [512, 2], [1, 256]]),
                            ALU.add)
                    self.tt("v", SM2[:, so:so + 256], self.SM1[:, 1024:1280],
                            self.SM1[:, 1280:1536], ALU.add)
                    D = SM2[:, so:so + 256]
                    seed = self.SM1[:, 0:256].bitcast(I16)
                    t = self.SM1[:, 256:512]
                    self.ts("v", seed, D.bitcast(I16), -1, 0x7EF4, ALU.mult, ALU.add)
                    r0 = seed.bitcast(BF16)
                    self.tt("v", t, D, r0, ALU.mult)
                    self.ts("v", t, t, -1.0, 2.0, ALU.mult, ALU.add)
                    self.tt("v", SM2[:, so:so + 256], r0, t, ALU.mult)
                    rn = self.g_v()
                    rd6 = bass.AP(self.SM1, 1536 + so, [[2048, 128], [0, 6], [1, 256]])
                    self.tt("v", bass.AP(self.LEB, eb, [[4096, 128], [256, 6], [1, 256]]),
                            bass.AP(self.LEB, eb, [[4096, 128], [256, 6], [1, 256]]),
                            rd6, ALU.mult)
                    vn = self.g_v()
                    rd2 = bass.AP(self.SM1, 1536 + so, [[2048, 128], [0, 2], [1, 256]])
                    self.tt("g", bass.AP(self.LEB, eb + 1536, [[4096, 128], [256, 2], [1, 256]]),
                            bass.AP(self.LEB, eb + 1536, [[4096, 128], [256, 2], [1, 256]]),
                            rd2, ALU.mult, w=(rn,))
                    gn = self.g_g()
                    vg[m] = (vn, gn)
            ctx_done = self.g_t()
            qts_free[p] = ctx_done
            for u in range(2):
                wl = [ctx_done, xsdma] if u == 0 else []
                if u == 0 and xo_gate is not None:
                    wl.append(xo_gate)
                self.stt("v", self.XO[:, u * 1024:(u + 1) * 1024],
                         self.PSB[:, u * 1024:(u + 1) * 1024], 1.0,
                         self.XS[:, u * 1024:(u + 1) * 1024],
                         ALU.mult, ALU.add, w=wl)
            xs_free = self.g_v()
            self.dma("y", bass.AP(H["x1"], nb * 256 * DM, [[1024, 128], [131072, 2], [1, 1024]]),
                     bass.AP(self.XO, 0, [[2048, 128], [1024, 2], [1, 1024]]),
                     w=(self.g_v(),))
            xo_gate = self.g_y()

    def fff(self, f, s_, p2, dest, tbase=0, w_tsl=()):
        H = self.h
        fo = {"q": 0, "k": 1, "v": 2}[f]
        for u in range(2):
            po = 1024 + u * 256
            for kc in range(8):
                self.mm(self.PSA[:, po:po + 15],
                        self.TSL[:, tbase + kc * 256 + u * 128:
                                 tbase + kc * 256 + u * 128 + 128],
                        self.NWT[:, kc * 45 + fo * 15:kc * 45 + fo * 15 + 15],
                        start=(kc == 0), stop=False,
                        w=(self.g_s(), self.g_v()) + tuple(w_tsl)
                        if (u == 0 and kc == 0) else ())
            self.mm(self.PSA[:, po:po + 15], self.ONESR[:, 0:128],
                    self.NBR[:, fo * 15:fo * 15 + 15], start=False, stop=True)
        tr = self.g_t()
        for u in range(2):
            self.vcopy(self.ROUTE[:, u * 15:u * 15 + 15],
                       self.PSA[:, 1024 + u * 256:1024 + u * 256 + 15],
                       w=(tr,) if u == 0 else ())
        R = self.ROUTE
        C = self.RSC

        def Sv(j):
            return bass.AP(R, j, [[32, 128], [15, 2], [1, 1]])

        def c2(a):
            return C[:, a:a + 2]

        b0, b1, b2, b3 = c2(32), c2(34), c2(36), c2(38)
        c0, c1, cc2 = c2(40), c2(42), c2(44)
        sel, d, t = c2(46), c2(48), c2(50)
        n = [c2(52 + 2 * k) for k in range(4)]
        leaf = c2(60)
        mq = [c2(0 + 2 * k) for k in range(8)]
        Z = self.RSC[:, 16:18]
        O1 = self.RSC[:, 18:20]
        self.tt("v", b0, Sv(0), Z, ALU.is_ge, w=(self.g_v(), self.g_g()))
        self.tt("v", d, Sv(2), Sv(1), ALU.subtract)
        self.tt("v", sel, d, b0, ALU.mult)
        self.tt("v", sel, sel, Sv(1), ALU.add)
        self.tt("v", b1, sel, Z, ALU.is_ge)
        self.tt("v", c0, O1, b0, ALU.subtract)
        self.tt("v", c1, O1, b1, ALU.subtract)
        self.tt("v", n[0], c0, c1, ALU.mult)
        self.tt("v", n[1], c0, b1, ALU.mult)
        self.tt("v", n[2], b0, c1, ALU.mult)
        self.tt("v", n[3], b0, b1, ALU.mult)
        self.tt("v", sel, Sv(3), n[0], ALU.mult)
        for k, jj in enumerate((4, 5, 6)):
            self.tt("v", t, Sv(jj), n[k + 1], ALU.mult)
            self.tt("v", sel, sel, t, ALU.add)
        self.tt("v", b2, sel, Z, ALU.is_ge)
        self.tt("v", cc2, O1, b2, ALU.subtract)
        for k in range(4):
            self.tt("v", mq[2 * k], n[k], cc2, ALU.mult)
            self.tt("v", mq[2 * k + 1], n[k], b2, ALU.mult)
        self.tt("v", sel, Sv(7), mq[0], ALU.mult)
        for k in range(1, 8):
            self.tt("v", t, Sv(7 + k), mq[k], ALU.mult)
            self.tt("v", sel, sel, t, ALU.add)
        self.tt("v", b3, sel, Z, ALU.is_ge)
        self.tt("v", leaf, b0, b0, ALU.add)
        self.tt("v", leaf, leaf, b1, ALU.add)
        self.tt("v", leaf, leaf, leaf, ALU.add)
        self.tt("v", leaf, leaf, b2, ALU.add)
        self.tt("v", leaf, leaf, leaf, ALU.add)
        self.tt("v", leaf, leaf, b3, ALU.add)
        for u in range(2):
            self.ts("v", self.SM1[:, 2016 + u * 16:2016 + (u + 1) * 16], self.IOTA[:, :],
                    leaf[:, u:u + 1], None, ALU.is_equal,
                    w=(self.g_t(),) if u == 0 else ())
        ohv = self.g_v()
        w1_half = None
        for oc in range(8):
            for kc in range(8):
                self.mm(self.PSB[:, oc * 256:(oc + 1) * 256],
                        self.WBa[:, kc * 1024 + oc * 128:kc * 1024 + oc * 128 + 128],
                        self.TSL[:, tbase + kc * 256:tbase + (kc + 1) * 256],
                        start=(kc == 0), stop=False,
                        w=(self.g_s(),) if (oc == 0 and kc == 0) else ())
            self.mm(self.PSB[:, oc * 256:(oc + 1) * 256],
                    self.ROWS[:, oc * 128:(oc + 1) * 128],
                    self.ONESR[:, 0:256], start=False, stop=True)
            if oc == 3:
                w1_half = self.g_t()
        w1_done = self.g_t()
        for u in range(2):
            self.mm(self.PSA[0:16, 1536 + u * 128:1536 + (u + 1) * 128],
                    self.SM1[:, 2016 + u * 16:2016 + (u + 1) * 16], self.IDN[:, :],
                    w=(ohv,) if u == 0 else ())
        ohmm = self.g_t()
        self.vcopy(self.OHT[0:16, 0:256], self.PSA[0:16, 1536:1792], w=(ohmm,))
        oht = self.g_v()
        for c in range(8):
            dst = c * 256 if c < 4 else 1024 + (c - 4) * 256
            self.mm(self.PSA[:, dst:dst + 256],
                    self.E16[:, c * 128:(c + 1) * 128], self.OHT[:, 0:256],
                    w=(oht, self.g_v()) if c == 0 else ())
        maskmm = self.g_t()
        relu_half = [None, None]
        for half in range(2):
            wre = [w1_half if half == 0 else ohmm]
            if half == 0:
                wre.append(self.g_v())
            self.act(self.SM1[:, half * 1024:(half + 1) * 1024],
                     self.PSB[:, half * 1024:(half + 1) * 1024], AF.Relu, w=wre)
            relu_half[half] = self.g_s()
        hm = [None, None]
        for half in range(2):
            self.tt("v", self.SM1[:, half * 1024:(half + 1) * 1024],
                    self.SM1[:, half * 1024:(half + 1) * 1024],
                    self.PSA[:, half * 1024:(half + 1) * 1024], ALU.mult,
                    w=(maskmm, relu_half[half]))
            hm[half] = self.g_v()
        if dest in ("qT", "kT"):
            for kc in range(8):
                for oc in range(8):
                    wlist = ()
                    if oc == 0 and kc == 0:
                        wlist = (hm[0],)
                    elif oc == 0 and kc == 4:
                        wlist = (hm[1],)
                    self.mm(self.PSB[:, oc * 256:(oc + 1) * 256],
                            self.WBb[:, kc * 1024 + oc * 128:kc * 1024 + oc * 128 + 128],
                            self.SM1[:, kc * 256:(kc + 1) * 256],
                            start=(kc == 0), stop=False, w=wlist)
            for oc in range(8):
                self.mm(self.PSB[:, oc * 256:(oc + 1) * 256],
                        self.B2C[:, oc * 128:(oc + 1) * 128], self.OHT[:, 0:256],
                        start=False, stop=True)
            if dest == "qT":
                qstage = self.XO[:, p2 * 1024:(p2 + 1) * 1024].bitcast(BF16)
                self.act(qstage, self.PSB[:, :], AF.Copy, w=(self.g_t(), self.g_y()))
                qev = self.g_s()
                self.dma("y", bass.AP(H["qTs"], s_ * 262144, [[2048, 128], [1, 2048]]),
                         qstage,
                         w=(qev,))
            else:
                self.act(bass.AP(self.KT, s_ * 256, [[32768, 128], [4096, 8], [1, 256]]),
                         self.PSB[:, :], AF.Copy, w=(self.g_t(),))
        else:
            for u in range(2):
                po = u * 1024
                for kc in range(8):
                    for half in range(2):
                        wlist = ()
                        if u == 0 and half == 0 and kc == 0:
                            wlist = (hm[0],)
                        elif u == 0 and half == 0 and kc == 4:
                            wlist = (hm[1],)
                        self.mm(self.PSB[:, po + half * 512:po + (half + 1) * 512],
                                self.SM1[:, kc * 256 + u * 128:kc * 256 + u * 128 + 128],
                                self.WBb[:, kc * 1024 + half * 512:
                                         kc * 1024 + (half + 1) * 512],
                                start=(kc == 0), stop=False, w=wlist)
                for half in range(2):
                    self.mm(self.PSB[:, po + half * 512:po + (half + 1) * 512],
                            self.OHT[:, u * 128:(u + 1) * 128],
                            self.B2C[:, half * 512:(half + 1) * 512],
                            start=False, stop=True)
                j = 2 * s_ + u
                self.act(self.VT[:, j * 1024:(j + 1) * 1024],
                         self.PSB[:, po:po + 1024], AF.Copy, w=(self.g_t(),))
        return w1_done

    def _load_fff_weights(self, f):
        H = self.h
        self.memset("v", self.RSC[:, 16:18], 0.0, w=(self.g_t(),))
        self.memset("v", self.RSC[:, 18:20], 1.0)
        self.dma("g", bass.AP(self.WBa, 0, [[8192, 128], [1024, 8], [1, 1024]]),
                 bass.AP(H[f"w1_{f}"], 0, [[1024, 128], [131072, 8], [1, 1024]]))
        self.dma("g", bass.AP(self.WBb, 0, [[8192, 128], [1024, 8], [1, 1024]]),
                 bass.AP(H[f"w2_{f}"], 0, [[1024, 128], [131072, 8], [1, 1024]]))
        self.dma("g", self.ROWS[0:1, :], H[f"b1_{f}"][:, :])
        self.dma("g", self.B2C[0:16, :], H[f"b2_{f}"][:, :])
        return self.g_gd()

    def pC1(self):
        H = self.h
        self.memset("v", self.OHT[:, :], 0.0)
        wg = self._load_fff_weights("q")
        self.dma("g", bass.AP(self.NWT, 0, [[360, 128], [45, 8], [1, 45]]),
                 bass.AP(H["node_wt"], 0, [[45, 128], [45 * 128, 8], [1, 45]]))
        self.dma("g", self.NBR[0:1, 0:45], H["node_b"][:, :])
        lnb = self.LEB
        sc_f32 = self.LEB[:, 2048:4096].bitcast(F32)
        xs_free = [None, None]
        lnb_free = ((self.st, 0),)
        for s_ in range(NSLAB):
            p2 = s_ % 2
            xdma = [None, None]
            for u in range(2):
                w = [xs_free[u]] if xs_free[u] else []
                if s_ == 0 and u == 0:
                    w.append(wg)
                self.dma("s", bass.AP(self.XS, u * 1024, [[2048, 128], [1, 1024]]),
                         bass.AP(H["x1"], s_ * SLAB * DM + u * 131072,
                                 [[1024, 128], [1, 1024]]),
                         w=w)
                xdma[u] = (self.dq2, self.dq2.n)
            for u in range(2):
                self.ln_tile(u * 1024, u, lnb[:, u * 1024:(u + 1) * 1024], sc_f32,
                             w_x=(xdma[u],), w_apply=lnb_free)
                xs_free[u] = self.g_s()
            self.transpose_slab(lnb, 0, w_first=(self.g_v(),))
            lnb_free = (self.g_t(),)
            tev = self.g_s()
            self.dma("y", bass.AP(H["xlnT"], s_ * 262144, [[2048, 128], [1, 2048]]),
                     bass.AP(self.TSL, 0, [[4096, 128], [1, 2048]]),
                     w=(tev,))
            self.fff("q", s_, p2, "qT")

    def pC1b(self):
        H = self.h
        wg = self._load_fff_weights("k")
        tsl_free = {}
        for s_ in range(NSLAB):
            pr = s_ % 2
            w = [tsl_free[pr]] if pr in tsl_free else []
            if s_ == 0:
                w.append(wg)
            self.dma("y", bass.AP(self.TSL, pr * 2048, [[4096, 128], [1, 2048]]),
                     bass.AP(H["xlnT"], s_ * 262144, [[2048, 128], [1, 2048]]), w=w)
            tdma = self.g_y()
            tsl_free[pr] = self.fff("k", s_, pr, "kT", tbase=pr * 2048,
                                    w_tsl=(tdma,))

    def pC2(self):
        H = self.h
        wg = self._load_fff_weights("v")
        tsl_free = {}
        for s_ in range(NSLAB):
            pr = s_ % 2
            w = [tsl_free[pr]] if pr in tsl_free else []
            if s_ == 0:
                w.append(wg)
            self.dma("y", bass.AP(self.TSL, pr * 2048, [[4096, 128], [1, 2048]]),
                     bass.AP(H["xlnT"], s_ * 262144, [[2048, 128], [1, 2048]]), w=w)
            tdma = self.g_y()
            tsl_free[pr] = self.fff("v", s_, pr, "v", tbase=pr * 2048,
                                    w_tsl=(tdma,))

    def pD(self):
        H = self.h
        QSQ = self.SM1
        for piece in range(8):
            for c in range(8):
                self.tt("v", QSQ[:, 0:512],
                        self.KT[:, c * S + piece * 512:c * S + (piece + 1) * 512],
                        self.KT[:, c * S + piece * 512:c * S + (piece + 1) * 512],
                        ALU.mult, w=(self.g_t(),))
                self.mm(self.PSA[0:1, 0:512], self.ONESC[:, 0:1], QSQ[:, 0:512],
                        start=(c == 0), stop=(c == 7), w=(self.g_v(),))
            self.act(self.SROW[0:1, SR_STAGE:SR_STAGE + 512], self.PSA[0:1, 0:512],
                     AF.Sqrt, w=(self.g_t(),))
            if piece == 0:
                self.vred(self.SROW[0:1, SR_KMAX:SR_KMAX + 1],
                          self.SROW[0:1, SR_STAGE:SR_STAGE + 512], ALU.max,
                          w=(self.g_s(),))
            else:
                self.vred(self.SROW[0:1, SR_KMAX + 1:SR_KMAX + 2],
                          self.SROW[0:1, SR_STAGE:SR_STAGE + 512], ALU.max,
                          w=(self.g_s(),))
                self.tt("v", self.SROW[0:1, SR_KMAX:SR_KMAX + 1],
                        self.SROW[0:1, SR_KMAX:SR_KMAX + 1],
                        self.SROW[0:1, SR_KMAX + 1:SR_KMAX + 2], ALU.max)
        self.ts("v", self.SROW[0:1, SR_KNEG:SR_KNEG + 1],
                self.SROW[0:1, SR_KMAX:SR_KMAX + 1], -1.0, None, ALU.mult)
        kstar = self.g_v()
        exp_free = [None, None]
        leb_free = [None, None]
        qts_free = {}
        xs_free = None
        xo_gate = None
        for nb in range(16):
            p = nb % 2
            self.dma("y", bass.AP(self.TSL, p * 2048, [[4096, 128], [1, 2048]]),
                     bass.AP(H["qTs"], nb * 262144, [[2048, 128], [1, 2048]]),
                     w=[qts_free[p]] if p in qts_free else [])
            self.dma("y", bass.AP(self.XS, 0, [[2048, 128], [1024, 2], [1, 1024]]),
                     bass.AP(H["x1"], nb * 256 * DM, [[1024, 128], [131072, 2], [1, 1024]]),
                     w=[xs_free] if xs_free else [])
            qdma = self.g_y()
            self.tt("v", QSQ[:, 0:2048], self.TSL[:, p * 2048:p * 2048 + 2048],
                    self.TSL[:, p * 2048:p * 2048 + 2048], ALU.mult, w=(qdma, kstar))
            qsqv = self.g_v()
            for c in range(8):
                self.mm(self.PSA[0:1, 1280:1536], self.ONESC[:, 0:1],
                        QSQ[:, c * 256:(c + 1) * 256], start=(c == 0), stop=(c == 7),
                        w=(qsqv, self.g_s()) if c == 0 else ())
            self.act(self.SROW[0:1, SR_STAGE:SR_STAGE + 256], self.PSA[0:1, 1280:1536],
                     AF.Sqrt, w=(self.g_t(),))
            self.ts("v", self.ROWS[0:1, 0:256], self.SROW[0:1, SR_STAGE:SR_STAGE + 256],
                    self.SROW[0:1, SR_KNEG:SR_KNEG + 1], None, ALU.mult, w=(self.g_s(),))
            negb = self.g_v()
            expd_p = [None, None]
            for pair in range(17):
                if pair < 16:
                    for mi in range(2):
                        m = pair * 2 + mi
                        pcol = (pair % 2) * 512 + mi * 256
                        wlist = []
                        if mi == 0 and exp_free[pair % 2] is not None:
                            wlist.append(exp_free[pair % 2])
                        if m == 0:
                            wlist.append(negb)
                        for kc in range(8):
                            self.mm(self.PSA[:, pcol:pcol + 256],
                                    self.KT[:, kc * S + m * 128:kc * S + (m + 1) * 128],
                                    self.TSL[:, p * 2048 + kc * 256:
                                             p * 2048 + (kc + 1) * 256],
                                    start=(kc == 0), stop=False,
                                    w=wlist if kc == 0 else ())
                        self.mm(self.PSA[:, pcol:pcol + 256], self.ONESR[:, 0:128],
                                self.ROWS[:, 0:256], start=False, stop=True)
                    sc_done = self.g_t()
                    wexp = [sc_done]
                    if leb_free[pair % 2] is not None:
                        wexp.append(leb_free[pair % 2])
                    self.act(self.LEB[:, (pair % 2) * 512:(pair % 2) * 512 + 512],
                             self.PSA[:, (pair % 2) * 512:(pair % 2) * 512 + 512],
                             AF.Exp, scale=1.0 / 32.0, w=wexp)
                    expd_p[pair % 2] = self.g_s()
                    exp_free[pair % 2] = expd_p[pair % 2]
                if pair >= 1:
                    pp = pair - 1
                    expd = expd_p[pp % 2]
                    for mi in range(2):
                        mm_ = pp * 2 + mi
                        lb = (pp % 2) * 512 + mi * 256
                        for ns in range(2):
                            wl = []
                            if mi == 0 and ns == 0:
                                wl.append(expd)
                                if mm_ == 0 and xo_gate is not None:
                                    wl.append(self.g_v())
                            for half in range(2):
                                self.mm(self.PSB[:, ns * 1024 + half * 512:
                                                 ns * 1024 + (half + 1) * 512],
                                        self.LEB[:, lb + ns * 128:lb + ns * 128 + 128],
                                        self.VT[:, mm_ * 1024 + half * 512:
                                                mm_ * 1024 + (half + 1) * 512],
                                        start=(mm_ == 0), stop=(mm_ == 31),
                                        w=wl if half == 0 else ())
                        self.mm(self.PSA[0:1, 1024:1280], self.ONESC[:, 0:1],
                                self.LEB[:, lb:lb + 256],
                                start=(mm_ == 0), stop=(mm_ == 31))
                    leb_free[pp % 2] = self.g_t()
            ctx_done = self.g_t()
            qts_free[p] = ctx_done
            self.act(self.ROWS[0:1, 256:512], self.PSA[0:1, 1024:1280], AF.Copy,
                     w=(ctx_done,))
            drow = self.g_s()
            for u in range(2):
                self.mm(self.PSA[0:128, 1536 + u:1537 + u],
                        self.ROWS[:, 256 + u * 128:256 + (u + 1) * 128],
                        self.ONESR[:, 0:1], w=(drow,) if u == 0 else ())
            self.vrecip(self.SCL[:, 2:4], self.PSA[:, 1536:1538], w=(self.g_t(),))
            for u in range(2):
                wl = []
                if u == 0 and xo_gate is not None:
                    wl.append(xo_gate)
                self.stt("v", self.XO[:, u * 1024:(u + 1) * 1024],
                         self.PSB[:, u * 1024:(u + 1) * 1024], self.SCL[:, 2 + u:3 + u],
                         self.XS[:, u * 1024:(u + 1) * 1024],
                         ALU.mult, ALU.add, w=wl)
            xs_free = self.g_v()
            self.dma("y", bass.AP(H["x2"], nb * 256 * DM, [[1024, 128], [131072, 2], [1, 1024]]),
                     bass.AP(self.XO, 0, [[2048, 128], [1024, 2], [1, 1024]]),
                     w=(self.g_v(),))
            xo_gate = self.g_y()

    def pE(self):
        H = self.h
        KT = self.KT
        self.memset("v", KT[:, KE_CQ:KE_CQ + 4096], 0.0, w=(self.g_t(),))
        self.memset("v", KT[:, KE_KV:KE_KV + S], 0.0)
        self.memset("v", KT[:, KE_WQC:KE_WQC + 512], 0.0)
        self.memset("v", KT[:, KE_BQC:KE_BQC + 1024], 0.0)
        self.memset("v", KT[:, KE_BFIN:KE_BFIN + 1024], 0.0)
        self.memset("v", self.VT[:, 2048:6144], 0.0)
        self.dma("g", bass.AP(self.WBa, 0, [[8192, 128], [1024, 8], [1, 1024]]),
                 bass.AP(H["w_o"], 0, [[1024, 128], [131072, 8], [1, 1024]]))
        for nm, col in (("w_uqT", KE_UQT), ("w_ukT", KE_UKT), ("w_uvT", KE_UVT)):
            self.dma("g", bass.AP(KT, col, [[32768, 128], [32, 8], [1, 32]]),
                     bass.AP(H[nm], 0, [[32, 128], [4096, 8], [1, 32]]))
        self.dma("g", KT[0:32, KE_WQR:KE_WQR + 256], H["w_qr"][:, :])
        self.dma("g", bass.AP(self.STAT, 0, [[16, 128], [1, 8]]),
                 bass.AP(H["b_uq"], 0, [[1, 128], [128, 8]]),
                 allow_slow_non_contiguous=True)
        self.dma("g", bass.AP(self.STAT, 8, [[16, 128], [1, 8]]),
                 bass.AP(H["b_uv"], 0, [[1, 128], [128, 8]]),
                 allow_slow_non_contiguous=True)
        self.dma("g", self.SROW[0:1, SR_STAGE:SR_STAGE + 256], H["b_qr"][:, :])
        self.dma("g", self.SM1[0:1, 1024:2048], H["b_o"][:, :])
        gd = self.g_gd()
        self.vcopy(self.SM1[:, 0:16], self.STAT[:, 0:16], w=(gd,))
        self.vcopy(bass.AP(KT, KE_BQC + 32, [[32768, 1], [64, 8], [1, 32]]),
                   bass.AP(self.SROW, SR_STAGE, [[576, 1], [32, 8], [1, 32]]))
        for h8 in range(8):
            uq = KT[:, KE_UQT + h8 * 32:KE_UQT + (h8 + 1) * 32]
            uk = KT[:, KE_UKT + h8 * 32:KE_UKT + (h8 + 1) * 32]
            self.mm(self.PSA[0:32, h8 * 64:h8 * 64 + 32], uq, uk,
                    w=(gd,) if h8 == 0 else ())
            self.mm(self.PSA[0:1, h8 * 64 + 32:h8 * 64 + 64],
                    self.SM1[:, h8:h8 + 1], uk,
                    w=(self.g_v(),) if h8 == 0 else ())
        tg = self.g_t()
        for h8 in range(8):
            self.vcopy(KT[0:32, KE_WQC + h8 * 64:KE_WQC + h8 * 64 + 32],
                       self.PSA[0:32, h8 * 64:h8 * 64 + 32], w=(tg,) if h8 == 0 else ())
            self.vcopy(KT[0:1, KE_BQC + h8 * 64:KE_BQC + h8 * 64 + 32],
                       self.PSA[0:1, h8 * 64 + 32:h8 * 64 + 64])
            self.vcopy(KT[0:32, KE_WQC + h8 * 64 + 32:KE_WQC + (h8 + 1) * 64],
                       KT[0:32, KE_WQR + h8 * 32:KE_WQR + (h8 + 1) * 32])
        vg = self.g_v()
        for h8 in range(8):
            for half in range(2):
                self.mm(self.PSB[0:32, half * 512:(half + 1) * 512],
                        KT[:, KE_UVT + h8 * 32:KE_UVT + (h8 + 1) * 32],
                        self.WBa[:, h8 * 1024 + half * 512:h8 * 1024 + (half + 1) * 512],
                        w=(vg,) if h8 == 0 and half == 0 else
                        (self.g_s(),) if half == 0 else ())
            self.act(KT[(h8 % 4) * 32:(h8 % 4) * 32 + 32,
                        KE_UVO + (h8 // 4) * 1024:KE_UVO + (h8 // 4 + 1) * 1024],
                     self.PSB[0:32, 0:1024], AF.Copy, w=(self.g_t(),))
        for kc in range(8):
            for half in range(2):
                self.mm(self.PSA[0:1, 1024 + half * 512:1024 + (half + 1) * 512],
                        self.SM1[:, 8 + kc:9 + kc],
                        self.WBa[:, kc * 1024 + half * 512:kc * 1024 + (half + 1) * 512],
                        start=(kc == 0), stop=False,
                        w=(self.g_v(), gd) if kc == 0 and half == 0 else ())
        for half in range(2):
            self.mm(self.PSA[0:1, 1024 + half * 512:1024 + (half + 1) * 512],
                    self.ONESR[0:1, 0:1],
                    self.SM1[0:1, 1024 + half * 512:1024 + (half + 1) * 512],
                    start=False, stop=True)
        self.vcopy(KT[0:1, KE_BFIN:KE_BFIN + 1024], self.PSA[0:1, 1024:2048],
                   w=(self.g_t(),))
        self.dma("g", bass.AP(self.WBb, 0, [[8192, 128], [32, 8], [1, 32]]),
                 bass.AP(H["w_dq"], 0, [[32, 128], [4096, 8], [1, 32]]))
        self.dma("g", bass.AP(self.WBb, 256, [[8192, 128], [64, 8], [1, 64]]),
                 bass.AP(H["w_kv"], 0, [[64, 128], [8192, 8], [1, 64]]))
        self.dma("g", self.ROWS[0:1, 0:32], H["b_dq"][:, :])
        self.dma("g", self.ROWS[0:1, 32:96], H["b_kv"][:, :])
        wg = self.g_gd()
        self.memset("v", KT[64:65, KE_KV:KE_KV + S], 1.0, w=(self.g_t(),))
        EIDF = KT[:, 14848:15104].bitcast(F32)
        self.vcopy(EIDF, self.IDN[:, :])
        onk = self.g_v()
        lnb = self.LEB
        sc_f32 = self.LEB[:, 2048:4096].bitcast(F32)
        xs_free = None
        lnb_free = ((self.st, 0),)
        for s_ in range(NSLAB):
            w = [xs_free] if xs_free else []
            if s_ == 0:
                w.append(wg)
            self.dma("y", bass.AP(self.XS, 0, [[2048, 128], [1024, 2], [1, 1024]]),
                     bass.AP(H["x2"], s_ * SLAB * DM, [[1024, 128], [131072, 2], [1, 1024]]),
                     w=w)
            xdma = self.g_y()
            for u in range(2):
                self.ln_tile(u * 1024, u, lnb[:, u * 1024:(u + 1) * 1024], sc_f32,
                             w_x=(xdma,) if u == 0 else (), w_apply=lnb_free)
            xs_free = self.g_s()
            self.transpose_slab(lnb, 0, w_first=(self.g_v(),))
            lnb_free = (self.g_t(),)
            tev = self.g_s()
            for kc in range(8):
                self.mm(self.PSA[0:32, 0:256], self.WBb[:, kc * 32:(kc + 1) * 32],
                        self.TSL[:, kc * 256:(kc + 1) * 256],
                        start=(kc == 0), stop=False,
                        w=(tev, self.g_v()) if kc == 0 else ())
            self.mm(self.PSA[0:32, 0:256], self.ROWS[:, 0:32], self.ONESR[:, 0:256],
                    start=False, stop=True)
            for kc in range(8):
                self.mm(self.PSA[0:64, 256:512],
                        self.WBb[:, 256 + kc * 64:256 + (kc + 1) * 64],
                        self.TSL[:, kc * 256:(kc + 1) * 256],
                        start=(kc == 0), stop=False)
            self.mm(self.PSA[0:64, 256:512], self.ROWS[:, 32:96], self.ONESR[:, 0:256],
                    start=False, stop=True)
            self.vcopy(KT[0:32, KE_CQ + s_ * 256:KE_CQ + (s_ + 1) * 256],
                       self.PSA[0:32, 0:256], w=(self.g_t(),))
            self.vcopy(KT[0:64, KE_KV + s_ * 256:KE_KV + (s_ + 1) * 256],
                       self.PSA[0:64, 256:512])
        prj = self.g_v()
        self.memset("v", bass.AP(self.VT, 32, [[32768, 128], [33, 32], [1, 1]]), 1.0,
                    w=(self.g_t(),))
        for mq4 in range(8):
            for k4 in range(4):
                m = mq4 * 4 + k4
                w = ()
                if m == 0:
                    w = (prj, onk, self.g_v())
                elif k4 == 0:
                    w = (self.g_v(),)
                self.mm(self.PSB[:, k4 * 128:k4 * 128 + 65],
                        KT[0:65, KE_KV + m * 128:KE_KV + (m + 1) * 128],
                        self.IDN[0:65, 0:65], w=w)
            self.vcopy(bass.AP(self.VT, mq4 * 4 * 33, [[32768, 128], [33, 4], [1, 32]]),
                       bass.AP(self.PSB, 0, [[2048, 128], [128, 4], [1, 32]]),
                       w=(self.g_t(),))
        QSQ = self.SM1
        for piece in range(8):
            self.tt("v", QSQ[0:64, 0:512],
                    KT[0:64, KE_KV + piece * 512:KE_KV + (piece + 1) * 512],
                    KT[0:64, KE_KV + piece * 512:KE_KV + (piece + 1) * 512], ALU.mult)
            self.mm(self.PSA[0:1, 0:512], self.ONESC[0:64, 0:1], QSQ[0:64, 0:512],
                    w=(self.g_v(),))
            self.act(self.SROW[0:1, SR_STAGE:SR_STAGE + 512], self.PSA[0:1, 0:512],
                     AF.Sqrt, w=(self.g_t(),))
            if piece == 0:
                self.vred(self.SROW[0:1, SR_KMAX:SR_KMAX + 1],
                          self.SROW[0:1, SR_STAGE:SR_STAGE + 512], ALU.max,
                          w=(self.g_s(),))
            else:
                self.vred(self.SROW[0:1, SR_KMAX + 1:SR_KMAX + 2],
                          self.SROW[0:1, SR_STAGE:SR_STAGE + 512], ALU.max,
                          w=(self.g_s(),))
                self.tt("v", self.SROW[0:1, SR_KMAX:SR_KMAX + 1],
                        self.SROW[0:1, SR_KMAX:SR_KMAX + 1],
                        self.SROW[0:1, SR_KMAX + 1:SR_KMAX + 2], ALU.max)
        self.ts("v", self.SROW[0:1, SR_KNEG:SR_KNEG + 1],
                self.SROW[0:1, SR_KMAX:SR_KMAX + 1], -1.0, None, ALU.mult)
        kstar = self.g_v()
        scale = 1.0 / math.sqrt(160.0)
        exp_free = [None, None]
        leb_free = [None, None, None, None]
        expd_all = [None, None, None, None]
        norm_done = [None, None]
        rc_last = None
        xs_gate = None
        xo_gate = None
        xs2_gate = None
        QA = self.VT
        BCV = KT[0:32, 15360:16384].bitcast(F32)
        qa_free = [None, None]
        XIN = [KT[:, 16384 + k * 4096:16384 + k * 4096 + 2048].bitcast(F32)
               for k in range(4)]
        xin_free = [None, None, None, None]
        for nb in range(8):
            xin_dma = [None] * 4
            for k in range(4):
                wdma_l = [xin_free[k]] if xin_free[k] else []
                self.dma("s", XIN[k],
                         bass.AP(H["x2"], (nb * 512 + k * 128) * DM,
                                 [[1024, 128], [1, 1024]]),
                         w=wdma_l)
                xin_dma[k] = (self.dq2, self.dq2.n)
            SQR = self.TSL[0:1, 1024:3072].bitcast(F32)
            for hp in range(4):
                h8 = 2 * hp
                pq = 1024 + (hp % 2) * 512
                wq = ()
                if hp == 0:
                    wq = (kstar,) + ((exp_free[1],) if exp_free[1] else ())
                    if rc_last is not None:
                        wq = wq + (rc_last,)
                elif hp == 1:
                    wq = (kstar,)
                elif qa_free[hp % 2] is not None:
                    wq = (qa_free[hp % 2],)
                self.mm(self.PSA[0:128, pq:pq + 512],
                        KT[:, KE_WQC + h8 * 64:KE_WQC + (h8 + 2) * 64],
                        KT[:, KE_CQ + nb * 512:KE_CQ + (nb + 1) * 512],
                        start=True, stop=False, w=wq)
                self.mm(self.PSA[0:128, pq:pq + 512],
                        KT[:, KE_BQC + h8 * 64:KE_BQC + (h8 + 2) * 64],
                        self.ONESR[:, 0:512], start=False, stop=True)
                tq = self.g_t()
                self.act(QSQ[0:128, 0:512], self.PSA[0:128, pq:pq + 512], AF.Square,
                         w=(tq,))
                for g2 in range(2):
                    self.act(QA[0:64, 2048 + (h8 + g2) * 512:
                                2048 + (h8 + g2 + 1) * 512],
                             self.PSA[g2 * 64:(g2 + 1) * 64, pq:pq + 512], AF.Copy)
                sq = self.g_s()
                qa_free[hp % 2] = sq
                for g2 in range(2):
                    self.mm(self.PSB[0:1, g2 * 512:(g2 + 1) * 512],
                            self.ONESC[g2 * 64:(g2 + 1) * 64, 0:1],
                            QSQ[g2 * 64:(g2 + 1) * 64, 0:512],
                            w=(sq, rc_last) if (hp == 0 and g2 == 0 and
                                                rc_last is not None)
                            else (sq,) if g2 == 0 else ())
                self.act(SQR, self.PSB[0:1, 0:1024], AF.Sqrt, w=(self.g_t(),))
                self.ts("v", QA[64:65, 2048 + h8 * 512:2048 + (h8 + 2) * 512],
                        SQR, self.SROW[0:1, SR_KNEG:SR_KNEG + 1], None, ALU.mult,
                        w=(self.g_s(),))
            qaug = self.g_v()

            def dance_stages(hprev, ctxh_gate):
                hv1 = hprev % 2
                st = {}

                def st0():
                    self.act(self.SROW[0:1, SR_STAGE:SR_STAGE + 512],
                             self.PSB[32:33, hv1 * 512:hv1 * 512 + 512], AF.Copy,
                             w=(ctxh_gate,))
                    st["drow"] = self.g_s()

                def st1():
                    for u in range(4):
                        wl = ()
                        if u == 0:
                            wl = (st["drow"],)
                            if rc_last is not None:
                                wl = wl + (rc_last,)
                        self.mm(self.PSB[0:128, 1540 + u:1541 + u],
                                self.SROW[0:1, SR_STAGE + u * 128:
                                          SR_STAGE + (u + 1) * 128],
                                self.SROW[0:1, SR_ONES:SR_ONES + 1], w=wl)
                    self.vrecip(self.FQ4[:, 0:4], self.PSB[:, 1540:1544],
                                w=(self.g_t(),))
                    st["rc"] = self.g_v()

                def st2():
                    for u in range(4):
                        self.mm(self.PSB[0:1, 1024 + u * 128:1024 + (u + 1) * 128],
                                self.FQ4[:, u:u + 1], EIDF,
                                w=(st["rc"],) if u == 0 else ())
                    self.act(self.SROW[0:1, SR_STAGE:SR_STAGE + 512],
                             self.PSB[0:1, 1024:1536], AF.Copy, w=(self.g_t(),))
                    st["rrow"] = self.g_s()

                def st3():
                    self.mm(self.PSB[0:32, 1024:1536],
                            self.SROW[0:1, SR_ONES:SR_ONES + 32],
                            self.SROW[0:1, SR_STAGE:SR_STAGE + 512], w=(st["rrow"],))
                    self.vcopy(BCV, self.PSB[0:32, 1024:1536], w=(self.g_t(),))
                    self.tt("v", self.TSL[(hprev % 4) * 32:(hprev % 4) * 32 + 32,
                                          (hprev // 4) * 512:(hprev // 4) * 512 + 512],
                            self.PSB[0:32, hv1 * 512:hv1 * 512 + 512], BCV, ALU.mult)
                    norm_done[hv1] = self.g_v()
                return [st0, st1, st2, st3], st

            pend = None
            ctxh = None
            for h8 in range(8):
                hv = h8 % 2
                if ctxh is not None:
                    pend = dance_stages(h8 - 1, ctxh)
                for pair in range(17):
                    if pair < 16:
                        pc = (pair % 2) * 1024
                        sl = pair % 4
                        wl = []
                        if exp_free[pair % 2] is not None:
                            wl.append(exp_free[pair % 2])
                        if h8 == 0 and pair <= 1:
                            wl.append(qaug)
                        for mi in range(2):
                            m = pair * 2 + mi
                            self.mm(self.PSA[:, pc + mi * 512:pc + (mi + 1) * 512],
                                    KT[:, KE_KV + m * 128:KE_KV + (m + 1) * 128],
                                    QA[:, 2048 + h8 * 512:2048 + (h8 + 1) * 512],
                                    w=wl if mi == 0 else ())
                        sc = self.g_t()
                        wexp = [sc]
                        if leb_free[sl] is not None:
                            wexp.append(leb_free[sl])
                        self.act(self.LEB[:, sl * 1024:(sl + 1) * 1024],
                                 self.PSA[:, pc:pc + 1024], AF.Exp, scale=scale, w=wexp)
                        expd_all[sl] = self.g_s()
                        exp_free[pair % 2] = expd_all[sl]
                    if pair >= 1:
                        pp = pair - 1
                        psl = pp % 4
                        expd = expd_all[psl]
                        for mi in range(2):
                            m = pp * 2 + mi
                            wl = []
                            if mi == 0:
                                wl.append(expd)
                                if pp == 0 and norm_done[hv] is not None:
                                    wl.append(norm_done[hv])
                            self.mm(self.PSB[0:33, hv * 512:hv * 512 + 512],
                                    bass.AP(self.VT, m * 33, [[32768, 128], [1, 33]]),
                                    self.LEB[:, psl * 1024 + mi * 512:
                                             psl * 1024 + (mi + 1) * 512],
                                    start=(m == 0), stop=(m == 31), w=wl)
                        leb_free[psl] = self.g_t()
                    if pend is not None and pair < 4:
                        pend[0][pair]()
                        if pair == 1:
                            rc_last = pend[1]["rc"]
                ctxh = self.g_t()
            tail = dance_stages(7, ctxh)
            for k_, f_ in enumerate(tail[0]):
                f_()
                if k_ == 1:
                    rc_last = tail[1]["rc"]
            ctxn = self.g_v()
            for ns in range(4):
                obuf = self.XO if ns < 2 else self.XS
                wl = [ctxn, xin_dma[ns]] if ns == 0 else [self.g_v(), xin_dma[ns]]
                if ns == 0 and xo_gate is not None:
                    wl.append(xo_gate)
                if ns == 2 and xs2_gate is not None:
                    wl.append(xs2_gate)
                for j in range(2):
                    for half in range(2):
                        self.mm(self.PSB[:, 1024 + half * 512:1024 + (half + 1) * 512],
                                self.TSL[:, j * 512 + ns * 128:j * 512 + ns * 128 + 128],
                                KT[:, KE_UVO + j * 1024 + half * 512:
                                   KE_UVO + j * 1024 + (half + 1) * 512],
                                start=(j == 0), stop=False,
                                w=wl if (j == 0 and half == 0) else ())
                for half in range(2):
                    self.mm(self.PSB[:, 1024 + half * 512:1024 + (half + 1) * 512],
                            self.ONESR[:, 0:128],
                            KT[:, KE_BFIN + half * 512:KE_BFIN + (half + 1) * 512],
                            start=False, stop=True)
                self.stt("v", obuf[:, (ns % 2) * 1024:(ns % 2 + 1) * 1024],
                         self.PSB[:, 1024:2048], 1.0, XIN[ns],
                         ALU.mult, ALU.add, w=(self.g_t(),))
                xin_free[ns] = self.g_v()
                if ns % 2 == 1:
                    self.dma("y", bass.AP(H["x3"], (nb * 512 + (ns - 1) * 128) * DM,
                                          [[1024, 128], [131072, 2], [1, 1024]]),
                             bass.AP(obuf, 0, [[2048, 128], [1024, 2], [1, 1024]]),
                             w=(self.g_v(),))
                    if ns == 1:
                        xo_gate = self.g_y()
                    else:
                        xs2_gate = self.g_y()

    def quant_tile(self, xin, xout, dqp, dqn, perblock=False, w0=(), w_out=()):
        e = "v"
        F1 = self.SM1[:, 0:2048].bitcast(F32)
        F2 = self.LEB[:, 0:2048].bitcast(F32)
        F3 = self.KT[:, 0:2048].bitcast(F32)
        F4 = self.FQ4
        AM = F4[:, 0:32]
        EB = F4[:, 32:64]
        RS = F4[:, 64:96]
        SC = F4[:, 96:128]
        x3d = xin.rearrange("p (a b) -> p a b", b=32)
        self.vred(AM, x3d, ALU.max, apply_absolute_value=True, w=w0)
        self.ts(e, EB.bitcast(I32), AM.bitcast(I32), 23, None, ALU.logical_shift_right)
        self.ts(e, EB.bitcast(I32), EB.bitcast(I32), 1, None, ALU.max)
        self.ts(e, RS.bitcast(I32), EB.bitcast(I32), -1, 256, ALU.mult, ALU.add)
        self.ts(e, RS.bitcast(I32), RS.bitcast(I32), 23, None, ALU.logical_shift_left)
        self.ts(e, SC.bitcast(I32), EB.bitcast(I32), -3, None, ALU.add)
        self.ts(e, SC.bitcast(I32), SC.bitcast(I32), 23, None, ALU.logical_shift_left)
        if perblock:
            self.tt(e, RS, RS, dqn, ALU.mult)
            self.tt(e, SC, SC, dqp, ALU.mult)
        else:
            self.ts(e, RS, RS, dqn, None, ALU.mult)
            self.ts(e, SC, SC, dqp, None, ALU.mult)
        rs_b = bass.AP(self.FQ4, 64, [[128, 128], [1, 32], [0, 32]])
        sc_b = bass.AP(self.FQ4, 96, [[128, 128], [1, 32], [0, 32]])
        self.tt(e, F1.rearrange("p (a b) -> p a b", b=32), x3d, rs_b, ALU.mult)
        v1 = self.g_v()
        self.act(F2, F1, AF.Abs, w=(v1,))
        s1 = self.g_s()
        self.ts(e, F2, F2, 6.0, None, ALU.min, w=(s1,))
        self.ts(e, F3.bitcast(I32), F2.bitcast(I32), 23, None, ALU.logical_shift_right)
        self.ts(e, F3.bitcast(I32), F3.bitcast(I32), 127, 129, ALU.max, ALU.min)
        self.ts(e, F1.bitcast(I32), F3.bitcast(I32), -1, 255, ALU.mult, ALU.add)
        self.ts(e, F1.bitcast(I32), F1.bitcast(I32), 23, None, ALU.logical_shift_left)
        self.tt(e, F2, F2, F1, ALU.mult)
        self.ts(e, F2, F2, 8388608.0, None, ALU.add)
        self.ts(e, F2, F2, 8388608.0, None, ALU.subtract)
        self.ts(e, F3.bitcast(I32), F3.bitcast(I32), 23, None, ALU.logical_shift_left)
        self.ts(e, F1.bitcast(I32), xin.bitcast(I32), -2147483648, None,
                ALU.bitwise_and)
        self.tt(e, F1.bitcast(I32).rearrange("p (a b) -> p a b", b=32),
                F1.bitcast(I32).rearrange("p (a b) -> p a b", b=32),
                sc_b.bitcast(I32), ALU.bitwise_or)
        self.tt(e, F2, F2, F3, ALU.mult)
        self.tt(e, xout, F2, F1, ALU.mult, w=w_out)

    def pW(self):
        H = self.h
        XQB = self.LEB[:, 2048:4096]
        DQW = self.RSC
        for oc in range(8):
            self.dma("y", self.XO[:, 0:1024],
                     bass.AP(H["w_f"], oc * 128 * DM, [[1024, 128], [1, 1024]]),
                     w=(self.g_v(), self.g_t()))
            self.dma("y", DQW[:, 0:32],
                     bass.AP(H["delta_w"], oc * 128 * 32, [[32, 128], [1, 32]]))
            wdma = self.g_y()
            self.act(DQW[:, 0:32], DQW[:, 0:32], AF.Tanh, scale=4.0, w=(wdma,))
            self.act(DQW[:, 32:64], DQW[:, 0:32], AF.Exp, scale=-0.5 * LN2)
            self.act(DQW[:, 0:32], DQW[:, 0:32], AF.Exp, scale=0.5 * LN2)
            dq = self.g_s()
            self.quant_tile(self.XO[:, 0:1024], XQB[:, 0:1024],
                            DQW[:, 0:32], DQW[:, 32:64],
                            perblock=True, w0=(dq, wdma), w_out=(self.g_t(),))
            qd = self.g_v()
            for dc in range(8):
                w = ()
                if dc == 0:
                    w = (qd, self.g_s())
                elif dc == 4:
                    w = (self.g_s(),)
                self.mm(self.PSA[:, 1024 + (dc % 4) * 128:1024 + (dc % 4) * 128 + 128],
                        XQB[:, dc * 128:(dc + 1) * 128], self.IDN[:, :], w=w)
                if dc % 4 == 3:
                    self.act(bass.AP(self.WBa, (dc - 3) * 1024 + oc * 128,
                                     [[8192, 128], [1024, 4], [1, 128]]),
                             self.PSA[:, 1024:1536], AF.Copy, w=(self.g_t(),))
        self.dma("g", self.ROWS[0:1, :], H["b_f"][:, :])

    def pF(self):
        H = self.h
        dqp = self.SCL[:, 0:1]
        dqn = self.SCL[:, 1:2]
        XQB = self.LEB[:, 2048:4096]
        lnf = self.VT[:, 0:2048].bitcast(F32)
        lnscr = self.KT[:, 2048:4096].bitcast(F32)
        xs_free = None
        xo_gate = None
        rowg = self.g_gd()
        for s_ in range(NSLAB):
            p2 = s_ % 2
            w = [xs_free] if xs_free else []
            if s_ == 0:
                w.append(rowg)
            self.dma("y", bass.AP(self.XS, 0, [[2048, 128], [1024, 2], [1, 1024]]),
                     bass.AP(H["x3"], s_ * SLAB * DM, [[1024, 128], [131072, 2], [1, 1024]]),
                     w=w)
            xdma = self.g_y()
            for u in range(2):
                self.ln_tile(u * 1024, u, lnf,
                             lnscr, w_x=(xdma, self.g_v()) if u == 0 else (self.g_v(),))
                lng = self.g_s()
                self.quant_tile(lnf, XQB[:, u * 1024:(u + 1) * 1024], dqp, dqn,
                                w0=(lng, self.g_t()), w_out=(self.g_t(),))
                qd = self.g_v()
                for c in range(8):
                    w2 = ()
                    if c == 0:
                        w2 = (qd, self.g_s())
                    elif c == 4:
                        w2 = (self.g_s(),)
                    self.mm(self.PSA[:, 1024 + (c % 4) * 128:1024 + (c % 4) * 128 + 128],
                            XQB[:, u * 1024 + c * 128:u * 1024 + (c + 1) * 128],
                            self.IDN[:, :], w=w2)
                    if c % 4 == 3:
                        self.act(bass.AP(self.TSL, p2 * 2048 + (c - 3) * 256 + u * 128,
                                         [[4096, 128], [256, 4], [1, 128]]),
                                 self.PSA[:, 1024:1536], AF.Copy, w=(self.g_t(),))
            tq = self.g_s()
            for u in range(2):
                po = u * 1024
                for kc in range(8):
                    for half in range(2):
                        self.mm(self.PSB[:, po + half * 512:po + (half + 1) * 512],
                                self.TSL[:, p2 * 2048 + kc * 256 + u * 128:
                                         p2 * 2048 + kc * 256 + u * 128 + 128],
                                self.WBa[:, kc * 1024 + half * 512:
                                         kc * 1024 + (half + 1) * 512],
                                start=(kc == 0), stop=False,
                                w=(tq, self.g_v()) if (u == 0 and kc == 0 and half == 0)
                                else ())
                for half in range(2):
                    self.mm(self.PSB[:, po + half * 512:po + (half + 1) * 512],
                            self.ONESR[:, 0:128],
                            self.ROWS[:, half * 512:(half + 1) * 512],
                            start=False, stop=True)
                wst = [self.g_t()]
                if u == 0 and xo_gate is not None:
                    wst.append(xo_gate)
                self.stt("v", self.XO[:, u * 1024:(u + 1) * 1024],
                         self.PSB[:, po:po + 1024], 1.0,
                         self.XS[:, u * 1024:(u + 1) * 1024],
                         ALU.mult, ALU.add, w=wst)
            xs_free = self.g_v()
            self.dma("y", bass.AP(H["out"], s_ * SLAB * DM, [[1024, 128], [131072, 2], [1, 1024]]),
                     bass.AP(self.XO, 0, [[2048, 128], [1024, 2], [1, 1024]]),
                     w=(self.g_v(),))
            xo_gate = self.g_y()

    def build(self, phases):
        table = {"0": self.p0, "A1": self.pA1, "A2": self.pA2, "B": self.pB,
                 "C1": self.pC1, "C1b": self.pC1b, "C2": self.pC2, "D": self.pD,
                 "E": self.pE, "W": self.pW, "F": self.pF}
        for ph in phases:
            table[ph]()
            self.fence()
        for nm in self.dbg.get("dump_sbuf", ()):
            t = getattr(self, nm)
            self.dma("y", self.h["dbg_" + nm][:, :], t[:, :])
        self.r.op("y", None, w=((self.dy, self.dy.n), (self.dg, self.dg.n)))
        rec = self.r
        with self.nc.Block() as blk:
            @blk.tensor
            def _(eng):
                rec.replay("t", eng)

            @blk.vector
            def _(eng):
                rec.replay("v", eng)

            @blk.scalar
            def _(eng):
                rec.replay("s", eng)

            @blk.gpsimd
            def _(eng):
                rec.replay("g", eng)

            @blk.sync
            def _(eng):
                rec.replay("y", eng)


ALL_PHASES = ("0", "A1", "A2", "B", "C1", "C1b", "C2", "D", "E", "W", "F")


def build_nc(phases=ALL_PHASES, dbg=None):
    nc = bass.Bass("TRN2", target_bir_lowering=False)
    mk = MK(nc, dbg=dbg)
    mk.build(phases)
    return nc, mk


def make_inputs(inp, core):
    x = np.asarray(inp["L0_attn_hidden_states"])
    lay = host_layout(inp)
    cst = host_constants()
    m = {"x": np.ascontiguousarray(x[core], dtype=np.float32)}
    m.update(lay)
    m.update(cst)
    return m


def kernel(**inputs):
    nc, mk = build_nc()
    in_maps = [make_inputs(inputs, c) for c in range(8)]
    res = run_bass_kernel_spmd(nc, in_maps, core_ids=list(range(8)))
    out = np.stack([np.asarray(r["out"]) for r in res.results], axis=0)
    return out.astype(np.float32)
